# version 04
# baseline (speedup 1.0000x reference)
import numpy as np
from contextlib import ExitStack
import concourse.bass as bass
import concourse.mybir as mybir

F32 = mybir.dt.float32
BF16 = mybir.dt.bfloat16
AF = mybir.ActivationFunctionType
ALU = mybir.AluOpType
AX = mybir.AxisListType
NEG = -1.0e9

ENGINES = ("sync", "gpsimd", "vector", "scalar", "tensor")

class Seq:
    def __init__(self, nc):
        self.nc = nc
        self.ops = []
        self.total = 0

    def barrier(self):
        return self.total

    def op(self, engine, emit, inc=1, wait=None):
        w = self.total if wait is None else wait
        self.ops.append((engine, emit, inc, w))
        self.total += inc
        return self.total

    def dma(self, queue, out, in_, wait=None, **kw):
        return self.op(queue, lambda e: e.dma_start(out=out, in_=in_, **kw), inc=16, wait=wait)

    def build(self, block, step):
        for engname in ENGINES:
            mine = [(i, o) for i, o in enumerate(self.ops) if o[0] == engname]
            if not mine:
                continue
            def mk(mine):
                def body(eng):
                    last_wait = -1
                    for _, (_, emit, inc, w) in mine:
                        if w > 0 and w != last_wait:
                            eng.wait_ge(step, w)
                            last_wait = w
                        emit(eng).then_inc(step, inc)
                return body
            getattr(block, engname)(mk(mine))
        total = self.total
        @block.sync
        def _(sync):
            sync.wait_ge(step, total)

def ceil_div(a, b):
    return -(-a // b)


E = 1024
NH = 16
DH = 64
FF0 = 2816
NEXP = 8
FF1 = 4096
DI = 512
NS = 256
GATE = 16

def core_token_rows(T, c):
    j = c % 4
    nslot = T // 512
    rows = []
    for t in range(nslot):
        s = 128 * (4 * t + j)
        rows.append(np.arange(s, s + 128))
    return np.concatenate(rows)

def host_core_token_slice(arr_bt, T, c):
    b = c // 4
    return arr_bt[b][core_token_rows(T, c)]

def host_assemble_full(parts, T):
    B = 2
    out = None
    for c in range(8):
        p = np.asarray(parts[c])
        if out is None:
            out = np.zeros((B, T) + p.shape[1:], p.dtype)
        out[c // 4][core_token_rows(T, c)] = p
    return out

def build_l2(T):
    TC = T // 4
    NTB = TC // 128
    NTS = TC // 512 if TC >= 512 else 1
    TSZ = min(TC, 512)
    nc = bass.Bass("TRN2", target_bir_lowering=False)
    x0 = nc.dram_tensor("x0", [TC, E], F32, kind="ExternalInput").ap()
    attn = nc.dram_tensor("attn", [TC, E], F32, kind="ExternalInput").ap()
    gateT = nc.dram_tensor("gateT", [E, NEXP], F32, kind="ExternalInput").ap()
    w1T = nc.dram_tensor("w1T", [NEXP, E, FF0], F32, kind="ExternalInput").ap()
    w3T = nc.dram_tensor("w3T", [NEXP, E, FF0], F32, kind="ExternalInput").ap()
    w2T = nc.dram_tensor("w2T", [NEXP, FF0, E], F32, kind="ExternalInput").ap()
    identf = nc.dram_tensor("identf", [128, 128], F32, kind="ExternalInput").ap()
    x2o = nc.dram_tensor("x2", [TC, E], F32, kind="ExternalOutput").ap()
    x1d = nc.dram_tensor("x1d", [TC, E], F32).ap()
    yd = nc.dram_tensor("yd", [TC, E], F32).ap()

    seq = Seq(nc)
    es = ExitStack()
    sb = lambda n, s, d=F32: es.enter_context(nc.sbuf_tensor(n, s, d)).ap()
    ps = lambda n, s: es.enter_context(nc.psum_tensor(n, s, F32)).ap()

    xbf = [sb(f"xbf{i}", [128, TC], BF16) for i in range(8)]
    tmA = sb("tmA", [128, E])
    tmB = sb("tmB", [128, E])
    wte = sb("wte", [128, NTB * NEXP])
    sident = sb("sident", [128, 128])
    gat = sb("gat", [128, 64], BF16)
    gatf = sb("gatf", [128, 64])
    xtr = sb("xtr", [128, 128])
    wArena = sb("wArena", [128, 8 * FF0], BF16)
    wA = [wArena[:, i * FF0:(i + 1) * FF0] for i in range(8)]
    w2s = [wArena[:, i * E:(i + 1) * E] for i in range(22)]
    hA = [sb(f"hA{i}", [128, TC], BF16) for i in range(22)]
    cols = sb("cols", [128, 8])
    st1 = sb("st1", [128, 16])
    yev = sb("yev", [128, 512])
    p_tr = ps("p_tr", [128, 128])
    p_g = ps("p_g", [128, 16])
    p_a = ps("p_a", [128, 512])
    p_b = ps("p_b", [128, 512])

    NF0 = FF0 // 128
    with (nc.semaphore("step") as step, nc.Block() as block):
        seq.dma("sync", sident[:, :], identf)
        seq.dma("gpsimd", gat.rearrange("p (i e) -> p i e", i=8), gateT.rearrange("(i p) e -> p i e", p=128))
        seq.dma("sync", gatf.rearrange("p (i e) -> p i e", i=8), gateT.rearrange("(i p) e -> p i e", p=128))
        seq.op("vector", lambda v: v.memset(cols[:, 0:1], 1e-5))
        EPS = cols[:, 0:1]

        def ln_tb(tb, src_emitters, out_dram, transpose_to=None, extra=None, do_gating=False):
            for fn in src_emitters:
                fn()
            if extra is not None:
                extra()
            b = seq.barrier()
            seq.op("vector", lambda v: v.tensor_reduce(st1[:, 0:1], tmA[:, :], AX.XYZW, ALU.add), wait=b)
            seq.op("scalar", lambda s: s.activation(tmB[:, :], tmA[:, :], AF.Square), wait=b)
            seq.op("vector", lambda v: v.tensor_reduce(st1[:, 1:2], tmB[:, :], AX.XYZW, ALU.add))
            seq.op("vector", lambda v: v.tensor_scalar(st1[:, 2:3], st1[:, 0:1], 1.0 / E, None, ALU.mult))
            seq.op("vector", lambda v: v.tensor_scalar(st1[:, 3:4], st1[:, 1:2], 1.0 / E, None, ALU.mult))
            seq.op("vector", lambda v: v.tensor_tensor(st1[:, 4:5], st1[:, 2:3], st1[:, 2:3], ALU.mult))
            seq.op("vector", lambda v: v.tensor_tensor(st1[:, 5:6], st1[:, 3:4], st1[:, 4:5], ALU.subtract))
            seq.op("scalar", lambda s: s.activation(st1[:, 6:7], st1[:, 5:6], AF.Sqrt, bias=EPS))
            seq.op("vector", lambda v: v.reciprocal(st1[:, 7:8], st1[:, 6:7]))
            seq.op("vector", lambda v: v.tensor_scalar(tmA[:, :], tmA[:, :], st1[:, 2:3], st1[:, 7:8], ALU.subtract, ALU.mult))
            if out_dram is not None:
                seq.dma("sync", out_dram[tb * 128:(tb + 1) * 128, :], tmA[:, :])
            if transpose_to is not None:
                for fc in range(8):
                    seq.op("tensor", lambda t, fc=fc: t.transpose(p_tr[:, :], tmA[:, fc * 128:(fc + 1) * 128], sident[:, :]))
                    seq.op("vector", lambda v, fc=fc, tb=tb: v.tensor_copy(transpose_to[fc][:, tb * 128:(tb + 1) * 128], p_tr[:, :]))
                    if do_gating:
                        seq.op("vector", lambda v, fc=fc: v.tensor_copy(xtr[:, :], p_tr[:, :]))
                        seq.op("tensor", lambda t, fc=fc: t.matmul(p_g[:, 0:8], xtr[:, :], gatf[:, fc * 8:(fc + 1) * 8], start=(fc == 0), stop=(fc == 7)))


        def gating_post(tb):
            seq.op("vector", lambda v: v.tensor_copy(st1[:, 0:8], p_g[:, 0:8]))
            seq.op("vector", lambda v: v.reduce_max(st1[:, 8:9], st1[:, 0:8], AX.XYZW, negate=True))
            seq.op("scalar", lambda s: s.activation(st1[:, 0:8], st1[:, 0:8], AF.Exp, bias=st1[:, 8:9], accum_out=st1[:, 9:10]))
            seq.op("vector", lambda v: v.reciprocal(st1[:, 10:11], st1[:, 9:10]))
            seq.op("vector", lambda v: v.tensor_scalar(st1[:, 0:8], st1[:, 0:8], st1[:, 10:11], None, ALU.mult))
            seq.op("vector", lambda v: v.reduce_max(st1[:, 11:12], st1[:, 0:8], AX.XYZW))
            seq.op("vector", lambda v: v.tensor_scalar(st1[:, 12:13], st1[:, 11:12], -1.0, None, ALU.mult))
            seq.op("vector", lambda v: v.tensor_scalar(tmB[:, 0:8], st1[:, 0:8], st1[:, 11:12], None, ALU.is_ge))
            seq.op("vector", lambda v: v.scalar_tensor_tensor(tmB[:, 8:16], tmB[:, 0:8], -2.0, st1[:, 0:8], ALU.mult, ALU.add))
            seq.op("vector", lambda v: v.reduce_max(st1[:, 13:14], tmB[:, 8:16], AX.XYZW))
            seq.op("vector", lambda v: v.tensor_scalar(tmB[:, 0:8], st1[:, 0:8], st1[:, 13:14], None, ALU.is_ge))
            seq.op("vector", lambda v: v.tensor_tensor(st1[:, 14:15], st1[:, 11:12], st1[:, 13:14], ALU.add))
            seq.op("vector", lambda v: v.reciprocal(st1[:, 15:16], st1[:, 14:15]))
            seq.op("vector", lambda v: v.tensor_tensor(tmB[:, 0:8], tmB[:, 0:8], st1[:, 0:8], ALU.mult))
            seq.op("vector", lambda v, tb=tb: v.tensor_scalar(wte[:, tb * 8:(tb + 1) * 8], tmB[:, 0:8], st1[:, 15:16], None, ALU.mult))

        for tb in range(NTB):
            ln_tb(tb,
                  [lambda tb=tb: seq.dma("sync", tmA[:, :], x0[tb * 128:tb * 128 + 128, :]),
                   lambda tb=tb: seq.dma("sync", tmB[:, :], attn[tb * 128:tb * 128 + 128, :])],
                  x1d, transpose_to=xbf, do_gating=True,
                  extra=lambda: seq.op("vector", lambda v: v.tensor_tensor(tmA[:, :], tmA[:, :], tmB[:, :], ALU.add)))
            gating_post(tb)

        for e in range(NEXP):
            for ic in range(8):
                seq.dma("gpsimd", wA[ic], w1T[e, ic * 128:(ic + 1) * 128, :])
            wbar = seq.barrier()
            for hc in range(NF0):
                for ts in range(NTS):
                    for ic in range(8):
                        seq.op("tensor", lambda t, ic=ic, hc=hc, ts=ts: t.matmul(p_a[:, 0:TSZ], wA[ic][:, hc * 128:(hc + 1) * 128], xbf[ic][:, ts * TSZ:(ts + 1) * TSZ], start=(ic == 0), stop=(ic == 7)), wait=wbar if (ic == 0 and hc == 0 and ts == 0) else None)
                    seq.op("scalar", lambda s, hc=hc, ts=ts: s.activation(hA[hc][:, ts * TSZ:(ts + 1) * TSZ], p_a[:, 0:TSZ], AF.Silu))
            for ic in range(8):
                seq.dma("gpsimd", wA[ic], w3T[e, ic * 128:(ic + 1) * 128, :])
            wbar = seq.barrier()
            for hc in range(NF0):
                for ts in range(NTS):
                    for ic in range(8):
                        seq.op("tensor", lambda t, ic=ic, hc=hc, ts=ts: t.matmul(p_a[:, 0:TSZ], wA[ic][:, hc * 128:(hc + 1) * 128], xbf[ic][:, ts * TSZ:(ts + 1) * TSZ], start=(ic == 0), stop=(ic == 7)), wait=wbar if (ic == 0 and hc == 0 and ts == 0) else None)
                    seq.op("vector", lambda v, hc=hc, ts=ts: v.tensor_tensor(hA[hc][:, ts * TSZ:(ts + 1) * TSZ], hA[hc][:, ts * TSZ:(ts + 1) * TSZ], p_a[:, 0:TSZ], ALU.mult))
            for hc in range(NF0):
                seq.dma("gpsimd", w2s[hc], w2T[e, hc * 128:(hc + 1) * 128, :])
            wbar = seq.barrier()
            for tb in range(NTB):
                for oh in range(2):
                    for hc in range(NF0):
                        seq.op("tensor", lambda t, hc=hc, tb=tb, oh=oh: t.matmul(p_b[:, 0:512], hA[hc][:, tb * 128:(tb + 1) * 128], w2s[hc][:, oh * 512:(oh + 1) * 512], start=(hc == 0), stop=(hc == NF0 - 1)), wait=wbar if (hc == 0 and tb == 0 and oh == 0) else None)
                    seq.op("vector", lambda v, tb=tb, e=e: v.tensor_scalar(yev[:, 0:512], p_b[:, 0:512], wte[:, tb * 8 + e:tb * 8 + e + 1], None, ALU.mult))
                    if e == 0:
                        seq.dma("gpsimd", yd[tb * 128:(tb + 1) * 128, oh * 512:(oh + 1) * 512], yev[:, 0:512])
                    else:
                        seq.dma("gpsimd", yd[tb * 128:(tb + 1) * 128, oh * 512:(oh + 1) * 512], yev[:, 0:512], accum_op=ALU.add)

        for tb in range(NTB):
            ln_tb(tb,
                  [lambda tb=tb: seq.dma("sync", tmA[:, :], x1d[tb * 128:tb * 128 + 128, :]),
                   lambda tb=tb: seq.dma("sync", tmB[:, :], yd[tb * 128:tb * 128 + 128, :])],
                  None, transpose_to=None,
                  extra=lambda: seq.op("vector", lambda v: v.tensor_tensor(tmA[:, :], tmA[:, :], tmB[:, :], ALU.add)))
            seq.dma("sync", x2o[tb * 128:(tb + 1) * 128, :], tmA[:, :])

        seq.build(block, step)
    es.close()
    return nc

def host_l2_inputs(T, inp, attn_full):
    ident = np.eye(128, dtype=np.float32)
    w1T = np.ascontiguousarray(np.transpose(inp["L0_ffn_w1"], (0, 2, 1)))
    w3T = np.ascontiguousarray(np.transpose(inp["L0_ffn_w3"], (0, 2, 1)))
    w2T = np.ascontiguousarray(np.transpose(inp["L0_ffn_w2"], (0, 2, 1)))
    gateT = np.ascontiguousarray(inp["L0_ffn_gate_w"].T)
    maps = []
    for c in range(8):
        maps.append(dict(
            x0=np.ascontiguousarray(host_core_token_slice(inp["L0_attn_hidden_states"], T, c)),
            attn=np.ascontiguousarray(host_core_token_slice(attn_full, T, c)),
            gateT=gateT, w1T=w1T, w3T=w3T, w2T=w2T, identf=ident))
    return maps

def build_l3(T):
    TC = T // 4
    NTB = TC // 128
    NTS = ceil_div(TC, 512)
    TSZ = min(TC, 512)
    nc = bass.Bass("TRN2", target_bir_lowering=False)
    x2 = nc.dram_tensor("x2", [TC, E], F32, kind="ExternalInput").ap()
    grow = nc.dram_tensor("grow", [1, E], F32, kind="ExternalInput").ap()
    brow = nc.dram_tensor("brow", [1, E], F32, kind="ExternalInput").ap()
    wup = nc.dram_tensor("wup", [E, 2 * DI], F32, kind="ExternalInput").ap()
    bupc = nc.dram_tensor("bupc", [128, 8], F32, kind="ExternalInput").ap()
    identf = nc.dram_tensor("identf", [128, 128], F32, kind="ExternalInput").ap()
    leftT = nc.dram_tensor("leftT", [DI, TC], F32, kind="ExternalOutput").ap()
    rightT = nc.dram_tensor("rightT", [DI, TC], F32, kind="ExternalOutput").ap()

    seq = Seq(nc)
    es = ExitStack()
    sb = lambda n, s, d=F32: es.enter_context(nc.sbuf_tensor(n, s, d)).ap()
    ps = lambda n, s: es.enter_context(nc.psum_tensor(n, s, F32)).ap()
    xbf = [sb(f"xbf{i}", [128, TC], BF16) for i in range(8)]
    tmA = sb("tmA", [128, E])
    tmB = sb("tmB", [128, E])
    grep = sb("grep", [128, E])
    brep = sb("brep", [128, E])
    growS = sb("growS", [1, E])
    browS = sb("browS", [1, E])
    sones = sb("sones", [1, 128])
    sident = sb("sident", [128, 128])
    wupS = [sb(f"wupS{i}", [128, 2 * DI], BF16) for i in range(8)]
    bupS = sb("bupS", [128, 8])
    st1 = sb("st1", [128, 16])
    cols = sb("cols", [128, 1])
    oev = sb("oev", [128, 512])
    p_tr = ps("p_tr", [128, 128])
    p_a = ps("p_a", [128, 512])
    p_r = ps("p_r", [128, 512])

    with (nc.semaphore("step") as step, nc.Block() as block):
        seq.dma("sync", sident[:, :], identf)
        seq.dma("sync", growS[:, :], grow)
        seq.dma("sync", browS[:, :], brow)
        seq.dma("sync", bupS[:, :], bupc)
        for ic in range(8):
            seq.dma("gpsimd", wupS[ic][:, :], wup[ic * 128:(ic + 1) * 128, :])
        seq.op("vector", lambda v: v.memset(cols[:, 0:1], 1e-5))
        seq.op("vector", lambda v: v.memset(sones[:, :], 1.0))
        for half in range(2):
            seq.op("tensor", lambda t, half=half: t.matmul(p_r[:, 0:512], sones[0:1, :], growS[0:1, half * 512:(half + 1) * 512], start=True, stop=True))
            seq.op("vector", lambda v, half=half: v.tensor_copy(grep[:, half * 512:(half + 1) * 512], p_r[:, 0:512]))
            seq.op("tensor", lambda t, half=half: t.matmul(p_r[:, 0:512], sones[0:1, :], browS[0:1, half * 512:(half + 1) * 512], start=True, stop=True))
            seq.op("vector", lambda v, half=half: v.tensor_copy(brep[:, half * 512:(half + 1) * 512], p_r[:, 0:512]))

        for tb in range(NTB):
            seq.dma("sync", tmA[:, :], x2[tb * 128:tb * 128 + 128, :])
            seq.op("vector", lambda v: v.tensor_reduce(st1[:, 0:1], tmA[:, :], AX.XYZW, ALU.add))
            seq.op("scalar", lambda s: s.activation(tmB[:, :], tmA[:, :], AF.Square))
            seq.op("vector", lambda v: v.tensor_reduce(st1[:, 1:2], tmB[:, :], AX.XYZW, ALU.add))
            seq.op("vector", lambda v: v.tensor_scalar(st1[:, 2:3], st1[:, 0:1], 1.0 / E, None, ALU.mult))
            seq.op("vector", lambda v: v.tensor_scalar(st1[:, 3:4], st1[:, 1:2], 1.0 / E, None, ALU.mult))
            seq.op("vector", lambda v: v.tensor_tensor(st1[:, 4:5], st1[:, 2:3], st1[:, 2:3], ALU.mult))
            seq.op("vector", lambda v: v.tensor_tensor(st1[:, 5:6], st1[:, 3:4], st1[:, 4:5], ALU.subtract))
            seq.op("scalar", lambda s: s.activation(st1[:, 6:7], st1[:, 5:6], AF.Sqrt, bias=cols[:, 0:1]))
            seq.op("vector", lambda v: v.reciprocal(st1[:, 7:8], st1[:, 6:7]))
            seq.op("vector", lambda v: v.tensor_scalar(tmA[:, :], tmA[:, :], st1[:, 2:3], st1[:, 7:8], ALU.subtract, ALU.mult))
            seq.op("vector", lambda v: v.tensor_tensor(tmA[:, :], tmA[:, :], grep[:, :], ALU.mult))
            seq.op("vector", lambda v: v.tensor_tensor(tmA[:, :], tmA[:, :], brep[:, :], ALU.add))
            for fc in range(8):
                seq.op("tensor", lambda t, fc=fc: t.transpose(p_tr[:, :], tmA[:, fc * 128:(fc + 1) * 128], sident[:, :]))
                seq.op("vector", lambda v, fc=fc, tb=tb: v.tensor_copy(xbf[fc][:, tb * 128:(tb + 1) * 128], p_tr[:, :]))

        for of in range(8):
            for ts in range(NTS):
                for ic in range(8):
                    seq.op("tensor", lambda t, ic=ic, of=of, ts=ts: t.matmul(p_a[:, 0:TSZ], wupS[ic][:, of * 128:(of + 1) * 128], xbf[ic][:, ts * TSZ:(ts + 1) * TSZ], start=(ic == 0), stop=(ic == 7)))
                seq.op("scalar", lambda s, of=of: s.activation(oev[:, 0:TSZ], p_a[:, 0:TSZ], AF.Identity, bias=bupS[:, of:of + 1]))
                dst = leftT if of < 4 else rightT
                ofr = (of % 4) * 128
                seq.dma("sync", dst[ofr:ofr + 128, ts * TSZ:(ts + 1) * TSZ], oev[:, 0:TSZ])
        seq.build(block, step)
    es.close()
    return nc

def host_l3_inputs(T, inp, x2_full_tm):
    ident = np.eye(128, dtype=np.float32)
    maps = []
    for c in range(8):
        maps.append(dict(
            x2=np.ascontiguousarray(host_core_token_slice(x2_full_tm, T, c)),
            grow=np.ascontiguousarray(inp["L1_attn_ln_gamma"][None, :]),
            brow=np.ascontiguousarray(inp["L1_attn_ln_beta"][None, :]),
            wup=np.ascontiguousarray(inp["L1_attn_W_up"]),
            bupc=np.ascontiguousarray(inp["L1_attn_b_up"].reshape(8, 128).T),
            identf=ident))
    return maps

def build_l4(T):
    TCH = min(1024, T)
    NCH = T // TCH
    NS2 = TCH // 512
    nc = bass.Bass("TRN2", target_bir_lowering=False)
    leftF = nc.dram_tensor("leftF", [DI, T], F32, kind="ExternalInput").ap()
    cwc = nc.dram_tensor("cwc", [128, 4, 3], F32, kind="ExternalInput").ap()
    cbc = nc.dram_tensor("cbc", [128, 4], F32, kind="ExternalInput").ap()
    wdel = nc.dram_tensor("wdel", [DI, 128], F32, kind="ExternalInput").ap()
    bdc = nc.dram_tensor("bdc", [128, 1], F32, kind="ExternalInput").ap()
    wB = nc.dram_tensor("wB", [DI, NS], F32, kind="ExternalInput").ap()
    wC = nc.dram_tensor("wC", [DI, NS], F32, kind="ExternalInput").ap()
    aarr = nc.dram_tensor("aarr", [128, NS], F32, kind="ExternalInput").ap()
    dcol = nc.dram_tensor("dcol", [128, 1], F32, kind="ExternalInput").ap()
    patD = nc.dram_tensor("patD", [128, 4 * 128], F32, kind="ExternalInput").ap()
    patB = nc.dram_tensor("patB", [128, 32 * 128], F32, kind="ExternalInput").ap()
    patY = nc.dram_tensor("patY", [128, 4 * 128], F32, kind="ExternalInput").ap()
    yso = nc.dram_tensor("ys", [128, T], F32, kind="ExternalOutput").ap()

    seq = Seq(nc)
    es = ExitStack()
    sb = lambda n, s, d=F32: es.enter_context(nc.sbuf_tensor(n, s, d)).ap()
    ps = lambda n, s: es.enter_context(nc.psum_tensor(n, s, F32)).ap()
    lt = [sb(f"lt{i}", [128, TCH + 2]) for i in range(4)]
    uf = [sb(f"uf{i}", [128, TCH]) for i in range(4)]
    ub = [sb(f"ub{i}", [128, TCH], BF16) for i in range(4)]
    wdS = [sb(f"wdS{i}", [128, 128], BF16) for i in range(4)]
    wbS = [sb(f"wbS{i}", [128, NS], BF16) for i in range(4)]
    wcS = [sb(f"wcS{i}", [128, NS], BF16) for i in range(4)]
    acol = sb("acol", [128, NS])
    dcolS = sb("dcolS", [128, 1])
    bdcS = sb("bdcS", [128, 1])
    cwS = sb("cwS", [128, 12])
    cbS = sb("cbS", [128, 4])
    patDS = sb("patDS", [128, 4 * 128])
    patBS = sb("patBS", [128, 32 * 128], BF16)
    patYS = sb("patYS", [128, 4 * 128], BF16)
    deltaS = sb("deltaS", [128, TCH])
    wSt = sb("wSt", [128, TCH])
    BTb = [sb(f"BTb{i}", [128, TCH], BF16) for i in range(2)]
    CTb = [sb(f"CTb{i}", [128, TCH], BF16) for i in range(2)]
    dRep = [sb(f"dRep{i}", [128, TCH]) for i in range(4)]
    wRep = [sb(f"wRep{i}", [128, TCH]) for i in range(4)]
    dAS = sb("dAS", [128, TCH])
    bS = sb("bS", [128, TCH])
    hS = sb("hS", [128, TCH])
    ymulb = sb("ymulb", [128, TCH], BF16)
    hstate = sb("hstate", [128, NS])
    ysS = sb("ysS", [128, TCH])
    sc1 = sb("sc1", [128, TCH])
    sc2 = sb("sc2", [128, TCH])
    consts = sb("consts", [128, 4])
    p_d = ps("p_d", [128, 512])
    p_B = ps("p_B", [128, TCH])
    p_C = ps("p_C", [128, TCH])
    p_Y = ps("p_Y", [128, TCH])

    with (nc.semaphore("step") as step, nc.Block() as block):
        seq.dma("sync", cwS.rearrange("p (f k) -> p f k", f=4), cwc)
        seq.dma("sync", cbS[:, :], cbc)
        seq.dma("sync", bdcS[:, :], bdc)
        seq.dma("sync", dcolS[:, :], dcol)
        seq.dma("sync", patDS[:, :], patD)
        seq.dma("gpsimd", patBS[:, :], patB)
        seq.dma("gpsimd", patYS[:, :], patY)
        for ic in range(4):
            seq.dma("gpsimd", wdS[ic][:, :], wdel[ic * 128:(ic + 1) * 128, :])
            seq.dma("gpsimd", wbS[ic][:, :], wB[ic * 128:(ic + 1) * 128, :])
            seq.dma("gpsimd", wcS[ic][:, :], wC[ic * 128:(ic + 1) * 128, :])
        seq.op("vector", lambda v: v.memset(consts[:, 0:1], 1.0))
        seq.op("vector", lambda v: v.memset(consts[:, 1:2], -1.0))
        seq.op("vector", lambda v: v.memset(hstate[:, :], 0.0))
        C_ONE, C_M1 = consts[:, 0:1], consts[:, 1:2]
        seq.dma("sync", acol[:, :], aarr)
        seq.op("scalar", lambda s: s.activation(acol[:, :], acol[:, :], AF.Exp))
        seq.op("vector", lambda v: v.tensor_scalar(acol[:, :], acol[:, :], -1.0, None, ALU.mult))

        for k in range(NCH):
            ks = k * TCH
            for fc in range(4):
                if k == 0:
                    seq.op("vector", lambda v, fc=fc: v.memset(lt[fc][:, 0:2], 0.0))
                    seq.dma("sync", lt[fc][:, 2:TCH + 2], leftF[fc * 128:(fc + 1) * 128, 0:TCH])
                else:
                    seq.dma("sync", lt[fc][:, :], leftF[fc * 128:(fc + 1) * 128, ks - 2:ks + TCH])
                seq.op("vector", lambda v, fc=fc: v.tensor_scalar(sc1[:, :], lt[fc][:, 2:TCH + 2], cwS[:, fc * 3 + 2:fc * 3 + 3], cbS[:, fc:fc + 1], ALU.mult, ALU.add))
                seq.op("vector", lambda v, fc=fc: v.scalar_tensor_tensor(sc1[:, :], lt[fc][:, 1:TCH + 1], cwS[:, fc * 3 + 1:fc * 3 + 2], sc1[:, :], ALU.mult, ALU.add))
                seq.op("vector", lambda v, fc=fc: v.scalar_tensor_tensor(sc1[:, :], lt[fc][:, 0:TCH], cwS[:, fc * 3 + 0:fc * 3 + 1], sc1[:, :], ALU.mult, ALU.add))
                seq.op("scalar", lambda s, fc=fc: s.activation(uf[fc][:, :], sc1[:, :], AF.Silu))
                seq.op("vector", lambda v, fc=fc: v.tensor_copy(ub[fc][:, :], uf[fc][:, :]))
            for sub in range(NS2):
                for ic in range(4):
                    seq.op("tensor", lambda t, ic=ic, sub=sub: t.matmul(p_d[:, 0:512], wdS[ic][:, :], ub[ic][:, sub * 512:(sub + 1) * 512], start=(ic == 0), stop=(ic == 3)))
                seq.op("scalar", lambda s: s.activation(sc1[:, 0:512], p_d[:, 0:512], AF.Identity, bias=bdcS[:, 0:1]))
                seq.op("scalar", lambda s: s.activation(sc2[:, 0:512], sc1[:, 0:512], AF.Abs))
                seq.op("scalar", lambda s: s.activation(sc2[:, 0:512], sc2[:, 0:512], AF.Exp, scale=C_M1))
                seq.op("scalar", lambda s: s.activation(sc2[:, 0:512], sc2[:, 0:512], AF.Ln, bias=C_ONE))
                seq.op("vector", lambda v: v.tensor_scalar(sc1[:, 0:512], sc1[:, 0:512], 0.0, None, ALU.max))
                seq.op("vector", lambda v, sub=sub: v.tensor_tensor(deltaS[:, sub * 512:(sub + 1) * 512], sc1[:, 0:512], sc2[:, 0:512], ALU.add))
            seq.op("vector", lambda v: v.tensor_tensor(wSt[:, :], deltaS[:, :], uf[0][:, :], ALU.mult))
            for nh in range(2):
                for sub in range(NS2):
                    for ic in range(4):
                        seq.op("tensor", lambda t, ic=ic, nh=nh, sub=sub: t.matmul(p_d[:, 0:512], wbS[ic][:, nh * 128:(nh + 1) * 128], ub[ic][:, sub * 512:(sub + 1) * 512], start=(ic == 0), stop=(ic == 3)))
                    seq.op("vector", lambda v, nh=nh, sub=sub: v.tensor_copy(BTb[nh][:, sub * 512:(sub + 1) * 512], p_d[:, 0:512]))
                    for ic in range(4):
                        seq.op("tensor", lambda t, ic=ic, nh=nh, sub=sub: t.matmul(p_d[:, 0:512], wcS[ic][:, nh * 128:(nh + 1) * 128], ub[ic][:, sub * 512:(sub + 1) * 512], start=(ic == 0), stop=(ic == 3)))
                    seq.op("vector", lambda v, nh=nh, sub=sub: v.tensor_copy(CTb[nh][:, sub * 512:(sub + 1) * 512], p_d[:, 0:512]))
            for dg in range(4):
                for sub in range(NS2):
                    seq.op("tensor", lambda t, dg=dg, sub=sub: t.matmul(p_d[:, 0:512], patDS[:, dg * 128:(dg + 1) * 128], deltaS[:, sub * 512:(sub + 1) * 512], start=True, stop=True))
                    seq.op("vector", lambda v, dg=dg, sub=sub: v.tensor_copy(dRep[dg][:, sub * 512:(sub + 1) * 512], p_d[:, 0:512]))
                    seq.op("tensor", lambda t, dg=dg, sub=sub: t.matmul(p_d[:, 0:512], patDS[:, dg * 128:(dg + 1) * 128], wSt[:, sub * 512:(sub + 1) * 512], start=True, stop=True))
                    seq.op("vector", lambda v, dg=dg, sub=sub: v.tensor_copy(wRep[dg][:, sub * 512:(sub + 1) * 512], p_d[:, 0:512]))
            for ng in range(64):
                j, nh = ng % 32, ng // 32
                for sub in range(NS2):
                    seq.op("tensor", lambda t, j=j, nh=nh, sub=sub: t.matmul(p_B[:, sub * 512:(sub + 1) * 512], patBS[:, j * 128:(j + 1) * 128], BTb[nh][:, sub * 512:(sub + 1) * 512], start=True, stop=True))
                    seq.op("tensor", lambda t, j=j, nh=nh, sub=sub: t.matmul(p_C[:, sub * 512:(sub + 1) * 512], patBS[:, j * 128:(j + 1) * 128], CTb[nh][:, sub * 512:(sub + 1) * 512], start=True, stop=True))
                for dg in range(4):
                    g = ng * 4 + dg
                    seq.op("scalar", lambda s, dg=dg, ng=ng: s.activation(dAS[:, :], dRep[dg][:, :], AF.Exp, scale=acol[:, ng * 4 + dg:ng * 4 + dg + 1]))
                    seq.op("vector", lambda v, dg=dg: v.tensor_tensor(bS[:, :], wRep[dg][:, :], p_B[:, :], ALU.mult))
                    seq.op("vector", lambda v, g=g: v.tensor_tensor_scan(hS[:, :], dAS[:, :], bS[:, :], hstate[:, g:g + 1], ALU.mult, ALU.add))
                    seq.op("vector", lambda v, g=g: v.tensor_copy(hstate[:, g:g + 1], hS[:, TCH - 1:TCH]))
                    seq.op("vector", lambda v: v.tensor_tensor(ymulb[:, :], hS[:, :], p_C[:, :], ALU.mult))
                    for sub in range(NS2):
                        seq.op("tensor", lambda t, dg=dg, ng=ng, sub=sub: t.matmul(p_Y[:, sub * 512:(sub + 1) * 512], patYS[:, dg * 128:(dg + 1) * 128], ymulb[:, sub * 512:(sub + 1) * 512], start=(ng == 0 and dg == 0), stop=(ng == 63 and dg == 3)))
            for sub in range(NS2):
                seq.op("vector", lambda v, sub=sub: v.scalar_tensor_tensor(ysS[:, sub * 512:(sub + 1) * 512], uf[0][:, sub * 512:(sub + 1) * 512], dcolS[:, 0:1], p_Y[:, sub * 512:(sub + 1) * 512], ALU.mult, ALU.add))
            seq.dma("sync", yso[:, ks:ks + TCH], ysS[:, :])
        seq.build(block, step)
    es.close()
    return nc

def _rot_blocks(arr, ds):
    blocks = [arr[i * 128:(i + 1) * 128] for i in range(arr.shape[0] // 128)]
    order = [(ds + i) % len(blocks) for i in range(len(blocks))]
    return np.concatenate([blocks[i] for i in order], axis=0)

def host_l4_inputs(T, inp, leftT_by_batch):
    cw = inp["L1_attn_conv_w"][:, 0, :]
    cb = inp["L1_attn_conv_b"]
    Wd = inp["L1_attn_W_delta"]
    bd = inp["L1_attn_b_delta"]
    WB_ = inp["L1_attn_W_B"]; WC_ = inp["L1_attn_W_C"]
    Alog = inp["L1_attn_A_log"]; Dv = inp["L1_attn_D"]
    patD = np.zeros((128, 4, 128), np.float32)
    for dg in range(4):
        for p in range(128):
            patD[dg * 32 + p // 4, dg, p] = 1.0
    patB = np.zeros((128, 32, 128), np.float32)
    for j in range(32):
        for p in range(128):
            patB[j * 4 + p % 4, j, p] = 1.0
    patY = np.zeros((128, 4, 128), np.float32)
    for dg in range(4):
        for p in range(128):
            patY[p, dg, dg * 32 + p // 4] = 1.0
    patD_in = np.ascontiguousarray(patD.reshape(128, 4 * 128))
    patB_in = np.ascontiguousarray(patB.reshape(128, 32 * 128))
    patY_in = np.ascontiguousarray(patY.reshape(128, 4 * 128))
    maps = []
    for c in range(8):
        b, ds = c // 4, c % 4
        dlo = ds * 128
        p = np.arange(128)
        i_, nsub = p // 4, p % 4
        dgi, ngi = np.meshgrid(np.arange(4), np.arange(64), indexing="ij")
        aarr = np.zeros((128, 256), np.float32)
        for dg in range(4):
            for ng in range(64):
                aarr[:, ng * 4 + dg] = Alog[dlo + dg * 32 + i_, ng * 4 + nsub]
        maps.append(dict(
            leftF=np.ascontiguousarray(_rot_blocks(leftT_by_batch[b], ds)),
            cwc=np.ascontiguousarray(_rot_blocks(cw, ds).reshape(4, 128, 3).transpose(1, 0, 2)),
            cbc=np.ascontiguousarray(_rot_blocks(cb[:, None], ds).reshape(4, 128).T),
            wdel=np.ascontiguousarray(_rot_blocks(Wd, ds)[:, dlo:dlo + 128]),
            bdc=np.ascontiguousarray(bd[dlo:dlo + 128, None]),
            wB=np.ascontiguousarray(_rot_blocks(WB_, ds)),
            wC=np.ascontiguousarray(_rot_blocks(WC_, ds)),
            aarr=aarr, dcol=np.ascontiguousarray(Dv[dlo:dlo + 128, None]),
            patD=patD_in, patB=patB_in, patY=patY_in))
    return maps

def build_l5(T):
    TC = T // 4
    NTB = TC // 128
    TSZ2 = min(1024, TC)
    NTCG = TC // TSZ2
    NS5 = TSZ2 // 512 if TSZ2 >= 512 else 1
    S5 = min(512, TSZ2)
    nc = bass.Bass("TRN2", target_bir_lowering=False)
    x2 = nc.dram_tensor("x2", [TC, E], F32, kind="ExternalInput").ap()
    ysT = nc.dram_tensor("ysT", [DI, TC], F32, kind="ExternalInput").ap()
    rightT = nc.dram_tensor("rightT", [DI, TC], F32, kind="ExternalInput").ap()
    wdown = nc.dram_tensor("wdown", [DI, E], F32, kind="ExternalInput").ap()
    bdrow = nc.dram_tensor("bdrow", [1, E], F32, kind="ExternalInput").ap()
    wred = nc.dram_tensor("wred", [E, GATE], F32, kind="ExternalInput").ap()
    wg = nc.dram_tensor("wg", [NEXP, GATE], F32, kind="ExternalInput").ap()
    w1f = nc.dram_tensor("w1f", [NEXP, E, FF1], F32, kind="ExternalInput").ap()
    b1c = nc.dram_tensor("b1c", [NEXP, 128, FF1 // 128], F32, kind="ExternalInput").ap()
    w2f = nc.dram_tensor("w2f", [NEXP, FF1, E], F32, kind="ExternalInput").ap()
    b2r = nc.dram_tensor("b2r", [NEXP, 1, E], F32, kind="ExternalInput").ap()
    identf = nc.dram_tensor("identf", [128, 128], F32, kind="ExternalInput").ap()
    outT = nc.dram_tensor("out", [TC, E], F32, kind="ExternalOutput").ap()
    x3d = nc.dram_tensor("x3d", [TC, E], F32).ap()
    yd = nc.dram_tensor("yd", [TC, E], F32).ap()

    seq = Seq(nc)
    es = ExitStack()
    sb = lambda n, s, d=F32: es.enter_context(nc.sbuf_tensor(n, s, d)).ap()
    ps = lambda n, s: es.enter_context(nc.psum_tensor(n, s, F32)).ap()
    NHC = FF1 // 128
    arena1 = sb("arena1", [128, NHC * 1024], BF16)
    arena2 = sb("arena2", [128, 8 * FF1], BF16)
    x3bf = [sb(f"x3bf{i}", [128, TC], BF16) for i in range(8)]
    tmA = sb("tmA", [128, E])
    tmB = sb("tmB", [128, E])
    tmC = sb("tmC", [128, E])
    xtr = sb("xtr", [128, 128])
    wredf = sb("wredf", [128, 8 * GATE])
    wgS = sb("wgS", [8, GATE])
    wgT = sb("wgT", [16, 8])
    redT = sb("redT", [16, 128])
    bdr = sb("bdr", [1, E], BF16)
    b2S = sb("b2S", [1, E], BF16)
    b1S = sb("b1S", [128, NHC])
    onesb = sb("onesb", [1, 128], BF16)
    sident = sb("sident", [128, 128])
    st1 = sb("st1", [128, 16])
    sel8 = sb("sel8", [128, 8])
    wsel = sb("wsel", [128, NTB * 8])
    cols = sb("cols", [128, 4])
    yev = sb("yev", [128, 512])
    p_tr = ps("p_tr", [128, 128])
    p_a = ps("p_a", [128, 512])
    p_b = ps("p_b", [128, 512])
    p_g = ps("p_g", [128, 16])
    p_r = ps("p_r", [128, 16])

    a1f = arena1.bitcast(F32)
    ysS = [a1f[:, i * TC:(i + 1) * TC] for i in range(4)]
    srS = [a1f[:, (4 + i) * TC:(5 + i) * TC] for i in range(4)]
    hG = [arena1[:, i * 1024:(i + 1) * 1024] for i in range(NHC)]
    gdS = [arena2[:, i * TC:(i + 1) * TC] for i in range(4)]
    wdnS = [arena2[:, 4 * TC + i * E:4 * TC + (i + 1) * E] for i in range(4)]
    w1S = [arena2[:, i * FF1:(i + 1) * FF1] for i in range(8)]
    w2S = [arena2[:, i * 1024:(i + 1) * 1024] for i in range(NHC)]

    with (nc.semaphore("step") as step, nc.Block() as block):
        seq.dma("sync", sident[:, :], identf)
        seq.dma("sync", wredf.rearrange("p (i g) -> p i g", i=8), wred.rearrange("(i p) g -> p i g", p=128))
        seq.dma("sync", wgS[:, :], wg)
        seq.dma("gpsimd", bdr[:, :], bdrow)
        seq.op("vector", lambda v: v.memset(cols[:, 0:1], 1e-5))
        seq.op("vector", lambda v: v.memset(cols[:, 1:2], 0.0))
        seq.op("vector", lambda v: v.memset(onesb[:, :], 1.0))
        EPS, ZERO = cols[:, 0:1], cols[:, 1:2]
        seq.op("scalar", lambda s: s.activation(st1[0:8, 0:GATE], wgS[:, :], AF.Square))
        seq.op("vector", lambda v: v.tensor_reduce(st1[0:8, 0:1] if False else sel8[0:8, 0:1], st1[0:8, 0:GATE], AX.XYZW, ALU.add))
        seq.op("scalar", lambda s: s.activation(sel8[0:8, 1:2], sel8[0:8, 0:1], AF.Sqrt, bias=ZERO[0:8, :]))
        seq.op("vector", lambda v: v.tensor_scalar(sel8[0:8, 1:2], sel8[0:8, 1:2], 1e-4, None, ALU.max))
        seq.op("vector", lambda v: v.reciprocal(sel8[0:8, 2:3], sel8[0:8, 1:2]))
        seq.op("vector", lambda v: v.tensor_scalar(wgS[:, :], wgS[:, :], sel8[0:8, 2:3], None, ALU.mult))
        seq.op("tensor", lambda t: t.transpose(p_g[0:16, 0:8], wgS[:, :], sident[0:8, 0:8]))
        seq.op("vector", lambda v: v.tensor_copy(wgT[:, :], p_g[0:16, 0:8]))
        for i in range(4):
            seq.dma("sync", ysS[i], ysT[i * 128:(i + 1) * 128, :])
            seq.dma("sync", srS[i], rightT[i * 128:(i + 1) * 128, :])
            seq.op("scalar", lambda s, i=i: s.activation(srS[i], srS[i], AF.Silu))
            seq.op("vector", lambda v, i=i: v.tensor_tensor(gdS[i], ysS[i], srS[i], ALU.mult))
            seq.dma("gpsimd", wdnS[i], wdown[i * 128:(i + 1) * 128, :])
        for tb in range(NTB):
            seq.dma("sync", tmA[:, :], x2[tb * 128:tb * 128 + 128, :])
            for oh in range(2):
                for ic in range(4):
                    seq.op("tensor", lambda t, ic=ic, oh=oh, tb=tb: t.matmul(p_a[:, 0:512], gdS[ic][:, tb * 128:(tb + 1) * 128], wdnS[ic][:, oh * 512:(oh + 1) * 512], start=(ic == 0), stop=False))
                seq.op("tensor", lambda t, oh=oh: t.matmul(p_a[:, 0:512], onesb[0:1, :], bdr[0:1, oh * 512:(oh + 1) * 512], start=False, stop=True))
                seq.op("vector", lambda v, oh=oh: v.scalar_tensor_tensor(tmB[:, oh * 512:(oh + 1) * 512], tmA[:, oh * 512:(oh + 1) * 512], 2.0, p_a[:, 0:512], ALU.mult, ALU.add))
            seq.op("vector", lambda v: v.tensor_reduce(st1[:, 0:1], tmB[:, :], AX.XYZW, ALU.add))
            seq.op("scalar", lambda s: s.activation(tmC[:, :], tmB[:, :], AF.Square))
            seq.op("vector", lambda v: v.tensor_reduce(st1[:, 1:2], tmC[:, :], AX.XYZW, ALU.add))
            seq.op("vector", lambda v: v.tensor_scalar(st1[:, 2:3], st1[:, 0:1], 1.0 / E, None, ALU.mult))
            seq.op("vector", lambda v: v.tensor_scalar(st1[:, 3:4], st1[:, 1:2], 1.0 / E, None, ALU.mult))
            seq.op("vector", lambda v: v.tensor_tensor(st1[:, 4:5], st1[:, 2:3], st1[:, 2:3], ALU.mult))
            seq.op("vector", lambda v: v.tensor_tensor(st1[:, 5:6], st1[:, 3:4], st1[:, 4:5], ALU.subtract))
            seq.op("scalar", lambda s: s.activation(st1[:, 6:7], st1[:, 5:6], AF.Sqrt, bias=EPS))
            seq.op("vector", lambda v: v.reciprocal(st1[:, 7:8], st1[:, 6:7]))
            seq.op("vector", lambda v: v.tensor_scalar(tmB[:, :], tmB[:, :], st1[:, 2:3], st1[:, 7:8], ALU.subtract, ALU.mult))
            seq.dma("sync", x3d[tb * 128:(tb + 1) * 128, :], tmB[:, :])
            for fc in range(8):
                seq.op("tensor", lambda t, fc=fc: t.transpose(p_tr[:, :], tmB[:, fc * 128:(fc + 1) * 128], sident[:, :]))
                seq.op("vector", lambda v, fc=fc, tb=tb: v.tensor_copy(x3bf[fc][:, tb * 128:(tb + 1) * 128], p_tr[:, :]))
                seq.op("vector", lambda v: v.tensor_copy(xtr[:, :], p_tr[:, :]))
                seq.op("tensor", lambda t, fc=fc: t.matmul(p_r[:, 0:GATE], xtr[:, :], wredf[:, fc * GATE:(fc + 1) * GATE], start=(fc == 0), stop=(fc == 7)))
            seq.op("vector", lambda v: v.tensor_copy(tmC[:, 0:GATE], p_r[:, 0:GATE]))
            seq.op("tensor", lambda t: t.transpose(p_tr[0:16, 0:128], tmC[:, 0:GATE], sident[:, :]))
            seq.op("vector", lambda v: v.tensor_copy(redT[:, :], p_tr[0:16, 0:128]))
            seq.op("tensor", lambda t: t.matmul(p_g[:, 0:8], redT[:, :], wgT[:, :], start=True, stop=True))
            seq.op("vector", lambda v: v.tensor_copy(st1[:, 0:8], p_g[:, 0:8]))
            seq.op("vector", lambda v: v.reduce_max(st1[:, 8:9], st1[:, 0:8], AX.XYZW, negate=True))
            seq.op("scalar", lambda s: s.activation(tmC[:, 0:8], st1[:, 0:8], AF.Exp, bias=st1[:, 8:9], accum_out=st1[:, 9:10]))
            seq.op("vector", lambda v: v.reciprocal(st1[:, 10:11], st1[:, 9:10]))
            seq.op("vector", lambda v: v.tensor_scalar(st1[:, 11:12], st1[:, 8:9], -1.0, None, ALU.mult))
            seq.op("vector", lambda v: v.tensor_scalar(sel8[:, 0:8], st1[:, 0:8], st1[:, 11:12], None, ALU.is_ge))
            seq.op("vector", lambda v: v.memset(st1[:, 12:13], 1.0))
            for e0 in range(8):
                seq.op("vector", lambda v, e0=e0: v.tensor_tensor(sel8[:, e0:e0 + 1], sel8[:, e0:e0 + 1], st1[:, 12:13], ALU.mult))
                if e0 < 7:
                    seq.op("vector", lambda v, e0=e0: v.tensor_scalar(st1[:, 13:14], sel8[:, e0:e0 + 1], -1.0, 1.0, ALU.mult, ALU.add))
                    seq.op("vector", lambda v: v.tensor_tensor(st1[:, 12:13], st1[:, 12:13], st1[:, 13:14], ALU.mult))
            seq.op("vector", lambda v, tb=tb: v.tensor_scalar(wsel[:, tb * 8:(tb + 1) * 8], sel8[:, 0:8], st1[:, 10:11], None, ALU.mult))
        for e in range(NEXP):
            seq.dma("sync", b1S[:, :], b1c[e])
            seq.dma("gpsimd", b2S[:, :], b2r[e])
            for tcg in range(NTCG):
                t0 = tcg * TSZ2
                for ic in range(8):
                    seq.dma("gpsimd", w1S[ic], w1f[e, ic * 128:(ic + 1) * 128, :])
                wb = seq.barrier()
                for hc in range(NHC):
                    for s5 in range(NS5):
                        for ic in range(8):
                            seq.op("tensor", lambda t, ic=ic, hc=hc, s5=s5, t0=t0: t.matmul(p_a[:, 0:S5], w1S[ic][:, hc * 128:(hc + 1) * 128], x3bf[ic][:, t0 + s5 * 512:t0 + s5 * 512 + S5], start=(ic == 0), stop=(ic == 7)))
                        seq.op("scalar", lambda s, hc=hc, s5=s5: s.activation(hG[hc][:, s5 * 512:s5 * 512 + S5], p_a[:, 0:S5], AF.Gelu, bias=b1S[:, hc:hc + 1]))
                for hc in range(NHC):
                    seq.dma("gpsimd", w2S[hc], w2f[e, hc * 128:(hc + 1) * 128, :])
                wb = seq.barrier()
                for tb0 in range(TSZ2 // 128):
                    tb = tcg * (TSZ2 // 128) + tb0
                    for oh in range(2):
                        for hc in range(NHC):
                            seq.op("tensor", lambda t, hc=hc, tb0=tb0, oh=oh: t.matmul(p_b[:, 0:512], hG[hc][:, tb0 * 128:(tb0 + 1) * 128], w2S[hc][:, oh * 512:(oh + 1) * 512], start=(hc == 0), stop=False))
                        seq.op("tensor", lambda t, oh=oh: t.matmul(p_b[:, 0:512], onesb[0:1, :], b2S[0:1, oh * 512:(oh + 1) * 512], start=False, stop=True))
                        seq.op("vector", lambda v, tb=tb, e=e: v.tensor_scalar(yev[:, 0:512], p_b[:, 0:512], wsel[:, tb * 8 + e:tb * 8 + e + 1], None, ALU.mult))
                        if e == 0:
                            seq.dma("gpsimd", yd[tb * 128:(tb + 1) * 128, oh * 512:(oh + 1) * 512], yev[:, 0:512])
                        else:
                            seq.dma("gpsimd", yd[tb * 128:(tb + 1) * 128, oh * 512:(oh + 1) * 512], yev[:, 0:512], accum_op=ALU.add)
        for tb in range(NTB):
            seq.dma("sync", tmA[:, :], x3d[tb * 128:tb * 128 + 128, :])
            seq.dma("sync", tmB[:, :], yd[tb * 128:tb * 128 + 128, :])
            seq.op("vector", lambda v: v.tensor_tensor(tmA[:, :], tmA[:, :], tmB[:, :], ALU.add))
            seq.op("vector", lambda v: v.tensor_reduce(st1[:, 0:1], tmA[:, :], AX.XYZW, ALU.add))
            seq.op("scalar", lambda s: s.activation(tmC[:, :], tmA[:, :], AF.Square))
            seq.op("vector", lambda v: v.tensor_reduce(st1[:, 1:2], tmC[:, :], AX.XYZW, ALU.add))
            seq.op("vector", lambda v: v.tensor_scalar(st1[:, 2:3], st1[:, 0:1], 1.0 / E, None, ALU.mult))
            seq.op("vector", lambda v: v.tensor_scalar(st1[:, 3:4], st1[:, 1:2], 1.0 / E, None, ALU.mult))
            seq.op("vector", lambda v: v.tensor_tensor(st1[:, 4:5], st1[:, 2:3], st1[:, 2:3], ALU.mult))
            seq.op("vector", lambda v: v.tensor_tensor(st1[:, 5:6], st1[:, 3:4], st1[:, 4:5], ALU.subtract))
            seq.op("scalar", lambda s: s.activation(st1[:, 6:7], st1[:, 5:6], AF.Sqrt, bias=EPS))
            seq.op("vector", lambda v: v.reciprocal(st1[:, 7:8], st1[:, 6:7]))
            seq.op("vector", lambda v: v.tensor_scalar(tmA[:, :], tmA[:, :], st1[:, 2:3], st1[:, 7:8], ALU.subtract, ALU.mult))
            seq.dma("sync", outT[tb * 128:(tb + 1) * 128, :], tmA[:, :])
        seq.build(block, step)
    es.close()
    return nc

def host_l5_inputs(T, inp, x2_full_tm, ys_by_batch, rightT_by_core):
    maps = []
    b1cols = np.ascontiguousarray(inp["L1_ffn_b1"].reshape(NEXP, FF1 // 128, 128).transpose(0, 2, 1))
    for c in range(8):
        b = c // 4
        ysTc = ys_by_batch[b][:, core_token_rows(T, c)]
        maps.append(dict(
            x2=np.ascontiguousarray(host_core_token_slice(x2_full_tm, T, c)),
            ysT=np.ascontiguousarray(ysTc),
            rightT=np.ascontiguousarray(rightT_by_core[c]),
            wdown=np.ascontiguousarray(inp["L1_attn_W_down"]),
            bdrow=np.ascontiguousarray(inp["L1_attn_b_down"][None, :]),
            wred=np.ascontiguousarray(inp["L1_ffn_w_red"]),
            wg=np.ascontiguousarray(inp["L1_ffn_wg"]),
            w1f=np.ascontiguousarray(inp["L1_ffn_w1"]),
            b1c=b1cols,
            w2f=np.ascontiguousarray(inp["L1_ffn_w2"]),
            b2r=np.ascontiguousarray(inp["L1_ffn_b2"][:, None, :]),
            identf=np.eye(128, dtype=np.float32)))
    return maps


def build_l1(T):
    TC = T // 4
    NSL = TC // 128
    NKT = T // 512
    nc = bass.Bass("TRN2", target_bir_lowering=False)
    hT = nc.dram_tensor("hT", [E, T], F32, kind="ExternalInput").ap()
    hqT = nc.dram_tensor("hqT", [E, TC], F32, kind="ExternalInput").ap()
    htm = nc.dram_tensor("htm", [T, E], F32, kind="ExternalInput").ap()
    wqT = nc.dram_tensor("wqT", [E, E], F32, kind="ExternalInput").ap()
    wkT = nc.dram_tensor("wkT", [E, E], F32, kind="ExternalInput").ap()
    bqc = nc.dram_tensor("bqc", [128, 8], F32, kind="ExternalInput").ap()
    bkc = nc.dram_tensor("bkc", [128, 8], F32, kind="ExternalInput").ap()
    gum = nc.dram_tensor("gum", [TC, T], F32, kind="ExternalInput").ap()
    mask4 = nc.dram_tensor("mask4", [128, 2048], F32, kind="ExternalInput").ap()
    rowinv = nc.dram_tensor("rowinv", [128, 16], F32, kind="ExternalInput").ap()
    identf = nc.dram_tensor("identf", [128, 128], F32, kind="ExternalInput").ap()
    attno = nc.dram_tensor("attn", [TC, E], F32, kind="ExternalOutput").ap()
    kTd = nc.dram_tensor("kTd", [8, 128, T], BF16).ap()

    seq = Seq(nc)
    es = ExitStack()
    sb = lambda n, s, d=F32: es.enter_context(nc.sbuf_tensor(n, s, d)).ap()
    wS = [sb(f"wS{i}", [128, E], BF16) for i in range(8)]
    hq = [sb(f"hq{i}", [128, TC], BF16) for i in range(8)]
    qT = [sb(f"qT{i}", [128, TC], BF16) for i in range(8)]
    kt = sb("kt", [128, 8 * 512], BF16)
    kev = sb("kev", [128, 512], BF16)
    hv = sb("hv", [128, 4 * 1024], BF16)
    avg = sb("avg", [128, T])
    mskS = sb("mskS", [128, 2048])
    gmc = sb("gmc", [128, 512])
    exs = sb("exs", [128, 512])
    wbf = sb("wbf", [128, 512], BF16)
    wT = sb("wT", [128, 512], BF16)
    oev = sb("oev", [128, E])
    mrun = sb("mrun", [128, 16])
    zrun = sb("zrun", [128, 16])
    mnew = sb("mnew", [128, 4])
    nmnew = sb("nmnew", [128, 4])
    bs4 = sb("bs4", [128, 16])
    dsc = sb("dsc", [128, 4])
    bias2 = sb("bias2", [128, 16])
    sS = sb("sS", [128, 4])
    bqs = sb("bqs", [128, 8])
    bks = sb("bks", [128, 8])
    rinv = sb("rinv", [128, 16])
    cols = sb("cols", [128, 4])
    sident = sb("sident", [128, 128])
    identb = sb("identb", [128, 128], BF16)
    p_s = es.enter_context(nc.psum_tensor("p_s", [128, 2048], F32)).ap()
    p_av = es.enter_context(nc.psum_tensor("p_av", [128, 1024], F32)).ap()
    p_t = es.enter_context(nc.psum_tensor("p_t", [128, 128], BF16)).ap()

    LN16 = float(np.log(16.0))
    with (nc.semaphore("step") as step, nc.Block() as block):
        seq.dma("sync", sident[:, :], identf)
        seq.dma("gpsimd", identb[:, :], identf)
        seq.dma("sync", mskS[:, :], mask4)
        seq.dma("sync", bqs[:, :], bqc)
        seq.dma("sync", bks[:, :], bkc)
        seq.dma("sync", rinv[:, :], rowinv)
        seq.op("vector", lambda v: v.memset(cols[:, 0:1], 0.125))
        seq.op("vector", lambda v: v.memset(cols[:, 1:2], 0.5))
        seq.op("vector", lambda v: v.memset(avg[:, :], 0.0))
        C18, C05 = cols[:, 0:1], cols[:, 1:2]
        seq.op("vector", lambda v: v.tensor_scalar(bqs[:, :], bqs[:, :], 0.125, None, ALU.mult))
        for ic in range(8):
            seq.dma("gpsimd", wS[ic][:, :], wqT[ic * 128:(ic + 1) * 128, :])
            seq.dma("gpsimd", hq[ic][:, :], hqT[ic * 128:(ic + 1) * 128, :])
        NTS = ceil_div(TC, 512)
        TSZ = min(TC, 512)
        for of in range(8):
            for ts in range(NTS):
                for ic in range(8):
                    seq.op("tensor", lambda t, ic=ic, of=of, ts=ts: t.matmul(p_s[:, 0:TSZ], wS[ic][:, of * 128:(of + 1) * 128], hq[ic][:, ts * TSZ:(ts + 1) * TSZ], start=(ic == 0), stop=(ic == 7)))
                seq.op("scalar", lambda s, of=of, ts=ts: s.activation(qT[of][:, ts * TSZ:(ts + 1) * TSZ], p_s[:, 0:TSZ], AF.Identity, bias=bqs[:, of:of + 1], scale=C18))
        for ic in range(8):
            seq.dma("gpsimd", wS[ic][:, :], wkT[ic * 128:(ic + 1) * 128, :])
        for tc in range(NKT):
            seq.dma("gpsimd", kt.rearrange("p (i k) -> p i k", i=8),
                    hT.rearrange("(i p) t -> p i t", p=128)[:, :, tc * 512:(tc + 1) * 512])
            for of in range(8):
                for ic in range(8):
                    seq.op("tensor", lambda t, ic=ic, of=of: t.matmul(p_s[:, 0:512], wS[ic][:, of * 128:(of + 1) * 128], kt[:, ic * 512:(ic + 1) * 512], start=(ic == 0), stop=(ic == 7)))
                seq.op("scalar", lambda s, of=of: s.activation(kev[:, :], p_s[:, 0:512], AF.Identity, bias=bks[:, of:of + 1]))
                seq.dma("sync", kTd[of, :, tc * 512:(tc + 1) * 512], kev[:, :])
        for sl in range(NSL):
            nkc = sl + 1
            seq.op("vector", lambda v: v.memset(mrun[:, :], -1e30))
            seq.op("vector", lambda v: v.memset(zrun[:, :], 0.0))
            for kc in range(nkc):
                seq.dma("sync", kt.rearrange("p (i k) -> p i k", i=8),
                        kTd.rearrange("i p t -> p i t")[:, :, kc * 512:(kc + 1) * 512])
                for hg in range(4):
                    for h4 in range(4):
                        h = hg * 4 + h4
                        of, half = h // 2, (h % 2) * 64
                        seq.op("tensor", lambda t, of=of, half=half, h4=h4, sl=sl: t.matmul(
                            p_s[:, h4 * 512:(h4 + 1) * 512],
                            qT[of][half:half + 64, sl * 128:(sl + 1) * 128],
                            kt[half:half + 64, of * 512:(of + 1) * 512], start=True, stop=True))
                    if kc == nkc - 1:
                        seq.op("vector", lambda v: v.tensor_tensor(p_s[:, :], p_s[:, :], mskS[:, :], ALU.add))
                    seq.op("vector", lambda v: v.tensor_reduce(dsc[:, 0:4], p_s.rearrange("p (h k) -> p h k", h=4), AX.X, ALU.max))
                    seq.op("vector", lambda v, hg=hg: v.tensor_tensor(mnew[:, :], mrun[:, hg * 4:(hg + 1) * 4], dsc[:, :], ALU.max))
                    seq.op("vector", lambda v: v.tensor_scalar(nmnew[:, :], mnew[:, :], -1.0, None, ALU.mult))
                    for h4 in range(4):
                        seq.op("scalar", lambda s, h4=h4, hg=hg: s.activation(exs[:, :], p_s[:, h4 * 512:(h4 + 1) * 512], AF.Exp, bias=nmnew[:, h4:h4 + 1], accum_out=bs4[:, hg * 4 + h4:hg * 4 + h4 + 1]))
                    seq.op("vector", lambda v, hg=hg: v.tensor_tensor(dsc[:, :], mrun[:, hg * 4:(hg + 1) * 4], mnew[:, :], ALU.subtract))
                    seq.op("scalar", lambda s: s.activation(dsc[:, :], dsc[:, :], AF.Exp))
                    seq.op("vector", lambda v, hg=hg: v.tensor_tensor(zrun[:, hg * 4:(hg + 1) * 4], zrun[:, hg * 4:(hg + 1) * 4], dsc[:, :], ALU.mult))
                    seq.op("vector", lambda v, hg=hg: v.tensor_tensor(zrun[:, hg * 4:(hg + 1) * 4], zrun[:, hg * 4:(hg + 1) * 4], bs4[:, hg * 4:(hg + 1) * 4], ALU.add))
                    seq.op("vector", lambda v, hg=hg: v.tensor_copy(mrun[:, hg * 4:(hg + 1) * 4], mnew[:, :]))
            seq.op("scalar", lambda s: s.activation(bias2[:, :], zrun[:, :], AF.Ln))
            seq.op("vector", lambda v: v.tensor_tensor(bias2[:, :], bias2[:, :], mrun[:, :], ALU.add))
            seq.op("vector", lambda v: v.tensor_scalar(bias2[:, :], bias2[:, :], LN16, -1.0, ALU.add, ALU.mult))
            for kc in range(nkc):
                seq.dma("sync", kt.rearrange("p (i k) -> p i k", i=8),
                        kTd.rearrange("i p t -> p i t")[:, :, kc * 512:(kc + 1) * 512])
                for hg in range(4):
                    for h4 in range(4):
                        h = hg * 4 + h4
                        of, half = h // 2, (h % 2) * 64
                        seq.op("tensor", lambda t, of=of, half=half, h4=h4, sl=sl: t.matmul(
                            p_s[:, h4 * 512:(h4 + 1) * 512],
                            qT[of][half:half + 64, sl * 128:(sl + 1) * 128],
                            kt[half:half + 64, of * 512:(of + 1) * 512], start=True, stop=True))
                    if kc == nkc - 1:
                        seq.op("vector", lambda v: v.tensor_tensor(p_s[:, :], p_s[:, :], mskS[:, :], ALU.add))
                    for h4 in range(4):
                        h = hg * 4 + h4
                        seq.op("scalar", lambda s, h4=h4, h=h: s.activation(exs[:, :], p_s[:, h4 * 512:(h4 + 1) * 512], AF.Exp, bias=bias2[:, h:h + 1]))
                        if h == 0:
                            seq.op("vector", lambda v, kc=kc: v.tensor_copy(avg[:, kc * 512:(kc + 1) * 512], exs[:, :]))
                        else:
                            seq.op("vector", lambda v, kc=kc: v.tensor_tensor(avg[:, kc * 512:(kc + 1) * 512], avg[:, kc * 512:(kc + 1) * 512], exs[:, :], ALU.add))
            for kc in range(nkc):
                seq.dma("sync", gmc[:, :], gum[sl * 128:(sl + 1) * 128, kc * 512:(kc + 1) * 512])
                seq.op("vector", lambda v, kc=kc: v.tensor_tensor(exs[:, :], avg[:, kc * 512:(kc + 1) * 512], gmc[:, :], ALU.add))
                seq.op("scalar", lambda s: s.activation(wbf[:, :], exs[:, :], AF.Exp, scale=C05, accum_out=sS[:, 0:1]))
                if kc == 0:
                    seq.op("vector", lambda v: v.tensor_copy(sS[:, 1:2], sS[:, 0:1]))
                else:
                    seq.op("vector", lambda v: v.tensor_tensor(sS[:, 1:2], sS[:, 1:2], sS[:, 0:1], ALU.add))
                for sub in range(4):
                    seq.op("tensor", lambda t, sub=sub: t.transpose(p_t[:, :], wbf[:, sub * 128:(sub + 1) * 128], identb[:, :]))
                    seq.op("vector", lambda v, sub=sub: v.tensor_copy(wT[:, sub * 128:(sub + 1) * 128], p_t[:, :]))
                seq.dma("gpsimd", hv.rearrange("p (s e) -> p s e", s=4),
                        htm.rearrange("(c s p) e -> p s e c", p=128, s=4)[:, :, :, kc].squeeze())
                for sub in range(4):
                    for oh in range(2):
                        seq.op("tensor", lambda t, sub=sub, oh=oh, kc=kc, nkc=nkc: t.matmul(
                            p_av[:, oh * 512:(oh + 1) * 512],
                            wT[:, sub * 128:(sub + 1) * 128],
                            hv[:, sub * 1024 + oh * 512:sub * 1024 + (oh + 1) * 512],
                            start=(kc == 0 and sub == 0), stop=(kc == nkc - 1 and sub == 3)))
            seq.op("vector", lambda v, sl=sl: v.tensor_tensor(sS[:, 1:2], sS[:, 1:2], rinv[:, sl:sl + 1], ALU.add))
            seq.op("vector", lambda v: v.reciprocal(sS[:, 2:3], sS[:, 1:2]))
            seq.op("vector", lambda v, sl=sl: v.tensor_scalar(sS[:, 3:4], rinv[:, sl:sl + 1], -1.0, 1.0, ALU.mult, ALU.add))
            seq.op("vector", lambda v: v.tensor_scalar(oev[:, :], p_av[:, :], sS[:, 2:3], sS[:, 3:4], ALU.mult, ALU.mult))
            seq.dma("sync", attno[sl * 128:(sl + 1) * 128, :], oev[:, :])
        seq.build(block, step)
    es.close()
    return nc

def host_l1_inputs(T, inp, gumbel):
    ipw = inp["L0_attn_in_proj_weight"]
    ipb = inp["L0_attn_in_proj_bias"]
    x = inp["L0_attn_hidden_states"]
    wqT = np.ascontiguousarray(ipw[:E].T)
    wkT = np.ascontiguousarray(ipw[E:2 * E].T)
    bq = ipb[:E]; bk = ipb[E:2 * E]
    ident = np.eye(128, dtype=np.float32)
    maps = []
    for c in range(8):
        b, j = c // 4, c % 4
        rows = core_token_rows(T, c)
        g = gumbel[b][rows].copy()
        kk = np.arange(T)[None, :]
        g[kk >= rows[:, None]] = NEG
        m4 = np.zeros((128, 512), np.float32)
        p = np.arange(128)[:, None]
        m4[np.arange(512)[None, :] >= (128 * j + p)] = NEG
        mask4 = np.tile(m4, (1, 4))
        rowinv = np.zeros((128, 16), np.float32)
        if j == 0:
            rowinv[0, 0] = 1.0
        maps.append(dict(
            hT=np.ascontiguousarray(x[b].T),
            hqT=np.ascontiguousarray(x[b][rows].T),
            htm=np.ascontiguousarray(x[b]),
            wqT=wqT, wkT=wkT,
            bqc=np.ascontiguousarray(bq.reshape(8, 128).T),
            bkc=np.ascontiguousarray(bk.reshape(8, 128).T),
            gum=np.ascontiguousarray(g),
            mask4=np.ascontiguousarray(mask4),
            rowinv=rowinv, identf=ident))
    return maps

I32 = mybir.dt.int32
GROUPS = [[0, 1, 2, 3], [4, 5, 6, 7]]

def build_fused(T, phases=(1,2,3,4,5), with_cc=True, debug=False):
    TC = T // 4
    NSL = TC // 128
    NKT = T // 512
    NTB = NSL
    NTS = ceil_div(TC, 512)
    TSZ = min(TC, 512)
    NF0 = FF0 // 128
    NHC = FF1 // 128
    TCH = min(1024, T)
    NCH = T // TCH
    NS2 = TCH // 512
    TSZ2 = min(1024, TC)
    NTCG = TC // TSZ2
    NS5 = TSZ2 // 512 if TSZ2 >= 512 else 1
    S5 = min(512, TSZ2)

    nc = bass.Bass("TRN2", target_bir_lowering=False)
    D = lambda n, s, dt=F32, kind=None: (nc.dram_tensor(n, s, dt, kind=kind) if kind else nc.dram_tensor(n, s, dt))
    hT = D("hT", [E, T], kind="ExternalInput").ap()
    hqT = D("hqT", [E, TC], kind="ExternalInput").ap()
    htm = D("htm", [T, E], kind="ExternalInput").ap()
    wqT = D("wqT", [E, E], kind="ExternalInput").ap()
    wkT = D("wkT", [E, E], kind="ExternalInput").ap()
    bqc = D("bqc", [128, 8], kind="ExternalInput").ap()
    bkc = D("bkc", [128, 8], kind="ExternalInput").ap()
    gum = D("gum", [TC, T], kind="ExternalInput").ap()
    mask4 = D("mask4", [128, 2048], kind="ExternalInput").ap()
    rowinv = D("rowinv", [128, 16], kind="ExternalInput").ap()
    identf = D("identf", [128, 128], kind="ExternalInput").ap()
    x0 = D("x0", [TC, E], kind="ExternalInput").ap()
    gateT = D("gateT", [E, NEXP], kind="ExternalInput").ap()
    w1T = D("w1T", [NEXP, E, FF0], kind="ExternalInput").ap()
    w3T = D("w3T", [NEXP, E, FF0], kind="ExternalInput").ap()
    w2T = D("w2T", [NEXP, FF0, E], kind="ExternalInput").ap()
    grow = D("grow", [1, E], kind="ExternalInput").ap()
    brow = D("brow", [1, E], kind="ExternalInput").ap()
    wup = D("wup", [E, 2 * DI], kind="ExternalInput").ap()
    bupc = D("bupc", [128, 8], kind="ExternalInput").ap()
    cwc = D("cwc", [128, 4, 3], kind="ExternalInput").ap()
    cbc = D("cbc", [128, 4], kind="ExternalInput").ap()
    wdel = D("wdel", [DI, 128], kind="ExternalInput").ap()
    bdc = D("bdc", [128, 1], kind="ExternalInput").ap()
    wB = D("wB", [DI, NS], kind="ExternalInput").ap()
    wC = D("wC", [DI, NS], kind="ExternalInput").ap()
    aarr = D("aarr", [128, NS], kind="ExternalInput").ap()
    dcol = D("dcol", [128, 1], kind="ExternalInput").ap()
    patD = D("patD", [128, 4 * 128], kind="ExternalInput").ap()
    patB = D("patB", [128, 32 * 128], kind="ExternalInput").ap()
    patY = D("patY", [128, 4 * 128], kind="ExternalInput").ap()
    selpat = D("selpat", [128, 4 * 128], kind="ExternalInput").ap()
    wdown = D("wdown", [DI, E], kind="ExternalInput").ap()
    bdrow = D("bdrow", [1, E], kind="ExternalInput").ap()
    wred = D("wred", [E, GATE], kind="ExternalInput").ap()
    wg = D("wg", [NEXP, GATE], kind="ExternalInput").ap()
    w1f = D("w1f", [NEXP, E, FF1], kind="ExternalInput").ap()
    b1c = D("b1c", [NEXP, 128, FF1 // 128], kind="ExternalInput").ap()
    w2f = D("w2f", [NEXP, FF1, E], kind="ExternalInput").ap()
    b2r = D("b2r", [NEXP, 1, E], kind="ExternalInput").ap()
    jval = D("jval", [1, 1], I32, kind="ExternalInput").ap()
    outT = D("out", [TC, E], F32, kind="ExternalOutput").ap()
    dk = "ExternalOutput" if debug else None
    kTd = D("kTd", [8, 128, T], BF16).ap()
    attnD = D("attnD", [TC, E], kind=dk).ap()
    x1d = D("x1d", [TC, E], kind=dk).ap()
    ydA = D("ydA", [TC, E], kind=dk).ap()
    x2d = D("x2d", [TC, E], kind=dk).ap()
    leftIn = nc.dram_tensor("leftIn", [DI, TC], F32)
    leftAG = nc.dram_tensor("leftAG", [4 * DI, TC], F32)
    rightD = D("rightD", [DI, TC], kind=dk).ap()
    ysIn = nc.dram_tensor("ysIn", [128, T], F32)
    ysAG = nc.dram_tensor("ysAG", [DI, T], F32)
    if debug:
        dbgL = nc.dram_tensor("dbgL", [DI, TC], F32, kind="ExternalOutput")
        dbgLA = nc.dram_tensor("dbgLA", [4 * DI, TC], F32, kind="ExternalOutput")
        dbgY = nc.dram_tensor("dbgY", [128, T], F32, kind="ExternalOutput")
        dbgYA = nc.dram_tensor("dbgYA", [DI, T], F32, kind="ExternalOutput")
    x3d = D("x3d", [TC, E], kind=dk).ap()
    ydB = D("ydB", [TC, E], kind=dk).ap()

    seq = Seq(nc)
    rj = nc.gpsimd.alloc_register("rj")
    rtmp = nc.gpsimd.alloc_register("rtmp")
    LN16 = float(np.log(16.0))

    with (nc.semaphore("step") as step, nc.Block() as block):
        def _phase1():
            with ExitStack() as es:
                sb = lambda n, s, d=F32: es.enter_context(nc.sbuf_tensor(n, s, d)).ap()
                wS = [sb(f"a_wS{i}", [128, E], BF16) for i in range(8)]
                hq = [sb(f"a_hq{i}", [128, TC], BF16) for i in range(8)]
                qT = [sb(f"a_qT{i}", [128, TC], BF16) for i in range(8)]
                kt = sb("a_kt", [128, 8 * 512], BF16)
                kev = sb("a_kev", [128, 512], BF16)
                hv = sb("a_hv", [128, 4 * 1024], BF16)
                avg = sb("a_avg", [128, T])
                mskS = sb("a_mskS", [128, 2048])
                gmc = sb("a_gmc", [128, 512])
                exs = sb("a_exs", [128, 512])
                wbf = sb("a_wbf", [128, 512], BF16)
                wT = sb("a_wT", [128, 512], BF16)
                oev = sb("a_oev", [128, E])
                mrun = sb("a_mrun", [128, 16])
                zrun = sb("a_zrun", [128, 16])
                mnew = sb("a_mnew", [128, 4])
                nmnew = sb("a_nmnew", [128, 4])
                bs4 = sb("a_bs4", [128, 16])
                dsc = sb("a_dsc", [128, 4])
                bias2 = sb("a_bias2", [128, 16])
                sS = sb("a_sS", [128, 4])
                bqs = sb("a_bqs", [128, 8])
                bks = sb("a_bks", [128, 8])
                rinv = sb("a_rinv", [128, 16])
                cols = sb("a_cols", [128, 4])
                sident = sb("a_sident", [128, 128])
                identb = sb("a_identb", [128, 128], BF16)
                jvS = sb("a_jv", [1, 1], I32)
                p_s = es.enter_context(nc.psum_tensor("a_ps", [128, 2048], F32)).ap()
                p_av = es.enter_context(nc.psum_tensor("a_pav", [128, 1024], F32)).ap()
                p_t = es.enter_context(nc.psum_tensor("a_pt", [128, 128], BF16)).ap()

                seq.dma("sync", sident[:, :], identf)
                seq.dma("gpsimd", identb[:, :], identf)
                seq.dma("sync", mskS[:, :], mask4)
                seq.dma("sync", bqs[:, :], bqc)
                seq.dma("sync", bks[:, :], bkc)
                seq.dma("sync", rinv[:, :], rowinv)
                seq.dma("sync", jvS[:, :], jval)
                def _ld(g):
                    g.reg_load(rj, jvS[0:1, 0:1])
                    return g.engine_nop()
                seq.op("gpsimd", _ld)
                seq.op("vector", lambda v: v.memset(cols[:, 0:1], 0.125))
                seq.op("vector", lambda v: v.memset(cols[:, 1:2], 0.5))
                seq.op("vector", lambda v: v.memset(avg[:, :], 0.0))
                C18, C05 = cols[:, 0:1], cols[:, 1:2]
                seq.op("vector", lambda v: v.tensor_scalar(bqs[:, :], bqs[:, :], 0.125, None, ALU.mult))
                for ic in range(8):
                    seq.dma("gpsimd", wS[ic][:, :], wqT[ic * 128:(ic + 1) * 128, :])
                    seq.dma("gpsimd", hq[ic][:, :], hqT[ic * 128:(ic + 1) * 128, :])
                for of in range(8):
                    for ts in range(NTS):
                        for ic in range(8):
                            seq.op("tensor", lambda t, ic=ic, of=of, ts=ts: t.matmul(p_s[:, 0:TSZ], wS[ic][:, of * 128:(of + 1) * 128], hq[ic][:, ts * TSZ:(ts + 1) * TSZ], start=(ic == 0), stop=(ic == 7)))
                        seq.op("scalar", lambda s, of=of, ts=ts: s.activation(qT[of][:, ts * TSZ:(ts + 1) * TSZ], p_s[:, 0:TSZ], AF.Identity, bias=bqs[:, of:of + 1], scale=C18))
                for ic in range(8):
                    seq.dma("gpsimd", wS[ic][:, :], wkT[ic * 128:(ic + 1) * 128, :])
                for tc in range(NKT):
                    seq.dma("gpsimd", kt.rearrange("p (i k) -> p i k", i=8),
                            hT.rearrange("(i p) t -> p i t", p=128)[:, :, tc * 512:(tc + 1) * 512])
                    for of in range(8):
                        for ic in range(8):
                            seq.op("tensor", lambda t, ic=ic, of=of: t.matmul(p_s[:, 0:512], wS[ic][:, of * 128:(of + 1) * 128], kt[:, ic * 512:(ic + 1) * 512], start=(ic == 0), stop=(ic == 7)))
                        seq.op("scalar", lambda s, of=of: s.activation(kev[:, :], p_s[:, 0:512], AF.Identity, bias=bks[:, of:of + 1]))
                        seq.dma("sync", kTd[of, :, tc * 512:(tc + 1) * 512], kev[:, :])
                for sl in range(NSL):
                    nkc = sl + 1
                    seq.op("vector", lambda v: v.memset(mrun[:, :], -1e30))
                    seq.op("vector", lambda v: v.memset(zrun[:, :], 0.0))
                    for kc in range(nkc):
                        seq.dma("sync", kt.rearrange("p (i k) -> p i k", i=8),
                                kTd.rearrange("i p t -> p i t")[:, :, kc * 512:(kc + 1) * 512])
                        for hg in range(4):
                            for h4 in range(4):
                                h = hg * 4 + h4
                                of, half = h // 2, (h % 2) * 64
                                seq.op("tensor", lambda t, of=of, half=half, h4=h4, sl=sl: t.matmul(
                                    p_s[:, h4 * 512:(h4 + 1) * 512],
                                    qT[of][half:half + 64, sl * 128:(sl + 1) * 128],
                                    kt[half:half + 64, of * 512:(of + 1) * 512], start=True, stop=True))
                            if kc == nkc - 1:
                                seq.op("vector", lambda v: v.tensor_tensor(p_s[:, :], p_s[:, :], mskS[:, :], ALU.add))
                            seq.op("vector", lambda v: v.tensor_reduce(dsc[:, 0:4], p_s.rearrange("p (h k) -> p h k", h=4), AX.X, ALU.max))
                            seq.op("vector", lambda v, hg=hg: v.tensor_tensor(mnew[:, :], mrun[:, hg * 4:(hg + 1) * 4], dsc[:, :], ALU.max))
                            seq.op("vector", lambda v: v.tensor_scalar(nmnew[:, :], mnew[:, :], -1.0, None, ALU.mult))
                            for h4 in range(4):
                                seq.op("scalar", lambda s, h4=h4, hg=hg: s.activation(exs[:, :], p_s[:, h4 * 512:(h4 + 1) * 512], AF.Exp, bias=nmnew[:, h4:h4 + 1], accum_out=bs4[:, hg * 4 + h4:hg * 4 + h4 + 1]))
                            seq.op("vector", lambda v, hg=hg: v.tensor_tensor(dsc[:, :], mrun[:, hg * 4:(hg + 1) * 4], mnew[:, :], ALU.subtract))
                            seq.op("scalar", lambda s: s.activation(dsc[:, :], dsc[:, :], AF.Exp))
                            seq.op("vector", lambda v, hg=hg: v.tensor_tensor(zrun[:, hg * 4:(hg + 1) * 4], zrun[:, hg * 4:(hg + 1) * 4], dsc[:, :], ALU.mult))
                            seq.op("vector", lambda v, hg=hg: v.tensor_tensor(zrun[:, hg * 4:(hg + 1) * 4], zrun[:, hg * 4:(hg + 1) * 4], bs4[:, hg * 4:(hg + 1) * 4], ALU.add))
                            seq.op("vector", lambda v, hg=hg: v.tensor_copy(mrun[:, hg * 4:(hg + 1) * 4], mnew[:, :]))
                    seq.op("scalar", lambda s: s.activation(bias2[:, :], zrun[:, :], AF.Ln))
                    seq.op("vector", lambda v: v.tensor_tensor(bias2[:, :], bias2[:, :], mrun[:, :], ALU.add))
                    seq.op("vector", lambda v: v.tensor_scalar(bias2[:, :], bias2[:, :], LN16, -1.0, ALU.add, ALU.mult))
                    for kc in range(nkc):
                        seq.dma("sync", kt.rearrange("p (i k) -> p i k", i=8),
                                kTd.rearrange("i p t -> p i t")[:, :, kc * 512:(kc + 1) * 512])
                        for hg in range(4):
                            for h4 in range(4):
                                h = hg * 4 + h4
                                of, half = h // 2, (h % 2) * 64
                                seq.op("tensor", lambda t, of=of, half=half, h4=h4, sl=sl: t.matmul(
                                    p_s[:, h4 * 512:(h4 + 1) * 512],
                                    qT[of][half:half + 64, sl * 128:(sl + 1) * 128],
                                    kt[half:half + 64, of * 512:(of + 1) * 512], start=True, stop=True))
                            if kc == nkc - 1:
                                seq.op("vector", lambda v: v.tensor_tensor(p_s[:, :], p_s[:, :], mskS[:, :], ALU.add))
                            for h4 in range(4):
                                h = hg * 4 + h4
                                seq.op("scalar", lambda s, h4=h4, h=h: s.activation(exs[:, :], p_s[:, h4 * 512:(h4 + 1) * 512], AF.Exp, bias=bias2[:, h:h + 1]))
                                if h == 0:
                                    seq.op("vector", lambda v, kc=kc: v.tensor_copy(avg[:, kc * 512:(kc + 1) * 512], exs[:, :]))
                                else:
                                    seq.op("vector", lambda v, kc=kc: v.tensor_tensor(avg[:, kc * 512:(kc + 1) * 512], avg[:, kc * 512:(kc + 1) * 512], exs[:, :], ALU.add))
                    for kc in range(nkc):
                        seq.dma("sync", gmc[:, :], gum[sl * 128:(sl + 1) * 128, kc * 512:(kc + 1) * 512])
                        seq.op("vector", lambda v, kc=kc: v.tensor_tensor(exs[:, :], avg[:, kc * 512:(kc + 1) * 512], gmc[:, :], ALU.add))
                        seq.op("scalar", lambda s: s.activation(wbf[:, :], exs[:, :], AF.Exp, scale=C05, accum_out=sS[:, 0:1]))
                        if kc == 0:
                            seq.op("vector", lambda v: v.tensor_copy(sS[:, 1:2], sS[:, 0:1]))
                        else:
                            seq.op("vector", lambda v: v.tensor_tensor(sS[:, 1:2], sS[:, 1:2], sS[:, 0:1], ALU.add))
                        for sub in range(4):
                            seq.op("tensor", lambda t, sub=sub: t.transpose(p_t[:, :], wbf[:, sub * 128:(sub + 1) * 128], identb[:, :]))
                            seq.op("vector", lambda v, sub=sub: v.tensor_copy(wT[:, sub * 128:(sub + 1) * 128], p_t[:, :]))
                        seq.dma("gpsimd", hv.rearrange("p (s e) -> p s e", s=4),
                                htm.rearrange("(c s p) e -> p s e c", p=128, s=4)[:, :, :, kc].squeeze())
                        for sub in range(4):
                            for oh in range(2):
                                seq.op("tensor", lambda t, sub=sub, oh=oh, kc=kc, nkc=nkc: t.matmul(
                                    p_av[:, oh * 512:(oh + 1) * 512],
                                    wT[:, sub * 128:(sub + 1) * 128],
                                    hv[:, sub * 1024 + oh * 512:sub * 1024 + (oh + 1) * 512],
                                    start=(kc == 0 and sub == 0), stop=(kc == nkc - 1 and sub == 3)))
                    seq.op("vector", lambda v, sl=sl: v.tensor_tensor(sS[:, 1:2], sS[:, 1:2], rinv[:, sl:sl + 1], ALU.add))
                    seq.op("vector", lambda v: v.reciprocal(sS[:, 2:3], sS[:, 1:2]))
                    seq.op("vector", lambda v, sl=sl: v.tensor_scalar(sS[:, 3:4], rinv[:, sl:sl + 1], -1.0, 1.0, ALU.mult, ALU.add))
                    seq.op("vector", lambda v: v.tensor_scalar(oev[:, :], p_av[:, :], sS[:, 2:3], sS[:, 3:4], ALU.mult, ALU.mult))
                    seq.dma("sync", attnD[sl * 128:(sl + 1) * 128, :], oev[:, :])

        if 1 in phases: _phase1()
        def _phase2():
            with ExitStack() as es:
                sb = lambda n, s, d=F32: es.enter_context(nc.sbuf_tensor(n, s, d)).ap()
                xbf = [sb(f"b_xbf{i}", [128, TC], BF16) for i in range(8)]
                tmA = sb("b_tmA", [128, E])
                tmB = sb("b_tmB", [128, E])
                wte = sb("b_wte", [128, NTB * NEXP])
                sident = sb("b_sident", [128, 128])
                gat = sb("b_gat", [128, 64], BF16)
                gatf = sb("b_gatf", [128, 64])
                xtr = sb("b_xtr", [128, 128])
                wArena = sb("b_wArena", [128, 8 * FF0], BF16)
                wA = [wArena[:, i * FF0:(i + 1) * FF0] for i in range(8)]
                w2s = [wArena[:, i * E:(i + 1) * E] for i in range(22)]
                hA = [sb(f"b_hA{i}", [128, TC], BF16) for i in range(22)]
                cols = sb("b_cols", [128, 8])
                st1 = sb("b_st1", [128, 16])
                yev = sb("b_yev", [128, 512])
                p_tr = es.enter_context(nc.psum_tensor("b_ptr", [128, 128], F32)).ap()
                p_g = es.enter_context(nc.psum_tensor("b_pg", [128, 16], F32)).ap()
                p_a = es.enter_context(nc.psum_tensor("b_pa", [128, 512], F32)).ap()
                p_b = es.enter_context(nc.psum_tensor("b_pb", [128, 512], F32)).ap()

                seq.dma("sync", sident[:, :], identf)
                seq.dma("gpsimd", gat.rearrange("p (i e) -> p i e", i=8), gateT.rearrange("(i p) e -> p i e", p=128))
                seq.dma("sync", gatf.rearrange("p (i e) -> p i e", i=8), gateT.rearrange("(i p) e -> p i e", p=128))
                seq.op("vector", lambda v: v.memset(cols[:, 0:1], 1e-5))
                EPS = cols[:, 0:1]

                def ln_tb(tb, src_emitters, out_dram, transpose_to=None, extra=None, do_gating=False, out_dram2=None):
                    for fn in src_emitters:
                        fn()
                    if extra is not None:
                        extra()
                    b0 = seq.barrier()
                    seq.op("vector", lambda v: v.tensor_reduce(st1[:, 0:1], tmA[:, :], AX.XYZW, ALU.add), wait=b0)
                    seq.op("scalar", lambda s: s.activation(tmB[:, :], tmA[:, :], AF.Square), wait=b0)
                    seq.op("vector", lambda v: v.tensor_reduce(st1[:, 1:2], tmB[:, :], AX.XYZW, ALU.add))
                    seq.op("vector", lambda v: v.tensor_scalar(st1[:, 2:3], st1[:, 0:1], 1.0 / E, None, ALU.mult))
                    seq.op("vector", lambda v: v.tensor_scalar(st1[:, 3:4], st1[:, 1:2], 1.0 / E, None, ALU.mult))
                    seq.op("vector", lambda v: v.tensor_tensor(st1[:, 4:5], st1[:, 2:3], st1[:, 2:3], ALU.mult))
                    seq.op("vector", lambda v: v.tensor_tensor(st1[:, 5:6], st1[:, 3:4], st1[:, 4:5], ALU.subtract))
                    seq.op("scalar", lambda s: s.activation(st1[:, 6:7], st1[:, 5:6], AF.Sqrt, bias=EPS))
                    seq.op("vector", lambda v: v.reciprocal(st1[:, 7:8], st1[:, 6:7]))
                    seq.op("vector", lambda v: v.tensor_scalar(tmA[:, :], tmA[:, :], st1[:, 2:3], st1[:, 7:8], ALU.subtract, ALU.mult))
                    if out_dram is not None:
                        seq.dma("sync", out_dram[tb * 128:(tb + 1) * 128, :], tmA[:, :])
                    if out_dram2 is not None:
                        seq.dma("sync", out_dram2[tb * 128:(tb + 1) * 128, :], tmA[:, :])
                    if transpose_to is not None:
                        for fc in range(8):
                            seq.op("tensor", lambda t, fc=fc: t.transpose(p_tr[:, :], tmA[:, fc * 128:(fc + 1) * 128], sident[:, :]))
                            seq.op("vector", lambda v, fc=fc, tb=tb: v.tensor_copy(transpose_to[fc][:, tb * 128:(tb + 1) * 128], p_tr[:, :]))
                            if do_gating:
                                seq.op("vector", lambda v, fc=fc: v.tensor_copy(xtr[:, :], p_tr[:, :]))
                                seq.op("tensor", lambda t, fc=fc: t.matmul(p_g[:, 0:8], xtr[:, :], gatf[:, fc * 8:(fc + 1) * 8], start=(fc == 0), stop=(fc == 7)))

                def gating_post(tb):
                    seq.op("vector", lambda v: v.tensor_copy(st1[:, 0:8], p_g[:, 0:8]))
                    seq.op("vector", lambda v: v.reduce_max(st1[:, 8:9], st1[:, 0:8], AX.XYZW, negate=True))
                    seq.op("scalar", lambda s: s.activation(st1[:, 0:8], st1[:, 0:8], AF.Exp, bias=st1[:, 8:9], accum_out=st1[:, 9:10]))
                    seq.op("vector", lambda v: v.reciprocal(st1[:, 10:11], st1[:, 9:10]))
                    seq.op("vector", lambda v: v.tensor_scalar(st1[:, 0:8], st1[:, 0:8], st1[:, 10:11], None, ALU.mult))
                    seq.op("vector", lambda v: v.reduce_max(st1[:, 11:12], st1[:, 0:8], AX.XYZW))
                    seq.op("vector", lambda v: v.tensor_scalar(tmB[:, 0:8], st1[:, 0:8], st1[:, 11:12], None, ALU.is_ge))
                    seq.op("vector", lambda v: v.scalar_tensor_tensor(tmB[:, 8:16], tmB[:, 0:8], -2.0, st1[:, 0:8], ALU.mult, ALU.add))
                    seq.op("vector", lambda v: v.reduce_max(st1[:, 13:14], tmB[:, 8:16], AX.XYZW))
                    seq.op("vector", lambda v: v.tensor_scalar(tmB[:, 0:8], st1[:, 0:8], st1[:, 13:14], None, ALU.is_ge))
                    seq.op("vector", lambda v: v.tensor_tensor(st1[:, 14:15], st1[:, 11:12], st1[:, 13:14], ALU.add))
                    seq.op("vector", lambda v: v.reciprocal(st1[:, 15:16], st1[:, 14:15]))
                    seq.op("vector", lambda v: v.tensor_tensor(tmB[:, 0:8], tmB[:, 0:8], st1[:, 0:8], ALU.mult))
                    seq.op("vector", lambda v, tb=tb: v.tensor_scalar(wte[:, tb * 8:(tb + 1) * 8], tmB[:, 0:8], st1[:, 15:16], None, ALU.mult))

                for tb in range(NTB):
                    ln_tb(tb,
                          [lambda tb=tb: seq.dma("sync", tmA[:, :], x0[tb * 128:tb * 128 + 128, :]),
                           lambda tb=tb: seq.dma("sync", tmB[:, :], attnD[tb * 128:tb * 128 + 128, :])],
                          x1d, transpose_to=xbf, do_gating=True,
                          extra=lambda: seq.op("vector", lambda v: v.tensor_tensor(tmA[:, :], tmA[:, :], tmB[:, :], ALU.add)))
                    gating_post(tb)

                for e in range(NEXP):
                    for ic in range(8):
                        seq.dma("gpsimd", wA[ic], w1T[e, ic * 128:(ic + 1) * 128, :])
                    for hc in range(NF0):
                        for ts in range(NTS):
                            for ic in range(8):
                                seq.op("tensor", lambda t, ic=ic, hc=hc, ts=ts: t.matmul(p_a[:, 0:TSZ], wA[ic][:, hc * 128:(hc + 1) * 128], xbf[ic][:, ts * TSZ:(ts + 1) * TSZ], start=(ic == 0), stop=(ic == 7)))
                            seq.op("scalar", lambda s, hc=hc, ts=ts: s.activation(hA[hc][:, ts * TSZ:(ts + 1) * TSZ], p_a[:, 0:TSZ], AF.Silu))
                    for ic in range(8):
                        seq.dma("gpsimd", wA[ic], w3T[e, ic * 128:(ic + 1) * 128, :])
                    for hc in range(NF0):
                        for ts in range(NTS):
                            for ic in range(8):
                                seq.op("tensor", lambda t, ic=ic, hc=hc, ts=ts: t.matmul(p_a[:, 0:TSZ], wA[ic][:, hc * 128:(hc + 1) * 128], xbf[ic][:, ts * TSZ:(ts + 1) * TSZ], start=(ic == 0), stop=(ic == 7)))
                            seq.op("vector", lambda v, hc=hc, ts=ts: v.tensor_tensor(hA[hc][:, ts * TSZ:(ts + 1) * TSZ], hA[hc][:, ts * TSZ:(ts + 1) * TSZ], p_a[:, 0:TSZ], ALU.mult))
                    for hc in range(NF0):
                        seq.dma("gpsimd", w2s[hc], w2T[e, hc * 128:(hc + 1) * 128, :])
                    for tb in range(NTB):
                        for oh in range(2):
                            for hc in range(NF0):
                                seq.op("tensor", lambda t, hc=hc, tb=tb, oh=oh: t.matmul(p_b[:, 0:512], hA[hc][:, tb * 128:(tb + 1) * 128], w2s[hc][:, oh * 512:(oh + 1) * 512], start=(hc == 0), stop=(hc == NF0 - 1)))
                            seq.op("vector", lambda v, tb=tb, e=e: v.tensor_scalar(yev[:, 0:512], p_b[:, 0:512], wte[:, tb * 8 + e:tb * 8 + e + 1], None, ALU.mult))
                            if e == 0:
                                seq.dma("gpsimd", ydA[tb * 128:(tb + 1) * 128, oh * 512:(oh + 1) * 512], yev[:, 0:512])
                            else:
                                seq.dma("gpsimd", ydA[tb * 128:(tb + 1) * 128, oh * 512:(oh + 1) * 512], yev[:, 0:512], accum_op=ALU.add)

                for tb in range(NTB):
                    ln_tb(tb,
                          [lambda tb=tb: seq.dma("sync", tmA[:, :], x1d[tb * 128:tb * 128 + 128, :]),
                           lambda tb=tb: seq.dma("sync", tmB[:, :], ydA[tb * 128:tb * 128 + 128, :])],
                          x2d, transpose_to=None,
                          extra=lambda: seq.op("vector", lambda v: v.tensor_tensor(tmA[:, :], tmA[:, :], tmB[:, :], ALU.add)))

        if 2 in phases: _phase2()
        def _phase3():
            with ExitStack() as es:
                sb = lambda n, s, d=F32: es.enter_context(nc.sbuf_tensor(n, s, d)).ap()
                xbf = [sb(f"c_xbf{i}", [128, TC], BF16) for i in range(8)]
                tmA = sb("c_tmA", [128, E])
                tmB = sb("c_tmB", [128, E])
                grep = sb("c_grep", [128, E])
                brep = sb("c_brep", [128, E])
                growS = sb("c_growS", [1, E])
                browS = sb("c_browS", [1, E])
                sones = sb("c_sones", [1, 128])
                sident = sb("c_sident", [128, 128])
                wupS = [sb(f"c_wupS{i}", [128, 2 * DI], BF16) for i in range(8)]
                bupS = sb("c_bupS", [128, 8])
                st1 = sb("c_st1", [128, 16])
                cols = sb("c_cols", [128, 1])
                oev = sb("c_oev", [128, 512])
                p_tr = es.enter_context(nc.psum_tensor("c_ptr", [128, 128], F32)).ap()
                p_a = es.enter_context(nc.psum_tensor("c_pa", [128, 512], F32)).ap()
                p_r = es.enter_context(nc.psum_tensor("c_pr", [128, 512], F32)).ap()

                seq.dma("sync", sident[:, :], identf)
                seq.dma("sync", growS[:, :], grow)
                seq.dma("sync", browS[:, :], brow)
                seq.dma("sync", bupS[:, :], bupc)
                for ic in range(8):
                    seq.dma("gpsimd", wupS[ic][:, :], wup[ic * 128:(ic + 1) * 128, :])
                seq.op("vector", lambda v: v.memset(cols[:, 0:1], 1e-5))
                seq.op("vector", lambda v: v.memset(sones[:, :], 1.0))
                for half in range(2):
                    seq.op("tensor", lambda t, half=half: t.matmul(p_r[:, 0:512], sones[0:1, :], growS[0:1, half * 512:(half + 1) * 512], start=True, stop=True))
                    seq.op("vector", lambda v, half=half: v.tensor_copy(grep[:, half * 512:(half + 1) * 512], p_r[:, 0:512]))
                    seq.op("tensor", lambda t, half=half: t.matmul(p_r[:, 0:512], sones[0:1, :], browS[0:1, half * 512:(half + 1) * 512], start=True, stop=True))
                    seq.op("vector", lambda v, half=half: v.tensor_copy(brep[:, half * 512:(half + 1) * 512], p_r[:, 0:512]))
                for tb in range(NTB):
                    seq.dma("sync", tmA[:, :], x2d[tb * 128:tb * 128 + 128, :])
                    seq.op("vector", lambda v: v.tensor_reduce(st1[:, 0:1], tmA[:, :], AX.XYZW, ALU.add))
                    seq.op("scalar", lambda s: s.activation(tmB[:, :], tmA[:, :], AF.Square))
                    seq.op("vector", lambda v: v.tensor_reduce(st1[:, 1:2], tmB[:, :], AX.XYZW, ALU.add))
                    seq.op("vector", lambda v: v.tensor_scalar(st1[:, 2:3], st1[:, 0:1], 1.0 / E, None, ALU.mult))
                    seq.op("vector", lambda v: v.tensor_scalar(st1[:, 3:4], st1[:, 1:2], 1.0 / E, None, ALU.mult))
                    seq.op("vector", lambda v: v.tensor_tensor(st1[:, 4:5], st1[:, 2:3], st1[:, 2:3], ALU.mult))
                    seq.op("vector", lambda v: v.tensor_tensor(st1[:, 5:6], st1[:, 3:4], st1[:, 4:5], ALU.subtract))
                    seq.op("scalar", lambda s: s.activation(st1[:, 6:7], st1[:, 5:6], AF.Sqrt, bias=cols[:, 0:1]))
                    seq.op("vector", lambda v: v.reciprocal(st1[:, 7:8], st1[:, 6:7]))
                    seq.op("vector", lambda v: v.tensor_scalar(tmA[:, :], tmA[:, :], st1[:, 2:3], st1[:, 7:8], ALU.subtract, ALU.mult))
                    seq.op("vector", lambda v: v.tensor_tensor(tmA[:, :], tmA[:, :], grep[:, :], ALU.mult))
                    seq.op("vector", lambda v: v.tensor_tensor(tmA[:, :], tmA[:, :], brep[:, :], ALU.add))
                    for fc in range(8):
                        seq.op("tensor", lambda t, fc=fc: t.transpose(p_tr[:, :], tmA[:, fc * 128:(fc + 1) * 128], sident[:, :]))
                        seq.op("vector", lambda v, fc=fc, tb=tb: v.tensor_copy(xbf[fc][:, tb * 128:(tb + 1) * 128], p_tr[:, :]))
                for of in range(8):
                    for ts in range(NTS):
                        for ic in range(8):
                            seq.op("tensor", lambda t, ic=ic, of=of, ts=ts: t.matmul(p_a[:, 0:TSZ], wupS[ic][:, of * 128:(of + 1) * 128], xbf[ic][:, ts * TSZ:(ts + 1) * TSZ], start=(ic == 0), stop=(ic == 7)))
                        seq.op("scalar", lambda s, of=of: s.activation(oev[:, 0:TSZ], p_a[:, 0:TSZ], AF.Identity, bias=bupS[:, of:of + 1]))
                        dst = leftIn.ap() if of < 4 else rightD
                        ofr = (of % 4) * 128
                        seq.dma("sync", dst[ofr:ofr + 128, ts * TSZ:(ts + 1) * TSZ], oev[:, 0:TSZ])

        if 3 in phases: _phase3()
        if with_cc: seq.op("gpsimd", lambda g: g.collective_compute(
            "AllGather", ALU.bypass, replica_groups=GROUPS,
            ins=[leftIn.ap().opt()], outs=[leftAG.ap().opt()]), inc=1)

        def _phase4():
            with ExitStack() as es:
                sb = lambda n, s, d=F32: es.enter_context(nc.sbuf_tensor(n, s, d)).ap()
                lt = [sb(f"d_lt{i}", [128, TCH + 2]) for i in range(4)]
                uf = [sb(f"d_uf{i}", [128, TCH]) for i in range(4)]
                ub = [sb(f"d_ub{i}", [128, TCH], BF16) for i in range(4)]
                uown = sb("d_uown", [128, TCH])
                wdS = [sb(f"d_wdS{i}", [128, 128], BF16) for i in range(4)]
                wbS = [sb(f"d_wbS{i}", [128, NS], BF16) for i in range(4)]
                wcS = [sb(f"d_wcS{i}", [128, NS], BF16) for i in range(4)]
                selS = sb("d_selS", [128, 4 * 128])
                acol = sb("d_acol", [128, NS])
                dcolS = sb("d_dcolS", [128, 1])
                bdcS = sb("d_bdcS", [128, 1])
                cwS = sb("d_cwS", [128, 12])
                cbS = sb("d_cbS", [128, 4])
                patDS = sb("d_patDS", [128, 4 * 128])
                patBS = sb("d_patBS", [128, 32 * 128], BF16)
                patYS = sb("d_patYS", [128, 4 * 128], BF16)
                deltaS = sb("d_deltaS", [128, TCH])
                wSt = sb("d_wSt", [128, TCH])
                BTb = [sb(f"d_BTb{i}", [128, TCH], BF16) for i in range(2)]
                CTb = [sb(f"d_CTb{i}", [128, TCH], BF16) for i in range(2)]
                dRep = [sb(f"d_dRep{i}", [128, TCH]) for i in range(4)]
                wRep = [sb(f"d_wRep{i}", [128, TCH]) for i in range(4)]
                dAS = sb("d_dAS", [128, TCH])
                bS = sb("d_bS", [128, TCH])
                hS = sb("d_hS", [128, TCH])
                ymulb = sb("d_ymulb", [128, TCH], BF16)
                hstate = sb("d_hstate", [128, NS])
                ysS = sb("d_ysS", [128, TCH])
                sc1 = sb("d_sc1", [128, TCH])
                sc2 = sb("d_sc2", [128, TCH])
                consts = sb("d_consts", [128, 4])
                p_d = es.enter_context(nc.psum_tensor("d_pd", [128, 512], F32)).ap()
                p_B = es.enter_context(nc.psum_tensor("d_pB", [128, TCH], F32)).ap()
                p_C = es.enter_context(nc.psum_tensor("d_pC", [128, TCH], F32)).ap()
                p_Y = es.enter_context(nc.psum_tensor("d_pY", [128, TCH], F32)).ap()

                seq.dma("sync", cwS.rearrange("p (f k) -> p f k", f=4), cwc)
                seq.dma("sync", cbS[:, :], cbc)
                seq.dma("sync", bdcS[:, :], bdc)
                seq.dma("sync", dcolS[:, :], dcol)
                seq.dma("sync", patDS[:, :], patD)
                seq.dma("sync", selS[:, :], selpat)
                seq.dma("gpsimd", patBS[:, :], patB)
                seq.dma("gpsimd", patYS[:, :], patY)
                for ic in range(4):
                    seq.dma("gpsimd", wdS[ic][:, :], wdel[ic * 128:(ic + 1) * 128, :])
                    seq.dma("gpsimd", wbS[ic][:, :], wB[ic * 128:(ic + 1) * 128, :])
                    seq.dma("gpsimd", wcS[ic][:, :], wC[ic * 128:(ic + 1) * 128, :])
                seq.op("vector", lambda v: v.memset(consts[:, 0:1], 1.0))
                seq.op("vector", lambda v: v.memset(consts[:, 1:2], -1.0))
                seq.op("vector", lambda v: v.memset(hstate[:, :], 0.0))
                C_ONE, C_M1 = consts[:, 0:1], consts[:, 1:2]
                seq.dma("sync", acol[:, :], aarr)
                seq.op("scalar", lambda s: s.activation(acol[:, :], acol[:, :], AF.Exp))
                seq.op("vector", lambda v: v.tensor_scalar(acol[:, :], acol[:, :], -1.0, None, ALU.mult))

                lag = leftAG.ap()
                for k in range(NCH):
                    ks = k * TCH
                    for fc in range(4):
                        if k == 0:
                            seq.op("vector", lambda v, fc=fc: v.memset(lt[fc][:, 0:2], 0.0))
                        else:
                            seq.op("vector", lambda v, fc=fc: v.tensor_copy(lt[fc][:, 0:2], lt[fc][:, TCH:TCH + 2]))
                        for r_ in range(4):
                            src = bass.AP(leftAG, (r_ * 512 + fc * 128) * TC + (k * (TCH // 512)) * 128,
                                          [[TC, 128], [128, TCH // 512], [1, 128]])
                            dst = lt[fc][:, 2:TCH + 2].rearrange("p (a rr q) -> p a rr q", a=TCH // 512, rr=4)[:, :, r_:r_ + 1, :].squeeze(2)
                            seq.dma("sync", dst, src)
                        seq.op("vector", lambda v, fc=fc: v.tensor_scalar(sc1[:, :], lt[fc][:, 2:TCH + 2], cwS[:, fc * 3 + 2:fc * 3 + 3], cbS[:, fc:fc + 1], ALU.mult, ALU.add))
                        seq.op("vector", lambda v, fc=fc: v.scalar_tensor_tensor(sc1[:, :], lt[fc][:, 1:TCH + 1], cwS[:, fc * 3 + 1:fc * 3 + 2], sc1[:, :], ALU.mult, ALU.add))
                        seq.op("vector", lambda v, fc=fc: v.scalar_tensor_tensor(sc1[:, :], lt[fc][:, 0:TCH], cwS[:, fc * 3 + 0:fc * 3 + 1], sc1[:, :], ALU.mult, ALU.add))
                        seq.op("scalar", lambda s, fc=fc: s.activation(uf[fc][:, :], sc1[:, :], AF.Silu))
                        seq.op("vector", lambda v, fc=fc: v.tensor_copy(ub[fc][:, :], uf[fc][:, :]))
                    for sub in range(NS2):
                        for ic in range(4):
                            seq.op("tensor", lambda t, ic=ic, sub=sub: t.matmul(p_d[:, 0:512], selS[:, ic * 128:(ic + 1) * 128], uf[ic][:, sub * 512:(sub + 1) * 512], start=(ic == 0), stop=(ic == 3)))
                        seq.op("vector", lambda v, sub=sub: v.tensor_copy(uown[:, sub * 512:(sub + 1) * 512], p_d[:, 0:512]))
                    for sub in range(NS2):
                        for ic in range(4):
                            seq.op("tensor", lambda t, ic=ic, sub=sub: t.matmul(p_d[:, 0:512], wdS[ic][:, :], ub[ic][:, sub * 512:(sub + 1) * 512], start=(ic == 0), stop=(ic == 3)))
                        seq.op("scalar", lambda s, sub=sub: s.activation(sc1[:, 0:512], p_d[:, 0:512], AF.Identity, bias=bdcS[:, 0:1]))
                        seq.op("scalar", lambda s: s.activation(sc2[:, 0:512], sc1[:, 0:512], AF.Abs))
                        seq.op("scalar", lambda s: s.activation(sc2[:, 0:512], sc2[:, 0:512], AF.Exp, scale=C_M1))
                        seq.op("scalar", lambda s: s.activation(sc2[:, 0:512], sc2[:, 0:512], AF.Ln, bias=C_ONE))
                        seq.op("vector", lambda v: v.tensor_scalar(sc1[:, 0:512], sc1[:, 0:512], 0.0, None, ALU.max))
                        seq.op("vector", lambda v, sub=sub: v.tensor_tensor(deltaS[:, sub * 512:(sub + 1) * 512], sc1[:, 0:512], sc2[:, 0:512], ALU.add))
                    seq.op("vector", lambda v: v.tensor_tensor(wSt[:, :], deltaS[:, :], uown[:, :], ALU.mult))
                    for nh in range(2):
                        for sub in range(NS2):
                            for ic in range(4):
                                seq.op("tensor", lambda t, ic=ic, nh=nh, sub=sub: t.matmul(p_d[:, 0:512], wbS[ic][:, nh * 128:(nh + 1) * 128], ub[ic][:, sub * 512:(sub + 1) * 512], start=(ic == 0), stop=(ic == 3)))
                            seq.op("vector", lambda v, nh=nh, sub=sub: v.tensor_copy(BTb[nh][:, sub * 512:(sub + 1) * 512], p_d[:, 0:512]))
                            for ic in range(4):
                                seq.op("tensor", lambda t, ic=ic, nh=nh, sub=sub: t.matmul(p_d[:, 0:512], wcS[ic][:, nh * 128:(nh + 1) * 128], ub[ic][:, sub * 512:(sub + 1) * 512], start=(ic == 0), stop=(ic == 3)))
                            seq.op("vector", lambda v, nh=nh, sub=sub: v.tensor_copy(CTb[nh][:, sub * 512:(sub + 1) * 512], p_d[:, 0:512]))
                    for dg in range(4):
                        for sub in range(NS2):
                            seq.op("tensor", lambda t, dg=dg, sub=sub: t.matmul(p_d[:, 0:512], patDS[:, dg * 128:(dg + 1) * 128], deltaS[:, sub * 512:(sub + 1) * 512], start=True, stop=True))
                            seq.op("vector", lambda v, dg=dg, sub=sub: v.tensor_copy(dRep[dg][:, sub * 512:(sub + 1) * 512], p_d[:, 0:512]))
                            seq.op("tensor", lambda t, dg=dg, sub=sub: t.matmul(p_d[:, 0:512], patDS[:, dg * 128:(dg + 1) * 128], wSt[:, sub * 512:(sub + 1) * 512], start=True, stop=True))
                            seq.op("vector", lambda v, dg=dg, sub=sub: v.tensor_copy(wRep[dg][:, sub * 512:(sub + 1) * 512], p_d[:, 0:512]))
                    for ng in range(64):
                        j, nh = ng % 32, ng // 32
                        for sub in range(NS2):
                            seq.op("tensor", lambda t, j=j, nh=nh, sub=sub: t.matmul(p_B[:, sub * 512:(sub + 1) * 512], patBS[:, j * 128:(j + 1) * 128], BTb[nh][:, sub * 512:(sub + 1) * 512], start=True, stop=True))
                            seq.op("tensor", lambda t, j=j, nh=nh, sub=sub: t.matmul(p_C[:, sub * 512:(sub + 1) * 512], patBS[:, j * 128:(j + 1) * 128], CTb[nh][:, sub * 512:(sub + 1) * 512], start=True, stop=True))
                        for dg in range(4):
                            g = ng * 4 + dg
                            seq.op("scalar", lambda s, dg=dg, ng=ng: s.activation(dAS[:, :], dRep[dg][:, :], AF.Exp, scale=acol[:, ng * 4 + dg:ng * 4 + dg + 1]))
                            seq.op("vector", lambda v, dg=dg: v.tensor_tensor(bS[:, :], wRep[dg][:, :], p_B[:, :], ALU.mult))
                            seq.op("vector", lambda v, g=g: v.tensor_tensor_scan(hS[:, :], dAS[:, :], bS[:, :], hstate[:, g:g + 1], ALU.mult, ALU.add))
                            seq.op("vector", lambda v, g=g: v.tensor_copy(hstate[:, g:g + 1], hS[:, TCH - 1:TCH]))
                            seq.op("vector", lambda v: v.tensor_tensor(ymulb[:, :], hS[:, :], p_C[:, :], ALU.mult))
                            for sub in range(NS2):
                                seq.op("tensor", lambda t, dg=dg, ng=ng, sub=sub: t.matmul(p_Y[:, sub * 512:(sub + 1) * 512], patYS[:, dg * 128:(dg + 1) * 128], ymulb[:, sub * 512:(sub + 1) * 512], start=(ng == 0 and dg == 0), stop=(ng == 63 and dg == 3)))
                    for sub in range(NS2):
                        seq.op("vector", lambda v, sub=sub: v.scalar_tensor_tensor(ysS[:, sub * 512:(sub + 1) * 512], uown[:, sub * 512:(sub + 1) * 512], dcolS[:, 0:1], p_Y[:, sub * 512:(sub + 1) * 512], ALU.mult, ALU.add))
                    seq.dma("sync", ysIn.ap()[:, ks:ks + TCH], ysS[:, :])

        if 4 in phases: _phase4()
        if with_cc: seq.op("gpsimd", lambda g: g.collective_compute(
            "AllGather", ALU.bypass, replica_groups=GROUPS,
            ins=[ysIn.ap().opt()], outs=[ysAG.ap().opt()]), inc=1)

        def _phase5():
            with ExitStack() as es:
                sb = lambda n, s, d=F32: es.enter_context(nc.sbuf_tensor(n, s, d)).ap()
                arena1 = sb("e_arena1", [128, NHC * 1024], BF16)
                arena2 = sb("e_arena2", [128, 8 * FF1], BF16)
                x3bf = [sb(f"e_x3bf{i}", [128, TC], BF16) for i in range(8)]
                tmA = sb("e_tmA", [128, E])
                tmB = sb("e_tmB", [128, E])
                tmC = sb("e_tmC", [128, E])
                xtr = sb("e_xtr", [128, 128])
                wredf = sb("e_wredf", [128, 8 * GATE])
                wgS = sb("e_wgS", [8, GATE])
                wgT = sb("e_wgT", [16, 8])
                redT = sb("e_redT", [16, 128])
                bdr = sb("e_bdr", [1, E], BF16)
                b2S = sb("e_b2S", [1, E], BF16)
                b1S = sb("e_b1S", [128, NHC])
                onesb = sb("e_onesb", [1, 128], BF16)
                sident = sb("e_sident", [128, 128])
                st1 = sb("e_st1", [128, 16])
                sel8 = sb("e_sel8", [128, 8])
                wsel = sb("e_wsel", [128, NTB * 8])
                cols = sb("e_cols", [128, 4])
                yev = sb("e_yev", [128, 512])
                p_tr = es.enter_context(nc.psum_tensor("e_ptr", [128, 128], F32)).ap()
                p_a = es.enter_context(nc.psum_tensor("e_pa", [128, 512], F32)).ap()
                p_b = es.enter_context(nc.psum_tensor("e_pb", [128, 512], F32)).ap()
                p_g = es.enter_context(nc.psum_tensor("e_pg", [128, 16], F32)).ap()
                p_r = es.enter_context(nc.psum_tensor("e_pr", [128, 16], F32)).ap()

                a1f = arena1.bitcast(F32)
                ysS = [a1f[:, i * TC:(i + 1) * TC] for i in range(4)]
                srS = [a1f[:, (4 + i) * TC:(5 + i) * TC] for i in range(4)]
                hG = [arena1[:, i * 1024:(i + 1) * 1024] for i in range(NHC)]
                gdS = [arena2[:, i * TC:(i + 1) * TC] for i in range(4)]
                wdnS = [arena2[:, 4 * TC + i * E:4 * TC + (i + 1) * E] for i in range(4)]
                w1S = [arena2[:, i * FF1:(i + 1) * FF1] for i in range(8)]
                w2S = [arena2[:, i * 1024:(i + 1) * 1024] for i in range(NHC)]

                seq.dma("sync", sident[:, :], identf)
                seq.dma("sync", wredf.rearrange("p (i g) -> p i g", i=8), wred.rearrange("(i p) g -> p i g", p=128))
                seq.dma("sync", wgS[:, :], wg)
                seq.dma("gpsimd", bdr[:, :], bdrow)
                seq.op("vector", lambda v: v.memset(cols[:, 0:1], 1e-5))
                seq.op("vector", lambda v: v.memset(cols[:, 1:2], 0.0))
                seq.op("vector", lambda v: v.memset(onesb[:, :], 1.0))
                EPS, ZERO = cols[:, 0:1], cols[:, 1:2]
                seq.op("scalar", lambda s: s.activation(st1[0:8, 0:GATE], wgS[:, :], AF.Square))
                seq.op("vector", lambda v: v.tensor_reduce(sel8[0:8, 0:1], st1[0:8, 0:GATE], AX.XYZW, ALU.add))
                seq.op("scalar", lambda s: s.activation(sel8[0:8, 1:2], sel8[0:8, 0:1], AF.Sqrt, bias=ZERO[0:8, :]))
                seq.op("vector", lambda v: v.tensor_scalar(sel8[0:8, 1:2], sel8[0:8, 1:2], 1e-4, None, ALU.max))
                seq.op("vector", lambda v: v.reciprocal(sel8[0:8, 2:3], sel8[0:8, 1:2]))
                seq.op("vector", lambda v: v.tensor_scalar(wgS[:, :], wgS[:, :], sel8[0:8, 2:3], None, ALU.mult))
                seq.op("tensor", lambda t: t.transpose(p_g[0:16, 0:8], wgS[:, :], sident[0:8, 0:8]))
                seq.op("vector", lambda v: v.tensor_copy(wgT[:, :], p_g[0:16, 0:8]))
                for i in range(4):
                    def _ys(g, i=i):
                        g.reg_alu(rtmp, rj, i * 128 * T, ALU.add)
                        src = bass.AP(ysAG, rtmp, [[T, 128], [512, NSL], [1, 128]])
                        return g.dma_start(out=ysS[i].rearrange("p (a q) -> p a q", a=NSL), in_=src)
                    seq.op("gpsimd", _ys, inc=16)
                    seq.dma("sync", srS[i], rightD[i * 128:(i + 1) * 128, :])
                    seq.op("scalar", lambda s, i=i: s.activation(srS[i], srS[i], AF.Silu))
                    seq.op("vector", lambda v, i=i: v.tensor_tensor(gdS[i], ysS[i], srS[i], ALU.mult))
                    seq.dma("gpsimd", wdnS[i], wdown[i * 128:(i + 1) * 128, :])
                for tb in range(NTB):
                    seq.dma("sync", tmA[:, :], x2d[tb * 128:tb * 128 + 128, :])
                    for oh in range(2):
                        for ic in range(4):
                            seq.op("tensor", lambda t, ic=ic, oh=oh, tb=tb: t.matmul(p_a[:, 0:512], gdS[ic][:, tb * 128:(tb + 1) * 128], wdnS[ic][:, oh * 512:(oh + 1) * 512], start=(ic == 0), stop=False))
                        seq.op("tensor", lambda t, oh=oh: t.matmul(p_a[:, 0:512], onesb[0:1, :], bdr[0:1, oh * 512:(oh + 1) * 512], start=False, stop=True))
                        seq.op("vector", lambda v, oh=oh: v.scalar_tensor_tensor(tmB[:, oh * 512:(oh + 1) * 512], tmA[:, oh * 512:(oh + 1) * 512], 2.0, p_a[:, 0:512], ALU.mult, ALU.add))
                    seq.op("vector", lambda v: v.tensor_reduce(st1[:, 0:1], tmB[:, :], AX.XYZW, ALU.add))
                    seq.op("scalar", lambda s: s.activation(tmC[:, :], tmB[:, :], AF.Square))
                    seq.op("vector", lambda v: v.tensor_reduce(st1[:, 1:2], tmC[:, :], AX.XYZW, ALU.add))
                    seq.op("vector", lambda v: v.tensor_scalar(st1[:, 2:3], st1[:, 0:1], 1.0 / E, None, ALU.mult))
                    seq.op("vector", lambda v: v.tensor_scalar(st1[:, 3:4], st1[:, 1:2], 1.0 / E, None, ALU.mult))
                    seq.op("vector", lambda v: v.tensor_tensor(st1[:, 4:5], st1[:, 2:3], st1[:, 2:3], ALU.mult))
                    seq.op("vector", lambda v: v.tensor_tensor(st1[:, 5:6], st1[:, 3:4], st1[:, 4:5], ALU.subtract))
                    seq.op("scalar", lambda s: s.activation(st1[:, 6:7], st1[:, 5:6], AF.Sqrt, bias=EPS))
                    seq.op("vector", lambda v: v.reciprocal(st1[:, 7:8], st1[:, 6:7]))
                    seq.op("vector", lambda v: v.tensor_scalar(tmB[:, :], tmB[:, :], st1[:, 2:3], st1[:, 7:8], ALU.subtract, ALU.mult))
                    seq.dma("sync", x3d[tb * 128:(tb + 1) * 128, :], tmB[:, :])
                    for fc in range(8):
                        seq.op("tensor", lambda t, fc=fc: t.transpose(p_tr[:, :], tmB[:, fc * 128:(fc + 1) * 128], sident[:, :]))
                        seq.op("vector", lambda v, fc=fc, tb=tb: v.tensor_copy(x3bf[fc][:, tb * 128:(tb + 1) * 128], p_tr[:, :]))
                        seq.op("vector", lambda v: v.tensor_copy(xtr[:, :], p_tr[:, :]))
                        seq.op("tensor", lambda t, fc=fc: t.matmul(p_r[:, 0:GATE], xtr[:, :], wredf[:, fc * GATE:(fc + 1) * GATE], start=(fc == 0), stop=(fc == 7)))
                    seq.op("vector", lambda v: v.tensor_copy(tmC[:, 0:GATE], p_r[:, 0:GATE]))
                    seq.op("tensor", lambda t: t.transpose(p_tr[0:16, 0:128], tmC[:, 0:GATE], sident[:, :]))
                    seq.op("vector", lambda v: v.tensor_copy(redT[:, :], p_tr[0:16, 0:128]))
                    seq.op("tensor", lambda t: t.matmul(p_g[:, 0:8], redT[:, :], wgT[:, :], start=True, stop=True))
                    seq.op("vector", lambda v: v.tensor_copy(st1[:, 0:8], p_g[:, 0:8]))
                    seq.op("vector", lambda v: v.reduce_max(st1[:, 8:9], st1[:, 0:8], AX.XYZW, negate=True))
                    seq.op("scalar", lambda s: s.activation(tmC[:, 0:8], st1[:, 0:8], AF.Exp, bias=st1[:, 8:9], accum_out=st1[:, 9:10]))
                    seq.op("vector", lambda v: v.reciprocal(st1[:, 10:11], st1[:, 9:10]))
                    seq.op("vector", lambda v: v.tensor_scalar(st1[:, 11:12], st1[:, 8:9], -1.0, None, ALU.mult))
                    seq.op("vector", lambda v: v.tensor_scalar(sel8[:, 0:8], st1[:, 0:8], st1[:, 11:12], None, ALU.is_ge))
                    seq.op("vector", lambda v: v.memset(st1[:, 12:13], 1.0))
                    for e0 in range(8):
                        seq.op("vector", lambda v, e0=e0: v.tensor_tensor(sel8[:, e0:e0 + 1], sel8[:, e0:e0 + 1], st1[:, 12:13], ALU.mult))
                        if e0 < 7:
                            seq.op("vector", lambda v, e0=e0: v.tensor_scalar(st1[:, 13:14], sel8[:, e0:e0 + 1], -1.0, 1.0, ALU.mult, ALU.add))
                            seq.op("vector", lambda v: v.tensor_tensor(st1[:, 12:13], st1[:, 12:13], st1[:, 13:14], ALU.mult))
                    seq.op("vector", lambda v, tb=tb: v.tensor_scalar(wsel[:, tb * 8:(tb + 1) * 8], sel8[:, 0:8], st1[:, 10:11], None, ALU.mult))
                for e in range(NEXP):
                    seq.dma("sync", b1S[:, :], b1c[e])
                    seq.dma("gpsimd", b2S[:, :], b2r[e])
                    for tcg in range(NTCG):
                        t0 = tcg * TSZ2
                        for ic in range(8):
                            seq.dma("gpsimd", w1S[ic], w1f[e, ic * 128:(ic + 1) * 128, :])
                        for hc in range(NHC):
                            for s5 in range(NS5):
                                for ic in range(8):
                                    seq.op("tensor", lambda t, ic=ic, hc=hc, s5=s5, t0=t0: t.matmul(p_a[:, 0:S5], w1S[ic][:, hc * 128:(hc + 1) * 128], x3bf[ic][:, t0 + s5 * 512:t0 + s5 * 512 + S5], start=(ic == 0), stop=(ic == 7)))
                                seq.op("scalar", lambda s, hc=hc, s5=s5: s.activation(hG[hc][:, s5 * 512:s5 * 512 + S5], p_a[:, 0:S5], AF.Gelu, bias=b1S[:, hc:hc + 1]))
                        for hc in range(NHC):
                            seq.dma("gpsimd", w2S[hc], w2f[e, hc * 128:(hc + 1) * 128, :])
                        for tb0 in range(TSZ2 // 128):
                            tb = tcg * (TSZ2 // 128) + tb0
                            for oh in range(2):
                                for hc in range(NHC):
                                    seq.op("tensor", lambda t, hc=hc, tb0=tb0, oh=oh: t.matmul(p_b[:, 0:512], hG[hc][:, tb0 * 128:(tb0 + 1) * 128], w2S[hc][:, oh * 512:(oh + 1) * 512], start=(hc == 0), stop=False))
                                seq.op("tensor", lambda t, oh=oh: t.matmul(p_b[:, 0:512], onesb[0:1, :], b2S[0:1, oh * 512:(oh + 1) * 512], start=False, stop=True))
                                seq.op("vector", lambda v, tb=tb, e=e: v.tensor_scalar(yev[:, 0:512], p_b[:, 0:512], wsel[:, tb * 8 + e:tb * 8 + e + 1], None, ALU.mult))
                                if e == 0:
                                    seq.dma("gpsimd", ydB[tb * 128:(tb + 1) * 128, oh * 512:(oh + 1) * 512], yev[:, 0:512])
                                else:
                                    seq.dma("gpsimd", ydB[tb * 128:(tb + 1) * 128, oh * 512:(oh + 1) * 512], yev[:, 0:512], accum_op=ALU.add)
                for tb in range(NTB):
                    seq.dma("sync", tmA[:, :], x3d[tb * 128:tb * 128 + 128, :])
                    seq.dma("sync", tmB[:, :], ydB[tb * 128:tb * 128 + 128, :])
                    seq.op("vector", lambda v: v.tensor_tensor(tmA[:, :], tmA[:, :], tmB[:, :], ALU.add))
                    seq.op("vector", lambda v: v.tensor_reduce(st1[:, 0:1], tmA[:, :], AX.XYZW, ALU.add))
                    seq.op("scalar", lambda s: s.activation(tmC[:, :], tmA[:, :], AF.Square))
                    seq.op("vector", lambda v: v.tensor_reduce(st1[:, 1:2], tmC[:, :], AX.XYZW, ALU.add))
                    seq.op("vector", lambda v: v.tensor_scalar(st1[:, 2:3], st1[:, 0:1], 1.0 / E, None, ALU.mult))
                    seq.op("vector", lambda v: v.tensor_scalar(st1[:, 3:4], st1[:, 1:2], 1.0 / E, None, ALU.mult))
                    seq.op("vector", lambda v: v.tensor_tensor(st1[:, 4:5], st1[:, 2:3], st1[:, 2:3], ALU.mult))
                    seq.op("vector", lambda v: v.tensor_tensor(st1[:, 5:6], st1[:, 3:4], st1[:, 4:5], ALU.subtract))
                    seq.op("scalar", lambda s: s.activation(st1[:, 6:7], st1[:, 5:6], AF.Sqrt, bias=EPS))
                    seq.op("vector", lambda v: v.reciprocal(st1[:, 7:8], st1[:, 6:7]))
                    seq.op("vector", lambda v: v.tensor_scalar(tmA[:, :], tmA[:, :], st1[:, 2:3], st1[:, 7:8], ALU.subtract, ALU.mult))
                    seq.dma("sync", outT[tb * 128:(tb + 1) * 128, :], tmA[:, :])
        if 5 in phases: _phase5()
        if debug:
            seq.dma("sync", dbgL.ap()[:, :], leftIn.ap()[:, :])
            seq.dma("sync", dbgLA.ap()[:, :], leftAG.ap()[:, :])
            seq.dma("sync", dbgY.ap()[:, :], ysIn.ap()[:, :])
            seq.dma("sync", dbgYA.ap()[:, :], ysAG.ap()[:, :])
        seq.build(block, step)
    return nc


def host_fused_inputs(T, inp, gumbel):
    m1 = host_l1_inputs(T, inp, gumbel)
    m2 = host_l2_inputs(T, inp, np.zeros((2, T, 1), np.float32))
    m3 = host_l3_inputs(T, inp, inp["L0_attn_hidden_states"])
    maps = []
    cw = inp["L1_attn_conv_w"][:, 0, :]
    cb = inp["L1_attn_conv_b"]
    Wd = inp["L1_attn_W_delta"]; bd = inp["L1_attn_b_delta"]
    WB_ = inp["L1_attn_W_B"]; WC_ = inp["L1_attn_W_C"]
    Alog = inp["L1_attn_A_log"]; Dv = inp["L1_attn_D"]
    patDh = np.zeros((128, 4, 128), np.float32)
    for dg in range(4):
        for p in range(128):
            patDh[dg * 32 + p // 4, dg, p] = 1.0
    patBh = np.zeros((128, 32, 128), np.float32)
    for j in range(32):
        for p in range(128):
            patBh[j * 4 + p % 4, j, p] = 1.0
    patYh = np.zeros((128, 4, 128), np.float32)
    for dg in range(4):
        for p in range(128):
            patYh[p, dg, dg * 32 + p // 4] = 1.0
    b1cols = np.ascontiguousarray(inp["L1_ffn_b1"].reshape(NEXP, FF1 // 128, 128).transpose(0, 2, 1))
    for c in range(8):
        b, ds = c // 4, c % 4
        dlo = ds * 128
        p = np.arange(128)
        i_, nsub = p // 4, p % 4
        aarr = np.zeros((128, 256), np.float32)
        for dg in range(4):
            for ng in range(64):
                aarr[:, ng * 4 + dg] = Alog[dlo + dg * 32 + i_, ng * 4 + nsub]
        selp = np.zeros((128, 4, 128), np.float32)
        selp[:, ds] = np.eye(128, dtype=np.float32)
        mp = dict(m1[c])
        mp.update(
            x0=np.ascontiguousarray(host_core_token_slice(inp["L0_attn_hidden_states"], T, c)),
            gateT=m2[c]["gateT"], w1T=m2[c]["w1T"], w3T=m2[c]["w3T"], w2T=m2[c]["w2T"],
            grow=m3[c]["grow"], brow=m3[c]["brow"], wup=m3[c]["wup"], bupc=m3[c]["bupc"],
            cwc=np.ascontiguousarray(cw.reshape(4, 128, 3).transpose(1, 0, 2)),
            cbc=np.ascontiguousarray(cb.reshape(4, 128).T),
            wdel=np.ascontiguousarray(Wd[:, dlo:dlo + 128]),
            bdc=np.ascontiguousarray(bd[dlo:dlo + 128, None]),
            wB=np.ascontiguousarray(WB_), wC=np.ascontiguousarray(WC_),
            aarr=aarr, dcol=np.ascontiguousarray(Dv[dlo:dlo + 128, None]),
            patD=np.ascontiguousarray(patDh.reshape(128, 512)),
            patB=np.ascontiguousarray(patBh.reshape(128, 32 * 128)),
            patY=np.ascontiguousarray(patYh.reshape(128, 512)),
            selpat=np.ascontiguousarray(selp.reshape(128, 512)),
            wdown=np.ascontiguousarray(inp["L1_attn_W_down"]),
            bdrow=np.ascontiguousarray(inp["L1_attn_b_down"][None, :]),
            wred=np.ascontiguousarray(inp["L1_ffn_w_red"]),
            wg=np.ascontiguousarray(inp["L1_ffn_wg"]),
            w1f=np.ascontiguousarray(inp["L1_ffn_w1"]), b1c=b1cols,
            w2f=np.ascontiguousarray(inp["L1_ffn_w2"]),
            b2r=np.ascontiguousarray(inp["L1_ffn_b2"][:, None, :]),
            jval=np.array([[128 * (c % 4)]], np.int32))
        maps.append(mp)
    return maps


_BUILT = {}

def _get(name, builder, T):
    key = (name, T)
    if key not in _BUILT:
        _BUILT[key] = builder(T)
    return _BUILT[key]

def kernel(**inputs):
    from concourse.bass_utils import run_bass_kernel_spmd
    T = inputs["L0_attn_hidden_states"].shape[1]
    inp = {k: np.asarray(v) for k, v in inputs.items()}
    import jax
    cpu = jax.devices("cpu")[0]
    with jax.default_device(cpu):
        import jax.numpy as jnp
        u = jax.random.uniform(jax.random.key(42, impl="threefry2x32"), (2, T, T), jnp.float32, 1e-06, 1.0 - 1e-06)
        gumbel = np.asarray(-jnp.log(-jnp.log(u)), np.float32)
    nc = _get("fused", build_fused, T)
    maps = host_fused_inputs(T, inp, gumbel)
    del gumbel
    res = run_bass_kernel_spmd(nc, maps, core_ids=list(range(8)))
    out = host_assemble_full([r.get("out") for r in res.results], T)
    return out.astype(np.float32)


# version 05
# speedup vs baseline: 1.2063x; 1.2063x over previous
import numpy as np
from contextlib import ExitStack
import concourse.bass as bass
import concourse.mybir as mybir

F32 = mybir.dt.float32
BF16 = mybir.dt.bfloat16
AF = mybir.ActivationFunctionType
ALU = mybir.AluOpType
AX = mybir.AxisListType
NEG = -1.0e9

ENGINES = ("sync", "gpsimd", "vector", "scalar", "tensor")
STREAMS = ("sync", "gpsimd", "vector", "scalar", "tensor", "cc")
DMA_STREAMS = ("sync", "gpsimd")
NLANE = 8

_DT_SIZE = {str(F32): 4, str(BF16): 2, str(mybir.dt.int32): 4, str(mybir.dt.float16): 2,
            str(mybir.dt.uint8): 1, str(mybir.dt.int8): 1, str(mybir.dt.uint16): 2,
            str(mybir.dt.uint32): 4}

def _arg_ranges(arg):
    bap = getattr(arg, "bass_ap", None)
    if bap is None:
        return None
    try:
        name = bap.tensor.name
        space = type(bap.tensor).__name__
    except Exception:
        return None
    esz = _DT_SIZE.get(str(arg.dtype), 4)
    off = arg.offset
    if not isinstance(off, (int, np.integer)):
        return (space + ":" + name, 0, 1 << 60)
    ext = 0
    for step, count in arg.ap:
        ext += abs(int(step)) * (int(count) - 1)
    return (space + ":" + name, int(off) * esz, (int(off) + ext + 1) * esz)

def _sem_targets(stream, ordinal):
    if ordinal <= 0:
        return []
    if stream in DMA_STREAMS:
        lane = (ordinal - 1) % NLANE
        return [((stream, lane), 16 * ((ordinal - 1) // NLANE + 1))]
    return [((stream, 0), ordinal)]

def _sem_targets_all(stream, count):
    if count <= 0:
        return []
    if stream in DMA_STREAMS:
        out = []
        for lane in range(NLANE):
            c = count // NLANE + (1 if lane < count % NLANE else 0)
            if c:
                out.append(((stream, lane), 16 * c))
        return out
    return [((stream, 0), count)]

class Seq:
    def __init__(self, nc, mode="serial", waits=None):
        self.nc = nc
        self.mode = mode
        self.ops = []
        self.captured = {}
        self.waits = waits
        self.n_in_stream = {s: 0 for s in STREAMS}
        self.ordinals = []

    def op(self, engine, emit, inc=None, wait=None, barrier=False, no_inc=False):
        stream = "cc" if barrier else engine
        if not no_inc:
            self.n_in_stream[stream] += 1
        self.ordinals.append(self.n_in_stream[stream])
        self.ops.append((engine, stream, emit, wait, barrier, no_inc))
        return len(self.ops) - 1

    def barrier(self):
        return dict(self.n_in_stream)

    def dma(self, queue, out, in_, wait=None, **kw):
        return self.op(queue, lambda e: e.dma_start(out=out, in_=in_, **kw), wait=wait)

    def cc(self, emit):
        return self.op("gpsimd", emit, barrier=True)

    def _targets_for_dict(self, d):
        out = []
        for s, c in d.items():
            out.extend(_sem_targets_all(s, c))
        return out

    def build(self, block, step_unused=None):
        nc = self.nc
        sems = nc._seq_sems
        mode = self.mode
        if mode == "serial" and not hasattr(self, "_pref"):
            self._pref = []
            c = {s: 0 for s in STREAMS}
            for (_, stream, _, _, _, no_inc) in self.ops:
                self._pref.append(dict(c))
                if not no_inc:
                    c[stream] += 1
        for engname in ENGINES:
            mine = [(i, o) for i, o in enumerate(self.ops) if o[0] == engname]
            if not mine and engname != "sync":
                continue
            def mk(mine, engname):
                def body(eng):
                    maxw = {}
                    for i, (_, stream, emit, manual, barrier, no_inc) in mine:
                        if mode == "sched":
                            tgts = self.waits[i]
                        elif mode == "record":
                            tgts = []
                        else:
                            tgts = self._targets_for_dict(self._pref[i])
                        for semkey, v in tgts:
                            if v > maxw.get(semkey, 0):
                                eng.wait_ge(sems[semkey], int(v))
                                maxw[semkey] = v
                        inst = emit(eng)
                        if not no_inc:
                            if stream in DMA_STREAMS:
                                lane = (self.ordinals[i] - 1) % NLANE
                                inst.then_inc(sems[(stream, lane)], 16)
                            else:
                                inst.then_inc(sems[(stream, 0)], 1)
                        if mode == "record":
                            ri, ro = [], []
                            try:
                                mi = inst.ins
                                for a in mi.ins:
                                    r = _arg_ranges(a)
                                    if r: ri.append(r)
                                for a in mi.outs:
                                    r = _arg_ranges(a)
                                    if r: ro.append(r)
                            except Exception:
                                ri = [("ALL", 0, 1 << 60)]
                                ro = [("ALL", 0, 1 << 60)]
                            self.captured[i] = (ri, ro)
                    if engname == "sync":
                        for s in STREAMS:
                            for semkey, v in _sem_targets_all(s, self.n_in_stream[s]):
                                eng.wait_ge(sems[semkey], int(v))
                return body
            getattr(block, engname)(mk(mine, engname))

    SAME_OK = False
    def compute_waits(self):
        n = len(self.ops)
        hist_w = {}
        hist_r = {}
        self.kw_nstreams = {}
        for i2 in range(n):
            st2 = self.ops[i2][1]
            ri2, ro2 = self.captured.get(i2, ([("ALL", 0, 1 << 60)], [("ALL", 0, 1 << 60)]))
        waits = [None] * n
        floor = -1
        floor_snapshot = None
        COMPUTE = {"vector", "scalar", "tensor"}
        cnt_running = {s: 0 for s in STREAMS}
        for i in range(n):
            eng, stream, _, manual, barrier, no_inc = self.ops[i]
            ri, ro = self.captured.get(i, ([("ALL", 0, 1 << 60)], [("ALL", 0, 1 << 60)]))
            if any(k == "ALL" for k, _, _ in ri) or any(k == "ALL" for k, _, _ in ro):
                waits[i] = self._targets_for_dict(dict(cnt_running))
                floor = i
                floor_snapshot = _sem_targets(stream, self.ordinals[i])
                if not no_inc:
                    cnt_running[stream] += 1
                continue
            if barrier:
                waits[i] = self._targets_for_dict(dict(cnt_running))
                floor = i
                floor_snapshot = _sem_targets("cc", self.ordinals[i])
                self._add_hist(hist_w, hist_r, i, ri, ro)
                if not no_inc:
                    cnt_running[stream] += 1
                continue
            deps = {}
            same_ok = (eng in COMPUTE) and self.SAME_OK
            def scan(lst, lo, hi):
                if not lst:
                    return
                seen = set()
                ndma = 0
                for (j, jlo, jhi) in reversed(lst):
                    if j <= floor:
                        break
                    js = self.ops[j][1]
                    if js in DMA_STREAMS:
                        dk = (js, (self.ordinals[j] - 1) % NLANE)
                    else:
                        dk = (js, 0)
                    if dk in seen:
                        continue
                    if jlo < hi and lo < jhi:
                        seen.add(dk)
                        if same_ok and js == stream:
                            continue
                        cur = deps.get(dk, -1)
                        if j > cur:
                            deps[dk] = j
                        if len(seen) >= 20:
                            break
            for (key, lo, hi) in ri:
                scan(hist_w.get(key), lo, hi)
            for (key, lo, hi) in ro:
                scan(hist_w.get(key), lo, hi)
                scan(hist_r.get(key), lo, hi)
            tgts = {}
            if floor_snapshot:
                for semkey, v in floor_snapshot:
                    tgts[semkey] = max(tgts.get(semkey, 0), v)
            for (js, lane), j in deps.items():
                for semkey, v in _sem_targets(js, self.ordinals[j]):
                    tgts[semkey] = max(tgts.get(semkey, 0), v)
            if manual is not None:
                for semkey, v in self._targets_for_dict(manual):
                    tgts[semkey] = max(tgts.get(semkey, 0), v)
            waits[i] = list(tgts.items())
            self._add_hist(hist_w, hist_r, i, ri, ro)
            if not no_inc:
                cnt_running[stream] += 1
        return waits

    def _add_hist(self, hist_w, hist_r, i, ri, ro):
        for (key, lo, hi) in ri:
            hist_r.setdefault(key, []).append((i, lo, hi))
        for (key, lo, hi) in ro:
            hist_w.setdefault(key, []).append((i, lo, hi))

def make_seq_sems(nc, es):
    sems = {}
    for s in ("vector", "scalar", "tensor", "cc"):
        sems[(s, 0)] = es.enter_context(nc.semaphore(f"sem_{s}"))
    for s in DMA_STREAMS:
        for lane in range(NLANE):
            sems[(s, lane)] = es.enter_context(nc.semaphore(f"sem_{s}{lane}"))
    nc._seq_sems = sems
    return sems

def ceil_div(a, b):
    return -(-a // b)


E = 1024
NH = 16
DH = 64
FF0 = 2816
NEXP = 8
FF1 = 4096
DI = 512
NS = 256
GATE = 16

def core_token_rows(T, c):
    j = c % 4
    nslot = T // 512
    rows = []
    for t in range(nslot):
        s = 128 * (4 * t + j)
        rows.append(np.arange(s, s + 128))
    return np.concatenate(rows)

def host_core_token_slice(arr_bt, T, c):
    b = c // 4
    return arr_bt[b][core_token_rows(T, c)]

def host_assemble_full(parts, T):
    B = 2
    out = None
    for c in range(8):
        p = np.asarray(parts[c])
        if out is None:
            out = np.zeros((B, T) + p.shape[1:], p.dtype)
        out[c // 4][core_token_rows(T, c)] = p
    return out

def build_l2(T):
    TC = T // 4
    NTB = TC // 128
    NTS = TC // 512 if TC >= 512 else 1
    TSZ = min(TC, 512)
    nc = bass.Bass("TRN2", target_bir_lowering=False)
    x0 = nc.dram_tensor("x0", [TC, E], F32, kind="ExternalInput").ap()
    attn = nc.dram_tensor("attn", [TC, E], F32, kind="ExternalInput").ap()
    gateT = nc.dram_tensor("gateT", [E, NEXP], F32, kind="ExternalInput").ap()
    w1T = nc.dram_tensor("w1T", [NEXP, E, FF0], F32, kind="ExternalInput").ap()
    w3T = nc.dram_tensor("w3T", [NEXP, E, FF0], F32, kind="ExternalInput").ap()
    w2T = nc.dram_tensor("w2T", [NEXP, FF0, E], F32, kind="ExternalInput").ap()
    identf = nc.dram_tensor("identf", [128, 128], F32, kind="ExternalInput").ap()
    x2o = nc.dram_tensor("x2", [TC, E], F32, kind="ExternalOutput").ap()
    x1d = nc.dram_tensor("x1d", [TC, E], F32).ap()
    yd = nc.dram_tensor("yd", [TC, E], F32).ap()

    seq = Seq(nc)
    es = ExitStack()
    sb = lambda n, s, d=F32: es.enter_context(nc.sbuf_tensor(n, s, d)).ap()
    ps = lambda n, s: es.enter_context(nc.psum_tensor(n, s, F32)).ap()

    xbf = [sb(f"xbf{i}", [128, TC], BF16) for i in range(8)]
    tmA = sb("tmA", [128, E])
    tmB = sb("tmB", [128, E])
    wte = sb("wte", [128, NTB * NEXP])
    sident = sb("sident", [128, 128])
    gat = sb("gat", [128, 64], BF16)
    gatf = sb("gatf", [128, 64])
    xtr = sb("xtr", [128, 128])
    wArena = sb("wArena", [128, 8 * FF0], BF16)
    wA = [wArena[:, i * FF0:(i + 1) * FF0] for i in range(8)]
    w2s = [wArena[:, i * E:(i + 1) * E] for i in range(22)]
    hA = [sb(f"hA{i}", [128, TC], BF16) for i in range(22)]
    cols = sb("cols", [128, 8])
    st1 = sb("st1", [128, 16])
    yev = sb("yev", [128, 512])
    p_tr = ps("p_tr", [128, 128])
    p_g = ps("p_g", [128, 16])
    p_a = ps("p_a", [128, 512])
    p_b = ps("p_b", [128, 512])

    NF0 = FF0 // 128
    with (nc.semaphore("step") as step, nc.Block() as block):
        seq.dma("sync", sident[:, :], identf)
        seq.dma("gpsimd", gat.rearrange("p (i e) -> p i e", i=8), gateT.rearrange("(i p) e -> p i e", p=128))
        seq.dma("sync", gatf.rearrange("p (i e) -> p i e", i=8), gateT.rearrange("(i p) e -> p i e", p=128))
        seq.op("vector", lambda v: v.memset(cols[:, 0:1], 1e-5))
        EPS = cols[:, 0:1]

        def ln_tb(tb, src_emitters, out_dram, transpose_to=None, extra=None, do_gating=False):
            for fn in src_emitters:
                fn()
            if extra is not None:
                extra()
            b = seq.barrier()
            seq.op("vector", lambda v: v.tensor_reduce(st1[:, 0:1], tmA[:, :], AX.XYZW, ALU.add), wait=b)
            seq.op("scalar", lambda s: s.activation(tmB[:, :], tmA[:, :], AF.Square), wait=b)
            seq.op("vector", lambda v: v.tensor_reduce(st1[:, 1:2], tmB[:, :], AX.XYZW, ALU.add))
            seq.op("vector", lambda v: v.tensor_scalar(st1[:, 2:3], st1[:, 0:1], 1.0 / E, None, ALU.mult))
            seq.op("vector", lambda v: v.tensor_scalar(st1[:, 3:4], st1[:, 1:2], 1.0 / E, None, ALU.mult))
            seq.op("vector", lambda v: v.tensor_tensor(st1[:, 4:5], st1[:, 2:3], st1[:, 2:3], ALU.mult))
            seq.op("vector", lambda v: v.tensor_tensor(st1[:, 5:6], st1[:, 3:4], st1[:, 4:5], ALU.subtract))
            seq.op("scalar", lambda s: s.activation(st1[:, 6:7], st1[:, 5:6], AF.Sqrt, bias=EPS))
            seq.op("vector", lambda v: v.reciprocal(st1[:, 7:8], st1[:, 6:7]))
            seq.op("vector", lambda v: v.tensor_scalar(tmA[:, :], tmA[:, :], st1[:, 2:3], st1[:, 7:8], ALU.subtract, ALU.mult))
            if out_dram is not None:
                seq.dma("sync", out_dram[tb * 128:(tb + 1) * 128, :], tmA[:, :])
            if transpose_to is not None:
                for fc in range(8):
                    seq.op("tensor", lambda t, fc=fc: t.transpose(p_tr[:, :], tmA[:, fc * 128:(fc + 1) * 128], sident[:, :]))
                    seq.op("vector", lambda v, fc=fc, tb=tb: v.tensor_copy(transpose_to[fc][:, tb * 128:(tb + 1) * 128], p_tr[:, :]))
                    if do_gating:
                        seq.op("vector", lambda v, fc=fc: v.tensor_copy(xtr[:, :], p_tr[:, :]))
                        seq.op("tensor", lambda t, fc=fc: t.matmul(p_g[:, 0:8], xtr[:, :], gatf[:, fc * 8:(fc + 1) * 8], start=(fc == 0), stop=(fc == 7)))


        def gating_post(tb):
            seq.op("vector", lambda v: v.tensor_copy(st1[:, 0:8], p_g[:, 0:8]))
            seq.op("vector", lambda v: v.reduce_max(st1[:, 8:9], st1[:, 0:8], AX.XYZW, negate=True))
            seq.op("scalar", lambda s: s.activation(st1[:, 0:8], st1[:, 0:8], AF.Exp, bias=st1[:, 8:9], accum_out=st1[:, 9:10]))
            seq.op("vector", lambda v: v.reciprocal(st1[:, 10:11], st1[:, 9:10]))
            seq.op("vector", lambda v: v.tensor_scalar(st1[:, 0:8], st1[:, 0:8], st1[:, 10:11], None, ALU.mult))
            seq.op("vector", lambda v: v.reduce_max(st1[:, 11:12], st1[:, 0:8], AX.XYZW))
            seq.op("vector", lambda v: v.tensor_scalar(st1[:, 12:13], st1[:, 11:12], -1.0, None, ALU.mult))
            seq.op("vector", lambda v: v.tensor_scalar(tmB[:, 0:8], st1[:, 0:8], st1[:, 11:12], None, ALU.is_ge))
            seq.op("vector", lambda v: v.scalar_tensor_tensor(tmB[:, 8:16], tmB[:, 0:8], -2.0, st1[:, 0:8], ALU.mult, ALU.add))
            seq.op("vector", lambda v: v.reduce_max(st1[:, 13:14], tmB[:, 8:16], AX.XYZW))
            seq.op("vector", lambda v: v.tensor_scalar(tmB[:, 0:8], st1[:, 0:8], st1[:, 13:14], None, ALU.is_ge))
            seq.op("vector", lambda v: v.tensor_tensor(st1[:, 14:15], st1[:, 11:12], st1[:, 13:14], ALU.add))
            seq.op("vector", lambda v: v.reciprocal(st1[:, 15:16], st1[:, 14:15]))
            seq.op("vector", lambda v: v.tensor_tensor(tmB[:, 0:8], tmB[:, 0:8], st1[:, 0:8], ALU.mult))
            seq.op("vector", lambda v, tb=tb: v.tensor_scalar(wte[:, tb * 8:(tb + 1) * 8], tmB[:, 0:8], st1[:, 15:16], None, ALU.mult))

        for tb in range(NTB):
            ln_tb(tb,
                  [lambda tb=tb: seq.dma("sync", tmA[:, :], x0[tb * 128:tb * 128 + 128, :]),
                   lambda tb=tb: seq.dma("sync", tmB[:, :], attn[tb * 128:tb * 128 + 128, :])],
                  x1d, transpose_to=xbf, do_gating=True,
                  extra=lambda: seq.op("vector", lambda v: v.tensor_tensor(tmA[:, :], tmA[:, :], tmB[:, :], ALU.add)))
            gating_post(tb)

        for e in range(NEXP):
            for ic in range(8):
                seq.dma("gpsimd", wA[ic], w1T[e, ic * 128:(ic + 1) * 128, :])
            wbar = seq.barrier()
            for hc in range(NF0):
                for ts in range(NTS):
                    for ic in range(8):
                        seq.op("tensor", lambda t, ic=ic, hc=hc, ts=ts: t.matmul(p_a[:, 0:TSZ], wA[ic][:, hc * 128:(hc + 1) * 128], xbf[ic][:, ts * TSZ:(ts + 1) * TSZ], start=(ic == 0), stop=(ic == 7)), wait=wbar if (ic == 0 and hc == 0 and ts == 0) else None)
                    seq.op("scalar", lambda s, hc=hc, ts=ts: s.activation(hA[hc][:, ts * TSZ:(ts + 1) * TSZ], p_a[:, 0:TSZ], AF.Silu))
            for ic in range(8):
                seq.dma("gpsimd", wA[ic], w3T[e, ic * 128:(ic + 1) * 128, :])
            wbar = seq.barrier()
            for hc in range(NF0):
                for ts in range(NTS):
                    for ic in range(8):
                        seq.op("tensor", lambda t, ic=ic, hc=hc, ts=ts: t.matmul(p_a[:, 0:TSZ], wA[ic][:, hc * 128:(hc + 1) * 128], xbf[ic][:, ts * TSZ:(ts + 1) * TSZ], start=(ic == 0), stop=(ic == 7)), wait=wbar if (ic == 0 and hc == 0 and ts == 0) else None)
                    seq.op("vector", lambda v, hc=hc, ts=ts: v.tensor_tensor(hA[hc][:, ts * TSZ:(ts + 1) * TSZ], hA[hc][:, ts * TSZ:(ts + 1) * TSZ], p_a[:, 0:TSZ], ALU.mult))
            for hc in range(NF0):
                seq.dma("gpsimd", w2s[hc], w2T[e, hc * 128:(hc + 1) * 128, :])
            wbar = seq.barrier()
            for tb in range(NTB):
                for oh in range(2):
                    for hc in range(NF0):
                        seq.op("tensor", lambda t, hc=hc, tb=tb, oh=oh: t.matmul(p_b[:, 0:512], hA[hc][:, tb * 128:(tb + 1) * 128], w2s[hc][:, oh * 512:(oh + 1) * 512], start=(hc == 0), stop=(hc == NF0 - 1)), wait=wbar if (hc == 0 and tb == 0 and oh == 0) else None)
                    seq.op("vector", lambda v, tb=tb, e=e: v.tensor_scalar(yev[:, 0:512], p_b[:, 0:512], wte[:, tb * 8 + e:tb * 8 + e + 1], None, ALU.mult))
                    if e == 0:
                        seq.dma("gpsimd", yd[tb * 128:(tb + 1) * 128, oh * 512:(oh + 1) * 512], yev[:, 0:512])
                    else:
                        seq.dma("gpsimd", yd[tb * 128:(tb + 1) * 128, oh * 512:(oh + 1) * 512], yev[:, 0:512], accum_op=ALU.add)

        for tb in range(NTB):
            ln_tb(tb,
                  [lambda tb=tb: seq.dma("sync", tmA[:, :], x1d[tb * 128:tb * 128 + 128, :]),
                   lambda tb=tb: seq.dma("sync", tmB[:, :], yd[tb * 128:tb * 128 + 128, :])],
                  None, transpose_to=None,
                  extra=lambda: seq.op("vector", lambda v: v.tensor_tensor(tmA[:, :], tmA[:, :], tmB[:, :], ALU.add)))
            seq.dma("sync", x2o[tb * 128:(tb + 1) * 128, :], tmA[:, :])

        seq.build(block, step)
    es.close()
    return nc

def host_l2_inputs(T, inp, attn_full):
    ident = np.eye(128, dtype=np.float32)
    w1T = np.ascontiguousarray(np.transpose(inp["L0_ffn_w1"], (0, 2, 1)))
    w3T = np.ascontiguousarray(np.transpose(inp["L0_ffn_w3"], (0, 2, 1)))
    w2T = np.ascontiguousarray(np.transpose(inp["L0_ffn_w2"], (0, 2, 1)))
    gateT = np.ascontiguousarray(inp["L0_ffn_gate_w"].T)
    maps = []
    for c in range(8):
        maps.append(dict(
            x0=np.ascontiguousarray(host_core_token_slice(inp["L0_attn_hidden_states"], T, c)),
            attn=np.ascontiguousarray(host_core_token_slice(attn_full, T, c)),
            gateT=gateT, w1T=w1T, w3T=w3T, w2T=w2T, identf=ident))
    return maps

def build_l3(T):
    TC = T // 4
    NTB = TC // 128
    NTS = ceil_div(TC, 512)
    TSZ = min(TC, 512)
    nc = bass.Bass("TRN2", target_bir_lowering=False)
    x2 = nc.dram_tensor("x2", [TC, E], F32, kind="ExternalInput").ap()
    grow = nc.dram_tensor("grow", [1, E], F32, kind="ExternalInput").ap()
    brow = nc.dram_tensor("brow", [1, E], F32, kind="ExternalInput").ap()
    wup = nc.dram_tensor("wup", [E, 2 * DI], F32, kind="ExternalInput").ap()
    bupc = nc.dram_tensor("bupc", [128, 8], F32, kind="ExternalInput").ap()
    identf = nc.dram_tensor("identf", [128, 128], F32, kind="ExternalInput").ap()
    leftT = nc.dram_tensor("leftT", [DI, TC], F32, kind="ExternalOutput").ap()
    rightT = nc.dram_tensor("rightT", [DI, TC], F32, kind="ExternalOutput").ap()

    seq = Seq(nc)
    es = ExitStack()
    sb = lambda n, s, d=F32: es.enter_context(nc.sbuf_tensor(n, s, d)).ap()
    ps = lambda n, s: es.enter_context(nc.psum_tensor(n, s, F32)).ap()
    xbf = [sb(f"xbf{i}", [128, TC], BF16) for i in range(8)]
    tmA = sb("tmA", [128, E])
    tmB = sb("tmB", [128, E])
    grep = sb("grep", [128, E])
    brep = sb("brep", [128, E])
    growS = sb("growS", [1, E])
    browS = sb("browS", [1, E])
    sones = sb("sones", [1, 128])
    sident = sb("sident", [128, 128])
    wupS = [sb(f"wupS{i}", [128, 2 * DI], BF16) for i in range(8)]
    bupS = sb("bupS", [128, 8])
    st1 = sb("st1", [128, 16])
    cols = sb("cols", [128, 1])
    oev = sb("oev", [128, 512])
    p_tr = ps("p_tr", [128, 128])
    p_a = ps("p_a", [128, 512])
    p_r = ps("p_r", [128, 512])

    with (nc.semaphore("step") as step, nc.Block() as block):
        seq.dma("sync", sident[:, :], identf)
        seq.dma("sync", growS[:, :], grow)
        seq.dma("sync", browS[:, :], brow)
        seq.dma("sync", bupS[:, :], bupc)
        for ic in range(8):
            seq.dma("gpsimd", wupS[ic][:, :], wup[ic * 128:(ic + 1) * 128, :])
        seq.op("vector", lambda v: v.memset(cols[:, 0:1], 1e-5))
        seq.op("vector", lambda v: v.memset(sones[:, :], 1.0))
        for half in range(2):
            seq.op("tensor", lambda t, half=half: t.matmul(p_r[:, 0:512], sones[0:1, :], growS[0:1, half * 512:(half + 1) * 512], start=True, stop=True))
            seq.op("vector", lambda v, half=half: v.tensor_copy(grep[:, half * 512:(half + 1) * 512], p_r[:, 0:512]))
            seq.op("tensor", lambda t, half=half: t.matmul(p_r[:, 0:512], sones[0:1, :], browS[0:1, half * 512:(half + 1) * 512], start=True, stop=True))
            seq.op("vector", lambda v, half=half: v.tensor_copy(brep[:, half * 512:(half + 1) * 512], p_r[:, 0:512]))

        for tb in range(NTB):
            seq.dma("sync", tmA[:, :], x2[tb * 128:tb * 128 + 128, :])
            seq.op("vector", lambda v: v.tensor_reduce(st1[:, 0:1], tmA[:, :], AX.XYZW, ALU.add))
            seq.op("scalar", lambda s: s.activation(tmB[:, :], tmA[:, :], AF.Square))
            seq.op("vector", lambda v: v.tensor_reduce(st1[:, 1:2], tmB[:, :], AX.XYZW, ALU.add))
            seq.op("vector", lambda v: v.tensor_scalar(st1[:, 2:3], st1[:, 0:1], 1.0 / E, None, ALU.mult))
            seq.op("vector", lambda v: v.tensor_scalar(st1[:, 3:4], st1[:, 1:2], 1.0 / E, None, ALU.mult))
            seq.op("vector", lambda v: v.tensor_tensor(st1[:, 4:5], st1[:, 2:3], st1[:, 2:3], ALU.mult))
            seq.op("vector", lambda v: v.tensor_tensor(st1[:, 5:6], st1[:, 3:4], st1[:, 4:5], ALU.subtract))
            seq.op("scalar", lambda s: s.activation(st1[:, 6:7], st1[:, 5:6], AF.Sqrt, bias=cols[:, 0:1]))
            seq.op("vector", lambda v: v.reciprocal(st1[:, 7:8], st1[:, 6:7]))
            seq.op("vector", lambda v: v.tensor_scalar(tmA[:, :], tmA[:, :], st1[:, 2:3], st1[:, 7:8], ALU.subtract, ALU.mult))
            seq.op("vector", lambda v: v.tensor_tensor(tmA[:, :], tmA[:, :], grep[:, :], ALU.mult))
            seq.op("vector", lambda v: v.tensor_tensor(tmA[:, :], tmA[:, :], brep[:, :], ALU.add))
            for fc in range(8):
                seq.op("tensor", lambda t, fc=fc: t.transpose(p_tr[:, :], tmA[:, fc * 128:(fc + 1) * 128], sident[:, :]))
                seq.op("vector", lambda v, fc=fc, tb=tb: v.tensor_copy(xbf[fc][:, tb * 128:(tb + 1) * 128], p_tr[:, :]))

        for of in range(8):
            for ts in range(NTS):
                for ic in range(8):
                    seq.op("tensor", lambda t, ic=ic, of=of, ts=ts: t.matmul(p_a[:, 0:TSZ], wupS[ic][:, of * 128:(of + 1) * 128], xbf[ic][:, ts * TSZ:(ts + 1) * TSZ], start=(ic == 0), stop=(ic == 7)))
                seq.op("scalar", lambda s, of=of: s.activation(oev[:, 0:TSZ], p_a[:, 0:TSZ], AF.Identity, bias=bupS[:, of:of + 1]))
                dst = leftT if of < 4 else rightT
                ofr = (of % 4) * 128
                seq.dma("sync", dst[ofr:ofr + 128, ts * TSZ:(ts + 1) * TSZ], oev[:, 0:TSZ])
        seq.build(block, step)
    es.close()
    return nc

def host_l3_inputs(T, inp, x2_full_tm):
    ident = np.eye(128, dtype=np.float32)
    maps = []
    for c in range(8):
        maps.append(dict(
            x2=np.ascontiguousarray(host_core_token_slice(x2_full_tm, T, c)),
            grow=np.ascontiguousarray(inp["L1_attn_ln_gamma"][None, :]),
            brow=np.ascontiguousarray(inp["L1_attn_ln_beta"][None, :]),
            wup=np.ascontiguousarray(inp["L1_attn_W_up"]),
            bupc=np.ascontiguousarray(inp["L1_attn_b_up"].reshape(8, 128).T),
            identf=ident))
    return maps

def build_l4(T):
    TCH = min(1024, T)
    NCH = T // TCH
    NS2 = TCH // 512
    nc = bass.Bass("TRN2", target_bir_lowering=False)
    leftF = nc.dram_tensor("leftF", [DI, T], F32, kind="ExternalInput").ap()
    cwc = nc.dram_tensor("cwc", [128, 4, 3], F32, kind="ExternalInput").ap()
    cbc = nc.dram_tensor("cbc", [128, 4], F32, kind="ExternalInput").ap()
    wdel = nc.dram_tensor("wdel", [DI, 128], F32, kind="ExternalInput").ap()
    bdc = nc.dram_tensor("bdc", [128, 1], F32, kind="ExternalInput").ap()
    wB = nc.dram_tensor("wB", [DI, NS], F32, kind="ExternalInput").ap()
    wC = nc.dram_tensor("wC", [DI, NS], F32, kind="ExternalInput").ap()
    aarr = nc.dram_tensor("aarr", [128, NS], F32, kind="ExternalInput").ap()
    dcol = nc.dram_tensor("dcol", [128, 1], F32, kind="ExternalInput").ap()
    patD = nc.dram_tensor("patD", [128, 4 * 128], F32, kind="ExternalInput").ap()
    patB = nc.dram_tensor("patB", [128, 32 * 128], F32, kind="ExternalInput").ap()
    patY = nc.dram_tensor("patY", [128, 4 * 128], F32, kind="ExternalInput").ap()
    yso = nc.dram_tensor("ys", [128, T], F32, kind="ExternalOutput").ap()

    seq = Seq(nc)
    es = ExitStack()
    sb = lambda n, s, d=F32: es.enter_context(nc.sbuf_tensor(n, s, d)).ap()
    ps = lambda n, s: es.enter_context(nc.psum_tensor(n, s, F32)).ap()
    lt = [sb(f"lt{i}", [128, TCH + 2]) for i in range(4)]
    uf = [sb(f"uf{i}", [128, TCH]) for i in range(4)]
    ub = [sb(f"ub{i}", [128, TCH], BF16) for i in range(4)]
    wdS = [sb(f"wdS{i}", [128, 128], BF16) for i in range(4)]
    wbS = [sb(f"wbS{i}", [128, NS], BF16) for i in range(4)]
    wcS = [sb(f"wcS{i}", [128, NS], BF16) for i in range(4)]
    acol = sb("acol", [128, NS])
    dcolS = sb("dcolS", [128, 1])
    bdcS = sb("bdcS", [128, 1])
    cwS = sb("cwS", [128, 12])
    cbS = sb("cbS", [128, 4])
    patDS = sb("patDS", [128, 4 * 128])
    patBS = sb("patBS", [128, 32 * 128], BF16)
    patYS = sb("patYS", [128, 4 * 128], BF16)
    deltaS = sb("deltaS", [128, TCH])
    wSt = sb("wSt", [128, TCH])
    BTb = [sb(f"BTb{i}", [128, TCH], BF16) for i in range(2)]
    CTb = [sb(f"CTb{i}", [128, TCH], BF16) for i in range(2)]
    dRep = [sb(f"dRep{i}", [128, TCH]) for i in range(4)]
    wRep = [sb(f"wRep{i}", [128, TCH]) for i in range(4)]
    dAS = sb("dAS", [128, TCH])
    bS = sb("bS", [128, TCH])
    hS = sb("hS", [128, TCH])
    ymulb = sb("ymulb", [128, TCH], BF16)
    hstate = sb("hstate", [128, NS])
    ysS = sb("ysS", [128, TCH])
    sc1 = sb("sc1", [128, TCH])
    sc2 = sb("sc2", [128, TCH])
    consts = sb("consts", [128, 4])
    p_d = ps("p_d", [128, 512])
    p_B = ps("p_B", [128, TCH])
    p_C = ps("p_C", [128, TCH])
    p_Y = ps("p_Y", [128, TCH])

    with (nc.semaphore("step") as step, nc.Block() as block):
        seq.dma("sync", cwS.rearrange("p (f k) -> p f k", f=4), cwc)
        seq.dma("sync", cbS[:, :], cbc)
        seq.dma("sync", bdcS[:, :], bdc)
        seq.dma("sync", dcolS[:, :], dcol)
        seq.dma("sync", patDS[:, :], patD)
        seq.dma("gpsimd", patBS[:, :], patB)
        seq.dma("gpsimd", patYS[:, :], patY)
        for ic in range(4):
            seq.dma("gpsimd", wdS[ic][:, :], wdel[ic * 128:(ic + 1) * 128, :])
            seq.dma("gpsimd", wbS[ic][:, :], wB[ic * 128:(ic + 1) * 128, :])
            seq.dma("gpsimd", wcS[ic][:, :], wC[ic * 128:(ic + 1) * 128, :])
        seq.op("vector", lambda v: v.memset(consts[:, 0:1], 1.0))
        seq.op("vector", lambda v: v.memset(consts[:, 1:2], -1.0))
        seq.op("vector", lambda v: v.memset(hstate[:, :], 0.0))
        C_ONE, C_M1 = consts[:, 0:1], consts[:, 1:2]
        seq.dma("sync", acol[:, :], aarr)
        seq.op("scalar", lambda s: s.activation(acol[:, :], acol[:, :], AF.Exp))
        seq.op("vector", lambda v: v.tensor_scalar(acol[:, :], acol[:, :], -1.0, None, ALU.mult))

        for k in range(NCH):
            ks = k * TCH
            for fc in range(4):
                if k == 0:
                    seq.op("vector", lambda v, fc=fc: v.memset(lt[fc][:, 0:2], 0.0))
                    seq.dma("sync", lt[fc][:, 2:TCH + 2], leftF[fc * 128:(fc + 1) * 128, 0:TCH])
                else:
                    seq.dma("sync", lt[fc][:, :], leftF[fc * 128:(fc + 1) * 128, ks - 2:ks + TCH])
                seq.op("vector", lambda v, fc=fc: v.tensor_scalar(sc1[:, :], lt[fc][:, 2:TCH + 2], cwS[:, fc * 3 + 2:fc * 3 + 3], cbS[:, fc:fc + 1], ALU.mult, ALU.add))
                seq.op("vector", lambda v, fc=fc: v.scalar_tensor_tensor(sc1[:, :], lt[fc][:, 1:TCH + 1], cwS[:, fc * 3 + 1:fc * 3 + 2], sc1[:, :], ALU.mult, ALU.add))
                seq.op("vector", lambda v, fc=fc: v.scalar_tensor_tensor(sc1[:, :], lt[fc][:, 0:TCH], cwS[:, fc * 3 + 0:fc * 3 + 1], sc1[:, :], ALU.mult, ALU.add))
                seq.op("scalar", lambda s, fc=fc: s.activation(uf[fc][:, :], sc1[:, :], AF.Silu))
                seq.op("vector", lambda v, fc=fc: v.tensor_copy(ub[fc][:, :], uf[fc][:, :]))
            for sub in range(NS2):
                for ic in range(4):
                    seq.op("tensor", lambda t, ic=ic, sub=sub: t.matmul(p_d[:, 0:512], wdS[ic][:, :], ub[ic][:, sub * 512:(sub + 1) * 512], start=(ic == 0), stop=(ic == 3)))
                seq.op("scalar", lambda s: s.activation(sc1[:, 0:512], p_d[:, 0:512], AF.Identity, bias=bdcS[:, 0:1]))
                seq.op("scalar", lambda s: s.activation(sc2[:, 0:512], sc1[:, 0:512], AF.Abs))
                seq.op("scalar", lambda s: s.activation(sc2[:, 0:512], sc2[:, 0:512], AF.Exp, scale=C_M1))
                seq.op("scalar", lambda s: s.activation(sc2[:, 0:512], sc2[:, 0:512], AF.Ln, bias=C_ONE))
                seq.op("vector", lambda v: v.tensor_scalar(sc1[:, 0:512], sc1[:, 0:512], 0.0, None, ALU.max))
                seq.op("vector", lambda v, sub=sub: v.tensor_tensor(deltaS[:, sub * 512:(sub + 1) * 512], sc1[:, 0:512], sc2[:, 0:512], ALU.add))
            seq.op("vector", lambda v: v.tensor_tensor(wSt[:, :], deltaS[:, :], uf[0][:, :], ALU.mult))
            for nh in range(2):
                for sub in range(NS2):
                    for ic in range(4):
                        seq.op("tensor", lambda t, ic=ic, nh=nh, sub=sub: t.matmul(p_d[:, 0:512], wbS[ic][:, nh * 128:(nh + 1) * 128], ub[ic][:, sub * 512:(sub + 1) * 512], start=(ic == 0), stop=(ic == 3)))
                    seq.op("vector", lambda v, nh=nh, sub=sub: v.tensor_copy(BTb[nh][:, sub * 512:(sub + 1) * 512], p_d[:, 0:512]))
                    for ic in range(4):
                        seq.op("tensor", lambda t, ic=ic, nh=nh, sub=sub: t.matmul(p_d[:, 0:512], wcS[ic][:, nh * 128:(nh + 1) * 128], ub[ic][:, sub * 512:(sub + 1) * 512], start=(ic == 0), stop=(ic == 3)))
                    seq.op("vector", lambda v, nh=nh, sub=sub: v.tensor_copy(CTb[nh][:, sub * 512:(sub + 1) * 512], p_d[:, 0:512]))
            for dg in range(4):
                for sub in range(NS2):
                    seq.op("tensor", lambda t, dg=dg, sub=sub: t.matmul(p_d[:, 0:512], patDS[:, dg * 128:(dg + 1) * 128], deltaS[:, sub * 512:(sub + 1) * 512], start=True, stop=True))
                    seq.op("vector", lambda v, dg=dg, sub=sub: v.tensor_copy(dRep[dg][:, sub * 512:(sub + 1) * 512], p_d[:, 0:512]))
                    seq.op("tensor", lambda t, dg=dg, sub=sub: t.matmul(p_d[:, 0:512], patDS[:, dg * 128:(dg + 1) * 128], wSt[:, sub * 512:(sub + 1) * 512], start=True, stop=True))
                    seq.op("vector", lambda v, dg=dg, sub=sub: v.tensor_copy(wRep[dg][:, sub * 512:(sub + 1) * 512], p_d[:, 0:512]))
            for ng in range(64):
                j, nh = ng % 32, ng // 32
                for sub in range(NS2):
                    seq.op("tensor", lambda t, j=j, nh=nh, sub=sub: t.matmul(p_B[:, sub * 512:(sub + 1) * 512], patBS[:, j * 128:(j + 1) * 128], BTb[nh][:, sub * 512:(sub + 1) * 512], start=True, stop=True))
                    seq.op("tensor", lambda t, j=j, nh=nh, sub=sub: t.matmul(p_C[:, sub * 512:(sub + 1) * 512], patBS[:, j * 128:(j + 1) * 128], CTb[nh][:, sub * 512:(sub + 1) * 512], start=True, stop=True))
                for dg in range(4):
                    g = ng * 4 + dg
                    seq.op("scalar", lambda s, dg=dg, ng=ng: s.activation(dAS[:, :], dRep[dg][:, :], AF.Exp, scale=acol[:, ng * 4 + dg:ng * 4 + dg + 1]))
                    seq.op("vector", lambda v, dg=dg: v.tensor_tensor(bS[:, :], wRep[dg][:, :], p_B[:, :], ALU.mult))
                    seq.op("vector", lambda v, g=g: v.tensor_tensor_scan(hS[:, :], dAS[:, :], bS[:, :], hstate[:, g:g + 1], ALU.mult, ALU.add))
                    seq.op("vector", lambda v, g=g: v.tensor_copy(hstate[:, g:g + 1], hS[:, TCH - 1:TCH]))
                    seq.op("vector", lambda v: v.tensor_tensor(ymulb[:, :], hS[:, :], p_C[:, :], ALU.mult))
                    for sub in range(NS2):
                        seq.op("tensor", lambda t, dg=dg, ng=ng, sub=sub: t.matmul(p_Y[:, sub * 512:(sub + 1) * 512], patYS[:, dg * 128:(dg + 1) * 128], ymulb[:, sub * 512:(sub + 1) * 512], start=(ng == 0 and dg == 0), stop=(ng == 63 and dg == 3)))
            for sub in range(NS2):
                seq.op("vector", lambda v, sub=sub: v.scalar_tensor_tensor(ysS[:, sub * 512:(sub + 1) * 512], uf[0][:, sub * 512:(sub + 1) * 512], dcolS[:, 0:1], p_Y[:, sub * 512:(sub + 1) * 512], ALU.mult, ALU.add))
            seq.dma("sync", yso[:, ks:ks + TCH], ysS[:, :])
        seq.build(block, step)
    es.close()
    return nc

def _rot_blocks(arr, ds):
    blocks = [arr[i * 128:(i + 1) * 128] for i in range(arr.shape[0] // 128)]
    order = [(ds + i) % len(blocks) for i in range(len(blocks))]
    return np.concatenate([blocks[i] for i in order], axis=0)

def host_l4_inputs(T, inp, leftT_by_batch):
    cw = inp["L1_attn_conv_w"][:, 0, :]
    cb = inp["L1_attn_conv_b"]
    Wd = inp["L1_attn_W_delta"]
    bd = inp["L1_attn_b_delta"]
    WB_ = inp["L1_attn_W_B"]; WC_ = inp["L1_attn_W_C"]
    Alog = inp["L1_attn_A_log"]; Dv = inp["L1_attn_D"]
    patD = np.zeros((128, 4, 128), np.float32)
    for dg in range(4):
        for p in range(128):
            patD[dg * 32 + p // 4, dg, p] = 1.0
    patB = np.zeros((128, 32, 128), np.float32)
    for j in range(32):
        for p in range(128):
            patB[j * 4 + p % 4, j, p] = 1.0
    patY = np.zeros((128, 4, 128), np.float32)
    for dg in range(4):
        for p in range(128):
            patY[p, dg, dg * 32 + p // 4] = 1.0
    patD_in = np.ascontiguousarray(patD.reshape(128, 4 * 128))
    patB_in = np.ascontiguousarray(patB.reshape(128, 32 * 128))
    patY_in = np.ascontiguousarray(patY.reshape(128, 4 * 128))
    maps = []
    for c in range(8):
        b, ds = c // 4, c % 4
        dlo = ds * 128
        p = np.arange(128)
        i_, nsub = p // 4, p % 4
        dgi, ngi = np.meshgrid(np.arange(4), np.arange(64), indexing="ij")
        aarr = np.zeros((128, 256), np.float32)
        for dg in range(4):
            for ng in range(64):
                aarr[:, ng * 4 + dg] = Alog[dlo + dg * 32 + i_, ng * 4 + nsub]
        maps.append(dict(
            leftF=np.ascontiguousarray(_rot_blocks(leftT_by_batch[b], ds)),
            cwc=np.ascontiguousarray(_rot_blocks(cw, ds).reshape(4, 128, 3).transpose(1, 0, 2)),
            cbc=np.ascontiguousarray(_rot_blocks(cb[:, None], ds).reshape(4, 128).T),
            wdel=np.ascontiguousarray(_rot_blocks(Wd, ds)[:, dlo:dlo + 128]),
            bdc=np.ascontiguousarray(bd[dlo:dlo + 128, None]),
            wB=np.ascontiguousarray(_rot_blocks(WB_, ds)),
            wC=np.ascontiguousarray(_rot_blocks(WC_, ds)),
            aarr=aarr, dcol=np.ascontiguousarray(Dv[dlo:dlo + 128, None]),
            patD=patD_in, patB=patB_in, patY=patY_in))
    return maps

def build_l5(T):
    TC = T // 4
    NTB = TC // 128
    TSZ2 = min(1024, TC)
    NTCG = TC // TSZ2
    NS5 = TSZ2 // 512 if TSZ2 >= 512 else 1
    S5 = min(512, TSZ2)
    nc = bass.Bass("TRN2", target_bir_lowering=False)
    x2 = nc.dram_tensor("x2", [TC, E], F32, kind="ExternalInput").ap()
    ysT = nc.dram_tensor("ysT", [DI, TC], F32, kind="ExternalInput").ap()
    rightT = nc.dram_tensor("rightT", [DI, TC], F32, kind="ExternalInput").ap()
    wdown = nc.dram_tensor("wdown", [DI, E], F32, kind="ExternalInput").ap()
    bdrow = nc.dram_tensor("bdrow", [1, E], F32, kind="ExternalInput").ap()
    wred = nc.dram_tensor("wred", [E, GATE], F32, kind="ExternalInput").ap()
    wg = nc.dram_tensor("wg", [NEXP, GATE], F32, kind="ExternalInput").ap()
    w1f = nc.dram_tensor("w1f", [NEXP, E, FF1], F32, kind="ExternalInput").ap()
    b1c = nc.dram_tensor("b1c", [NEXP, 128, FF1 // 128], F32, kind="ExternalInput").ap()
    w2f = nc.dram_tensor("w2f", [NEXP, FF1, E], F32, kind="ExternalInput").ap()
    b2r = nc.dram_tensor("b2r", [NEXP, 1, E], F32, kind="ExternalInput").ap()
    identf = nc.dram_tensor("identf", [128, 128], F32, kind="ExternalInput").ap()
    outT = nc.dram_tensor("out", [TC, E], F32, kind="ExternalOutput").ap()
    x3d = nc.dram_tensor("x3d", [TC, E], F32).ap()
    yd = nc.dram_tensor("yd", [TC, E], F32).ap()

    seq = Seq(nc)
    es = ExitStack()
    sb = lambda n, s, d=F32: es.enter_context(nc.sbuf_tensor(n, s, d)).ap()
    ps = lambda n, s: es.enter_context(nc.psum_tensor(n, s, F32)).ap()
    NHC = FF1 // 128
    arena1 = sb("arena1", [128, NHC * 1024], BF16)
    arena2 = sb("arena2", [128, 8 * FF1], BF16)
    x3bf = [sb(f"x3bf{i}", [128, TC], BF16) for i in range(8)]
    tmA = sb("tmA", [128, E])
    tmB = sb("tmB", [128, E])
    tmC = sb("tmC", [128, E])
    xtr = sb("xtr", [128, 128])
    wredf = sb("wredf", [128, 8 * GATE])
    wgS = sb("wgS", [8, GATE])
    wgT = sb("wgT", [16, 8])
    redT = sb("redT", [16, 128])
    bdr = sb("bdr", [1, E], BF16)
    b2S = sb("b2S", [1, E], BF16)
    b1S = sb("b1S", [128, NHC])
    onesb = sb("onesb", [1, 128], BF16)
    sident = sb("sident", [128, 128])
    st1 = sb("st1", [128, 16])
    sel8 = sb("sel8", [128, 8])
    wsel = sb("wsel", [128, NTB * 8])
    cols = sb("cols", [128, 4])
    yev = sb("yev", [128, 512])
    p_tr = ps("p_tr", [128, 128])
    p_a = ps("p_a", [128, 512])
    p_b = ps("p_b", [128, 512])
    p_g = ps("p_g", [128, 16])
    p_r = ps("p_r", [128, 16])

    a1f = arena1.bitcast(F32)
    ysS = [a1f[:, i * TC:(i + 1) * TC] for i in range(4)]
    srS = [a1f[:, (4 + i) * TC:(5 + i) * TC] for i in range(4)]
    hG = [arena1[:, i * 1024:(i + 1) * 1024] for i in range(NHC)]
    gdS = [arena2[:, i * TC:(i + 1) * TC] for i in range(4)]
    wdnS = [arena2[:, 4 * TC + i * E:4 * TC + (i + 1) * E] for i in range(4)]
    w1S = [arena2[:, i * FF1:(i + 1) * FF1] for i in range(8)]
    w2S = [arena2[:, i * 1024:(i + 1) * 1024] for i in range(NHC)]

    with (nc.semaphore("step") as step, nc.Block() as block):
        seq.dma("sync", sident[:, :], identf)
        seq.dma("sync", wredf.rearrange("p (i g) -> p i g", i=8), wred.rearrange("(i p) g -> p i g", p=128))
        seq.dma("sync", wgS[:, :], wg)
        seq.dma("gpsimd", bdr[:, :], bdrow)
        seq.op("vector", lambda v: v.memset(cols[:, 0:1], 1e-5))
        seq.op("vector", lambda v: v.memset(cols[:, 1:2], 0.0))
        seq.op("vector", lambda v: v.memset(onesb[:, :], 1.0))
        EPS, ZERO = cols[:, 0:1], cols[:, 1:2]
        seq.op("scalar", lambda s: s.activation(st1[0:8, 0:GATE], wgS[:, :], AF.Square))
        seq.op("vector", lambda v: v.tensor_reduce(st1[0:8, 0:1] if False else sel8[0:8, 0:1], st1[0:8, 0:GATE], AX.XYZW, ALU.add))
        seq.op("scalar", lambda s: s.activation(sel8[0:8, 1:2], sel8[0:8, 0:1], AF.Sqrt, bias=ZERO[0:8, :]))
        seq.op("vector", lambda v: v.tensor_scalar(sel8[0:8, 1:2], sel8[0:8, 1:2], 1e-4, None, ALU.max))
        seq.op("vector", lambda v: v.reciprocal(sel8[0:8, 2:3], sel8[0:8, 1:2]))
        seq.op("vector", lambda v: v.tensor_scalar(wgS[:, :], wgS[:, :], sel8[0:8, 2:3], None, ALU.mult))
        seq.op("tensor", lambda t: t.transpose(p_g[0:16, 0:8], wgS[:, :], sident[0:8, 0:8]))
        seq.op("vector", lambda v: v.tensor_copy(wgT[:, :], p_g[0:16, 0:8]))
        for i in range(4):
            seq.dma("sync", ysS[i], ysT[i * 128:(i + 1) * 128, :])
            seq.dma("sync", srS[i], rightT[i * 128:(i + 1) * 128, :])
            seq.op("scalar", lambda s, i=i: s.activation(srS[i], srS[i], AF.Silu))
            seq.op("vector", lambda v, i=i: v.tensor_tensor(gdS[i], ysS[i], srS[i], ALU.mult))
            seq.dma("gpsimd", wdnS[i], wdown[i * 128:(i + 1) * 128, :])
        for tb in range(NTB):
            seq.dma("sync", tmA[:, :], x2[tb * 128:tb * 128 + 128, :])
            for oh in range(2):
                for ic in range(4):
                    seq.op("tensor", lambda t, ic=ic, oh=oh, tb=tb: t.matmul(p_a[:, 0:512], gdS[ic][:, tb * 128:(tb + 1) * 128], wdnS[ic][:, oh * 512:(oh + 1) * 512], start=(ic == 0), stop=False))
                seq.op("tensor", lambda t, oh=oh: t.matmul(p_a[:, 0:512], onesb[0:1, :], bdr[0:1, oh * 512:(oh + 1) * 512], start=False, stop=True))
                seq.op("vector", lambda v, oh=oh: v.scalar_tensor_tensor(tmB[:, oh * 512:(oh + 1) * 512], tmA[:, oh * 512:(oh + 1) * 512], 2.0, p_a[:, 0:512], ALU.mult, ALU.add))
            seq.op("vector", lambda v: v.tensor_reduce(st1[:, 0:1], tmB[:, :], AX.XYZW, ALU.add))
            seq.op("scalar", lambda s: s.activation(tmC[:, :], tmB[:, :], AF.Square))
            seq.op("vector", lambda v: v.tensor_reduce(st1[:, 1:2], tmC[:, :], AX.XYZW, ALU.add))
            seq.op("vector", lambda v: v.tensor_scalar(st1[:, 2:3], st1[:, 0:1], 1.0 / E, None, ALU.mult))
            seq.op("vector", lambda v: v.tensor_scalar(st1[:, 3:4], st1[:, 1:2], 1.0 / E, None, ALU.mult))
            seq.op("vector", lambda v: v.tensor_tensor(st1[:, 4:5], st1[:, 2:3], st1[:, 2:3], ALU.mult))
            seq.op("vector", lambda v: v.tensor_tensor(st1[:, 5:6], st1[:, 3:4], st1[:, 4:5], ALU.subtract))
            seq.op("scalar", lambda s: s.activation(st1[:, 6:7], st1[:, 5:6], AF.Sqrt, bias=EPS))
            seq.op("vector", lambda v: v.reciprocal(st1[:, 7:8], st1[:, 6:7]))
            seq.op("vector", lambda v: v.tensor_scalar(tmB[:, :], tmB[:, :], st1[:, 2:3], st1[:, 7:8], ALU.subtract, ALU.mult))
            seq.dma("sync", x3d[tb * 128:(tb + 1) * 128, :], tmB[:, :])
            for fc in range(8):
                seq.op("tensor", lambda t, fc=fc: t.transpose(p_tr[:, :], tmB[:, fc * 128:(fc + 1) * 128], sident[:, :]))
                seq.op("vector", lambda v, fc=fc, tb=tb: v.tensor_copy(x3bf[fc][:, tb * 128:(tb + 1) * 128], p_tr[:, :]))
                seq.op("vector", lambda v: v.tensor_copy(xtr[:, :], p_tr[:, :]))
                seq.op("tensor", lambda t, fc=fc: t.matmul(p_r[:, 0:GATE], xtr[:, :], wredf[:, fc * GATE:(fc + 1) * GATE], start=(fc == 0), stop=(fc == 7)))
            seq.op("vector", lambda v: v.tensor_copy(tmC[:, 0:GATE], p_r[:, 0:GATE]))
            seq.op("tensor", lambda t: t.transpose(p_tr[0:16, 0:128], tmC[:, 0:GATE], sident[:, :]))
            seq.op("vector", lambda v: v.tensor_copy(redT[:, :], p_tr[0:16, 0:128]))
            seq.op("tensor", lambda t: t.matmul(p_g[:, 0:8], redT[:, :], wgT[:, :], start=True, stop=True))
            seq.op("vector", lambda v: v.tensor_copy(st1[:, 0:8], p_g[:, 0:8]))
            seq.op("vector", lambda v: v.reduce_max(st1[:, 8:9], st1[:, 0:8], AX.XYZW, negate=True))
            seq.op("scalar", lambda s: s.activation(tmC[:, 0:8], st1[:, 0:8], AF.Exp, bias=st1[:, 8:9], accum_out=st1[:, 9:10]))
            seq.op("vector", lambda v: v.reciprocal(st1[:, 10:11], st1[:, 9:10]))
            seq.op("vector", lambda v: v.tensor_scalar(st1[:, 11:12], st1[:, 8:9], -1.0, None, ALU.mult))
            seq.op("vector", lambda v: v.tensor_scalar(sel8[:, 0:8], st1[:, 0:8], st1[:, 11:12], None, ALU.is_ge))
            seq.op("vector", lambda v: v.memset(st1[:, 12:13], 1.0))
            for e0 in range(8):
                seq.op("vector", lambda v, e0=e0: v.tensor_tensor(sel8[:, e0:e0 + 1], sel8[:, e0:e0 + 1], st1[:, 12:13], ALU.mult))
                if e0 < 7:
                    seq.op("vector", lambda v, e0=e0: v.tensor_scalar(st1[:, 13:14], sel8[:, e0:e0 + 1], -1.0, 1.0, ALU.mult, ALU.add))
                    seq.op("vector", lambda v: v.tensor_tensor(st1[:, 12:13], st1[:, 12:13], st1[:, 13:14], ALU.mult))
            seq.op("vector", lambda v, tb=tb: v.tensor_scalar(wsel[:, tb * 8:(tb + 1) * 8], sel8[:, 0:8], st1[:, 10:11], None, ALU.mult))
        for e in range(NEXP):
            seq.dma("sync", b1S[:, :], b1c[e])
            seq.dma("gpsimd", b2S[:, :], b2r[e])
            for tcg in range(NTCG):
                t0 = tcg * TSZ2
                for ic in range(8):
                    seq.dma("gpsimd", w1S[ic], w1f[e, ic * 128:(ic + 1) * 128, :])
                wb = seq.barrier()
                for hc in range(NHC):
                    for s5 in range(NS5):
                        for ic in range(8):
                            seq.op("tensor", lambda t, ic=ic, hc=hc, s5=s5, t0=t0: t.matmul(p_a[:, 0:S5], w1S[ic][:, hc * 128:(hc + 1) * 128], x3bf[ic][:, t0 + s5 * 512:t0 + s5 * 512 + S5], start=(ic == 0), stop=(ic == 7)))
                        seq.op("scalar", lambda s, hc=hc, s5=s5: s.activation(hG[hc][:, s5 * 512:s5 * 512 + S5], p_a[:, 0:S5], AF.Gelu, bias=b1S[:, hc:hc + 1]))
                for hc in range(NHC):
                    seq.dma("gpsimd", w2S[hc], w2f[e, hc * 128:(hc + 1) * 128, :])
                wb = seq.barrier()
                for tb0 in range(TSZ2 // 128):
                    tb = tcg * (TSZ2 // 128) + tb0
                    for oh in range(2):
                        for hc in range(NHC):
                            seq.op("tensor", lambda t, hc=hc, tb0=tb0, oh=oh: t.matmul(p_b[:, 0:512], hG[hc][:, tb0 * 128:(tb0 + 1) * 128], w2S[hc][:, oh * 512:(oh + 1) * 512], start=(hc == 0), stop=False))
                        seq.op("tensor", lambda t, oh=oh: t.matmul(p_b[:, 0:512], onesb[0:1, :], b2S[0:1, oh * 512:(oh + 1) * 512], start=False, stop=True))
                        seq.op("vector", lambda v, tb=tb, e=e: v.tensor_scalar(yev[:, 0:512], p_b[:, 0:512], wsel[:, tb * 8 + e:tb * 8 + e + 1], None, ALU.mult))
                        if e == 0:
                            seq.dma("gpsimd", yd[tb * 128:(tb + 1) * 128, oh * 512:(oh + 1) * 512], yev[:, 0:512])
                        else:
                            seq.dma("gpsimd", yd[tb * 128:(tb + 1) * 128, oh * 512:(oh + 1) * 512], yev[:, 0:512], accum_op=ALU.add)
        for tb in range(NTB):
            seq.dma("sync", tmA[:, :], x3d[tb * 128:tb * 128 + 128, :])
            seq.dma("sync", tmB[:, :], yd[tb * 128:tb * 128 + 128, :])
            seq.op("vector", lambda v: v.tensor_tensor(tmA[:, :], tmA[:, :], tmB[:, :], ALU.add))
            seq.op("vector", lambda v: v.tensor_reduce(st1[:, 0:1], tmA[:, :], AX.XYZW, ALU.add))
            seq.op("scalar", lambda s: s.activation(tmC[:, :], tmA[:, :], AF.Square))
            seq.op("vector", lambda v: v.tensor_reduce(st1[:, 1:2], tmC[:, :], AX.XYZW, ALU.add))
            seq.op("vector", lambda v: v.tensor_scalar(st1[:, 2:3], st1[:, 0:1], 1.0 / E, None, ALU.mult))
            seq.op("vector", lambda v: v.tensor_scalar(st1[:, 3:4], st1[:, 1:2], 1.0 / E, None, ALU.mult))
            seq.op("vector", lambda v: v.tensor_tensor(st1[:, 4:5], st1[:, 2:3], st1[:, 2:3], ALU.mult))
            seq.op("vector", lambda v: v.tensor_tensor(st1[:, 5:6], st1[:, 3:4], st1[:, 4:5], ALU.subtract))
            seq.op("scalar", lambda s: s.activation(st1[:, 6:7], st1[:, 5:6], AF.Sqrt, bias=EPS))
            seq.op("vector", lambda v: v.reciprocal(st1[:, 7:8], st1[:, 6:7]))
            seq.op("vector", lambda v: v.tensor_scalar(tmA[:, :], tmA[:, :], st1[:, 2:3], st1[:, 7:8], ALU.subtract, ALU.mult))
            seq.dma("sync", outT[tb * 128:(tb + 1) * 128, :], tmA[:, :])
        seq.build(block, step)
    es.close()
    return nc

def host_l5_inputs(T, inp, x2_full_tm, ys_by_batch, rightT_by_core):
    maps = []
    b1cols = np.ascontiguousarray(inp["L1_ffn_b1"].reshape(NEXP, FF1 // 128, 128).transpose(0, 2, 1))
    for c in range(8):
        b = c // 4
        ysTc = ys_by_batch[b][:, core_token_rows(T, c)]
        maps.append(dict(
            x2=np.ascontiguousarray(host_core_token_slice(x2_full_tm, T, c)),
            ysT=np.ascontiguousarray(ysTc),
            rightT=np.ascontiguousarray(rightT_by_core[c]),
            wdown=np.ascontiguousarray(inp["L1_attn_W_down"]),
            bdrow=np.ascontiguousarray(inp["L1_attn_b_down"][None, :]),
            wred=np.ascontiguousarray(inp["L1_ffn_w_red"]),
            wg=np.ascontiguousarray(inp["L1_ffn_wg"]),
            w1f=np.ascontiguousarray(inp["L1_ffn_w1"]),
            b1c=b1cols,
            w2f=np.ascontiguousarray(inp["L1_ffn_w2"]),
            b2r=np.ascontiguousarray(inp["L1_ffn_b2"][:, None, :]),
            identf=np.eye(128, dtype=np.float32)))
    return maps


def build_l1(T):
    TC = T // 4
    NSL = TC // 128
    NKT = T // 512
    nc = bass.Bass("TRN2", target_bir_lowering=False)
    hT = nc.dram_tensor("hT", [E, T], F32, kind="ExternalInput").ap()
    hqT = nc.dram_tensor("hqT", [E, TC], F32, kind="ExternalInput").ap()
    htm = nc.dram_tensor("htm", [T, E], F32, kind="ExternalInput").ap()
    wqT = nc.dram_tensor("wqT", [E, E], F32, kind="ExternalInput").ap()
    wkT = nc.dram_tensor("wkT", [E, E], F32, kind="ExternalInput").ap()
    bqc = nc.dram_tensor("bqc", [128, 8], F32, kind="ExternalInput").ap()
    bkc = nc.dram_tensor("bkc", [128, 8], F32, kind="ExternalInput").ap()
    gum = nc.dram_tensor("gum", [TC, T], F32, kind="ExternalInput").ap()
    mask4 = nc.dram_tensor("mask4", [128, 2048], F32, kind="ExternalInput").ap()
    rowinv = nc.dram_tensor("rowinv", [128, 16], F32, kind="ExternalInput").ap()
    identf = nc.dram_tensor("identf", [128, 128], F32, kind="ExternalInput").ap()
    attno = nc.dram_tensor("attn", [TC, E], F32, kind="ExternalOutput").ap()
    kTd = nc.dram_tensor("kTd", [8, 128, T], BF16).ap()

    seq = Seq(nc)
    es = ExitStack()
    sb = lambda n, s, d=F32: es.enter_context(nc.sbuf_tensor(n, s, d)).ap()
    wS = [sb(f"wS{i}", [128, E], BF16) for i in range(8)]
    hq = [sb(f"hq{i}", [128, TC], BF16) for i in range(8)]
    qT = [sb(f"qT{i}", [128, TC], BF16) for i in range(8)]
    kt = sb("kt", [128, 8 * 512], BF16)
    kev = sb("kev", [128, 512], BF16)
    hv = sb("hv", [128, 4 * 1024], BF16)
    avg = sb("avg", [128, T])
    mskS = sb("mskS", [128, 2048])
    gmc = sb("gmc", [128, 512])
    exs = sb("exs", [128, 512])
    wbf = sb("wbf", [128, 512], BF16)
    wT = sb("wT", [128, 512], BF16)
    oev = sb("oev", [128, E])
    mrun = sb("mrun", [128, 16])
    zrun = sb("zrun", [128, 16])
    mnew = sb("mnew", [128, 4])
    nmnew = sb("nmnew", [128, 4])
    bs4 = sb("bs4", [128, 16])
    dsc = sb("dsc", [128, 4])
    bias2 = sb("bias2", [128, 16])
    sS = sb("sS", [128, 4])
    bqs = sb("bqs", [128, 8])
    bks = sb("bks", [128, 8])
    rinv = sb("rinv", [128, 16])
    cols = sb("cols", [128, 4])
    sident = sb("sident", [128, 128])
    identb = sb("identb", [128, 128], BF16)
    p_s = es.enter_context(nc.psum_tensor("p_s", [128, 2048], F32)).ap()
    p_av = es.enter_context(nc.psum_tensor("p_av", [128, 1024], F32)).ap()
    p_t = es.enter_context(nc.psum_tensor("p_t", [128, 128], BF16)).ap()

    LN16 = float(np.log(16.0))
    with (nc.semaphore("step") as step, nc.Block() as block):
        seq.dma("sync", sident[:, :], identf)
        seq.dma("gpsimd", identb[:, :], identf)
        seq.dma("sync", mskS[:, :], mask4)
        seq.dma("sync", bqs[:, :], bqc)
        seq.dma("sync", bks[:, :], bkc)
        seq.dma("sync", rinv[:, :], rowinv)
        seq.op("vector", lambda v: v.memset(cols[:, 0:1], 0.125))
        seq.op("vector", lambda v: v.memset(cols[:, 1:2], 0.5))
        seq.op("vector", lambda v: v.memset(avg[:, :], 0.0))
        C18, C05 = cols[:, 0:1], cols[:, 1:2]
        seq.op("vector", lambda v: v.tensor_scalar(bqs[:, :], bqs[:, :], 0.125, None, ALU.mult))
        for ic in range(8):
            seq.dma("gpsimd", wS[ic][:, :], wqT[ic * 128:(ic + 1) * 128, :])
            seq.dma("gpsimd", hq[ic][:, :], hqT[ic * 128:(ic + 1) * 128, :])
        NTS = ceil_div(TC, 512)
        TSZ = min(TC, 512)
        for of in range(8):
            for ts in range(NTS):
                for ic in range(8):
                    seq.op("tensor", lambda t, ic=ic, of=of, ts=ts: t.matmul(p_s[:, 0:TSZ], wS[ic][:, of * 128:(of + 1) * 128], hq[ic][:, ts * TSZ:(ts + 1) * TSZ], start=(ic == 0), stop=(ic == 7)))
                seq.op("scalar", lambda s, of=of, ts=ts: s.activation(qT[of][:, ts * TSZ:(ts + 1) * TSZ], p_s[:, 0:TSZ], AF.Identity, bias=bqs[:, of:of + 1], scale=C18))
        for ic in range(8):
            seq.dma("gpsimd", wS[ic][:, :], wkT[ic * 128:(ic + 1) * 128, :])
        for tc in range(NKT):
            seq.dma("gpsimd", kt.rearrange("p (i k) -> p i k", i=8),
                    hT.rearrange("(i p) t -> p i t", p=128)[:, :, tc * 512:(tc + 1) * 512])
            for of in range(8):
                for ic in range(8):
                    seq.op("tensor", lambda t, ic=ic, of=of: t.matmul(p_s[:, 0:512], wS[ic][:, of * 128:(of + 1) * 128], kt[:, ic * 512:(ic + 1) * 512], start=(ic == 0), stop=(ic == 7)))
                seq.op("scalar", lambda s, of=of: s.activation(kev[:, :], p_s[:, 0:512], AF.Identity, bias=bks[:, of:of + 1]))
                seq.dma("sync", kTd[of, :, tc * 512:(tc + 1) * 512], kev[:, :])
        for sl in range(NSL):
            nkc = sl + 1
            seq.op("vector", lambda v: v.memset(mrun[:, :], -1e30))
            seq.op("vector", lambda v: v.memset(zrun[:, :], 0.0))
            for kc in range(nkc):
                seq.dma("sync", kt.rearrange("p (i k) -> p i k", i=8),
                        kTd.rearrange("i p t -> p i t")[:, :, kc * 512:(kc + 1) * 512])
                for hg in range(4):
                    for h4 in range(4):
                        h = hg * 4 + h4
                        of, half = h // 2, (h % 2) * 64
                        seq.op("tensor", lambda t, of=of, half=half, h4=h4, sl=sl: t.matmul(
                            p_s[:, h4 * 512:(h4 + 1) * 512],
                            qT[of][half:half + 64, sl * 128:(sl + 1) * 128],
                            kt[half:half + 64, of * 512:(of + 1) * 512], start=True, stop=True))
                    if kc == nkc - 1:
                        seq.op("vector", lambda v: v.tensor_tensor(p_s[:, :], p_s[:, :], mskS[:, :], ALU.add))
                    seq.op("vector", lambda v: v.tensor_reduce(dsc[:, 0:4], p_s.rearrange("p (h k) -> p h k", h=4), AX.X, ALU.max))
                    seq.op("vector", lambda v, hg=hg: v.tensor_tensor(mnew[:, :], mrun[:, hg * 4:(hg + 1) * 4], dsc[:, :], ALU.max))
                    seq.op("vector", lambda v: v.tensor_scalar(nmnew[:, :], mnew[:, :], -1.0, None, ALU.mult))
                    for h4 in range(4):
                        seq.op("scalar", lambda s, h4=h4, hg=hg: s.activation(exs[:, :], p_s[:, h4 * 512:(h4 + 1) * 512], AF.Exp, bias=nmnew[:, h4:h4 + 1], accum_out=bs4[:, hg * 4 + h4:hg * 4 + h4 + 1]))
                    seq.op("vector", lambda v, hg=hg: v.tensor_tensor(dsc[:, :], mrun[:, hg * 4:(hg + 1) * 4], mnew[:, :], ALU.subtract))
                    seq.op("scalar", lambda s: s.activation(dsc[:, :], dsc[:, :], AF.Exp))
                    seq.op("vector", lambda v, hg=hg: v.tensor_tensor(zrun[:, hg * 4:(hg + 1) * 4], zrun[:, hg * 4:(hg + 1) * 4], dsc[:, :], ALU.mult))
                    seq.op("vector", lambda v, hg=hg: v.tensor_tensor(zrun[:, hg * 4:(hg + 1) * 4], zrun[:, hg * 4:(hg + 1) * 4], bs4[:, hg * 4:(hg + 1) * 4], ALU.add))
                    seq.op("vector", lambda v, hg=hg: v.tensor_copy(mrun[:, hg * 4:(hg + 1) * 4], mnew[:, :]))
            seq.op("scalar", lambda s: s.activation(bias2[:, :], zrun[:, :], AF.Ln))
            seq.op("vector", lambda v: v.tensor_tensor(bias2[:, :], bias2[:, :], mrun[:, :], ALU.add))
            seq.op("vector", lambda v: v.tensor_scalar(bias2[:, :], bias2[:, :], LN16, -1.0, ALU.add, ALU.mult))
            for kc in range(nkc):
                seq.dma("sync", kt.rearrange("p (i k) -> p i k", i=8),
                        kTd.rearrange("i p t -> p i t")[:, :, kc * 512:(kc + 1) * 512])
                for hg in range(4):
                    for h4 in range(4):
                        h = hg * 4 + h4
                        of, half = h // 2, (h % 2) * 64
                        seq.op("tensor", lambda t, of=of, half=half, h4=h4, sl=sl: t.matmul(
                            p_s[:, h4 * 512:(h4 + 1) * 512],
                            qT[of][half:half + 64, sl * 128:(sl + 1) * 128],
                            kt[half:half + 64, of * 512:(of + 1) * 512], start=True, stop=True))
                    if kc == nkc - 1:
                        seq.op("vector", lambda v: v.tensor_tensor(p_s[:, :], p_s[:, :], mskS[:, :], ALU.add))
                    for h4 in range(4):
                        h = hg * 4 + h4
                        seq.op("scalar", lambda s, h4=h4, h=h: s.activation(exs[:, :], p_s[:, h4 * 512:(h4 + 1) * 512], AF.Exp, bias=bias2[:, h:h + 1]))
                        if h == 0:
                            seq.op("vector", lambda v, kc=kc: v.tensor_copy(avg[:, kc * 512:(kc + 1) * 512], exs[:, :]))
                        else:
                            seq.op("vector", lambda v, kc=kc: v.tensor_tensor(avg[:, kc * 512:(kc + 1) * 512], avg[:, kc * 512:(kc + 1) * 512], exs[:, :], ALU.add))
            for kc in range(nkc):
                seq.dma("sync", gmc[:, :], gum[sl * 128:(sl + 1) * 128, kc * 512:(kc + 1) * 512])
                seq.op("vector", lambda v, kc=kc: v.tensor_tensor(exs[:, :], avg[:, kc * 512:(kc + 1) * 512], gmc[:, :], ALU.add))
                seq.op("scalar", lambda s: s.activation(wbf[:, :], exs[:, :], AF.Exp, scale=C05, accum_out=sS[:, 0:1]))
                if kc == 0:
                    seq.op("vector", lambda v: v.tensor_copy(sS[:, 1:2], sS[:, 0:1]))
                else:
                    seq.op("vector", lambda v: v.tensor_tensor(sS[:, 1:2], sS[:, 1:2], sS[:, 0:1], ALU.add))
                for sub in range(4):
                    seq.op("tensor", lambda t, sub=sub: t.transpose(p_t[:, :], wbf[:, sub * 128:(sub + 1) * 128], identb[:, :]))
                    seq.op("vector", lambda v, sub=sub: v.tensor_copy(wT[:, sub * 128:(sub + 1) * 128], p_t[:, :]))
                seq.dma("gpsimd", hv.rearrange("p (s e) -> p s e", s=4),
                        htm.rearrange("(c s p) e -> p s e c", p=128, s=4)[:, :, :, kc].squeeze())
                for sub in range(4):
                    for oh in range(2):
                        seq.op("tensor", lambda t, sub=sub, oh=oh, kc=kc, nkc=nkc: t.matmul(
                            p_av[:, oh * 512:(oh + 1) * 512],
                            wT[:, sub * 128:(sub + 1) * 128],
                            hv[:, sub * 1024 + oh * 512:sub * 1024 + (oh + 1) * 512],
                            start=(kc == 0 and sub == 0), stop=(kc == nkc - 1 and sub == 3)))
            seq.op("vector", lambda v, sl=sl: v.tensor_tensor(sS[:, 1:2], sS[:, 1:2], rinv[:, sl:sl + 1], ALU.add))
            seq.op("vector", lambda v: v.reciprocal(sS[:, 2:3], sS[:, 1:2]))
            seq.op("vector", lambda v, sl=sl: v.tensor_scalar(sS[:, 3:4], rinv[:, sl:sl + 1], -1.0, 1.0, ALU.mult, ALU.add))
            seq.op("vector", lambda v: v.tensor_scalar(oev[:, :], p_av[:, :], sS[:, 2:3], sS[:, 3:4], ALU.mult, ALU.mult))
            seq.dma("sync", attno[sl * 128:(sl + 1) * 128, :], oev[:, :])
        seq.build(block, step)
    es.close()
    return nc

def host_l1_inputs(T, inp, gumbel):
    ipw = inp["L0_attn_in_proj_weight"]
    ipb = inp["L0_attn_in_proj_bias"]
    x = inp["L0_attn_hidden_states"]
    wqT = np.ascontiguousarray(ipw[:E].T)
    wkT = np.ascontiguousarray(ipw[E:2 * E].T)
    bq = ipb[:E]; bk = ipb[E:2 * E]
    ident = np.eye(128, dtype=np.float32)
    maps = []
    for c in range(8):
        b, j = c // 4, c % 4
        rows = core_token_rows(T, c)
        g = gumbel[b][rows].copy()
        kk = np.arange(T)[None, :]
        g[kk >= rows[:, None]] = NEG
        m4 = np.zeros((128, 512), np.float32)
        p = np.arange(128)[:, None]
        m4[np.arange(512)[None, :] >= (128 * j + p)] = NEG
        mask4 = np.tile(m4, (1, 4))
        rowinv = np.zeros((128, 16), np.float32)
        if j == 0:
            rowinv[0, 0] = 1.0
        maps.append(dict(
            hT=np.ascontiguousarray(x[b].T),
            hqT=np.ascontiguousarray(x[b][rows].T),
            htm=np.ascontiguousarray(x[b]),
            wqT=wqT, wkT=wkT,
            bqc=np.ascontiguousarray(bq.reshape(8, 128).T),
            bkc=np.ascontiguousarray(bk.reshape(8, 128).T),
            gum=np.ascontiguousarray(g),
            mask4=np.ascontiguousarray(mask4),
            rowinv=rowinv, identf=ident))
    return maps

I32 = mybir.dt.int32
GROUPS = [[0, 1, 2, 3], [4, 5, 6, 7]]

def build_fused(T, phases=(1,2,3,4,5), with_cc=True, debug=False, seq_mode='serial', seq_waits=None):
    TC = T // 4
    NSL = TC // 128
    NKT = T // 512
    NTB = NSL
    NTS = ceil_div(TC, 512)
    TSZ = min(TC, 512)
    NF0 = FF0 // 128
    NHC = FF1 // 128
    TCH = min(1024, T)
    NCH = T // TCH
    NS2 = TCH // 512
    TSZ2 = min(1024, TC)
    NTCG = TC // TSZ2
    NS5 = TSZ2 // 512 if TSZ2 >= 512 else 1
    S5 = min(512, TSZ2)

    nc = bass.Bass("TRN2", target_bir_lowering=False)
    D = lambda n, s, dt=F32, kind=None: (nc.dram_tensor(n, s, dt, kind=kind) if kind else nc.dram_tensor(n, s, dt))
    hT = D("hT", [E, T], kind="ExternalInput").ap()
    hqT = D("hqT", [E, TC], kind="ExternalInput").ap()
    htm = D("htm", [T, E], kind="ExternalInput").ap()
    wqT = D("wqT", [E, E], kind="ExternalInput").ap()
    wkT = D("wkT", [E, E], kind="ExternalInput").ap()
    bqc = D("bqc", [128, 8], kind="ExternalInput").ap()
    bkc = D("bkc", [128, 8], kind="ExternalInput").ap()
    gum = D("gum", [TC, T], kind="ExternalInput").ap()
    mask4 = D("mask4", [128, 2048], kind="ExternalInput").ap()
    rowinv = D("rowinv", [128, 16], kind="ExternalInput").ap()
    identf = D("identf", [128, 128], kind="ExternalInput").ap()
    x0 = D("x0", [TC, E], kind="ExternalInput").ap()
    gateT = D("gateT", [E, NEXP], kind="ExternalInput").ap()
    w1T = D("w1T", [NEXP, E, FF0], kind="ExternalInput").ap()
    w3T = D("w3T", [NEXP, E, FF0], kind="ExternalInput").ap()
    w2T = D("w2T", [NEXP, FF0, E], kind="ExternalInput").ap()
    grow = D("grow", [1, E], kind="ExternalInput").ap()
    brow = D("brow", [1, E], kind="ExternalInput").ap()
    wup = D("wup", [E, 2 * DI], kind="ExternalInput").ap()
    bupc = D("bupc", [128, 8], kind="ExternalInput").ap()
    cwc = D("cwc", [128, 4, 3], kind="ExternalInput").ap()
    cbc = D("cbc", [128, 4], kind="ExternalInput").ap()
    wdel = D("wdel", [DI, 128], kind="ExternalInput").ap()
    bdc = D("bdc", [128, 1], kind="ExternalInput").ap()
    wB = D("wB", [DI, NS], kind="ExternalInput").ap()
    wC = D("wC", [DI, NS], kind="ExternalInput").ap()
    aarr = D("aarr", [128, NS], kind="ExternalInput").ap()
    dcol = D("dcol", [128, 1], kind="ExternalInput").ap()
    patD = D("patD", [128, 4 * 128], kind="ExternalInput").ap()
    patB = D("patB", [128, 32 * 128], kind="ExternalInput").ap()
    patY = D("patY", [128, 4 * 128], kind="ExternalInput").ap()
    selpat = D("selpat", [128, 4 * 128], kind="ExternalInput").ap()
    wdown = D("wdown", [DI, E], kind="ExternalInput").ap()
    bdrow = D("bdrow", [1, E], kind="ExternalInput").ap()
    wred = D("wred", [E, GATE], kind="ExternalInput").ap()
    wg = D("wg", [NEXP, GATE], kind="ExternalInput").ap()
    w1f = D("w1f", [NEXP, E, FF1], kind="ExternalInput").ap()
    b1c = D("b1c", [NEXP, 128, FF1 // 128], kind="ExternalInput").ap()
    w2f = D("w2f", [NEXP, FF1, E], kind="ExternalInput").ap()
    b2r = D("b2r", [NEXP, 1, E], kind="ExternalInput").ap()
    jval = D("jval", [1, 1], I32, kind="ExternalInput").ap()
    outT = D("out", [TC, E], F32, kind="ExternalOutput").ap()
    dk = "ExternalOutput" if debug else None
    kTd = D("kTd", [8, 128, T], BF16).ap()
    attnD = D("attnD", [TC, E], kind=dk).ap()
    x1d = D("x1d", [TC, E], kind=dk).ap()
    ydA = D("ydA", [TC, E], kind=dk).ap()
    x2d = D("x2d", [TC, E], kind=dk).ap()
    leftIn = nc.dram_tensor("leftIn", [DI, TC], F32)
    leftAG = nc.dram_tensor("leftAG", [4 * DI, TC], F32)
    rightD = D("rightD", [DI, TC], kind=dk).ap()
    ysIn = nc.dram_tensor("ysIn", [128, T], F32)
    ysAG = nc.dram_tensor("ysAG", [DI, T], F32)
    if debug:
        dbgL = nc.dram_tensor("dbgL", [DI, TC], F32, kind="ExternalOutput")
        dbgLA = nc.dram_tensor("dbgLA", [4 * DI, TC], F32, kind="ExternalOutput")
        dbgY = nc.dram_tensor("dbgY", [128, T], F32, kind="ExternalOutput")
        dbgYA = nc.dram_tensor("dbgYA", [DI, T], F32, kind="ExternalOutput")
    x3d = D("x3d", [TC, E], kind=dk).ap()
    ydB = D("ydB", [TC, E], kind=dk).ap()

    seq = Seq(nc, mode=seq_mode, waits=seq_waits)
    rj = nc.gpsimd.alloc_register("rj")
    rtmp = nc.gpsimd.alloc_register("rtmp")
    LN16 = float(np.log(16.0))

    _es_sems = ExitStack()
    make_seq_sems(nc, _es_sems)
    with (nc.Block() as block):
        def _phase1():
            with ExitStack() as es:
                sb = lambda n, s, d=F32: es.enter_context(nc.sbuf_tensor(n, s, d)).ap()
                wS = [sb(f"a_wS{i}", [128, E], BF16) for i in range(8)]
                hq = [sb(f"a_hq{i}", [128, TC], BF16) for i in range(8)]
                qT = [sb(f"a_qT{i}", [128, TC], BF16) for i in range(8)]
                kt = sb("a_kt", [128, 8 * 512], BF16)
                kev = sb("a_kev", [128, 512], BF16)
                hv = sb("a_hv", [128, 4 * 1024], BF16)
                avg = sb("a_avg", [128, T])
                mskS = sb("a_mskS", [128, 2048])
                gmc = sb("a_gmc", [128, 512])
                exs = sb("a_exs", [128, 512])
                wbf = sb("a_wbf", [128, 512], BF16)
                wT = sb("a_wT", [128, 512], BF16)
                oev = sb("a_oev", [128, E])
                mrun = sb("a_mrun", [128, 16])
                zrun = sb("a_zrun", [128, 16])
                mnew = sb("a_mnew", [128, 4])
                nmnew = sb("a_nmnew", [128, 4])
                bs4 = sb("a_bs4", [128, 16])
                dsc = sb("a_dsc", [128, 4])
                bias2 = sb("a_bias2", [128, 16])
                sS = sb("a_sS", [128, 4])
                bqs = sb("a_bqs", [128, 8])
                bks = sb("a_bks", [128, 8])
                rinv = sb("a_rinv", [128, 16])
                cols = sb("a_cols", [128, 4])
                sident = sb("a_sident", [128, 128])
                identb = sb("a_identb", [128, 128], BF16)
                jvS = sb("a_jv", [1, 1], I32)
                p_s = es.enter_context(nc.psum_tensor("a_ps", [128, 2048], F32)).ap()
                p_av = es.enter_context(nc.psum_tensor("a_pav", [128, 1024], F32)).ap()
                p_t = es.enter_context(nc.psum_tensor("a_pt", [128, 1024], BF16)).ap()[:, 0:128]

                seq.dma("sync", sident[:, :], identf)
                seq.dma("gpsimd", identb[:, :], identf)
                seq.dma("sync", mskS[:, :], mask4)
                seq.dma("sync", bqs[:, :], bqc)
                seq.dma("sync", bks[:, :], bkc)
                seq.dma("sync", rinv[:, :], rowinv)
                seq.dma("sync", jvS[:, :], jval)
                def _ld(g):
                    g.reg_load(rj, jvS[0:1, 0:1])
                    return g.engine_nop()
                seq.op("gpsimd", _ld, wait=seq.barrier(), no_inc=True)
                seq.op("vector", lambda v: v.memset(cols[:, 0:1], 0.125))
                seq.op("vector", lambda v: v.memset(cols[:, 1:2], 0.5))
                seq.op("vector", lambda v: v.memset(avg[:, :], 0.0))
                C18, C05 = cols[:, 0:1], cols[:, 1:2]
                seq.op("vector", lambda v: v.tensor_scalar(bqs[:, :], bqs[:, :], 0.125, None, ALU.mult))
                for ic in range(8):
                    seq.dma("gpsimd", wS[ic][:, :], wqT[ic * 128:(ic + 1) * 128, :])
                    seq.dma("gpsimd", hq[ic][:, :], hqT[ic * 128:(ic + 1) * 128, :])
                for of in range(8):
                    for ts in range(NTS):
                        for ic in range(8):
                            seq.op("tensor", lambda t, ic=ic, of=of, ts=ts: t.matmul(p_s[:, 0:TSZ], wS[ic][:, of * 128:(of + 1) * 128], hq[ic][:, ts * TSZ:(ts + 1) * TSZ], start=(ic == 0), stop=(ic == 7)))
                        seq.op("scalar", lambda s, of=of, ts=ts: s.activation(qT[of][:, ts * TSZ:(ts + 1) * TSZ], p_s[:, 0:TSZ], AF.Identity, bias=bqs[:, of:of + 1], scale=C18))
                for ic in range(8):
                    seq.dma("gpsimd", wS[ic][:, :], wkT[ic * 128:(ic + 1) * 128, :])
                for tc in range(NKT):
                    seq.dma("gpsimd", kt.rearrange("p (i k) -> p i k", i=8),
                            hT.rearrange("(i p) t -> p i t", p=128)[:, :, tc * 512:(tc + 1) * 512])
                    for of in range(8):
                        for ic in range(8):
                            seq.op("tensor", lambda t, ic=ic, of=of: t.matmul(p_s[:, 0:512], wS[ic][:, of * 128:(of + 1) * 128], kt[:, ic * 512:(ic + 1) * 512], start=(ic == 0), stop=(ic == 7)))
                        seq.op("scalar", lambda s, of=of: s.activation(kev[:, :], p_s[:, 0:512], AF.Identity, bias=bks[:, of:of + 1]))
                        seq.dma("sync", kTd[of, :, tc * 512:(tc + 1) * 512], kev[:, :])
                for sl in range(NSL):
                    nkc = sl + 1
                    seq.op("vector", lambda v: v.memset(mrun[:, :], -1e30))
                    seq.op("vector", lambda v: v.memset(zrun[:, :], 0.0))
                    for kc in range(nkc):
                        seq.dma("sync", kt.rearrange("p (i k) -> p i k", i=8),
                                kTd.rearrange("i p t -> p i t")[:, :, kc * 512:(kc + 1) * 512])
                        for hg in range(4):
                            for h4 in range(4):
                                h = hg * 4 + h4
                                of, half = h // 2, (h % 2) * 64
                                seq.op("tensor", lambda t, of=of, half=half, h4=h4, sl=sl: t.matmul(
                                    p_s[:, h4 * 512:(h4 + 1) * 512],
                                    qT[of][half:half + 64, sl * 128:(sl + 1) * 128],
                                    kt[half:half + 64, of * 512:(of + 1) * 512], start=True, stop=True))
                            if kc == nkc - 1:
                                seq.op("vector", lambda v: v.tensor_tensor(p_s[:, :], p_s[:, :], mskS[:, :], ALU.add))
                            seq.op("vector", lambda v: v.tensor_reduce(dsc[:, 0:4], p_s.rearrange("p (h k) -> p h k", h=4), AX.X, ALU.max))
                            seq.op("vector", lambda v, hg=hg: v.tensor_tensor(mnew[:, :], mrun[:, hg * 4:(hg + 1) * 4], dsc[:, :], ALU.max))
                            seq.op("vector", lambda v: v.tensor_scalar(nmnew[:, :], mnew[:, :], -1.0, None, ALU.mult))
                            for h4 in range(4):
                                seq.op("scalar", lambda s, h4=h4, hg=hg: s.activation(exs[:, :], p_s[:, h4 * 512:(h4 + 1) * 512], AF.Exp, bias=nmnew[:, h4:h4 + 1], accum_out=bs4[:, hg * 4 + h4:hg * 4 + h4 + 1]))
                            seq.op("vector", lambda v, hg=hg: v.tensor_tensor(dsc[:, :], mrun[:, hg * 4:(hg + 1) * 4], mnew[:, :], ALU.subtract))
                            seq.op("scalar", lambda s: s.activation(dsc[:, :], dsc[:, :], AF.Exp))
                            seq.op("vector", lambda v, hg=hg: v.tensor_tensor(zrun[:, hg * 4:(hg + 1) * 4], zrun[:, hg * 4:(hg + 1) * 4], dsc[:, :], ALU.mult))
                            seq.op("vector", lambda v, hg=hg: v.tensor_tensor(zrun[:, hg * 4:(hg + 1) * 4], zrun[:, hg * 4:(hg + 1) * 4], bs4[:, hg * 4:(hg + 1) * 4], ALU.add))
                            seq.op("vector", lambda v, hg=hg: v.tensor_copy(mrun[:, hg * 4:(hg + 1) * 4], mnew[:, :]))
                    seq.op("scalar", lambda s: s.activation(bias2[:, :], zrun[:, :], AF.Ln))
                    seq.op("vector", lambda v: v.tensor_tensor(bias2[:, :], bias2[:, :], mrun[:, :], ALU.add))
                    seq.op("vector", lambda v: v.tensor_scalar(bias2[:, :], bias2[:, :], LN16, -1.0, ALU.add, ALU.mult))
                    for kc in range(nkc):
                        seq.dma("sync", kt.rearrange("p (i k) -> p i k", i=8),
                                kTd.rearrange("i p t -> p i t")[:, :, kc * 512:(kc + 1) * 512])
                        for hg in range(4):
                            for h4 in range(4):
                                h = hg * 4 + h4
                                of, half = h // 2, (h % 2) * 64
                                seq.op("tensor", lambda t, of=of, half=half, h4=h4, sl=sl: t.matmul(
                                    p_s[:, h4 * 512:(h4 + 1) * 512],
                                    qT[of][half:half + 64, sl * 128:(sl + 1) * 128],
                                    kt[half:half + 64, of * 512:(of + 1) * 512], start=True, stop=True))
                            if kc == nkc - 1:
                                seq.op("vector", lambda v: v.tensor_tensor(p_s[:, :], p_s[:, :], mskS[:, :], ALU.add))
                            for h4 in range(4):
                                h = hg * 4 + h4
                                seq.op("scalar", lambda s, h4=h4, h=h: s.activation(exs[:, :], p_s[:, h4 * 512:(h4 + 1) * 512], AF.Exp, bias=bias2[:, h:h + 1]))
                                if h == 0:
                                    seq.op("vector", lambda v, kc=kc: v.tensor_copy(avg[:, kc * 512:(kc + 1) * 512], exs[:, :]))
                                else:
                                    seq.op("vector", lambda v, kc=kc: v.tensor_tensor(avg[:, kc * 512:(kc + 1) * 512], avg[:, kc * 512:(kc + 1) * 512], exs[:, :], ALU.add))
                    for kc in range(nkc):
                        seq.dma("sync", gmc[:, :], gum[sl * 128:(sl + 1) * 128, kc * 512:(kc + 1) * 512])
                        seq.op("vector", lambda v, kc=kc: v.tensor_tensor(exs[:, :], avg[:, kc * 512:(kc + 1) * 512], gmc[:, :], ALU.add))
                        seq.op("scalar", lambda s: s.activation(wbf[:, :], exs[:, :], AF.Exp, scale=C05, accum_out=sS[:, 0:1]))
                        if kc == 0:
                            seq.op("vector", lambda v: v.tensor_copy(sS[:, 1:2], sS[:, 0:1]))
                        else:
                            seq.op("vector", lambda v: v.tensor_tensor(sS[:, 1:2], sS[:, 1:2], sS[:, 0:1], ALU.add))
                        for sub in range(4):
                            seq.op("tensor", lambda t, sub=sub: t.transpose(p_t[:, :], wbf[:, sub * 128:(sub + 1) * 128], identb[:, :]))
                            seq.op("vector", lambda v, sub=sub: v.tensor_copy(wT[:, sub * 128:(sub + 1) * 128], p_t[:, :]))
                        seq.dma("gpsimd", hv.rearrange("p (s e) -> p s e", s=4),
                                htm.rearrange("(c s p) e -> p s e c", p=128, s=4)[:, :, :, kc].squeeze())
                        for sub in range(4):
                            for oh in range(2):
                                seq.op("tensor", lambda t, sub=sub, oh=oh, kc=kc, nkc=nkc: t.matmul(
                                    p_av[:, oh * 512:(oh + 1) * 512],
                                    wT[:, sub * 128:(sub + 1) * 128],
                                    hv[:, sub * 1024 + oh * 512:sub * 1024 + (oh + 1) * 512],
                                    start=(kc == 0 and sub == 0), stop=(kc == nkc - 1 and sub == 3)))
                    seq.op("vector", lambda v, sl=sl: v.tensor_tensor(sS[:, 1:2], sS[:, 1:2], rinv[:, sl:sl + 1], ALU.add))
                    seq.op("vector", lambda v: v.reciprocal(sS[:, 2:3], sS[:, 1:2]))
                    seq.op("vector", lambda v, sl=sl: v.tensor_scalar(sS[:, 3:4], rinv[:, sl:sl + 1], -1.0, 1.0, ALU.mult, ALU.add))
                    seq.op("vector", lambda v: v.tensor_scalar(oev[:, :], p_av[:, :], sS[:, 2:3], sS[:, 3:4], ALU.mult, ALU.mult))
                    seq.dma("sync", attnD[sl * 128:(sl + 1) * 128, :], oev[:, :])

        if 1 in phases: _phase1()
        def _phase2():
            with ExitStack() as es:
                sb = lambda n, s, d=F32: es.enter_context(nc.sbuf_tensor(n, s, d)).ap()
                xbf = [sb(f"b_xbf{i}", [128, TC], BF16) for i in range(8)]
                tmA = sb("b_tmA", [128, E])
                tmB = sb("b_tmB", [128, E])
                wte = sb("b_wte", [128, NTB * NEXP])
                sident = sb("b_sident", [128, 128])
                gat = sb("b_gat", [128, 64], BF16)
                gatf = sb("b_gatf", [128, 64])
                xtr = sb("b_xtr", [128, 128])
                wArena = sb("b_wArena", [128, 8 * FF0], BF16)
                wA = [wArena[:, i * FF0:(i + 1) * FF0] for i in range(8)]
                w2s = [wArena[:, i * E:(i + 1) * E] for i in range(22)]
                hA = [sb(f"b_hA{i}", [128, TC], BF16) for i in range(22)]
                cols = sb("b_cols", [128, 8])
                st1 = sb("b_st1", [128, 16])
                yev = sb("b_yev", [128, 512])
                p_tr = es.enter_context(nc.psum_tensor("b_ptr", [128, 512], F32)).ap()[:, 0:128]
                p_g = es.enter_context(nc.psum_tensor("b_pg", [128, 512], F32)).ap()[:, 0:16]
                p_a = es.enter_context(nc.psum_tensor("b_pa", [128, 512], F32)).ap()
                p_b = es.enter_context(nc.psum_tensor("b_pb", [128, 512], F32)).ap()

                seq.dma("sync", sident[:, :], identf)
                seq.dma("gpsimd", gat.rearrange("p (i e) -> p i e", i=8), gateT.rearrange("(i p) e -> p i e", p=128))
                seq.dma("sync", gatf.rearrange("p (i e) -> p i e", i=8), gateT.rearrange("(i p) e -> p i e", p=128))
                seq.op("vector", lambda v: v.memset(cols[:, 0:1], 1e-5))
                EPS = cols[:, 0:1]

                def ln_tb(tb, src_emitters, out_dram, transpose_to=None, extra=None, do_gating=False, out_dram2=None):
                    for fn in src_emitters:
                        fn()
                    if extra is not None:
                        extra()
                    b0 = seq.barrier()
                    seq.op("vector", lambda v: v.tensor_reduce(st1[:, 0:1], tmA[:, :], AX.XYZW, ALU.add), wait=b0)
                    seq.op("scalar", lambda s: s.activation(tmB[:, :], tmA[:, :], AF.Square), wait=b0)
                    seq.op("vector", lambda v: v.tensor_reduce(st1[:, 1:2], tmB[:, :], AX.XYZW, ALU.add))
                    seq.op("vector", lambda v: v.tensor_scalar(st1[:, 2:3], st1[:, 0:1], 1.0 / E, None, ALU.mult))
                    seq.op("vector", lambda v: v.tensor_scalar(st1[:, 3:4], st1[:, 1:2], 1.0 / E, None, ALU.mult))
                    seq.op("vector", lambda v: v.tensor_tensor(st1[:, 4:5], st1[:, 2:3], st1[:, 2:3], ALU.mult))
                    seq.op("vector", lambda v: v.tensor_tensor(st1[:, 5:6], st1[:, 3:4], st1[:, 4:5], ALU.subtract))
                    seq.op("scalar", lambda s: s.activation(st1[:, 6:7], st1[:, 5:6], AF.Sqrt, bias=EPS))
                    seq.op("vector", lambda v: v.reciprocal(st1[:, 7:8], st1[:, 6:7]))
                    seq.op("vector", lambda v: v.tensor_scalar(tmA[:, :], tmA[:, :], st1[:, 2:3], st1[:, 7:8], ALU.subtract, ALU.mult))
                    if out_dram is not None:
                        seq.dma("sync", out_dram[tb * 128:(tb + 1) * 128, :], tmA[:, :])
                    if out_dram2 is not None:
                        seq.dma("sync", out_dram2[tb * 128:(tb + 1) * 128, :], tmA[:, :])
                    if transpose_to is not None:
                        for fc in range(8):
                            seq.op("tensor", lambda t, fc=fc: t.transpose(p_tr[:, :], tmA[:, fc * 128:(fc + 1) * 128], sident[:, :]))
                            seq.op("vector", lambda v, fc=fc, tb=tb: v.tensor_copy(transpose_to[fc][:, tb * 128:(tb + 1) * 128], p_tr[:, :]))
                            if do_gating:
                                seq.op("vector", lambda v, fc=fc: v.tensor_copy(xtr[:, :], p_tr[:, :]))
                                seq.op("tensor", lambda t, fc=fc: t.matmul(p_g[:, 0:8], xtr[:, :], gatf[:, fc * 8:(fc + 1) * 8], start=(fc == 0), stop=(fc == 7)))

                def gating_post(tb):
                    seq.op("vector", lambda v: v.tensor_copy(st1[:, 0:8], p_g[:, 0:8]))
                    seq.op("vector", lambda v: v.reduce_max(st1[:, 8:9], st1[:, 0:8], AX.XYZW, negate=True))
                    seq.op("scalar", lambda s: s.activation(st1[:, 0:8], st1[:, 0:8], AF.Exp, bias=st1[:, 8:9], accum_out=st1[:, 9:10]))
                    seq.op("vector", lambda v: v.reciprocal(st1[:, 10:11], st1[:, 9:10]))
                    seq.op("vector", lambda v: v.tensor_scalar(st1[:, 0:8], st1[:, 0:8], st1[:, 10:11], None, ALU.mult))
                    seq.op("vector", lambda v: v.reduce_max(st1[:, 11:12], st1[:, 0:8], AX.XYZW))
                    seq.op("vector", lambda v: v.tensor_scalar(tmB[:, 0:8], st1[:, 0:8], st1[:, 11:12], None, ALU.is_ge))
                    seq.op("vector", lambda v: v.scalar_tensor_tensor(tmB[:, 8:16], tmB[:, 0:8], -2.0, st1[:, 0:8], ALU.mult, ALU.add))
                    seq.op("vector", lambda v: v.reduce_max(st1[:, 13:14], tmB[:, 8:16], AX.XYZW))
                    seq.op("vector", lambda v: v.tensor_scalar(tmB[:, 0:8], st1[:, 0:8], st1[:, 13:14], None, ALU.is_ge))
                    seq.op("vector", lambda v: v.tensor_tensor(st1[:, 14:15], st1[:, 11:12], st1[:, 13:14], ALU.add))
                    seq.op("vector", lambda v: v.reciprocal(st1[:, 15:16], st1[:, 14:15]))
                    seq.op("vector", lambda v: v.tensor_tensor(tmB[:, 0:8], tmB[:, 0:8], st1[:, 0:8], ALU.mult))
                    seq.op("vector", lambda v, tb=tb: v.tensor_scalar(wte[:, tb * 8:(tb + 1) * 8], tmB[:, 0:8], st1[:, 15:16], None, ALU.mult))

                for tb in range(NTB):
                    ln_tb(tb,
                          [lambda tb=tb: seq.dma("sync", tmA[:, :], x0[tb * 128:tb * 128 + 128, :]),
                           lambda tb=tb: seq.dma("sync", tmB[:, :], attnD[tb * 128:tb * 128 + 128, :])],
                          x1d, transpose_to=xbf, do_gating=True,
                          extra=lambda: seq.op("vector", lambda v: v.tensor_tensor(tmA[:, :], tmA[:, :], tmB[:, :], ALU.add)))
                    gating_post(tb)

                for e in range(NEXP):
                    for ic in range(8):
                        seq.dma("gpsimd", wA[ic], w1T[e, ic * 128:(ic + 1) * 128, :])
                    for hc in range(NF0):
                        for ts in range(NTS):
                            for ic in range(8):
                                seq.op("tensor", lambda t, ic=ic, hc=hc, ts=ts: t.matmul(p_a[:, 0:TSZ], wA[ic][:, hc * 128:(hc + 1) * 128], xbf[ic][:, ts * TSZ:(ts + 1) * TSZ], start=(ic == 0), stop=(ic == 7)))
                            seq.op("scalar", lambda s, hc=hc, ts=ts: s.activation(hA[hc][:, ts * TSZ:(ts + 1) * TSZ], p_a[:, 0:TSZ], AF.Silu))
                    for ic in range(8):
                        seq.dma("gpsimd", wA[ic], w3T[e, ic * 128:(ic + 1) * 128, :])
                    for hc in range(NF0):
                        for ts in range(NTS):
                            for ic in range(8):
                                seq.op("tensor", lambda t, ic=ic, hc=hc, ts=ts: t.matmul(p_a[:, 0:TSZ], wA[ic][:, hc * 128:(hc + 1) * 128], xbf[ic][:, ts * TSZ:(ts + 1) * TSZ], start=(ic == 0), stop=(ic == 7)))
                            seq.op("vector", lambda v, hc=hc, ts=ts: v.tensor_tensor(hA[hc][:, ts * TSZ:(ts + 1) * TSZ], hA[hc][:, ts * TSZ:(ts + 1) * TSZ], p_a[:, 0:TSZ], ALU.mult))
                    for hc in range(NF0):
                        seq.dma("gpsimd", w2s[hc], w2T[e, hc * 128:(hc + 1) * 128, :])
                    for tb in range(NTB):
                        for oh in range(2):
                            for hc in range(NF0):
                                seq.op("tensor", lambda t, hc=hc, tb=tb, oh=oh: t.matmul(p_b[:, 0:512], hA[hc][:, tb * 128:(tb + 1) * 128], w2s[hc][:, oh * 512:(oh + 1) * 512], start=(hc == 0), stop=(hc == NF0 - 1)))
                            seq.op("vector", lambda v, tb=tb, e=e: v.tensor_scalar(yev[:, 0:512], p_b[:, 0:512], wte[:, tb * 8 + e:tb * 8 + e + 1], None, ALU.mult))
                            if e == 0:
                                seq.dma("gpsimd", ydA[tb * 128:(tb + 1) * 128, oh * 512:(oh + 1) * 512], yev[:, 0:512])
                            else:
                                seq.dma("gpsimd", ydA[tb * 128:(tb + 1) * 128, oh * 512:(oh + 1) * 512], yev[:, 0:512], accum_op=ALU.add)

                for tb in range(NTB):
                    ln_tb(tb,
                          [lambda tb=tb: seq.dma("sync", tmA[:, :], x1d[tb * 128:tb * 128 + 128, :]),
                           lambda tb=tb: seq.dma("sync", tmB[:, :], ydA[tb * 128:tb * 128 + 128, :])],
                          x2d, transpose_to=None,
                          extra=lambda: seq.op("vector", lambda v: v.tensor_tensor(tmA[:, :], tmA[:, :], tmB[:, :], ALU.add)))

        if 2 in phases: _phase2()
        def _phase3():
            with ExitStack() as es:
                sb = lambda n, s, d=F32: es.enter_context(nc.sbuf_tensor(n, s, d)).ap()
                xbf = [sb(f"c_xbf{i}", [128, TC], BF16) for i in range(8)]
                tmA = sb("c_tmA", [128, E])
                tmB = sb("c_tmB", [128, E])
                grep = sb("c_grep", [128, E])
                brep = sb("c_brep", [128, E])
                growS = sb("c_growS", [1, E])
                browS = sb("c_browS", [1, E])
                sones = sb("c_sones", [1, 128])
                sident = sb("c_sident", [128, 128])
                wupS = [sb(f"c_wupS{i}", [128, 2 * DI], BF16) for i in range(8)]
                bupS = sb("c_bupS", [128, 8])
                st1 = sb("c_st1", [128, 16])
                cols = sb("c_cols", [128, 1])
                oev = sb("c_oev", [128, 512])
                p_tr = es.enter_context(nc.psum_tensor("c_ptr", [128, 512], F32)).ap()[:, 0:128]
                p_a = es.enter_context(nc.psum_tensor("c_pa", [128, 512], F32)).ap()
                p_r = es.enter_context(nc.psum_tensor("c_pr", [128, 512], F32)).ap()

                seq.dma("sync", sident[:, :], identf)
                seq.dma("sync", growS[:, :], grow)
                seq.dma("sync", browS[:, :], brow)
                seq.dma("sync", bupS[:, :], bupc)
                for ic in range(8):
                    seq.dma("gpsimd", wupS[ic][:, :], wup[ic * 128:(ic + 1) * 128, :])
                seq.op("vector", lambda v: v.memset(cols[:, 0:1], 1e-5))
                seq.op("vector", lambda v: v.memset(sones[:, :], 1.0))
                for half in range(2):
                    seq.op("tensor", lambda t, half=half: t.matmul(p_r[:, 0:512], sones[0:1, :], growS[0:1, half * 512:(half + 1) * 512], start=True, stop=True))
                    seq.op("vector", lambda v, half=half: v.tensor_copy(grep[:, half * 512:(half + 1) * 512], p_r[:, 0:512]))
                    seq.op("tensor", lambda t, half=half: t.matmul(p_r[:, 0:512], sones[0:1, :], browS[0:1, half * 512:(half + 1) * 512], start=True, stop=True))
                    seq.op("vector", lambda v, half=half: v.tensor_copy(brep[:, half * 512:(half + 1) * 512], p_r[:, 0:512]))
                for tb in range(NTB):
                    seq.dma("sync", tmA[:, :], x2d[tb * 128:tb * 128 + 128, :])
                    seq.op("vector", lambda v: v.tensor_reduce(st1[:, 0:1], tmA[:, :], AX.XYZW, ALU.add))
                    seq.op("scalar", lambda s: s.activation(tmB[:, :], tmA[:, :], AF.Square))
                    seq.op("vector", lambda v: v.tensor_reduce(st1[:, 1:2], tmB[:, :], AX.XYZW, ALU.add))
                    seq.op("vector", lambda v: v.tensor_scalar(st1[:, 2:3], st1[:, 0:1], 1.0 / E, None, ALU.mult))
                    seq.op("vector", lambda v: v.tensor_scalar(st1[:, 3:4], st1[:, 1:2], 1.0 / E, None, ALU.mult))
                    seq.op("vector", lambda v: v.tensor_tensor(st1[:, 4:5], st1[:, 2:3], st1[:, 2:3], ALU.mult))
                    seq.op("vector", lambda v: v.tensor_tensor(st1[:, 5:6], st1[:, 3:4], st1[:, 4:5], ALU.subtract))
                    seq.op("scalar", lambda s: s.activation(st1[:, 6:7], st1[:, 5:6], AF.Sqrt, bias=cols[:, 0:1]))
                    seq.op("vector", lambda v: v.reciprocal(st1[:, 7:8], st1[:, 6:7]))
                    seq.op("vector", lambda v: v.tensor_scalar(tmA[:, :], tmA[:, :], st1[:, 2:3], st1[:, 7:8], ALU.subtract, ALU.mult))
                    seq.op("vector", lambda v: v.tensor_tensor(tmA[:, :], tmA[:, :], grep[:, :], ALU.mult))
                    seq.op("vector", lambda v: v.tensor_tensor(tmA[:, :], tmA[:, :], brep[:, :], ALU.add))
                    for fc in range(8):
                        seq.op("tensor", lambda t, fc=fc: t.transpose(p_tr[:, :], tmA[:, fc * 128:(fc + 1) * 128], sident[:, :]))
                        seq.op("vector", lambda v, fc=fc, tb=tb: v.tensor_copy(xbf[fc][:, tb * 128:(tb + 1) * 128], p_tr[:, :]))
                for of in range(8):
                    for ts in range(NTS):
                        for ic in range(8):
                            seq.op("tensor", lambda t, ic=ic, of=of, ts=ts: t.matmul(p_a[:, 0:TSZ], wupS[ic][:, of * 128:(of + 1) * 128], xbf[ic][:, ts * TSZ:(ts + 1) * TSZ], start=(ic == 0), stop=(ic == 7)))
                        seq.op("scalar", lambda s, of=of: s.activation(oev[:, 0:TSZ], p_a[:, 0:TSZ], AF.Identity, bias=bupS[:, of:of + 1]))
                        dst = leftIn.ap() if of < 4 else rightD
                        ofr = (of % 4) * 128
                        seq.dma("sync", dst[ofr:ofr + 128, ts * TSZ:(ts + 1) * TSZ], oev[:, 0:TSZ])

        if 3 in phases: _phase3()
        if with_cc: seq.cc(lambda g: g.collective_compute(
            "AllGather", ALU.bypass, replica_groups=GROUPS,
            ins=[leftIn.ap().opt()], outs=[leftAG.ap().opt()]))

        def _phase4():
            with ExitStack() as es:
                sb = lambda n, s, d=F32: es.enter_context(nc.sbuf_tensor(n, s, d)).ap()
                lt = [sb(f"d_lt{i}", [128, TCH + 2]) for i in range(4)]
                uf = [sb(f"d_uf{i}", [128, TCH]) for i in range(4)]
                ub = [sb(f"d_ub{i}", [128, TCH], BF16) for i in range(4)]
                uown = sb("d_uown", [128, TCH])
                wdS = [sb(f"d_wdS{i}", [128, 128], BF16) for i in range(4)]
                wbS = [sb(f"d_wbS{i}", [128, NS], BF16) for i in range(4)]
                wcS = [sb(f"d_wcS{i}", [128, NS], BF16) for i in range(4)]
                selS = sb("d_selS", [128, 4 * 128])
                acol = sb("d_acol", [128, NS])
                dcolS = sb("d_dcolS", [128, 1])
                bdcS = sb("d_bdcS", [128, 1])
                cwS = sb("d_cwS", [128, 12])
                cbS = sb("d_cbS", [128, 4])
                patDS = sb("d_patDS", [128, 4 * 128])
                patBS = sb("d_patBS", [128, 32 * 128], BF16)
                patYS = sb("d_patYS", [128, 4 * 128], BF16)
                deltaS = sb("d_deltaS", [128, TCH])
                wSt = sb("d_wSt", [128, TCH])
                BTb = [sb(f"d_BTb{i}", [128, TCH], BF16) for i in range(2)]
                CTb = [sb(f"d_CTb{i}", [128, TCH], BF16) for i in range(2)]
                dRep = [sb(f"d_dRep{i}", [128, TCH]) for i in range(4)]
                wRep = [sb(f"d_wRep{i}", [128, TCH]) for i in range(4)]
                dAS = sb("d_dAS", [128, TCH])
                bS = sb("d_bS", [128, TCH])
                hS = sb("d_hS", [128, TCH])
                ymulb = sb("d_ymulb", [128, TCH], BF16)
                hstate = sb("d_hstate", [128, NS])
                ysS = sb("d_ysS", [128, TCH])
                sc1 = sb("d_sc1", [128, TCH])
                sc2 = sb("d_sc2", [128, TCH])
                consts = sb("d_consts", [128, 4])
                p_d = es.enter_context(nc.psum_tensor("d_pd", [128, 512], F32)).ap()
                p_B = es.enter_context(nc.psum_tensor("d_pB", [128, TCH], F32)).ap()
                p_C = es.enter_context(nc.psum_tensor("d_pC", [128, TCH], F32)).ap()
                p_Y = es.enter_context(nc.psum_tensor("d_pY", [128, TCH], F32)).ap()

                seq.dma("sync", cwS.rearrange("p (f k) -> p f k", f=4), cwc)
                seq.dma("sync", cbS[:, :], cbc)
                seq.dma("sync", bdcS[:, :], bdc)
                seq.dma("sync", dcolS[:, :], dcol)
                seq.dma("sync", patDS[:, :], patD)
                seq.dma("sync", selS[:, :], selpat)
                seq.dma("gpsimd", patBS[:, :], patB)
                seq.dma("gpsimd", patYS[:, :], patY)
                for ic in range(4):
                    seq.dma("gpsimd", wdS[ic][:, :], wdel[ic * 128:(ic + 1) * 128, :])
                    seq.dma("gpsimd", wbS[ic][:, :], wB[ic * 128:(ic + 1) * 128, :])
                    seq.dma("gpsimd", wcS[ic][:, :], wC[ic * 128:(ic + 1) * 128, :])
                seq.op("vector", lambda v: v.memset(consts[:, 0:1], 1.0))
                seq.op("vector", lambda v: v.memset(consts[:, 1:2], -1.0))
                seq.op("vector", lambda v: v.memset(hstate[:, :], 0.0))
                C_ONE, C_M1 = consts[:, 0:1], consts[:, 1:2]
                seq.dma("sync", acol[:, :], aarr)
                seq.op("scalar", lambda s: s.activation(acol[:, :], acol[:, :], AF.Exp))
                seq.op("vector", lambda v: v.tensor_scalar(acol[:, :], acol[:, :], -1.0, None, ALU.mult))

                lag = leftAG.ap()
                for k in range(NCH):
                    ks = k * TCH
                    for fc in range(4):
                        if k == 0:
                            seq.op("vector", lambda v, fc=fc: v.memset(lt[fc][:, 0:2], 0.0))
                        else:
                            seq.op("vector", lambda v, fc=fc: v.tensor_copy(lt[fc][:, 0:2], lt[fc][:, TCH:TCH + 2]))
                        for r_ in range(4):
                            src = bass.AP(leftAG, (r_ * 512 + fc * 128) * TC + (k * (TCH // 512)) * 128,
                                          [[TC, 128], [128, TCH // 512], [1, 128]])
                            dst = lt[fc][:, 2:TCH + 2].rearrange("p (a rr q) -> p a rr q", a=TCH // 512, rr=4)[:, :, r_:r_ + 1, :].squeeze(2)
                            seq.dma("sync", dst, src)
                        seq.op("vector", lambda v, fc=fc: v.tensor_scalar(sc1[:, :], lt[fc][:, 2:TCH + 2], cwS[:, fc * 3 + 2:fc * 3 + 3], cbS[:, fc:fc + 1], ALU.mult, ALU.add))
                        seq.op("vector", lambda v, fc=fc: v.scalar_tensor_tensor(sc1[:, :], lt[fc][:, 1:TCH + 1], cwS[:, fc * 3 + 1:fc * 3 + 2], sc1[:, :], ALU.mult, ALU.add))
                        seq.op("vector", lambda v, fc=fc: v.scalar_tensor_tensor(sc1[:, :], lt[fc][:, 0:TCH], cwS[:, fc * 3 + 0:fc * 3 + 1], sc1[:, :], ALU.mult, ALU.add))
                        seq.op("scalar", lambda s, fc=fc: s.activation(uf[fc][:, :], sc1[:, :], AF.Silu))
                        seq.op("vector", lambda v, fc=fc: v.tensor_copy(ub[fc][:, :], uf[fc][:, :]))
                    for sub in range(NS2):
                        for ic in range(4):
                            seq.op("tensor", lambda t, ic=ic, sub=sub: t.matmul(p_d[:, 0:512], selS[:, ic * 128:(ic + 1) * 128], uf[ic][:, sub * 512:(sub + 1) * 512], start=(ic == 0), stop=(ic == 3)))
                        seq.op("vector", lambda v, sub=sub: v.tensor_copy(uown[:, sub * 512:(sub + 1) * 512], p_d[:, 0:512]))
                    for sub in range(NS2):
                        for ic in range(4):
                            seq.op("tensor", lambda t, ic=ic, sub=sub: t.matmul(p_d[:, 0:512], wdS[ic][:, :], ub[ic][:, sub * 512:(sub + 1) * 512], start=(ic == 0), stop=(ic == 3)))
                        seq.op("scalar", lambda s, sub=sub: s.activation(sc1[:, 0:512], p_d[:, 0:512], AF.Identity, bias=bdcS[:, 0:1]))
                        seq.op("scalar", lambda s: s.activation(sc2[:, 0:512], sc1[:, 0:512], AF.Abs))
                        seq.op("scalar", lambda s: s.activation(sc2[:, 0:512], sc2[:, 0:512], AF.Exp, scale=C_M1))
                        seq.op("scalar", lambda s: s.activation(sc2[:, 0:512], sc2[:, 0:512], AF.Ln, bias=C_ONE))
                        seq.op("vector", lambda v: v.tensor_scalar(sc1[:, 0:512], sc1[:, 0:512], 0.0, None, ALU.max))
                        seq.op("vector", lambda v, sub=sub: v.tensor_tensor(deltaS[:, sub * 512:(sub + 1) * 512], sc1[:, 0:512], sc2[:, 0:512], ALU.add))
                    seq.op("vector", lambda v: v.tensor_tensor(wSt[:, :], deltaS[:, :], uown[:, :], ALU.mult))
                    for nh in range(2):
                        for sub in range(NS2):
                            for ic in range(4):
                                seq.op("tensor", lambda t, ic=ic, nh=nh, sub=sub: t.matmul(p_d[:, 0:512], wbS[ic][:, nh * 128:(nh + 1) * 128], ub[ic][:, sub * 512:(sub + 1) * 512], start=(ic == 0), stop=(ic == 3)))
                            seq.op("vector", lambda v, nh=nh, sub=sub: v.tensor_copy(BTb[nh][:, sub * 512:(sub + 1) * 512], p_d[:, 0:512]))
                            for ic in range(4):
                                seq.op("tensor", lambda t, ic=ic, nh=nh, sub=sub: t.matmul(p_d[:, 0:512], wcS[ic][:, nh * 128:(nh + 1) * 128], ub[ic][:, sub * 512:(sub + 1) * 512], start=(ic == 0), stop=(ic == 3)))
                            seq.op("vector", lambda v, nh=nh, sub=sub: v.tensor_copy(CTb[nh][:, sub * 512:(sub + 1) * 512], p_d[:, 0:512]))
                    for dg in range(4):
                        for sub in range(NS2):
                            seq.op("tensor", lambda t, dg=dg, sub=sub: t.matmul(p_d[:, 0:512], patDS[:, dg * 128:(dg + 1) * 128], deltaS[:, sub * 512:(sub + 1) * 512], start=True, stop=True))
                            seq.op("vector", lambda v, dg=dg, sub=sub: v.tensor_copy(dRep[dg][:, sub * 512:(sub + 1) * 512], p_d[:, 0:512]))
                            seq.op("tensor", lambda t, dg=dg, sub=sub: t.matmul(p_d[:, 0:512], patDS[:, dg * 128:(dg + 1) * 128], wSt[:, sub * 512:(sub + 1) * 512], start=True, stop=True))
                            seq.op("vector", lambda v, dg=dg, sub=sub: v.tensor_copy(wRep[dg][:, sub * 512:(sub + 1) * 512], p_d[:, 0:512]))
                    for ng in range(64):
                        j, nh = ng % 32, ng // 32
                        for sub in range(NS2):
                            seq.op("tensor", lambda t, j=j, nh=nh, sub=sub: t.matmul(p_B[:, sub * 512:(sub + 1) * 512], patBS[:, j * 128:(j + 1) * 128], BTb[nh][:, sub * 512:(sub + 1) * 512], start=True, stop=True))
                            seq.op("tensor", lambda t, j=j, nh=nh, sub=sub: t.matmul(p_C[:, sub * 512:(sub + 1) * 512], patBS[:, j * 128:(j + 1) * 128], CTb[nh][:, sub * 512:(sub + 1) * 512], start=True, stop=True))
                        for dg in range(4):
                            g = ng * 4 + dg
                            seq.op("scalar", lambda s, dg=dg, ng=ng: s.activation(dAS[:, :], dRep[dg][:, :], AF.Exp, scale=acol[:, ng * 4 + dg:ng * 4 + dg + 1]))
                            seq.op("vector", lambda v, dg=dg: v.tensor_tensor(bS[:, :], wRep[dg][:, :], p_B[:, :], ALU.mult))
                            seq.op("vector", lambda v, g=g: v.tensor_tensor_scan(hS[:, :], dAS[:, :], bS[:, :], hstate[:, g:g + 1], ALU.mult, ALU.add))
                            seq.op("vector", lambda v, g=g: v.tensor_copy(hstate[:, g:g + 1], hS[:, TCH - 1:TCH]))
                            seq.op("vector", lambda v: v.tensor_tensor(ymulb[:, :], hS[:, :], p_C[:, :], ALU.mult))
                            for sub in range(NS2):
                                seq.op("tensor", lambda t, dg=dg, ng=ng, sub=sub: t.matmul(p_Y[:, sub * 512:(sub + 1) * 512], patYS[:, dg * 128:(dg + 1) * 128], ymulb[:, sub * 512:(sub + 1) * 512], start=(ng == 0 and dg == 0), stop=(ng == 63 and dg == 3)))
                    for sub in range(NS2):
                        seq.op("vector", lambda v, sub=sub: v.scalar_tensor_tensor(ysS[:, sub * 512:(sub + 1) * 512], uown[:, sub * 512:(sub + 1) * 512], dcolS[:, 0:1], p_Y[:, sub * 512:(sub + 1) * 512], ALU.mult, ALU.add))
                    seq.dma("sync", ysIn.ap()[:, ks:ks + TCH], ysS[:, :])

        if 4 in phases: _phase4()
        if with_cc: seq.cc(lambda g: g.collective_compute(
            "AllGather", ALU.bypass, replica_groups=GROUPS,
            ins=[ysIn.ap().opt()], outs=[ysAG.ap().opt()]))

        def _phase5():
            with ExitStack() as es:
                sb = lambda n, s, d=F32: es.enter_context(nc.sbuf_tensor(n, s, d)).ap()
                arena1 = sb("e_arena1", [128, NHC * 1024], BF16)
                arena2 = sb("e_arena2", [128, 8 * FF1], BF16)
                x3bf = [sb(f"e_x3bf{i}", [128, TC], BF16) for i in range(8)]
                tmA = sb("e_tmA", [128, E])
                tmB = sb("e_tmB", [128, E])
                tmC = sb("e_tmC", [128, E])
                xtr = sb("e_xtr", [128, 128])
                wredf = sb("e_wredf", [128, 8 * GATE])
                wgS = sb("e_wgS", [8, GATE])
                wgT = sb("e_wgT", [16, 8])
                redT = sb("e_redT", [16, 128])
                bdr = sb("e_bdr", [1, E], BF16)
                b2S = sb("e_b2S", [1, E], BF16)
                b1S = sb("e_b1S", [128, NHC])
                onesb = sb("e_onesb", [1, 128], BF16)
                sident = sb("e_sident", [128, 128])
                st1 = sb("e_st1", [128, 16])
                sel8 = sb("e_sel8", [128, 8])
                wsel = sb("e_wsel", [128, NTB * 8])
                cols = sb("e_cols", [128, 4])
                yev = sb("e_yev", [128, 512])
                p_tr = es.enter_context(nc.psum_tensor("e_ptr", [128, 512], F32)).ap()[:, 0:128]
                p_a = es.enter_context(nc.psum_tensor("e_pa", [128, 512], F32)).ap()
                p_b = es.enter_context(nc.psum_tensor("e_pb", [128, 512], F32)).ap()
                p_g = es.enter_context(nc.psum_tensor("e_pg", [128, 512], F32)).ap()[:, 0:16]
                p_r = es.enter_context(nc.psum_tensor("e_pr", [128, 512], F32)).ap()[:, 0:16]

                a1f = arena1.bitcast(F32)
                ysS = [a1f[:, i * TC:(i + 1) * TC] for i in range(4)]
                srS = [a1f[:, (4 + i) * TC:(5 + i) * TC] for i in range(4)]
                hG = [arena1[:, i * 1024:(i + 1) * 1024] for i in range(NHC)]
                gdS = [arena2[:, i * TC:(i + 1) * TC] for i in range(4)]
                wdnS = [arena2[:, 4 * TC + i * E:4 * TC + (i + 1) * E] for i in range(4)]
                w1S = [arena2[:, i * FF1:(i + 1) * FF1] for i in range(8)]
                w2S = [arena2[:, i * 1024:(i + 1) * 1024] for i in range(NHC)]

                seq.dma("sync", sident[:, :], identf)
                seq.dma("sync", wredf.rearrange("p (i g) -> p i g", i=8), wred.rearrange("(i p) g -> p i g", p=128))
                seq.dma("sync", wgS[:, :], wg)
                seq.dma("gpsimd", bdr[:, :], bdrow)
                seq.op("vector", lambda v: v.memset(cols[:, 0:1], 1e-5))
                seq.op("vector", lambda v: v.memset(cols[:, 1:2], 0.0))
                seq.op("vector", lambda v: v.memset(onesb[:, :], 1.0))
                EPS, ZERO = cols[:, 0:1], cols[:, 1:2]
                seq.op("scalar", lambda s: s.activation(st1[0:8, 0:GATE], wgS[:, :], AF.Square))
                seq.op("vector", lambda v: v.tensor_reduce(sel8[0:8, 0:1], st1[0:8, 0:GATE], AX.XYZW, ALU.add))
                seq.op("scalar", lambda s: s.activation(sel8[0:8, 1:2], sel8[0:8, 0:1], AF.Sqrt, bias=ZERO[0:8, :]))
                seq.op("vector", lambda v: v.tensor_scalar(sel8[0:8, 1:2], sel8[0:8, 1:2], 1e-4, None, ALU.max))
                seq.op("vector", lambda v: v.reciprocal(sel8[0:8, 2:3], sel8[0:8, 1:2]))
                seq.op("vector", lambda v: v.tensor_scalar(wgS[:, :], wgS[:, :], sel8[0:8, 2:3], None, ALU.mult))
                seq.op("tensor", lambda t: t.transpose(p_g[0:16, 0:8], wgS[:, :], sident[0:8, 0:8]))
                seq.op("vector", lambda v: v.tensor_copy(wgT[:, :], p_g[0:16, 0:8]))
                for i in range(4):
                    def _ys(g, i=i):
                        g.reg_alu(rtmp, rj, i * 128 * T, ALU.add)
                        src = bass.AP(ysAG, rtmp, [[T, 128], [512, NSL], [1, 128]])
                        return g.dma_start(out=ysS[i].rearrange("p (a q) -> p a q", a=NSL), in_=src)
                    seq.op("gpsimd", _ys, inc=16)
                    seq.dma("sync", srS[i], rightD[i * 128:(i + 1) * 128, :])
                    seq.op("scalar", lambda s, i=i: s.activation(srS[i], srS[i], AF.Silu))
                    seq.op("vector", lambda v, i=i: v.tensor_tensor(gdS[i], ysS[i], srS[i], ALU.mult))
                    seq.dma("gpsimd", wdnS[i], wdown[i * 128:(i + 1) * 128, :])
                for tb in range(NTB):
                    seq.dma("sync", tmA[:, :], x2d[tb * 128:tb * 128 + 128, :])
                    for oh in range(2):
                        for ic in range(4):
                            seq.op("tensor", lambda t, ic=ic, oh=oh, tb=tb: t.matmul(p_a[:, 0:512], gdS[ic][:, tb * 128:(tb + 1) * 128], wdnS[ic][:, oh * 512:(oh + 1) * 512], start=(ic == 0), stop=False))
                        seq.op("tensor", lambda t, oh=oh: t.matmul(p_a[:, 0:512], onesb[0:1, :], bdr[0:1, oh * 512:(oh + 1) * 512], start=False, stop=True))
                        seq.op("vector", lambda v, oh=oh: v.scalar_tensor_tensor(tmB[:, oh * 512:(oh + 1) * 512], tmA[:, oh * 512:(oh + 1) * 512], 2.0, p_a[:, 0:512], ALU.mult, ALU.add))
                    seq.op("vector", lambda v: v.tensor_reduce(st1[:, 0:1], tmB[:, :], AX.XYZW, ALU.add))
                    seq.op("scalar", lambda s: s.activation(tmC[:, :], tmB[:, :], AF.Square))
                    seq.op("vector", lambda v: v.tensor_reduce(st1[:, 1:2], tmC[:, :], AX.XYZW, ALU.add))
                    seq.op("vector", lambda v: v.tensor_scalar(st1[:, 2:3], st1[:, 0:1], 1.0 / E, None, ALU.mult))
                    seq.op("vector", lambda v: v.tensor_scalar(st1[:, 3:4], st1[:, 1:2], 1.0 / E, None, ALU.mult))
                    seq.op("vector", lambda v: v.tensor_tensor(st1[:, 4:5], st1[:, 2:3], st1[:, 2:3], ALU.mult))
                    seq.op("vector", lambda v: v.tensor_tensor(st1[:, 5:6], st1[:, 3:4], st1[:, 4:5], ALU.subtract))
                    seq.op("scalar", lambda s: s.activation(st1[:, 6:7], st1[:, 5:6], AF.Sqrt, bias=EPS))
                    seq.op("vector", lambda v: v.reciprocal(st1[:, 7:8], st1[:, 6:7]))
                    seq.op("vector", lambda v: v.tensor_scalar(tmB[:, :], tmB[:, :], st1[:, 2:3], st1[:, 7:8], ALU.subtract, ALU.mult))
                    seq.dma("sync", x3d[tb * 128:(tb + 1) * 128, :], tmB[:, :])
                    for fc in range(8):
                        seq.op("tensor", lambda t, fc=fc: t.transpose(p_tr[:, :], tmB[:, fc * 128:(fc + 1) * 128], sident[:, :]))
                        seq.op("vector", lambda v, fc=fc, tb=tb: v.tensor_copy(x3bf[fc][:, tb * 128:(tb + 1) * 128], p_tr[:, :]))
                        seq.op("vector", lambda v: v.tensor_copy(xtr[:, :], p_tr[:, :]))
                        seq.op("tensor", lambda t, fc=fc: t.matmul(p_r[:, 0:GATE], xtr[:, :], wredf[:, fc * GATE:(fc + 1) * GATE], start=(fc == 0), stop=(fc == 7)))
                    seq.op("vector", lambda v: v.tensor_copy(tmC[:, 0:GATE], p_r[:, 0:GATE]))
                    seq.op("tensor", lambda t: t.transpose(p_tr[0:16, 0:128], tmC[:, 0:GATE], sident[:, :]))
                    seq.op("vector", lambda v: v.tensor_copy(redT[:, :], p_tr[0:16, 0:128]))
                    seq.op("tensor", lambda t: t.matmul(p_g[:, 0:8], redT[:, :], wgT[:, :], start=True, stop=True))
                    seq.op("vector", lambda v: v.tensor_copy(st1[:, 0:8], p_g[:, 0:8]))
                    seq.op("vector", lambda v: v.reduce_max(st1[:, 8:9], st1[:, 0:8], AX.XYZW, negate=True))
                    seq.op("scalar", lambda s: s.activation(tmC[:, 0:8], st1[:, 0:8], AF.Exp, bias=st1[:, 8:9], accum_out=st1[:, 9:10]))
                    seq.op("vector", lambda v: v.reciprocal(st1[:, 10:11], st1[:, 9:10]))
                    seq.op("vector", lambda v: v.tensor_scalar(st1[:, 11:12], st1[:, 8:9], -1.0, None, ALU.mult))
                    seq.op("vector", lambda v: v.tensor_scalar(sel8[:, 0:8], st1[:, 0:8], st1[:, 11:12], None, ALU.is_ge))
                    seq.op("vector", lambda v: v.memset(st1[:, 12:13], 1.0))
                    for e0 in range(8):
                        seq.op("vector", lambda v, e0=e0: v.tensor_tensor(sel8[:, e0:e0 + 1], sel8[:, e0:e0 + 1], st1[:, 12:13], ALU.mult))
                        if e0 < 7:
                            seq.op("vector", lambda v, e0=e0: v.tensor_scalar(st1[:, 13:14], sel8[:, e0:e0 + 1], -1.0, 1.0, ALU.mult, ALU.add))
                            seq.op("vector", lambda v: v.tensor_tensor(st1[:, 12:13], st1[:, 12:13], st1[:, 13:14], ALU.mult))
                    seq.op("vector", lambda v, tb=tb: v.tensor_scalar(wsel[:, tb * 8:(tb + 1) * 8], sel8[:, 0:8], st1[:, 10:11], None, ALU.mult))
                for e in range(NEXP):
                    seq.dma("sync", b1S[:, :], b1c[e])
                    seq.dma("gpsimd", b2S[:, :], b2r[e])
                    for tcg in range(NTCG):
                        t0 = tcg * TSZ2
                        for ic in range(8):
                            seq.dma("gpsimd", w1S[ic], w1f[e, ic * 128:(ic + 1) * 128, :])
                        for hc in range(NHC):
                            for s5 in range(NS5):
                                for ic in range(8):
                                    seq.op("tensor", lambda t, ic=ic, hc=hc, s5=s5, t0=t0: t.matmul(p_a[:, 0:S5], w1S[ic][:, hc * 128:(hc + 1) * 128], x3bf[ic][:, t0 + s5 * 512:t0 + s5 * 512 + S5], start=(ic == 0), stop=(ic == 7)))
                                seq.op("scalar", lambda s, hc=hc, s5=s5: s.activation(hG[hc][:, s5 * 512:s5 * 512 + S5], p_a[:, 0:S5], AF.Gelu, bias=b1S[:, hc:hc + 1]))
                        for hc in range(NHC):
                            seq.dma("gpsimd", w2S[hc], w2f[e, hc * 128:(hc + 1) * 128, :])
                        for tb0 in range(TSZ2 // 128):
                            tb = tcg * (TSZ2 // 128) + tb0
                            for oh in range(2):
                                for hc in range(NHC):
                                    seq.op("tensor", lambda t, hc=hc, tb0=tb0, oh=oh: t.matmul(p_b[:, 0:512], hG[hc][:, tb0 * 128:(tb0 + 1) * 128], w2S[hc][:, oh * 512:(oh + 1) * 512], start=(hc == 0), stop=False))
                                seq.op("tensor", lambda t, oh=oh: t.matmul(p_b[:, 0:512], onesb[0:1, :], b2S[0:1, oh * 512:(oh + 1) * 512], start=False, stop=True))
                                seq.op("vector", lambda v, tb=tb, e=e: v.tensor_scalar(yev[:, 0:512], p_b[:, 0:512], wsel[:, tb * 8 + e:tb * 8 + e + 1], None, ALU.mult))
                                if e == 0:
                                    seq.dma("gpsimd", ydB[tb * 128:(tb + 1) * 128, oh * 512:(oh + 1) * 512], yev[:, 0:512])
                                else:
                                    seq.dma("gpsimd", ydB[tb * 128:(tb + 1) * 128, oh * 512:(oh + 1) * 512], yev[:, 0:512], accum_op=ALU.add)
                for tb in range(NTB):
                    seq.dma("sync", tmA[:, :], x3d[tb * 128:tb * 128 + 128, :])
                    seq.dma("sync", tmB[:, :], ydB[tb * 128:tb * 128 + 128, :])
                    seq.op("vector", lambda v: v.tensor_tensor(tmA[:, :], tmA[:, :], tmB[:, :], ALU.add))
                    seq.op("vector", lambda v: v.tensor_reduce(st1[:, 0:1], tmA[:, :], AX.XYZW, ALU.add))
                    seq.op("scalar", lambda s: s.activation(tmC[:, :], tmA[:, :], AF.Square))
                    seq.op("vector", lambda v: v.tensor_reduce(st1[:, 1:2], tmC[:, :], AX.XYZW, ALU.add))
                    seq.op("vector", lambda v: v.tensor_scalar(st1[:, 2:3], st1[:, 0:1], 1.0 / E, None, ALU.mult))
                    seq.op("vector", lambda v: v.tensor_scalar(st1[:, 3:4], st1[:, 1:2], 1.0 / E, None, ALU.mult))
                    seq.op("vector", lambda v: v.tensor_tensor(st1[:, 4:5], st1[:, 2:3], st1[:, 2:3], ALU.mult))
                    seq.op("vector", lambda v: v.tensor_tensor(st1[:, 5:6], st1[:, 3:4], st1[:, 4:5], ALU.subtract))
                    seq.op("scalar", lambda s: s.activation(st1[:, 6:7], st1[:, 5:6], AF.Sqrt, bias=EPS))
                    seq.op("vector", lambda v: v.reciprocal(st1[:, 7:8], st1[:, 6:7]))
                    seq.op("vector", lambda v: v.tensor_scalar(tmA[:, :], tmA[:, :], st1[:, 2:3], st1[:, 7:8], ALU.subtract, ALU.mult))
                    seq.dma("sync", outT[tb * 128:(tb + 1) * 128, :], tmA[:, :])
        if 5 in phases: _phase5()
        if debug:
            seq.dma("sync", dbgL.ap()[:, :], leftIn.ap()[:, :])
            seq.dma("sync", dbgLA.ap()[:, :], leftAG.ap()[:, :])
            seq.dma("sync", dbgY.ap()[:, :], ysIn.ap()[:, :])
            seq.dma("sync", dbgYA.ap()[:, :], ysAG.ap()[:, :])
        seq.build(block)
    _es_sems.close()
    return nc, seq


def host_fused_inputs(T, inp, gumbel):
    m1 = host_l1_inputs(T, inp, gumbel)
    m2 = host_l2_inputs(T, inp, np.zeros((2, T, 1), np.float32))
    m3 = host_l3_inputs(T, inp, inp["L0_attn_hidden_states"])
    maps = []
    cw = inp["L1_attn_conv_w"][:, 0, :]
    cb = inp["L1_attn_conv_b"]
    Wd = inp["L1_attn_W_delta"]; bd = inp["L1_attn_b_delta"]
    WB_ = inp["L1_attn_W_B"]; WC_ = inp["L1_attn_W_C"]
    Alog = inp["L1_attn_A_log"]; Dv = inp["L1_attn_D"]
    patDh = np.zeros((128, 4, 128), np.float32)
    for dg in range(4):
        for p in range(128):
            patDh[dg * 32 + p // 4, dg, p] = 1.0
    patBh = np.zeros((128, 32, 128), np.float32)
    for j in range(32):
        for p in range(128):
            patBh[j * 4 + p % 4, j, p] = 1.0
    patYh = np.zeros((128, 4, 128), np.float32)
    for dg in range(4):
        for p in range(128):
            patYh[p, dg, dg * 32 + p // 4] = 1.0
    b1cols = np.ascontiguousarray(inp["L1_ffn_b1"].reshape(NEXP, FF1 // 128, 128).transpose(0, 2, 1))
    for c in range(8):
        b, ds = c // 4, c % 4
        dlo = ds * 128
        p = np.arange(128)
        i_, nsub = p // 4, p % 4
        aarr = np.zeros((128, 256), np.float32)
        for dg in range(4):
            for ng in range(64):
                aarr[:, ng * 4 + dg] = Alog[dlo + dg * 32 + i_, ng * 4 + nsub]
        selp = np.zeros((128, 4, 128), np.float32)
        selp[:, ds] = np.eye(128, dtype=np.float32)
        mp = dict(m1[c])
        mp.update(
            x0=np.ascontiguousarray(host_core_token_slice(inp["L0_attn_hidden_states"], T, c)),
            gateT=m2[c]["gateT"], w1T=m2[c]["w1T"], w3T=m2[c]["w3T"], w2T=m2[c]["w2T"],
            grow=m3[c]["grow"], brow=m3[c]["brow"], wup=m3[c]["wup"], bupc=m3[c]["bupc"],
            cwc=np.ascontiguousarray(cw.reshape(4, 128, 3).transpose(1, 0, 2)),
            cbc=np.ascontiguousarray(cb.reshape(4, 128).T),
            wdel=np.ascontiguousarray(Wd[:, dlo:dlo + 128]),
            bdc=np.ascontiguousarray(bd[dlo:dlo + 128, None]),
            wB=np.ascontiguousarray(WB_), wC=np.ascontiguousarray(WC_),
            aarr=aarr, dcol=np.ascontiguousarray(Dv[dlo:dlo + 128, None]),
            patD=np.ascontiguousarray(patDh.reshape(128, 512)),
            patB=np.ascontiguousarray(patBh.reshape(128, 32 * 128)),
            patY=np.ascontiguousarray(patYh.reshape(128, 512)),
            selpat=np.ascontiguousarray(selp.reshape(128, 512)),
            wdown=np.ascontiguousarray(inp["L1_attn_W_down"]),
            bdrow=np.ascontiguousarray(inp["L1_attn_b_down"][None, :]),
            wred=np.ascontiguousarray(inp["L1_ffn_w_red"]),
            wg=np.ascontiguousarray(inp["L1_ffn_wg"]),
            w1f=np.ascontiguousarray(inp["L1_ffn_w1"]), b1c=b1cols,
            w2f=np.ascontiguousarray(inp["L1_ffn_w2"]),
            b2r=np.ascontiguousarray(inp["L1_ffn_b2"][:, None, :]),
            jval=np.array([[128 * (c % 4)]], np.int32))
        maps.append(mp)
    return maps


_BUILT = {}

def _get_sched(T):
    key = ("fused", T)
    if key not in _BUILT:
        _p, sq = build_fused(T, seq_mode="record")
        waits = sq.compute_waits()
        nc, _ = build_fused(T, seq_mode="sched", seq_waits=waits)
        _BUILT[key] = nc
    return _BUILT[key]

def kernel(**inputs):
    from concourse.bass_utils import run_bass_kernel_spmd
    T = inputs["L0_attn_hidden_states"].shape[1]
    inp = {k: np.asarray(v) for k, v in inputs.items()}
    import jax
    cpu = jax.devices("cpu")[0]
    with jax.default_device(cpu):
        import jax.numpy as jnp
        u = jax.random.uniform(jax.random.key(42, impl="threefry2x32"), (2, T, T), jnp.float32, 1e-06, 1.0 - 1e-06)
        gumbel = np.asarray(-jnp.log(-jnp.log(u)), np.float32)
    nc = _get_sched(T)
    maps = host_fused_inputs(T, inp, gumbel)
    del gumbel
    res = run_bass_kernel_spmd(nc, maps, core_ids=list(range(8)))
    out = host_assemble_full([r.get("out") for r in res.results], T)
    return out.astype(np.float32)


# version 06 (final)
# speedup vs baseline: 1.7323x; 1.4361x over previous
import numpy as np
from contextlib import ExitStack
import concourse.bass as bass
import concourse.mybir as mybir

F32 = mybir.dt.float32
BF16 = mybir.dt.bfloat16
AF = mybir.ActivationFunctionType
ALU = mybir.AluOpType
AX = mybir.AxisListType
NEG = -1.0e9

ENGINES = ("sync", "gpsimd", "vector", "scalar", "tensor")
STREAMS = ("sync", "gpsimd", "vector", "scalar", "tensor", "cc")
DMA_STREAMS = ("sync", "gpsimd")
NLANE = 8

_DT_SIZE = {str(F32): 4, str(BF16): 2, str(mybir.dt.int32): 4, str(mybir.dt.float16): 2,
            str(mybir.dt.uint8): 1, str(mybir.dt.int8): 1, str(mybir.dt.uint16): 2,
            str(mybir.dt.uint32): 4}

def _arg_ranges(arg):
    bap = getattr(arg, "bass_ap", None)
    if bap is None:
        return None
    try:
        name = bap.tensor.name
        space = type(bap.tensor).__name__
    except Exception:
        return None
    esz = _DT_SIZE.get(str(arg.dtype), 4)
    off = arg.offset
    if not isinstance(off, (int, np.integer)):
        return (space + ":" + name, 0, 1 << 60)
    ext = 0
    for step, count in arg.ap:
        ext += abs(int(step)) * (int(count) - 1)
    return (space + ":" + name, int(off) * esz, (int(off) + ext + 1) * esz)

def _sem_targets(stream, ordinal):
    if ordinal <= 0:
        return []
    if stream in DMA_STREAMS:
        lane = (ordinal - 1) % NLANE
        return [((stream, lane), 16 * ((ordinal - 1) // NLANE + 1))]
    return [((stream, 0), ordinal)]

def _sem_targets_all(stream, count):
    if count <= 0:
        return []
    if stream in DMA_STREAMS:
        out = []
        for lane in range(NLANE):
            c = count // NLANE + (1 if lane < count % NLANE else 0)
            if c:
                out.append(((stream, lane), 16 * c))
        return out
    return [((stream, 0), count)]

class Seq:
    def __init__(self, nc, mode="serial", waits=None):
        self.nc = nc
        self.mode = mode
        self.ops = []
        self.captured = {}
        self.waits = waits
        self.n_in_stream = {s: 0 for s in STREAMS}
        self.ordinals = []

    def op(self, engine, emit, inc=None, wait=None, barrier=False, no_inc=False):
        stream = "cc" if barrier else engine
        if not no_inc:
            self.n_in_stream[stream] += 1
        self.ordinals.append(self.n_in_stream[stream])
        self.ops.append((engine, stream, emit, wait, barrier, no_inc))
        return len(self.ops) - 1

    def barrier(self):
        return dict(self.n_in_stream)

    def dma(self, queue, out, in_, wait=None, **kw):
        return self.op(queue, lambda e: e.dma_start(out=out, in_=in_, **kw), wait=wait)

    def cc(self, emit):
        return self.op("gpsimd", emit, barrier=True)

    def _targets_for_dict(self, d):
        out = []
        for s, c in d.items():
            out.extend(_sem_targets_all(s, c))
        return out

    def build(self, block, step_unused=None):
        nc = self.nc
        sems = nc._seq_sems
        mode = self.mode
        if mode == "serial" and not hasattr(self, "_pref"):
            self._pref = []
            c = {s: 0 for s in STREAMS}
            for (_, stream, _, _, _, no_inc) in self.ops:
                self._pref.append(dict(c))
                if not no_inc:
                    c[stream] += 1
        for engname in ENGINES:
            mine = [(i, o) for i, o in enumerate(self.ops) if o[0] == engname]
            if not mine and engname != "sync":
                continue
            def mk(mine, engname):
                def body(eng):
                    maxw = {}
                    for i, (_, stream, emit, manual, barrier, no_inc) in mine:
                        if mode == "sched":
                            tgts = self.waits[i]
                        elif mode == "record":
                            tgts = []
                        else:
                            tgts = self._targets_for_dict(self._pref[i])
                        for semkey, v in tgts:
                            if v > maxw.get(semkey, 0):
                                eng.wait_ge(sems[semkey], int(v))
                                maxw[semkey] = v
                        inst = emit(eng)
                        if not no_inc:
                            if stream in DMA_STREAMS:
                                lane = (self.ordinals[i] - 1) % NLANE
                                inst.then_inc(sems[(stream, lane)], 16)
                            else:
                                inst.then_inc(sems[(stream, 0)], 1)
                        if mode == "record":
                            ri, ro = [], []
                            try:
                                mi = inst.ins
                                for a in mi.ins:
                                    r = _arg_ranges(a)
                                    if r: ri.append(r)
                                for a in mi.outs:
                                    r = _arg_ranges(a)
                                    if r: ro.append(r)
                            except Exception:
                                ri = [("ALL", 0, 1 << 60)]
                                ro = [("ALL", 0, 1 << 60)]
                            self.captured[i] = (ri, ro)
                    if engname == "sync":
                        for s in STREAMS:
                            for semkey, v in _sem_targets_all(s, self.n_in_stream[s]):
                                eng.wait_ge(sems[semkey], int(v))
                return body
            getattr(block, engname)(mk(mine, engname))

    SAME_OK = False
    def compute_waits(self):
        n = len(self.ops)
        hist_w = {}
        hist_r = {}
        self.kw_nstreams = {}
        for i2 in range(n):
            st2 = self.ops[i2][1]
            ri2, ro2 = self.captured.get(i2, ([("ALL", 0, 1 << 60)], [("ALL", 0, 1 << 60)]))
        waits = [None] * n
        floor = -1
        floor_snapshot = None
        COMPUTE = {"vector", "scalar", "tensor"}
        cnt_running = {s: 0 for s in STREAMS}
        for i in range(n):
            eng, stream, _, manual, barrier, no_inc = self.ops[i]
            ri, ro = self.captured.get(i, ([("ALL", 0, 1 << 60)], [("ALL", 0, 1 << 60)]))
            if any(k == "ALL" for k, _, _ in ri) or any(k == "ALL" for k, _, _ in ro):
                waits[i] = self._targets_for_dict(dict(cnt_running))
                floor = i
                floor_snapshot = _sem_targets(stream, self.ordinals[i])
                if not no_inc:
                    cnt_running[stream] += 1
                continue
            if barrier:
                waits[i] = self._targets_for_dict(dict(cnt_running))
                floor = i
                floor_snapshot = _sem_targets("cc", self.ordinals[i])
                self._add_hist(hist_w, hist_r, i, ri, ro)
                if not no_inc:
                    cnt_running[stream] += 1
                continue
            deps = {}
            same_ok = (eng == "tensor")
            def scan(lst, lo, hi):
                if not lst:
                    return
                seen = set()
                ndma = 0
                for (j, jlo, jhi) in reversed(lst):
                    if j <= floor:
                        break
                    js = self.ops[j][1]
                    if js in DMA_STREAMS:
                        dk = (js, (self.ordinals[j] - 1) % NLANE)
                    else:
                        dk = (js, 0)
                    if dk in seen:
                        continue
                    if jlo < hi and lo < jhi:
                        seen.add(dk)
                        if same_ok and js == stream:
                            continue
                        cur = deps.get(dk, -1)
                        if j > cur:
                            deps[dk] = j
                        if len(seen) >= 20:
                            break
            for (key, lo, hi) in ri:
                scan(hist_w.get(key), lo, hi)
            for (key, lo, hi) in ro:
                scan(hist_w.get(key), lo, hi)
                scan(hist_r.get(key), lo, hi)
            tgts = {}
            if floor_snapshot:
                for semkey, v in floor_snapshot:
                    tgts[semkey] = max(tgts.get(semkey, 0), v)
            for (js, lane), j in deps.items():
                for semkey, v in _sem_targets(js, self.ordinals[j]):
                    tgts[semkey] = max(tgts.get(semkey, 0), v)
            if manual is not None:
                for semkey, v in self._targets_for_dict(manual):
                    tgts[semkey] = max(tgts.get(semkey, 0), v)
            waits[i] = list(tgts.items())
            self._add_hist(hist_w, hist_r, i, ri, ro)
            if not no_inc:
                cnt_running[stream] += 1
        return waits

    def _add_hist(self, hist_w, hist_r, i, ri, ro):
        for (key, lo, hi) in ri:
            hist_r.setdefault(key, []).append((i, lo, hi))
        for (key, lo, hi) in ro:
            hist_w.setdefault(key, []).append((i, lo, hi))

def make_seq_sems(nc, es):
    sems = {}
    for s in ("vector", "scalar", "tensor", "cc"):
        sems[(s, 0)] = es.enter_context(nc.semaphore(f"sem_{s}"))
    for s in DMA_STREAMS:
        for lane in range(NLANE):
            sems[(s, lane)] = es.enter_context(nc.semaphore(f"sem_{s}{lane}"))
    nc._seq_sems = sems
    return sems

def ceil_div(a, b):
    return -(-a // b)


E = 1024
NH = 16
DH = 64
FF0 = 2816
NEXP = 8
FF1 = 4096
DI = 512
NS = 256
GATE = 16

def core_token_rows(T, c):
    j = c % 4
    nslot = T // 512
    rows = []
    for t in range(nslot):
        s = 128 * (4 * t + j)
        rows.append(np.arange(s, s + 128))
    return np.concatenate(rows)

def host_core_token_slice(arr_bt, T, c):
    b = c // 4
    return arr_bt[b][core_token_rows(T, c)]

def host_assemble_full(parts, T):
    B = 2
    out = None
    for c in range(8):
        p = np.asarray(parts[c])
        if out is None:
            out = np.zeros((B, T) + p.shape[1:], p.dtype)
        out[c // 4][core_token_rows(T, c)] = p
    return out

def build_l2(T):
    TC = T // 4
    NTB = TC // 128
    NTS = TC // 512 if TC >= 512 else 1
    TSZ = min(TC, 512)
    nc = bass.Bass("TRN2", target_bir_lowering=False)
    x0 = nc.dram_tensor("x0", [TC, E], F32, kind="ExternalInput").ap()
    attn = nc.dram_tensor("attn", [TC, E], F32, kind="ExternalInput").ap()
    gateT = nc.dram_tensor("gateT", [E, NEXP], F32, kind="ExternalInput").ap()
    w1T = nc.dram_tensor("w1T", [NEXP, E, FF0], F32, kind="ExternalInput").ap()
    w3T = nc.dram_tensor("w3T", [NEXP, E, FF0], F32, kind="ExternalInput").ap()
    w2T = nc.dram_tensor("w2T", [NEXP, FF0, E], F32, kind="ExternalInput").ap()
    identf = nc.dram_tensor("identf", [128, 128], F32, kind="ExternalInput").ap()
    x2o = nc.dram_tensor("x2", [TC, E], F32, kind="ExternalOutput").ap()
    x1d = nc.dram_tensor("x1d", [TC, E], F32).ap()
    yd = nc.dram_tensor("yd", [TC, E], F32).ap()

    seq = Seq(nc)
    es = ExitStack()
    sb = lambda n, s, d=F32: es.enter_context(nc.sbuf_tensor(n, s, d)).ap()
    ps = lambda n, s: es.enter_context(nc.psum_tensor(n, s, F32)).ap()

    xbf = [sb(f"xbf{i}", [128, TC], BF16) for i in range(8)]
    tmA = sb("tmA", [128, E])
    tmB = sb("tmB", [128, E])
    wte = sb("wte", [128, NTB * NEXP])
    sident = sb("sident", [128, 128])
    gat = sb("gat", [128, 64], BF16)
    gatf = sb("gatf", [128, 64])
    xtr = sb("xtr", [128, 128])
    wArena = sb("wArena", [128, 8 * FF0], BF16)
    wA = [wArena[:, i * FF0:(i + 1) * FF0] for i in range(8)]
    w2s = [wArena[:, i * E:(i + 1) * E] for i in range(22)]
    hA = [sb(f"hA{i}", [128, TC], BF16) for i in range(22)]
    cols = sb("cols", [128, 8])
    st1 = sb("st1", [128, 16])
    yev = sb("yev", [128, 512])
    p_tr = ps("p_tr", [128, 128])
    p_g = ps("p_g", [128, 16])
    p_a = ps("p_a", [128, 512])
    p_b = ps("p_b", [128, 512])

    NF0 = FF0 // 128
    with (nc.semaphore("step") as step, nc.Block() as block):
        seq.dma("sync", sident[:, :], identf)
        seq.dma("gpsimd", gat.rearrange("p (i e) -> p i e", i=8), gateT.rearrange("(i p) e -> p i e", p=128))
        seq.dma("sync", gatf.rearrange("p (i e) -> p i e", i=8), gateT.rearrange("(i p) e -> p i e", p=128))
        seq.op("vector", lambda v: v.memset(cols[:, 0:1], 1e-5))
        EPS = cols[:, 0:1]

        def ln_tb(tb, src_emitters, out_dram, transpose_to=None, extra=None, do_gating=False):
            for fn in src_emitters:
                fn()
            if extra is not None:
                extra()
            b = seq.barrier()
            seq.op("vector", lambda v: v.tensor_reduce(st1[:, 0:1], tmA[:, :], AX.XYZW, ALU.add), wait=b)
            seq.op("scalar", lambda s: s.activation(tmB[:, :], tmA[:, :], AF.Square), wait=b)
            seq.op("vector", lambda v: v.tensor_reduce(st1[:, 1:2], tmB[:, :], AX.XYZW, ALU.add))
            seq.op("vector", lambda v: v.tensor_scalar(st1[:, 2:3], st1[:, 0:1], 1.0 / E, None, ALU.mult))
            seq.op("vector", lambda v: v.tensor_scalar(st1[:, 3:4], st1[:, 1:2], 1.0 / E, None, ALU.mult))
            seq.op("vector", lambda v: v.tensor_tensor(st1[:, 4:5], st1[:, 2:3], st1[:, 2:3], ALU.mult))
            seq.op("vector", lambda v: v.tensor_tensor(st1[:, 5:6], st1[:, 3:4], st1[:, 4:5], ALU.subtract))
            seq.op("scalar", lambda s: s.activation(st1[:, 6:7], st1[:, 5:6], AF.Sqrt, bias=EPS))
            seq.op("vector", lambda v: v.reciprocal(st1[:, 7:8], st1[:, 6:7]))
            seq.op("vector", lambda v: v.tensor_scalar(tmA[:, :], tmA[:, :], st1[:, 2:3], st1[:, 7:8], ALU.subtract, ALU.mult))
            if out_dram is not None:
                seq.dma("sync", out_dram[tb * 128:(tb + 1) * 128, :], tmA[:, :])
            if transpose_to is not None:
                for fc in range(8):
                    seq.op("tensor", lambda t, fc=fc: t.transpose(p_tr[:, :], tmA[:, fc * 128:(fc + 1) * 128], sident[:, :]))
                    seq.op("vector", lambda v, fc=fc, tb=tb: v.tensor_copy(transpose_to[fc][:, tb * 128:(tb + 1) * 128], p_tr[:, :]))
                    if do_gating:
                        seq.op("vector", lambda v, fc=fc: v.tensor_copy(xtr[:, :], p_tr[:, :]))
                        seq.op("tensor", lambda t, fc=fc: t.matmul(p_g[:, 0:8], xtr[:, :], gatf[:, fc * 8:(fc + 1) * 8], start=(fc == 0), stop=(fc == 7)))


        def gating_post(tb):
            seq.op("vector", lambda v: v.tensor_copy(st1[:, 0:8], p_g[:, 0:8]))
            seq.op("vector", lambda v: v.reduce_max(st1[:, 8:9], st1[:, 0:8], AX.XYZW, negate=True))
            seq.op("scalar", lambda s: s.activation(st1[:, 0:8], st1[:, 0:8], AF.Exp, bias=st1[:, 8:9], accum_out=st1[:, 9:10]))
            seq.op("vector", lambda v: v.reciprocal(st1[:, 10:11], st1[:, 9:10]))
            seq.op("vector", lambda v: v.tensor_scalar(st1[:, 0:8], st1[:, 0:8], st1[:, 10:11], None, ALU.mult))
            seq.op("vector", lambda v: v.reduce_max(st1[:, 11:12], st1[:, 0:8], AX.XYZW))
            seq.op("vector", lambda v: v.tensor_scalar(st1[:, 12:13], st1[:, 11:12], -1.0, None, ALU.mult))
            seq.op("vector", lambda v: v.tensor_scalar(tmB[:, 0:8], st1[:, 0:8], st1[:, 11:12], None, ALU.is_ge))
            seq.op("vector", lambda v: v.scalar_tensor_tensor(tmB[:, 8:16], tmB[:, 0:8], -2.0, st1[:, 0:8], ALU.mult, ALU.add))
            seq.op("vector", lambda v: v.reduce_max(st1[:, 13:14], tmB[:, 8:16], AX.XYZW))
            seq.op("vector", lambda v: v.tensor_scalar(tmB[:, 0:8], st1[:, 0:8], st1[:, 13:14], None, ALU.is_ge))
            seq.op("vector", lambda v: v.tensor_tensor(st1[:, 14:15], st1[:, 11:12], st1[:, 13:14], ALU.add))
            seq.op("vector", lambda v: v.reciprocal(st1[:, 15:16], st1[:, 14:15]))
            seq.op("vector", lambda v: v.tensor_tensor(tmB[:, 0:8], tmB[:, 0:8], st1[:, 0:8], ALU.mult))
            seq.op("vector", lambda v, tb=tb: v.tensor_scalar(wte[:, tb * 8:(tb + 1) * 8], tmB[:, 0:8], st1[:, 15:16], None, ALU.mult))

        for tb in range(NTB):
            ln_tb(tb,
                  [lambda tb=tb: seq.dma("sync", tmA[:, :], x0[tb * 128:tb * 128 + 128, :]),
                   lambda tb=tb: seq.dma("sync", tmB[:, :], attn[tb * 128:tb * 128 + 128, :])],
                  x1d, transpose_to=xbf, do_gating=True,
                  extra=lambda: seq.op("vector", lambda v: v.tensor_tensor(tmA[:, :], tmA[:, :], tmB[:, :], ALU.add)))
            gating_post(tb)

        for e in range(NEXP):
            for ic in range(8):
                seq.dma("gpsimd", wA[ic], w1T[e, ic * 128:(ic + 1) * 128, :])
            wbar = seq.barrier()
            for hc in range(NF0):
                for ts in range(NTS):
                    for ic in range(8):
                        seq.op("tensor", lambda t, ic=ic, hc=hc, ts=ts: t.matmul(p_a[:, 0:TSZ], wA[ic][:, hc * 128:(hc + 1) * 128], xbf[ic][:, ts * TSZ:(ts + 1) * TSZ], start=(ic == 0), stop=(ic == 7)), wait=wbar if (ic == 0 and hc == 0 and ts == 0) else None)
                    seq.op("scalar", lambda s, hc=hc, ts=ts: s.activation(hA[hc][:, ts * TSZ:(ts + 1) * TSZ], p_a[:, 0:TSZ], AF.Silu))
            for ic in range(8):
                seq.dma("gpsimd", wA[ic], w3T[e, ic * 128:(ic + 1) * 128, :])
            wbar = seq.barrier()
            for hc in range(NF0):
                for ts in range(NTS):
                    for ic in range(8):
                        seq.op("tensor", lambda t, ic=ic, hc=hc, ts=ts: t.matmul(p_a[:, 0:TSZ], wA[ic][:, hc * 128:(hc + 1) * 128], xbf[ic][:, ts * TSZ:(ts + 1) * TSZ], start=(ic == 0), stop=(ic == 7)), wait=wbar if (ic == 0 and hc == 0 and ts == 0) else None)
                    seq.op("vector", lambda v, hc=hc, ts=ts: v.tensor_tensor(hA[hc][:, ts * TSZ:(ts + 1) * TSZ], hA[hc][:, ts * TSZ:(ts + 1) * TSZ], p_a[:, 0:TSZ], ALU.mult))
            for hc in range(NF0):
                seq.dma("gpsimd", w2s[hc], w2T[e, hc * 128:(hc + 1) * 128, :])
            wbar = seq.barrier()
            for tb in range(NTB):
                for oh in range(2):
                    for hc in range(NF0):
                        seq.op("tensor", lambda t, hc=hc, tb=tb, oh=oh: t.matmul(p_b[:, 0:512], hA[hc][:, tb * 128:(tb + 1) * 128], w2s[hc][:, oh * 512:(oh + 1) * 512], start=(hc == 0), stop=(hc == NF0 - 1)), wait=wbar if (hc == 0 and tb == 0 and oh == 0) else None)
                    seq.op("vector", lambda v, tb=tb, e=e: v.tensor_scalar(yev[:, 0:512], p_b[:, 0:512], wte[:, tb * 8 + e:tb * 8 + e + 1], None, ALU.mult))
                    if e == 0:
                        seq.dma("gpsimd", yd[tb * 128:(tb + 1) * 128, oh * 512:(oh + 1) * 512], yev[:, 0:512])
                    else:
                        seq.dma("gpsimd", yd[tb * 128:(tb + 1) * 128, oh * 512:(oh + 1) * 512], yev[:, 0:512], accum_op=ALU.add)

        for tb in range(NTB):
            ln_tb(tb,
                  [lambda tb=tb: seq.dma("sync", tmA[:, :], x1d[tb * 128:tb * 128 + 128, :]),
                   lambda tb=tb: seq.dma("sync", tmB[:, :], yd[tb * 128:tb * 128 + 128, :])],
                  None, transpose_to=None,
                  extra=lambda: seq.op("vector", lambda v: v.tensor_tensor(tmA[:, :], tmA[:, :], tmB[:, :], ALU.add)))
            seq.dma("sync", x2o[tb * 128:(tb + 1) * 128, :], tmA[:, :])

        seq.build(block, step)
    es.close()
    return nc

def host_l2_inputs(T, inp, attn_full):
    ident = np.eye(128, dtype=np.float32)
    w1T = np.ascontiguousarray(np.transpose(inp["L0_ffn_w1"], (0, 2, 1)))
    w3T = np.ascontiguousarray(np.transpose(inp["L0_ffn_w3"], (0, 2, 1)))
    w2T = np.ascontiguousarray(np.transpose(inp["L0_ffn_w2"], (0, 2, 1)))
    gateT = np.ascontiguousarray(inp["L0_ffn_gate_w"].T)
    maps = []
    for c in range(8):
        maps.append(dict(
            x0=np.ascontiguousarray(host_core_token_slice(inp["L0_attn_hidden_states"], T, c)),
            attn=np.ascontiguousarray(host_core_token_slice(attn_full, T, c)),
            gateT=gateT, w1T=w1T, w3T=w3T, w2T=w2T, identf=ident))
    return maps

def build_l3(T):
    TC = T // 4
    NTB = TC // 128
    NTS = ceil_div(TC, 512)
    TSZ = min(TC, 512)
    nc = bass.Bass("TRN2", target_bir_lowering=False)
    x2 = nc.dram_tensor("x2", [TC, E], F32, kind="ExternalInput").ap()
    grow = nc.dram_tensor("grow", [1, E], F32, kind="ExternalInput").ap()
    brow = nc.dram_tensor("brow", [1, E], F32, kind="ExternalInput").ap()
    wup = nc.dram_tensor("wup", [E, 2 * DI], F32, kind="ExternalInput").ap()
    bupc = nc.dram_tensor("bupc", [128, 8], F32, kind="ExternalInput").ap()
    identf = nc.dram_tensor("identf", [128, 128], F32, kind="ExternalInput").ap()
    leftT = nc.dram_tensor("leftT", [DI, TC], F32, kind="ExternalOutput").ap()
    rightT = nc.dram_tensor("rightT", [DI, TC], F32, kind="ExternalOutput").ap()

    seq = Seq(nc)
    es = ExitStack()
    sb = lambda n, s, d=F32: es.enter_context(nc.sbuf_tensor(n, s, d)).ap()
    ps = lambda n, s: es.enter_context(nc.psum_tensor(n, s, F32)).ap()
    xbf = [sb(f"xbf{i}", [128, TC], BF16) for i in range(8)]
    tmA = sb("tmA", [128, E])
    tmB = sb("tmB", [128, E])
    grep = sb("grep", [128, E])
    brep = sb("brep", [128, E])
    growS = sb("growS", [1, E])
    browS = sb("browS", [1, E])
    sones = sb("sones", [1, 128])
    sident = sb("sident", [128, 128])
    wupS = [sb(f"wupS{i}", [128, 2 * DI], BF16) for i in range(8)]
    bupS = sb("bupS", [128, 8])
    st1 = sb("st1", [128, 16])
    cols = sb("cols", [128, 1])
    oev = sb("oev", [128, 512])
    p_tr = ps("p_tr", [128, 128])
    p_a = ps("p_a", [128, 512])
    p_r = ps("p_r", [128, 512])

    with (nc.semaphore("step") as step, nc.Block() as block):
        seq.dma("sync", sident[:, :], identf)
        seq.dma("sync", growS[:, :], grow)
        seq.dma("sync", browS[:, :], brow)
        seq.dma("sync", bupS[:, :], bupc)
        for ic in range(8):
            seq.dma("gpsimd", wupS[ic][:, :], wup[ic * 128:(ic + 1) * 128, :])
        seq.op("vector", lambda v: v.memset(cols[:, 0:1], 1e-5))
        seq.op("vector", lambda v: v.memset(sones[:, :], 1.0))
        for half in range(2):
            seq.op("tensor", lambda t, half=half: t.matmul(p_r[:, 0:512], sones[0:1, :], growS[0:1, half * 512:(half + 1) * 512], start=True, stop=True))
            seq.op("vector", lambda v, half=half: v.tensor_copy(grep[:, half * 512:(half + 1) * 512], p_r[:, 0:512]))
            seq.op("tensor", lambda t, half=half: t.matmul(p_r[:, 0:512], sones[0:1, :], browS[0:1, half * 512:(half + 1) * 512], start=True, stop=True))
            seq.op("vector", lambda v, half=half: v.tensor_copy(brep[:, half * 512:(half + 1) * 512], p_r[:, 0:512]))

        for tb in range(NTB):
            seq.dma("sync", tmA[:, :], x2[tb * 128:tb * 128 + 128, :])
            seq.op("vector", lambda v: v.tensor_reduce(st1[:, 0:1], tmA[:, :], AX.XYZW, ALU.add))
            seq.op("scalar", lambda s: s.activation(tmB[:, :], tmA[:, :], AF.Square))
            seq.op("vector", lambda v: v.tensor_reduce(st1[:, 1:2], tmB[:, :], AX.XYZW, ALU.add))
            seq.op("vector", lambda v: v.tensor_scalar(st1[:, 2:3], st1[:, 0:1], 1.0 / E, None, ALU.mult))
            seq.op("vector", lambda v: v.tensor_scalar(st1[:, 3:4], st1[:, 1:2], 1.0 / E, None, ALU.mult))
            seq.op("vector", lambda v: v.tensor_tensor(st1[:, 4:5], st1[:, 2:3], st1[:, 2:3], ALU.mult))
            seq.op("vector", lambda v: v.tensor_tensor(st1[:, 5:6], st1[:, 3:4], st1[:, 4:5], ALU.subtract))
            seq.op("scalar", lambda s: s.activation(st1[:, 6:7], st1[:, 5:6], AF.Sqrt, bias=cols[:, 0:1]))
            seq.op("vector", lambda v: v.reciprocal(st1[:, 7:8], st1[:, 6:7]))
            seq.op("vector", lambda v: v.tensor_scalar(tmA[:, :], tmA[:, :], st1[:, 2:3], st1[:, 7:8], ALU.subtract, ALU.mult))
            seq.op("vector", lambda v: v.tensor_tensor(tmA[:, :], tmA[:, :], grep[:, :], ALU.mult))
            seq.op("vector", lambda v: v.tensor_tensor(tmA[:, :], tmA[:, :], brep[:, :], ALU.add))
            for fc in range(8):
                seq.op("tensor", lambda t, fc=fc: t.transpose(p_tr[:, :], tmA[:, fc * 128:(fc + 1) * 128], sident[:, :]))
                seq.op("vector", lambda v, fc=fc, tb=tb: v.tensor_copy(xbf[fc][:, tb * 128:(tb + 1) * 128], p_tr[:, :]))

        for of in range(8):
            for ts in range(NTS):
                for ic in range(8):
                    seq.op("tensor", lambda t, ic=ic, of=of, ts=ts: t.matmul(p_a[:, 0:TSZ], wupS[ic][:, of * 128:(of + 1) * 128], xbf[ic][:, ts * TSZ:(ts + 1) * TSZ], start=(ic == 0), stop=(ic == 7)))
                seq.op("scalar", lambda s, of=of: s.activation(oev[:, 0:TSZ], p_a[:, 0:TSZ], AF.Identity, bias=bupS[:, of:of + 1]))
                dst = leftT if of < 4 else rightT
                ofr = (of % 4) * 128
                seq.dma("sync", dst[ofr:ofr + 128, ts * TSZ:(ts + 1) * TSZ], oev[:, 0:TSZ])
        seq.build(block, step)
    es.close()
    return nc

def host_l3_inputs(T, inp, x2_full_tm):
    ident = np.eye(128, dtype=np.float32)
    maps = []
    for c in range(8):
        maps.append(dict(
            x2=np.ascontiguousarray(host_core_token_slice(x2_full_tm, T, c)),
            grow=np.ascontiguousarray(inp["L1_attn_ln_gamma"][None, :]),
            brow=np.ascontiguousarray(inp["L1_attn_ln_beta"][None, :]),
            wup=np.ascontiguousarray(inp["L1_attn_W_up"]),
            bupc=np.ascontiguousarray(inp["L1_attn_b_up"].reshape(8, 128).T),
            identf=ident))
    return maps

def build_l4(T):
    TCH = min(1024, T)
    NCH = T // TCH
    NS2 = TCH // 512
    nc = bass.Bass("TRN2", target_bir_lowering=False)
    leftF = nc.dram_tensor("leftF", [DI, T], F32, kind="ExternalInput").ap()
    cwc = nc.dram_tensor("cwc", [128, 4, 3], F32, kind="ExternalInput").ap()
    cbc = nc.dram_tensor("cbc", [128, 4], F32, kind="ExternalInput").ap()
    wdel = nc.dram_tensor("wdel", [DI, 128], F32, kind="ExternalInput").ap()
    bdc = nc.dram_tensor("bdc", [128, 1], F32, kind="ExternalInput").ap()
    wB = nc.dram_tensor("wB", [DI, NS], F32, kind="ExternalInput").ap()
    wC = nc.dram_tensor("wC", [DI, NS], F32, kind="ExternalInput").ap()
    aarr = nc.dram_tensor("aarr", [128, NS], F32, kind="ExternalInput").ap()
    dcol = nc.dram_tensor("dcol", [128, 1], F32, kind="ExternalInput").ap()
    patD = nc.dram_tensor("patD", [128, 4 * 128], F32, kind="ExternalInput").ap()
    patB = nc.dram_tensor("patB", [128, 32 * 128], F32, kind="ExternalInput").ap()
    patY = nc.dram_tensor("patY", [128, 4 * 128], F32, kind="ExternalInput").ap()
    yso = nc.dram_tensor("ys", [128, T], F32, kind="ExternalOutput").ap()

    seq = Seq(nc)
    es = ExitStack()
    sb = lambda n, s, d=F32: es.enter_context(nc.sbuf_tensor(n, s, d)).ap()
    ps = lambda n, s: es.enter_context(nc.psum_tensor(n, s, F32)).ap()
    lt = [sb(f"lt{i}", [128, TCH + 2]) for i in range(4)]
    uf = [sb(f"uf{i}", [128, TCH]) for i in range(4)]
    ub = [sb(f"ub{i}", [128, TCH], BF16) for i in range(4)]
    wdS = [sb(f"wdS{i}", [128, 128], BF16) for i in range(4)]
    wbS = [sb(f"wbS{i}", [128, NS], BF16) for i in range(4)]
    wcS = [sb(f"wcS{i}", [128, NS], BF16) for i in range(4)]
    acol = sb("acol", [128, NS])
    dcolS = sb("dcolS", [128, 1])
    bdcS = sb("bdcS", [128, 1])
    cwS = sb("cwS", [128, 12])
    cbS = sb("cbS", [128, 4])
    patDS = sb("patDS", [128, 4 * 128])
    patBS = sb("patBS", [128, 32 * 128], BF16)
    patYS = sb("patYS", [128, 4 * 128], BF16)
    deltaS = sb("deltaS", [128, TCH])
    wSt = sb("wSt", [128, TCH])
    BTb = [sb(f"BTb{i}", [128, TCH], BF16) for i in range(2)]
    CTb = [sb(f"CTb{i}", [128, TCH], BF16) for i in range(2)]
    dRep = [sb(f"dRep{i}", [128, TCH]) for i in range(4)]
    wRep = [sb(f"wRep{i}", [128, TCH]) for i in range(4)]
    dAS = sb("dAS", [128, TCH])
    bS = sb("bS", [128, TCH])
    hS = sb("hS", [128, TCH])
    ymulb = sb("ymulb", [128, TCH], BF16)
    hstate = sb("hstate", [128, NS])
    ysS = sb("ysS", [128, TCH])
    sc1 = sb("sc1", [128, TCH])
    sc2 = sb("sc2", [128, TCH])
    consts = sb("consts", [128, 4])
    p_d = ps("p_d", [128, 512])
    p_B = ps("p_B", [128, TCH])
    p_C = ps("p_C", [128, TCH])
    p_Y = ps("p_Y", [128, TCH])

    with (nc.semaphore("step") as step, nc.Block() as block):
        seq.dma("sync", cwS.rearrange("p (f k) -> p f k", f=4), cwc)
        seq.dma("sync", cbS[:, :], cbc)
        seq.dma("sync", bdcS[:, :], bdc)
        seq.dma("sync", dcolS[:, :], dcol)
        seq.dma("sync", patDS[:, :], patD)
        seq.dma("gpsimd", patBS[:, :], patB)
        seq.dma("gpsimd", patYS[:, :], patY)
        for ic in range(4):
            seq.dma("gpsimd", wdS[ic][:, :], wdel[ic * 128:(ic + 1) * 128, :])
            seq.dma("gpsimd", wbS[ic][:, :], wB[ic * 128:(ic + 1) * 128, :])
            seq.dma("gpsimd", wcS[ic][:, :], wC[ic * 128:(ic + 1) * 128, :])
        seq.op("vector", lambda v: v.memset(consts[:, 0:1], 1.0))
        seq.op("vector", lambda v: v.memset(consts[:, 1:2], -1.0))
        seq.op("vector", lambda v: v.memset(hstate[:, :], 0.0))
        C_ONE, C_M1 = consts[:, 0:1], consts[:, 1:2]
        seq.dma("sync", acol[:, :], aarr)
        seq.op("scalar", lambda s: s.activation(acol[:, :], acol[:, :], AF.Exp))
        seq.op("vector", lambda v: v.tensor_scalar(acol[:, :], acol[:, :], -1.0, None, ALU.mult))

        for k in range(NCH):
            ks = k * TCH
            for fc in range(4):
                if k == 0:
                    seq.op("vector", lambda v, fc=fc: v.memset(lt[fc][:, 0:2], 0.0))
                    seq.dma("sync", lt[fc][:, 2:TCH + 2], leftF[fc * 128:(fc + 1) * 128, 0:TCH])
                else:
                    seq.dma("sync", lt[fc][:, :], leftF[fc * 128:(fc + 1) * 128, ks - 2:ks + TCH])
                seq.op("vector", lambda v, fc=fc: v.tensor_scalar(sc1[:, :], lt[fc][:, 2:TCH + 2], cwS[:, fc * 3 + 2:fc * 3 + 3], cbS[:, fc:fc + 1], ALU.mult, ALU.add))
                seq.op("vector", lambda v, fc=fc: v.scalar_tensor_tensor(sc1[:, :], lt[fc][:, 1:TCH + 1], cwS[:, fc * 3 + 1:fc * 3 + 2], sc1[:, :], ALU.mult, ALU.add))
                seq.op("vector", lambda v, fc=fc: v.scalar_tensor_tensor(sc1[:, :], lt[fc][:, 0:TCH], cwS[:, fc * 3 + 0:fc * 3 + 1], sc1[:, :], ALU.mult, ALU.add))
                seq.op("scalar", lambda s, fc=fc: s.activation(uf[fc][:, :], sc1[:, :], AF.Silu))
                seq.op("vector", lambda v, fc=fc: v.tensor_copy(ub[fc][:, :], uf[fc][:, :]))
            for sub in range(NS2):
                for ic in range(4):
                    seq.op("tensor", lambda t, ic=ic, sub=sub: t.matmul(p_d[:, 0:512], wdS[ic][:, :], ub[ic][:, sub * 512:(sub + 1) * 512], start=(ic == 0), stop=(ic == 3)))
                seq.op("scalar", lambda s: s.activation(sc1[:, 0:512], p_d[:, 0:512], AF.Identity, bias=bdcS[:, 0:1]))
                seq.op("scalar", lambda s: s.activation(sc2[:, 0:512], sc1[:, 0:512], AF.Abs))
                seq.op("scalar", lambda s: s.activation(sc2[:, 0:512], sc2[:, 0:512], AF.Exp, scale=C_M1))
                seq.op("scalar", lambda s: s.activation(sc2[:, 0:512], sc2[:, 0:512], AF.Ln, bias=C_ONE))
                seq.op("vector", lambda v: v.tensor_scalar(sc1[:, 0:512], sc1[:, 0:512], 0.0, None, ALU.max))
                seq.op("vector", lambda v, sub=sub: v.tensor_tensor(deltaS[:, sub * 512:(sub + 1) * 512], sc1[:, 0:512], sc2[:, 0:512], ALU.add))
            seq.op("vector", lambda v: v.tensor_tensor(wSt[:, :], deltaS[:, :], uf[0][:, :], ALU.mult))
            for nh in range(2):
                for sub in range(NS2):
                    for ic in range(4):
                        seq.op("tensor", lambda t, ic=ic, nh=nh, sub=sub: t.matmul(p_d[:, 0:512], wbS[ic][:, nh * 128:(nh + 1) * 128], ub[ic][:, sub * 512:(sub + 1) * 512], start=(ic == 0), stop=(ic == 3)))
                    seq.op("vector", lambda v, nh=nh, sub=sub: v.tensor_copy(BTb[nh][:, sub * 512:(sub + 1) * 512], p_d[:, 0:512]))
                    for ic in range(4):
                        seq.op("tensor", lambda t, ic=ic, nh=nh, sub=sub: t.matmul(p_d[:, 0:512], wcS[ic][:, nh * 128:(nh + 1) * 128], ub[ic][:, sub * 512:(sub + 1) * 512], start=(ic == 0), stop=(ic == 3)))
                    seq.op("vector", lambda v, nh=nh, sub=sub: v.tensor_copy(CTb[nh][:, sub * 512:(sub + 1) * 512], p_d[:, 0:512]))
            for dg in range(4):
                for sub in range(NS2):
                    seq.op("tensor", lambda t, dg=dg, sub=sub: t.matmul(p_d[:, 0:512], patDS[:, dg * 128:(dg + 1) * 128], deltaS[:, sub * 512:(sub + 1) * 512], start=True, stop=True))
                    seq.op("vector", lambda v, dg=dg, sub=sub: v.tensor_copy(dRep[dg][:, sub * 512:(sub + 1) * 512], p_d[:, 0:512]))
                    seq.op("tensor", lambda t, dg=dg, sub=sub: t.matmul(p_d[:, 0:512], patDS[:, dg * 128:(dg + 1) * 128], wSt[:, sub * 512:(sub + 1) * 512], start=True, stop=True))
                    seq.op("vector", lambda v, dg=dg, sub=sub: v.tensor_copy(wRep[dg][:, sub * 512:(sub + 1) * 512], p_d[:, 0:512]))
            for ng in range(64):
                j, nh = ng % 32, ng // 32
                for sub in range(NS2):
                    seq.op("tensor", lambda t, j=j, nh=nh, sub=sub: t.matmul(p_B[:, sub * 512:(sub + 1) * 512], patBS[:, j * 128:(j + 1) * 128], BTb[nh][:, sub * 512:(sub + 1) * 512], start=True, stop=True))
                    seq.op("tensor", lambda t, j=j, nh=nh, sub=sub: t.matmul(p_C[:, sub * 512:(sub + 1) * 512], patBS[:, j * 128:(j + 1) * 128], CTb[nh][:, sub * 512:(sub + 1) * 512], start=True, stop=True))
                for dg in range(4):
                    g = ng * 4 + dg
                    seq.op("scalar", lambda s, dg=dg, ng=ng: s.activation(dAS[:, :], dRep[dg][:, :], AF.Exp, scale=acol[:, ng * 4 + dg:ng * 4 + dg + 1]))
                    seq.op("vector", lambda v, dg=dg: v.tensor_tensor(bS[:, :], wRep[dg][:, :], p_B[:, :], ALU.mult))
                    seq.op("vector", lambda v, g=g: v.tensor_tensor_scan(hS[:, :], dAS[:, :], bS[:, :], hstate[:, g:g + 1], ALU.mult, ALU.add))
                    seq.op("vector", lambda v, g=g: v.tensor_copy(hstate[:, g:g + 1], hS[:, TCH - 1:TCH]))
                    seq.op("vector", lambda v: v.tensor_tensor(ymulb[:, :], hS[:, :], p_C[:, :], ALU.mult))
                    for sub in range(NS2):
                        seq.op("tensor", lambda t, dg=dg, ng=ng, sub=sub: t.matmul(p_Y[:, sub * 512:(sub + 1) * 512], patYS[:, dg * 128:(dg + 1) * 128], ymulb[:, sub * 512:(sub + 1) * 512], start=(ng == 0 and dg == 0), stop=(ng == 63 and dg == 3)))
            for sub in range(NS2):
                seq.op("vector", lambda v, sub=sub: v.scalar_tensor_tensor(ysS[:, sub * 512:(sub + 1) * 512], uf[0][:, sub * 512:(sub + 1) * 512], dcolS[:, 0:1], p_Y[:, sub * 512:(sub + 1) * 512], ALU.mult, ALU.add))
            seq.dma("sync", yso[:, ks:ks + TCH], ysS[:, :])
        seq.build(block, step)
    es.close()
    return nc

def _rot_blocks(arr, ds):
    blocks = [arr[i * 128:(i + 1) * 128] for i in range(arr.shape[0] // 128)]
    order = [(ds + i) % len(blocks) for i in range(len(blocks))]
    return np.concatenate([blocks[i] for i in order], axis=0)

def host_l4_inputs(T, inp, leftT_by_batch):
    cw = inp["L1_attn_conv_w"][:, 0, :]
    cb = inp["L1_attn_conv_b"]
    Wd = inp["L1_attn_W_delta"]
    bd = inp["L1_attn_b_delta"]
    WB_ = inp["L1_attn_W_B"]; WC_ = inp["L1_attn_W_C"]
    Alog = inp["L1_attn_A_log"]; Dv = inp["L1_attn_D"]
    patD = np.zeros((128, 4, 128), np.float32)
    for dg in range(4):
        for p in range(128):
            patD[dg * 32 + p // 4, dg, p] = 1.0
    patB = np.zeros((128, 32, 128), np.float32)
    for j in range(32):
        for p in range(128):
            patB[j * 4 + p % 4, j, p] = 1.0
    patY = np.zeros((128, 4, 128), np.float32)
    for dg in range(4):
        for p in range(128):
            patY[p, dg, dg * 32 + p // 4] = 1.0
    patD_in = np.ascontiguousarray(patD.reshape(128, 4 * 128))
    patB_in = np.ascontiguousarray(patB.reshape(128, 32 * 128))
    patY_in = np.ascontiguousarray(patY.reshape(128, 4 * 128))
    maps = []
    for c in range(8):
        b, ds = c // 4, c % 4
        dlo = ds * 128
        p = np.arange(128)
        i_, nsub = p // 4, p % 4
        dgi, ngi = np.meshgrid(np.arange(4), np.arange(64), indexing="ij")
        aarr = np.zeros((128, 256), np.float32)
        for dg in range(4):
            for ng in range(64):
                aarr[:, ng * 4 + dg] = Alog[dlo + dg * 32 + i_, ng * 4 + nsub]
        maps.append(dict(
            leftF=np.ascontiguousarray(_rot_blocks(leftT_by_batch[b], ds)),
            cwc=np.ascontiguousarray(_rot_blocks(cw, ds).reshape(4, 128, 3).transpose(1, 0, 2)),
            cbc=np.ascontiguousarray(_rot_blocks(cb[:, None], ds).reshape(4, 128).T),
            wdel=np.ascontiguousarray(_rot_blocks(Wd, ds)[:, dlo:dlo + 128]),
            bdc=np.ascontiguousarray(bd[dlo:dlo + 128, None]),
            wB=np.ascontiguousarray(_rot_blocks(WB_, ds)),
            wC=np.ascontiguousarray(_rot_blocks(WC_, ds)),
            aarr=aarr, dcol=np.ascontiguousarray(Dv[dlo:dlo + 128, None]),
            patD=patD_in, patB=patB_in, patY=patY_in))
    return maps

def build_l5(T):
    TC = T // 4
    NTB = TC // 128
    TSZ2 = min(1024, TC)
    NTCG = TC // TSZ2
    NS5 = TSZ2 // 512 if TSZ2 >= 512 else 1
    S5 = min(512, TSZ2)
    nc = bass.Bass("TRN2", target_bir_lowering=False)
    x2 = nc.dram_tensor("x2", [TC, E], F32, kind="ExternalInput").ap()
    ysT = nc.dram_tensor("ysT", [DI, TC], F32, kind="ExternalInput").ap()
    rightT = nc.dram_tensor("rightT", [DI, TC], F32, kind="ExternalInput").ap()
    wdown = nc.dram_tensor("wdown", [DI, E], F32, kind="ExternalInput").ap()
    bdrow = nc.dram_tensor("bdrow", [1, E], F32, kind="ExternalInput").ap()
    wred = nc.dram_tensor("wred", [E, GATE], F32, kind="ExternalInput").ap()
    wg = nc.dram_tensor("wg", [NEXP, GATE], F32, kind="ExternalInput").ap()
    w1f = nc.dram_tensor("w1f", [NEXP, E, FF1], F32, kind="ExternalInput").ap()
    b1c = nc.dram_tensor("b1c", [NEXP, 128, FF1 // 128], F32, kind="ExternalInput").ap()
    w2f = nc.dram_tensor("w2f", [NEXP, FF1, E], F32, kind="ExternalInput").ap()
    b2r = nc.dram_tensor("b2r", [NEXP, 1, E], F32, kind="ExternalInput").ap()
    identf = nc.dram_tensor("identf", [128, 128], F32, kind="ExternalInput").ap()
    outT = nc.dram_tensor("out", [TC, E], F32, kind="ExternalOutput").ap()
    x3d = nc.dram_tensor("x3d", [TC, E], F32).ap()
    yd = nc.dram_tensor("yd", [TC, E], F32).ap()

    seq = Seq(nc)
    es = ExitStack()
    sb = lambda n, s, d=F32: es.enter_context(nc.sbuf_tensor(n, s, d)).ap()
    ps = lambda n, s: es.enter_context(nc.psum_tensor(n, s, F32)).ap()
    NHC = FF1 // 128
    arena1 = sb("arena1", [128, NHC * 1024], BF16)
    arena2 = sb("arena2", [128, 8 * FF1], BF16)
    x3bf = [sb(f"x3bf{i}", [128, TC], BF16) for i in range(8)]
    tmA = sb("tmA", [128, E])
    tmB = sb("tmB", [128, E])
    tmC = sb("tmC", [128, E])
    xtr = sb("xtr", [128, 128])
    wredf = sb("wredf", [128, 8 * GATE])
    wgS = sb("wgS", [8, GATE])
    wgT = sb("wgT", [16, 8])
    redT = sb("redT", [16, 128])
    bdr = sb("bdr", [1, E], BF16)
    b2S = sb("b2S", [1, E], BF16)
    b1S = sb("b1S", [128, NHC])
    onesb = sb("onesb", [1, 128], BF16)
    sident = sb("sident", [128, 128])
    st1 = sb("st1", [128, 16])
    sel8 = sb("sel8", [128, 8])
    wsel = sb("wsel", [128, NTB * 8])
    cols = sb("cols", [128, 4])
    yev = sb("yev", [128, 512])
    p_tr = ps("p_tr", [128, 128])
    p_a = ps("p_a", [128, 512])
    p_b = ps("p_b", [128, 512])
    p_g = ps("p_g", [128, 16])
    p_r = ps("p_r", [128, 16])

    a1f = arena1.bitcast(F32)
    ysS = [a1f[:, i * TC:(i + 1) * TC] for i in range(4)]
    srS = [a1f[:, (4 + i) * TC:(5 + i) * TC] for i in range(4)]
    hG = [arena1[:, i * 1024:(i + 1) * 1024] for i in range(NHC)]
    gdS = [arena2[:, i * TC:(i + 1) * TC] for i in range(4)]
    wdnS = [arena2[:, 4 * TC + i * E:4 * TC + (i + 1) * E] for i in range(4)]
    w1S = [arena2[:, i * FF1:(i + 1) * FF1] for i in range(8)]
    w2S = [arena2[:, i * 1024:(i + 1) * 1024] for i in range(NHC)]

    with (nc.semaphore("step") as step, nc.Block() as block):
        seq.dma("sync", sident[:, :], identf)
        seq.dma("sync", wredf.rearrange("p (i g) -> p i g", i=8), wred.rearrange("(i p) g -> p i g", p=128))
        seq.dma("sync", wgS[:, :], wg)
        seq.dma("gpsimd", bdr[:, :], bdrow)
        seq.op("vector", lambda v: v.memset(cols[:, 0:1], 1e-5))
        seq.op("vector", lambda v: v.memset(cols[:, 1:2], 0.0))
        seq.op("vector", lambda v: v.memset(onesb[:, :], 1.0))
        EPS, ZERO = cols[:, 0:1], cols[:, 1:2]
        seq.op("scalar", lambda s: s.activation(st1[0:8, 0:GATE], wgS[:, :], AF.Square))
        seq.op("vector", lambda v: v.tensor_reduce(st1[0:8, 0:1] if False else sel8[0:8, 0:1], st1[0:8, 0:GATE], AX.XYZW, ALU.add))
        seq.op("scalar", lambda s: s.activation(sel8[0:8, 1:2], sel8[0:8, 0:1], AF.Sqrt, bias=ZERO[0:8, :]))
        seq.op("vector", lambda v: v.tensor_scalar(sel8[0:8, 1:2], sel8[0:8, 1:2], 1e-4, None, ALU.max))
        seq.op("vector", lambda v: v.reciprocal(sel8[0:8, 2:3], sel8[0:8, 1:2]))
        seq.op("vector", lambda v: v.tensor_scalar(wgS[:, :], wgS[:, :], sel8[0:8, 2:3], None, ALU.mult))
        seq.op("tensor", lambda t: t.transpose(p_g[0:16, 0:8], wgS[:, :], sident[0:8, 0:8]))
        seq.op("vector", lambda v: v.tensor_copy(wgT[:, :], p_g[0:16, 0:8]))
        for i in range(4):
            seq.dma("sync", ysS[i], ysT[i * 128:(i + 1) * 128, :])
            seq.dma("sync", srS[i], rightT[i * 128:(i + 1) * 128, :])
            seq.op("scalar", lambda s, i=i: s.activation(srS[i], srS[i], AF.Silu))
            seq.op("vector", lambda v, i=i: v.tensor_tensor(gdS[i], ysS[i], srS[i], ALU.mult))
            seq.dma("gpsimd", wdnS[i], wdown[i * 128:(i + 1) * 128, :])
        for tb in range(NTB):
            seq.dma("sync", tmA[:, :], x2[tb * 128:tb * 128 + 128, :])
            for oh in range(2):
                for ic in range(4):
                    seq.op("tensor", lambda t, ic=ic, oh=oh, tb=tb: t.matmul(p_a[:, 0:512], gdS[ic][:, tb * 128:(tb + 1) * 128], wdnS[ic][:, oh * 512:(oh + 1) * 512], start=(ic == 0), stop=False))
                seq.op("tensor", lambda t, oh=oh: t.matmul(p_a[:, 0:512], onesb[0:1, :], bdr[0:1, oh * 512:(oh + 1) * 512], start=False, stop=True))
                seq.op("vector", lambda v, oh=oh: v.scalar_tensor_tensor(tmB[:, oh * 512:(oh + 1) * 512], tmA[:, oh * 512:(oh + 1) * 512], 2.0, p_a[:, 0:512], ALU.mult, ALU.add))
            seq.op("vector", lambda v: v.tensor_reduce(st1[:, 0:1], tmB[:, :], AX.XYZW, ALU.add))
            seq.op("scalar", lambda s: s.activation(tmC[:, :], tmB[:, :], AF.Square))
            seq.op("vector", lambda v: v.tensor_reduce(st1[:, 1:2], tmC[:, :], AX.XYZW, ALU.add))
            seq.op("vector", lambda v: v.tensor_scalar(st1[:, 2:3], st1[:, 0:1], 1.0 / E, None, ALU.mult))
            seq.op("vector", lambda v: v.tensor_scalar(st1[:, 3:4], st1[:, 1:2], 1.0 / E, None, ALU.mult))
            seq.op("vector", lambda v: v.tensor_tensor(st1[:, 4:5], st1[:, 2:3], st1[:, 2:3], ALU.mult))
            seq.op("vector", lambda v: v.tensor_tensor(st1[:, 5:6], st1[:, 3:4], st1[:, 4:5], ALU.subtract))
            seq.op("scalar", lambda s: s.activation(st1[:, 6:7], st1[:, 5:6], AF.Sqrt, bias=EPS))
            seq.op("vector", lambda v: v.reciprocal(st1[:, 7:8], st1[:, 6:7]))
            seq.op("vector", lambda v: v.tensor_scalar(tmB[:, :], tmB[:, :], st1[:, 2:3], st1[:, 7:8], ALU.subtract, ALU.mult))
            seq.dma("sync", x3d[tb * 128:(tb + 1) * 128, :], tmB[:, :])
            for fc in range(8):
                seq.op("tensor", lambda t, fc=fc: t.transpose(p_tr[:, :], tmB[:, fc * 128:(fc + 1) * 128], sident[:, :]))
                seq.op("vector", lambda v, fc=fc, tb=tb: v.tensor_copy(x3bf[fc][:, tb * 128:(tb + 1) * 128], p_tr[:, :]))
                seq.op("vector", lambda v: v.tensor_copy(xtr[:, :], p_tr[:, :]))
                seq.op("tensor", lambda t, fc=fc: t.matmul(p_r[:, 0:GATE], xtr[:, :], wredf[:, fc * GATE:(fc + 1) * GATE], start=(fc == 0), stop=(fc == 7)))
            seq.op("vector", lambda v: v.tensor_copy(tmC[:, 0:GATE], p_r[:, 0:GATE]))
            seq.op("tensor", lambda t: t.transpose(p_tr[0:16, 0:128], tmC[:, 0:GATE], sident[:, :]))
            seq.op("vector", lambda v: v.tensor_copy(redT[:, :], p_tr[0:16, 0:128]))
            seq.op("tensor", lambda t: t.matmul(p_g[:, 0:8], redT[:, :], wgT[:, :], start=True, stop=True))
            seq.op("vector", lambda v: v.tensor_copy(st1[:, 0:8], p_g[:, 0:8]))
            seq.op("vector", lambda v: v.reduce_max(st1[:, 8:9], st1[:, 0:8], AX.XYZW, negate=True))
            seq.op("scalar", lambda s: s.activation(tmC[:, 0:8], st1[:, 0:8], AF.Exp, bias=st1[:, 8:9], accum_out=st1[:, 9:10]))
            seq.op("vector", lambda v: v.reciprocal(st1[:, 10:11], st1[:, 9:10]))
            seq.op("vector", lambda v: v.tensor_scalar(st1[:, 11:12], st1[:, 8:9], -1.0, None, ALU.mult))
            seq.op("vector", lambda v: v.tensor_scalar(sel8[:, 0:8], st1[:, 0:8], st1[:, 11:12], None, ALU.is_ge))
            seq.op("vector", lambda v: v.memset(st1[:, 12:13], 1.0))
            for e0 in range(8):
                seq.op("vector", lambda v, e0=e0: v.tensor_tensor(sel8[:, e0:e0 + 1], sel8[:, e0:e0 + 1], st1[:, 12:13], ALU.mult))
                if e0 < 7:
                    seq.op("vector", lambda v, e0=e0: v.tensor_scalar(st1[:, 13:14], sel8[:, e0:e0 + 1], -1.0, 1.0, ALU.mult, ALU.add))
                    seq.op("vector", lambda v: v.tensor_tensor(st1[:, 12:13], st1[:, 12:13], st1[:, 13:14], ALU.mult))
            seq.op("vector", lambda v, tb=tb: v.tensor_scalar(wsel[:, tb * 8:(tb + 1) * 8], sel8[:, 0:8], st1[:, 10:11], None, ALU.mult))
        for e in range(NEXP):
            seq.dma("sync", b1S[:, :], b1c[e])
            seq.dma("gpsimd", b2S[:, :], b2r[e])
            for tcg in range(NTCG):
                t0 = tcg * TSZ2
                for ic in range(8):
                    seq.dma("gpsimd", w1S[ic], w1f[e, ic * 128:(ic + 1) * 128, :])
                wb = seq.barrier()
                for hc in range(NHC):
                    for s5 in range(NS5):
                        for ic in range(8):
                            seq.op("tensor", lambda t, ic=ic, hc=hc, s5=s5, t0=t0: t.matmul(p_a[:, 0:S5], w1S[ic][:, hc * 128:(hc + 1) * 128], x3bf[ic][:, t0 + s5 * 512:t0 + s5 * 512 + S5], start=(ic == 0), stop=(ic == 7)))
                        seq.op("scalar", lambda s, hc=hc, s5=s5: s.activation(hG[hc][:, s5 * 512:s5 * 512 + S5], p_a[:, 0:S5], AF.Gelu, bias=b1S[:, hc:hc + 1]))
                for hc in range(NHC):
                    seq.dma("gpsimd", w2S[hc], w2f[e, hc * 128:(hc + 1) * 128, :])
                wb = seq.barrier()
                for tb0 in range(TSZ2 // 128):
                    tb = tcg * (TSZ2 // 128) + tb0
                    for oh in range(2):
                        for hc in range(NHC):
                            seq.op("tensor", lambda t, hc=hc, tb0=tb0, oh=oh: t.matmul(p_b[:, 0:512], hG[hc][:, tb0 * 128:(tb0 + 1) * 128], w2S[hc][:, oh * 512:(oh + 1) * 512], start=(hc == 0), stop=False))
                        seq.op("tensor", lambda t, oh=oh: t.matmul(p_b[:, 0:512], onesb[0:1, :], b2S[0:1, oh * 512:(oh + 1) * 512], start=False, stop=True))
                        seq.op("vector", lambda v, tb=tb, e=e: v.tensor_scalar(yev[:, 0:512], p_b[:, 0:512], wsel[:, tb * 8 + e:tb * 8 + e + 1], None, ALU.mult))
                        if e == 0:
                            seq.dma("gpsimd", yd[tb * 128:(tb + 1) * 128, oh * 512:(oh + 1) * 512], yev[:, 0:512])
                        else:
                            seq.dma("gpsimd", yd[tb * 128:(tb + 1) * 128, oh * 512:(oh + 1) * 512], yev[:, 0:512], accum_op=ALU.add)
        for tb in range(NTB):
            seq.dma("sync", tmA[:, :], x3d[tb * 128:tb * 128 + 128, :])
            seq.dma("sync", tmB[:, :], yd[tb * 128:tb * 128 + 128, :])
            seq.op("vector", lambda v: v.tensor_tensor(tmA[:, :], tmA[:, :], tmB[:, :], ALU.add))
            seq.op("vector", lambda v: v.tensor_reduce(st1[:, 0:1], tmA[:, :], AX.XYZW, ALU.add))
            seq.op("scalar", lambda s: s.activation(tmC[:, :], tmA[:, :], AF.Square))
            seq.op("vector", lambda v: v.tensor_reduce(st1[:, 1:2], tmC[:, :], AX.XYZW, ALU.add))
            seq.op("vector", lambda v: v.tensor_scalar(st1[:, 2:3], st1[:, 0:1], 1.0 / E, None, ALU.mult))
            seq.op("vector", lambda v: v.tensor_scalar(st1[:, 3:4], st1[:, 1:2], 1.0 / E, None, ALU.mult))
            seq.op("vector", lambda v: v.tensor_tensor(st1[:, 4:5], st1[:, 2:3], st1[:, 2:3], ALU.mult))
            seq.op("vector", lambda v: v.tensor_tensor(st1[:, 5:6], st1[:, 3:4], st1[:, 4:5], ALU.subtract))
            seq.op("scalar", lambda s: s.activation(st1[:, 6:7], st1[:, 5:6], AF.Sqrt, bias=EPS))
            seq.op("vector", lambda v: v.reciprocal(st1[:, 7:8], st1[:, 6:7]))
            seq.op("vector", lambda v: v.tensor_scalar(tmA[:, :], tmA[:, :], st1[:, 2:3], st1[:, 7:8], ALU.subtract, ALU.mult))
            seq.dma("sync", outT[tb * 128:(tb + 1) * 128, :], tmA[:, :])
        seq.build(block, step)
    es.close()
    return nc

def host_l5_inputs(T, inp, x2_full_tm, ys_by_batch, rightT_by_core):
    maps = []
    b1cols = np.ascontiguousarray(inp["L1_ffn_b1"].reshape(NEXP, FF1 // 128, 128).transpose(0, 2, 1))
    for c in range(8):
        b = c // 4
        ysTc = ys_by_batch[b][:, core_token_rows(T, c)]
        maps.append(dict(
            x2=np.ascontiguousarray(host_core_token_slice(x2_full_tm, T, c)),
            ysT=np.ascontiguousarray(ysTc),
            rightT=np.ascontiguousarray(rightT_by_core[c]),
            wdown=np.ascontiguousarray(inp["L1_attn_W_down"]),
            bdrow=np.ascontiguousarray(inp["L1_attn_b_down"][None, :]),
            wred=np.ascontiguousarray(inp["L1_ffn_w_red"]),
            wg=np.ascontiguousarray(inp["L1_ffn_wg"]),
            w1f=np.ascontiguousarray(inp["L1_ffn_w1"]),
            b1c=b1cols,
            w2f=np.ascontiguousarray(inp["L1_ffn_w2"]),
            b2r=np.ascontiguousarray(inp["L1_ffn_b2"][:, None, :]),
            identf=np.eye(128, dtype=np.float32)))
    return maps


def build_l1(T):
    TC = T // 4
    NSL = TC // 128
    NKT = T // 512
    nc = bass.Bass("TRN2", target_bir_lowering=False)
    hT = nc.dram_tensor("hT", [E, T], F32, kind="ExternalInput").ap()
    hqT = nc.dram_tensor("hqT", [E, TC], F32, kind="ExternalInput").ap()
    htm = nc.dram_tensor("htm", [T, E], F32, kind="ExternalInput").ap()
    wqT = nc.dram_tensor("wqT", [E, E], F32, kind="ExternalInput").ap()
    wkT = nc.dram_tensor("wkT", [E, E], F32, kind="ExternalInput").ap()
    bqc = nc.dram_tensor("bqc", [128, 8], F32, kind="ExternalInput").ap()
    bkc = nc.dram_tensor("bkc", [128, 8], F32, kind="ExternalInput").ap()
    gum = nc.dram_tensor("gum", [TC, T], F32, kind="ExternalInput").ap()
    mask4 = nc.dram_tensor("mask4", [128, 2048], F32, kind="ExternalInput").ap()
    rowinv = nc.dram_tensor("rowinv", [128, 16], F32, kind="ExternalInput").ap()
    identf = nc.dram_tensor("identf", [128, 128], F32, kind="ExternalInput").ap()
    attno = nc.dram_tensor("attn", [TC, E], F32, kind="ExternalOutput").ap()
    kTd = nc.dram_tensor("kTd", [8, 128, T], BF16).ap()

    seq = Seq(nc)
    es = ExitStack()
    sb = lambda n, s, d=F32: es.enter_context(nc.sbuf_tensor(n, s, d)).ap()
    wS = [sb(f"wS{i}", [128, E], BF16) for i in range(8)]
    hq = [sb(f"hq{i}", [128, TC], BF16) for i in range(8)]
    qT = [sb(f"qT{i}", [128, TC], BF16) for i in range(8)]
    kt = sb("kt", [128, 8 * 512], BF16)
    kev = sb("kev", [128, 512], BF16)
    hv = sb("hv", [128, 4 * 1024], BF16)
    avg = sb("avg", [128, T])
    mskS = sb("mskS", [128, 2048])
    gmc = sb("gmc", [128, 512])
    exs = sb("exs", [128, 512])
    wbf = sb("wbf", [128, 512], BF16)
    wT = sb("wT", [128, 512], BF16)
    oev = sb("oev", [128, E])
    mrun = sb("mrun", [128, 16])
    zrun = sb("zrun", [128, 16])
    mnew = sb("mnew", [128, 4])
    nmnew = sb("nmnew", [128, 4])
    bs4 = sb("bs4", [128, 16])
    dsc = sb("dsc", [128, 4])
    bias2 = sb("bias2", [128, 16])
    sS = sb("sS", [128, 4])
    bqs = sb("bqs", [128, 8])
    bks = sb("bks", [128, 8])
    rinv = sb("rinv", [128, 16])
    cols = sb("cols", [128, 4])
    sident = sb("sident", [128, 128])
    identb = sb("identb", [128, 128], BF16)
    p_s = es.enter_context(nc.psum_tensor("p_s", [128, 2048], F32)).ap()
    p_av = es.enter_context(nc.psum_tensor("p_av", [128, 1024], F32)).ap()
    p_t = es.enter_context(nc.psum_tensor("p_t", [128, 128], BF16)).ap()

    LN16 = float(np.log(16.0))
    with (nc.semaphore("step") as step, nc.Block() as block):
        seq.dma("sync", sident[:, :], identf)
        seq.dma("gpsimd", identb[:, :], identf)
        seq.dma("sync", mskS[:, :], mask4)
        seq.dma("sync", bqs[:, :], bqc)
        seq.dma("sync", bks[:, :], bkc)
        seq.dma("sync", rinv[:, :], rowinv)
        seq.op("vector", lambda v: v.memset(cols[:, 0:1], 0.125))
        seq.op("vector", lambda v: v.memset(cols[:, 1:2], 0.5))
        seq.op("vector", lambda v: v.memset(avg[:, :], 0.0))
        C18, C05 = cols[:, 0:1], cols[:, 1:2]
        seq.op("vector", lambda v: v.tensor_scalar(bqs[:, :], bqs[:, :], 0.125, None, ALU.mult))
        for ic in range(8):
            seq.dma("gpsimd", wS[ic][:, :], wqT[ic * 128:(ic + 1) * 128, :])
            seq.dma("gpsimd", hq[ic][:, :], hqT[ic * 128:(ic + 1) * 128, :])
        NTS = ceil_div(TC, 512)
        TSZ = min(TC, 512)
        for of in range(8):
            for ts in range(NTS):
                for ic in range(8):
                    seq.op("tensor", lambda t, ic=ic, of=of, ts=ts: t.matmul(p_s[:, 0:TSZ], wS[ic][:, of * 128:(of + 1) * 128], hq[ic][:, ts * TSZ:(ts + 1) * TSZ], start=(ic == 0), stop=(ic == 7)))
                seq.op("scalar", lambda s, of=of, ts=ts: s.activation(qT[of][:, ts * TSZ:(ts + 1) * TSZ], p_s[:, 0:TSZ], AF.Identity, bias=bqs[:, of:of + 1], scale=C18))
        for ic in range(8):
            seq.dma("gpsimd", wS[ic][:, :], wkT[ic * 128:(ic + 1) * 128, :])
        for tc in range(NKT):
            seq.dma("gpsimd", kt.rearrange("p (i k) -> p i k", i=8),
                    hT.rearrange("(i p) t -> p i t", p=128)[:, :, tc * 512:(tc + 1) * 512])
            for of in range(8):
                for ic in range(8):
                    seq.op("tensor", lambda t, ic=ic, of=of: t.matmul(p_s[:, 0:512], wS[ic][:, of * 128:(of + 1) * 128], kt[:, ic * 512:(ic + 1) * 512], start=(ic == 0), stop=(ic == 7)))
                seq.op("scalar", lambda s, of=of: s.activation(kev[:, :], p_s[:, 0:512], AF.Identity, bias=bks[:, of:of + 1]))
                seq.dma("sync", kTd[of, :, tc * 512:(tc + 1) * 512], kev[:, :])
        for sl in range(NSL):
            nkc = sl + 1
            seq.op("vector", lambda v: v.memset(mrun[:, :], -1e30))
            seq.op("vector", lambda v: v.memset(zrun[:, :], 0.0))
            for kc in range(nkc):
                seq.dma("sync", kt.rearrange("p (i k) -> p i k", i=8),
                        kTd.rearrange("i p t -> p i t")[:, :, kc * 512:(kc + 1) * 512])
                for hg in range(4):
                    for h4 in range(4):
                        h = hg * 4 + h4
                        of, half = h // 2, (h % 2) * 64
                        seq.op("tensor", lambda t, of=of, half=half, h4=h4, sl=sl: t.matmul(
                            p_s[:, h4 * 512:(h4 + 1) * 512],
                            qT[of][half:half + 64, sl * 128:(sl + 1) * 128],
                            kt[half:half + 64, of * 512:(of + 1) * 512], start=True, stop=True))
                    if kc == nkc - 1:
                        seq.op("vector", lambda v: v.tensor_tensor(p_s[:, :], p_s[:, :], mskS[:, :], ALU.add))
                    seq.op("vector", lambda v: v.tensor_reduce(dsc[:, 0:4], p_s.rearrange("p (h k) -> p h k", h=4), AX.X, ALU.max))
                    seq.op("vector", lambda v, hg=hg: v.tensor_tensor(mnew[:, :], mrun[:, hg * 4:(hg + 1) * 4], dsc[:, :], ALU.max))
                    seq.op("vector", lambda v: v.tensor_scalar(nmnew[:, :], mnew[:, :], -1.0, None, ALU.mult))
                    for h4 in range(4):
                        seq.op("scalar", lambda s, h4=h4, hg=hg: s.activation(exs[:, :], p_s[:, h4 * 512:(h4 + 1) * 512], AF.Exp, bias=nmnew[:, h4:h4 + 1], accum_out=bs4[:, hg * 4 + h4:hg * 4 + h4 + 1]))
                    seq.op("vector", lambda v, hg=hg: v.tensor_tensor(dsc[:, :], mrun[:, hg * 4:(hg + 1) * 4], mnew[:, :], ALU.subtract))
                    seq.op("scalar", lambda s: s.activation(dsc[:, :], dsc[:, :], AF.Exp))
                    seq.op("vector", lambda v, hg=hg: v.tensor_tensor(zrun[:, hg * 4:(hg + 1) * 4], zrun[:, hg * 4:(hg + 1) * 4], dsc[:, :], ALU.mult))
                    seq.op("vector", lambda v, hg=hg: v.tensor_tensor(zrun[:, hg * 4:(hg + 1) * 4], zrun[:, hg * 4:(hg + 1) * 4], bs4[:, hg * 4:(hg + 1) * 4], ALU.add))
                    seq.op("vector", lambda v, hg=hg: v.tensor_copy(mrun[:, hg * 4:(hg + 1) * 4], mnew[:, :]))
            seq.op("scalar", lambda s: s.activation(bias2[:, :], zrun[:, :], AF.Ln))
            seq.op("vector", lambda v: v.tensor_tensor(bias2[:, :], bias2[:, :], mrun[:, :], ALU.add))
            seq.op("vector", lambda v: v.tensor_scalar(bias2[:, :], bias2[:, :], LN16, -1.0, ALU.add, ALU.mult))
            for kc in range(nkc):
                seq.dma("sync", kt.rearrange("p (i k) -> p i k", i=8),
                        kTd.rearrange("i p t -> p i t")[:, :, kc * 512:(kc + 1) * 512])
                for hg in range(4):
                    for h4 in range(4):
                        h = hg * 4 + h4
                        of, half = h // 2, (h % 2) * 64
                        seq.op("tensor", lambda t, of=of, half=half, h4=h4, sl=sl: t.matmul(
                            p_s[:, h4 * 512:(h4 + 1) * 512],
                            qT[of][half:half + 64, sl * 128:(sl + 1) * 128],
                            kt[half:half + 64, of * 512:(of + 1) * 512], start=True, stop=True))
                    if kc == nkc - 1:
                        seq.op("vector", lambda v: v.tensor_tensor(p_s[:, :], p_s[:, :], mskS[:, :], ALU.add))
                    for h4 in range(4):
                        h = hg * 4 + h4
                        seq.op("scalar", lambda s, h4=h4, h=h: s.activation(exs[:, :], p_s[:, h4 * 512:(h4 + 1) * 512], AF.Exp, bias=bias2[:, h:h + 1]))
                        if h == 0:
                            seq.op("vector", lambda v, kc=kc: v.tensor_copy(avg[:, kc * 512:(kc + 1) * 512], exs[:, :]))
                        else:
                            seq.op("vector", lambda v, kc=kc: v.tensor_tensor(avg[:, kc * 512:(kc + 1) * 512], avg[:, kc * 512:(kc + 1) * 512], exs[:, :], ALU.add))
            for kc in range(nkc):
                seq.dma("sync", gmc[:, :], gum[sl * 128:(sl + 1) * 128, kc * 512:(kc + 1) * 512])
                seq.op("vector", lambda v, kc=kc: v.tensor_tensor(exs[:, :], avg[:, kc * 512:(kc + 1) * 512], gmc[:, :], ALU.add))
                seq.op("scalar", lambda s: s.activation(wbf[:, :], exs[:, :], AF.Exp, scale=C05, accum_out=sS[:, 0:1]))
                if kc == 0:
                    seq.op("vector", lambda v: v.tensor_copy(sS[:, 1:2], sS[:, 0:1]))
                else:
                    seq.op("vector", lambda v: v.tensor_tensor(sS[:, 1:2], sS[:, 1:2], sS[:, 0:1], ALU.add))
                for sub in range(4):
                    seq.op("tensor", lambda t, sub=sub: t.transpose(p_t[:, :], wbf[:, sub * 128:(sub + 1) * 128], identb[:, :]))
                    seq.op("vector", lambda v, sub=sub: v.tensor_copy(wT[:, sub * 128:(sub + 1) * 128], p_t[:, :]))
                seq.dma("gpsimd", hv.rearrange("p (s e) -> p s e", s=4),
                        htm.rearrange("(c s p) e -> p s e c", p=128, s=4)[:, :, :, kc].squeeze())
                for sub in range(4):
                    for oh in range(2):
                        seq.op("tensor", lambda t, sub=sub, oh=oh, kc=kc, nkc=nkc: t.matmul(
                            p_av[:, oh * 512:(oh + 1) * 512],
                            wT[:, sub * 128:(sub + 1) * 128],
                            hv[:, sub * 1024 + oh * 512:sub * 1024 + (oh + 1) * 512],
                            start=(kc == 0 and sub == 0), stop=(kc == nkc - 1 and sub == 3)))
            seq.op("vector", lambda v, sl=sl: v.tensor_tensor(sS[:, 1:2], sS[:, 1:2], rinv[:, sl:sl + 1], ALU.add))
            seq.op("vector", lambda v: v.reciprocal(sS[:, 2:3], sS[:, 1:2]))
            seq.op("vector", lambda v, sl=sl: v.tensor_scalar(sS[:, 3:4], rinv[:, sl:sl + 1], -1.0, 1.0, ALU.mult, ALU.add))
            seq.op("vector", lambda v: v.tensor_scalar(oev[:, :], p_av[:, :], sS[:, 2:3], sS[:, 3:4], ALU.mult, ALU.mult))
            seq.dma("sync", attno[sl * 128:(sl + 1) * 128, :], oev[:, :])
        seq.build(block, step)
    es.close()
    return nc

def host_l1_inputs(T, inp, gumbel):
    ipw = inp["L0_attn_in_proj_weight"]
    ipb = inp["L0_attn_in_proj_bias"]
    x = inp["L0_attn_hidden_states"]
    wqT = np.ascontiguousarray(ipw[:E].T)
    wkT = np.ascontiguousarray(ipw[E:2 * E].T)
    bq = ipb[:E]; bk = ipb[E:2 * E]
    ident = np.eye(128, dtype=np.float32)
    maps = []
    for c in range(8):
        b, j = c // 4, c % 4
        rows = core_token_rows(T, c)
        g = gumbel[b][rows].copy()
        kk = np.arange(T)[None, :]
        g[kk >= rows[:, None]] = NEG
        m4 = np.zeros((128, 512), np.float32)
        p = np.arange(128)[:, None]
        m4[np.arange(512)[None, :] >= (128 * j + p)] = NEG
        mask4 = np.tile(m4, (1, 4))
        rowinv = np.zeros((128, 16), np.float32)
        if j == 0:
            rowinv[0, 0] = 1.0
        maps.append(dict(
            hT=np.ascontiguousarray(x[b].T),
            hqT=np.ascontiguousarray(x[b][rows].T),
            htm=np.ascontiguousarray(x[b]),
            wqT=wqT, wkT=wkT,
            bqc=np.ascontiguousarray(bq.reshape(8, 128).T),
            bkc=np.ascontiguousarray(bk.reshape(8, 128).T),
            gum=np.ascontiguousarray(g),
            mask4=np.ascontiguousarray(mask4),
            rowinv=rowinv, identf=ident))
    return maps

I32 = mybir.dt.int32
GROUPS = [[0, 1, 2, 3], [4, 5, 6, 7]]

def build_fused(T, phases=(1,2,3,4,5), with_cc=True, debug=False, seq_mode='serial', seq_waits=None):
    TC = T // 4
    NSL = TC // 128
    NKT = T // 512
    NTB = NSL
    NTS = ceil_div(TC, 512)
    TSZ = min(TC, 512)
    NF0 = FF0 // 128
    NHC = FF1 // 128
    TCH = min(1024, T)
    NCH = T // TCH
    NS2 = TCH // 512
    TSZ2 = min(1024, TC)
    NTCG = TC // TSZ2
    NS5 = TSZ2 // 512 if TSZ2 >= 512 else 1
    S5 = min(512, TSZ2)

    nc = bass.Bass("TRN2", target_bir_lowering=False)
    D = lambda n, s, dt=F32, kind=None: (nc.dram_tensor(n, s, dt, kind=kind) if kind else nc.dram_tensor(n, s, dt))
    hT = D("hT", [E, T], kind="ExternalInput").ap()
    hqT = D("hqT", [E, TC], kind="ExternalInput").ap()
    htm = D("htm", [T, E], kind="ExternalInput").ap()
    wqT = D("wqT", [E, E], kind="ExternalInput").ap()
    wkT = D("wkT", [E, E], kind="ExternalInput").ap()
    bqc = D("bqc", [128, 8], kind="ExternalInput").ap()
    bkc = D("bkc", [128, 8], kind="ExternalInput").ap()
    gum = D("gum", [TC, T], kind="ExternalInput").ap()
    mask4 = D("mask4", [128, 2048], kind="ExternalInput").ap()
    rowinv = D("rowinv", [128, 16], kind="ExternalInput").ap()
    identf = D("identf", [128, 128], kind="ExternalInput").ap()
    x0 = D("x0", [TC, E], kind="ExternalInput").ap()
    gateT = D("gateT", [E, NEXP], kind="ExternalInput").ap()
    w1T = D("w1T", [NEXP, E, FF0], kind="ExternalInput").ap()
    w3T = D("w3T", [NEXP, E, FF0], kind="ExternalInput").ap()
    w2T = D("w2T", [NEXP, FF0, E], kind="ExternalInput").ap()
    grow = D("grow", [1, E], kind="ExternalInput").ap()
    brow = D("brow", [1, E], kind="ExternalInput").ap()
    wup = D("wup", [E, 2 * DI], kind="ExternalInput").ap()
    bupc = D("bupc", [128, 8], kind="ExternalInput").ap()
    cwc = D("cwc", [128, 4, 3], kind="ExternalInput").ap()
    cbc = D("cbc", [128, 4], kind="ExternalInput").ap()
    wdel = D("wdel", [DI, 128], kind="ExternalInput").ap()
    bdc = D("bdc", [128, 1], kind="ExternalInput").ap()
    wB = D("wB", [DI, NS], kind="ExternalInput").ap()
    wC = D("wC", [DI, NS], kind="ExternalInput").ap()
    aarr = D("aarr", [128, NS], kind="ExternalInput").ap()
    dcol = D("dcol", [128, 1], kind="ExternalInput").ap()
    patD = D("patD", [128, 4 * 128], kind="ExternalInput").ap()
    patB = D("patB", [128, 32 * 128], kind="ExternalInput").ap()
    patY = D("patY", [128, 4 * 128], kind="ExternalInput").ap()
    selpat = D("selpat", [128, 4 * 128], kind="ExternalInput").ap()
    wdown = D("wdown", [DI, E], kind="ExternalInput").ap()
    bdrow = D("bdrow", [1, E], kind="ExternalInput").ap()
    wred = D("wred", [E, GATE], kind="ExternalInput").ap()
    wg = D("wg", [NEXP, GATE], kind="ExternalInput").ap()
    w1f = D("w1f", [NEXP, E, FF1], kind="ExternalInput").ap()
    b1c = D("b1c", [NEXP, 128, FF1 // 128], kind="ExternalInput").ap()
    w2f = D("w2f", [NEXP, FF1, E], kind="ExternalInput").ap()
    b2r = D("b2r", [NEXP, 1, E], kind="ExternalInput").ap()
    jval = D("jval", [1, 1], I32, kind="ExternalInput").ap()
    outT = D("out", [TC, E], F32, kind="ExternalOutput").ap()
    dk = "ExternalOutput" if debug else None
    kTd = D("kTd", [8, 128, T], BF16).ap()
    attnD = D("attnD", [TC, E], kind=dk).ap()
    x1d = D("x1d", [TC, E], kind=dk).ap()
    ydA = D("ydA", [TC, E], kind=dk).ap()
    x2d = D("x2d", [TC, E], kind=dk).ap()
    leftIn = nc.dram_tensor("leftIn", [DI, TC], F32)
    leftAG = nc.dram_tensor("leftAG", [4 * DI, TC], F32)
    rightD = D("rightD", [DI, TC], kind=dk).ap()
    ysIn = nc.dram_tensor("ysIn", [128, T], F32)
    ysAG = nc.dram_tensor("ysAG", [DI, T], F32)
    if debug:
        dbgL = nc.dram_tensor("dbgL", [DI, TC], F32, kind="ExternalOutput")
        dbgLA = nc.dram_tensor("dbgLA", [4 * DI, TC], F32, kind="ExternalOutput")
        dbgY = nc.dram_tensor("dbgY", [128, T], F32, kind="ExternalOutput")
        dbgYA = nc.dram_tensor("dbgYA", [DI, T], F32, kind="ExternalOutput")
    x3d = D("x3d", [TC, E], kind=dk).ap()
    ydB = D("ydB", [TC, E], kind=dk).ap()

    seq = Seq(nc, mode=seq_mode, waits=seq_waits)
    rj = nc.gpsimd.alloc_register("rj")
    rtmp = nc.gpsimd.alloc_register("rtmp")
    LN16 = float(np.log(16.0))

    _es_sems = ExitStack()
    make_seq_sems(nc, _es_sems)
    with (nc.Block() as block):
        def _phase1():
            with ExitStack() as es:
                sb = lambda n, s, d=F32: es.enter_context(nc.sbuf_tensor(n, s, d)).ap()
                wS = [sb(f"a_wS{i}", [128, E], BF16) for i in range(8)]
                hq = [sb(f"a_hq{i}", [128, TC], BF16) for i in range(8)]
                qT = [sb(f"a_qT{i}", [128, TC], BF16) for i in range(8)]
                kt = sb("a_kt", [128, 8 * 512], BF16)
                kev = sb("a_kev", [128, 512], BF16)
                hv = sb("a_hv", [128, 4 * 1024], BF16)
                avg = sb("a_avg", [128, T])
                mskS = sb("a_mskS", [128, 2048])
                gmc = sb("a_gmc", [128, 512])
                exs = sb("a_exs", [128, 512])
                wbf = sb("a_wbf", [128, 512], BF16)
                wT = sb("a_wT", [128, 512], BF16)
                oev = sb("a_oev", [128, E])
                mrun = sb("a_mrun", [128, 16])
                zrun = sb("a_zrun", [128, 16])
                mnew = sb("a_mnew", [128, 4])
                nmnew = sb("a_nmnew", [128, 4])
                bs4 = sb("a_bs4", [128, 16])
                dsc = sb("a_dsc", [128, 4])
                bias2 = sb("a_bias2", [128, 16])
                sS = sb("a_sS", [128, 4])
                bqs = sb("a_bqs", [128, 8])
                bks = sb("a_bks", [128, 8])
                rinv = sb("a_rinv", [128, 16])
                cols = sb("a_cols", [128, 4])
                sident = sb("a_sident", [128, 128])
                identb = sb("a_identb", [128, 128], BF16)
                jvS = sb("a_jv", [1, 1], I32)
                p_s = es.enter_context(nc.psum_tensor("a_ps", [128, 2048], F32)).ap()
                p_av = es.enter_context(nc.psum_tensor("a_pav", [128, 1024], F32)).ap()
                p_t = es.enter_context(nc.psum_tensor("a_pt", [128, 1024], BF16)).ap()[:, 0:128]

                seq.dma("sync", sident[:, :], identf)
                seq.dma("gpsimd", identb[:, :], identf)
                seq.dma("sync", mskS[:, :], mask4)
                seq.dma("sync", bqs[:, :], bqc)
                seq.dma("sync", bks[:, :], bkc)
                seq.dma("sync", rinv[:, :], rowinv)
                seq.dma("sync", jvS[:, :], jval)
                def _ld(g):
                    g.reg_load(rj, jvS[0:1, 0:1])
                    return g.engine_nop()
                seq.op("gpsimd", _ld, wait=seq.barrier(), no_inc=True)
                seq.op("vector", lambda v: v.memset(cols[:, 0:1], 0.125))
                seq.op("vector", lambda v: v.memset(cols[:, 1:2], 0.5))
                seq.op("vector", lambda v: v.memset(avg[:, :], 0.0))
                C18, C05 = cols[:, 0:1], cols[:, 1:2]
                seq.op("vector", lambda v: v.tensor_scalar(bqs[:, :], bqs[:, :], 0.125, None, ALU.mult))
                for ic in range(8):
                    seq.dma("gpsimd", wS[ic][:, :], wqT[ic * 128:(ic + 1) * 128, :])
                    seq.dma("gpsimd", hq[ic][:, :], hqT[ic * 128:(ic + 1) * 128, :])
                for of in range(8):
                    for ts in range(NTS):
                        for ic in range(8):
                            seq.op("tensor", lambda t, ic=ic, of=of, ts=ts: t.matmul(p_s[:, 0:TSZ], wS[ic][:, of * 128:(of + 1) * 128], hq[ic][:, ts * TSZ:(ts + 1) * TSZ], start=(ic == 0), stop=(ic == 7)))
                        seq.op("scalar", lambda s, of=of, ts=ts: s.activation(qT[of][:, ts * TSZ:(ts + 1) * TSZ], p_s[:, 0:TSZ], AF.Identity, bias=bqs[:, of:of + 1], scale=C18))
                for ic in range(8):
                    seq.dma("gpsimd", wS[ic][:, :], wkT[ic * 128:(ic + 1) * 128, :])
                for tc in range(NKT):
                    seq.dma("gpsimd", kt.rearrange("p (i k) -> p i k", i=8),
                            hT.rearrange("(i p) t -> p i t", p=128)[:, :, tc * 512:(tc + 1) * 512])
                    for of in range(8):
                        for ic in range(8):
                            seq.op("tensor", lambda t, ic=ic, of=of: t.matmul(p_s[:, 0:512], wS[ic][:, of * 128:(of + 1) * 128], kt[:, ic * 512:(ic + 1) * 512], start=(ic == 0), stop=(ic == 7)))
                        seq.op("scalar", lambda s, of=of: s.activation(kev[:, :], p_s[:, 0:512], AF.Identity, bias=bks[:, of:of + 1]))
                        seq.dma("sync", kTd[of, :, tc * 512:(tc + 1) * 512], kev[:, :])
                for sl in range(NSL):
                    nkc = sl + 1
                    seq.op("vector", lambda v: v.memset(mrun[:, :], -1e30))
                    seq.op("vector", lambda v: v.memset(zrun[:, :], 0.0))
                    for kc in range(nkc):
                        seq.dma("sync", kt.rearrange("p (i k) -> p i k", i=8),
                                kTd.rearrange("i p t -> p i t")[:, :, kc * 512:(kc + 1) * 512])
                        for hg in range(4):
                            for h4 in range(4):
                                h = hg * 4 + h4
                                of, half = h // 2, (h % 2) * 64
                                seq.op("tensor", lambda t, of=of, half=half, h4=h4, sl=sl: t.matmul(
                                    p_s[:, h4 * 512:(h4 + 1) * 512],
                                    qT[of][half:half + 64, sl * 128:(sl + 1) * 128],
                                    kt[half:half + 64, of * 512:(of + 1) * 512], start=True, stop=True))
                            if kc == nkc - 1:
                                seq.op("vector", lambda v: v.tensor_tensor(p_s[:, :], p_s[:, :], mskS[:, :], ALU.add))
                            seq.op("vector", lambda v: v.tensor_reduce(dsc[:, 0:4], p_s.rearrange("p (h k) -> p h k", h=4), AX.X, ALU.max))
                            seq.op("vector", lambda v, hg=hg: v.tensor_tensor(mnew[:, :], mrun[:, hg * 4:(hg + 1) * 4], dsc[:, :], ALU.max))
                            seq.op("vector", lambda v: v.tensor_scalar(nmnew[:, :], mnew[:, :], -1.0, None, ALU.mult))
                            for h4 in range(4):
                                seq.op("scalar", lambda s, h4=h4, hg=hg: s.activation(exs[:, :], p_s[:, h4 * 512:(h4 + 1) * 512], AF.Exp, bias=nmnew[:, h4:h4 + 1], accum_out=bs4[:, hg * 4 + h4:hg * 4 + h4 + 1]))
                            seq.op("vector", lambda v, hg=hg: v.tensor_tensor(dsc[:, :], mrun[:, hg * 4:(hg + 1) * 4], mnew[:, :], ALU.subtract))
                            seq.op("scalar", lambda s: s.activation(dsc[:, :], dsc[:, :], AF.Exp))
                            seq.op("vector", lambda v, hg=hg: v.tensor_tensor(zrun[:, hg * 4:(hg + 1) * 4], zrun[:, hg * 4:(hg + 1) * 4], dsc[:, :], ALU.mult))
                            seq.op("vector", lambda v, hg=hg: v.tensor_tensor(zrun[:, hg * 4:(hg + 1) * 4], zrun[:, hg * 4:(hg + 1) * 4], bs4[:, hg * 4:(hg + 1) * 4], ALU.add))
                            seq.op("vector", lambda v, hg=hg: v.tensor_copy(mrun[:, hg * 4:(hg + 1) * 4], mnew[:, :]))
                    seq.op("scalar", lambda s: s.activation(bias2[:, :], zrun[:, :], AF.Ln))
                    seq.op("vector", lambda v: v.tensor_tensor(bias2[:, :], bias2[:, :], mrun[:, :], ALU.add))
                    seq.op("vector", lambda v: v.tensor_scalar(bias2[:, :], bias2[:, :], LN16, -1.0, ALU.add, ALU.mult))
                    for kc in range(nkc):
                        seq.dma("sync", kt.rearrange("p (i k) -> p i k", i=8),
                                kTd.rearrange("i p t -> p i t")[:, :, kc * 512:(kc + 1) * 512])
                        for hg in range(4):
                            for h4 in range(4):
                                h = hg * 4 + h4
                                of, half = h // 2, (h % 2) * 64
                                seq.op("tensor", lambda t, of=of, half=half, h4=h4, sl=sl: t.matmul(
                                    p_s[:, h4 * 512:(h4 + 1) * 512],
                                    qT[of][half:half + 64, sl * 128:(sl + 1) * 128],
                                    kt[half:half + 64, of * 512:(of + 1) * 512], start=True, stop=True))
                            if kc == nkc - 1:
                                seq.op("vector", lambda v: v.tensor_tensor(p_s[:, :], p_s[:, :], mskS[:, :], ALU.add))
                            for h4 in range(4):
                                h = hg * 4 + h4
                                seq.op("scalar", lambda s, h4=h4, h=h: s.activation(exs[:, :], p_s[:, h4 * 512:(h4 + 1) * 512], AF.Exp, bias=bias2[:, h:h + 1]))
                                if h == 0:
                                    seq.op("vector", lambda v, kc=kc: v.tensor_copy(avg[:, kc * 512:(kc + 1) * 512], exs[:, :]))
                                else:
                                    seq.op("vector", lambda v, kc=kc: v.tensor_tensor(avg[:, kc * 512:(kc + 1) * 512], avg[:, kc * 512:(kc + 1) * 512], exs[:, :], ALU.add))
                    for kc in range(nkc):
                        seq.dma("sync", gmc[:, :], gum[sl * 128:(sl + 1) * 128, kc * 512:(kc + 1) * 512])
                        seq.op("vector", lambda v, kc=kc: v.tensor_tensor(exs[:, :], avg[:, kc * 512:(kc + 1) * 512], gmc[:, :], ALU.add))
                        seq.op("scalar", lambda s: s.activation(wbf[:, :], exs[:, :], AF.Exp, scale=C05, accum_out=sS[:, 0:1]))
                        if kc == 0:
                            seq.op("vector", lambda v: v.tensor_copy(sS[:, 1:2], sS[:, 0:1]))
                        else:
                            seq.op("vector", lambda v: v.tensor_tensor(sS[:, 1:2], sS[:, 1:2], sS[:, 0:1], ALU.add))
                        for sub in range(4):
                            seq.op("tensor", lambda t, sub=sub: t.transpose(p_t[:, :], wbf[:, sub * 128:(sub + 1) * 128], identb[:, :]))
                            seq.op("vector", lambda v, sub=sub: v.tensor_copy(wT[:, sub * 128:(sub + 1) * 128], p_t[:, :]))
                        seq.dma("gpsimd", hv.rearrange("p (s e) -> p s e", s=4),
                                htm.rearrange("(c s p) e -> p s e c", p=128, s=4)[:, :, :, kc].squeeze())
                        for sub in range(4):
                            for oh in range(2):
                                seq.op("tensor", lambda t, sub=sub, oh=oh, kc=kc, nkc=nkc: t.matmul(
                                    p_av[:, oh * 512:(oh + 1) * 512],
                                    wT[:, sub * 128:(sub + 1) * 128],
                                    hv[:, sub * 1024 + oh * 512:sub * 1024 + (oh + 1) * 512],
                                    start=(kc == 0 and sub == 0), stop=(kc == nkc - 1 and sub == 3)))
                    seq.op("vector", lambda v, sl=sl: v.tensor_tensor(sS[:, 1:2], sS[:, 1:2], rinv[:, sl:sl + 1], ALU.add))
                    seq.op("vector", lambda v: v.reciprocal(sS[:, 2:3], sS[:, 1:2]))
                    seq.op("vector", lambda v, sl=sl: v.tensor_scalar(sS[:, 3:4], rinv[:, sl:sl + 1], -1.0, 1.0, ALU.mult, ALU.add))
                    seq.op("vector", lambda v: v.tensor_scalar(oev[:, :], p_av[:, :], sS[:, 2:3], sS[:, 3:4], ALU.mult, ALU.mult))
                    seq.dma("sync", attnD[sl * 128:(sl + 1) * 128, :], oev[:, :])

        if 1 in phases: _phase1()
        def _phase2():
            with ExitStack() as es:
                sb = lambda n, s, d=F32: es.enter_context(nc.sbuf_tensor(n, s, d)).ap()
                xbf = [sb(f"b_xbf{i}", [128, TC], BF16) for i in range(8)]
                tmA = sb("b_tmA", [128, E])
                tmB = sb("b_tmB", [128, E])
                wte = sb("b_wte", [128, NTB * NEXP])
                sident = sb("b_sident", [128, 128])
                gat = sb("b_gat", [128, 64], BF16)
                gatf = sb("b_gatf", [128, 64])
                xtr = sb("b_xtr", [128, 128])
                wArena = sb("b_wArena", [128, 8 * FF0], BF16)
                wA = [wArena[:, i * FF0:(i + 1) * FF0] for i in range(8)]
                w2s = [wArena[:, i * E:(i + 1) * E] for i in range(22)]
                hA = [sb(f"b_hA{i}", [128, TC], BF16) for i in range(22)]
                cols = sb("b_cols", [128, 8])
                st1 = sb("b_st1", [128, 16])
                yev = sb("b_yev", [128, 512])
                p_tr = es.enter_context(nc.psum_tensor("b_ptr", [128, 512], F32)).ap()[:, 0:128]
                p_g = es.enter_context(nc.psum_tensor("b_pg", [128, 512], F32)).ap()[:, 0:16]
                p_a = es.enter_context(nc.psum_tensor("b_pa", [128, 512], F32)).ap()
                p_b = es.enter_context(nc.psum_tensor("b_pb", [128, 512], F32)).ap()

                seq.dma("sync", sident[:, :], identf)
                seq.dma("gpsimd", gat.rearrange("p (i e) -> p i e", i=8), gateT.rearrange("(i p) e -> p i e", p=128))
                seq.dma("sync", gatf.rearrange("p (i e) -> p i e", i=8), gateT.rearrange("(i p) e -> p i e", p=128))
                seq.op("vector", lambda v: v.memset(cols[:, 0:1], 1e-5))
                EPS = cols[:, 0:1]

                def ln_tb(tb, src_emitters, out_dram, transpose_to=None, extra=None, do_gating=False, out_dram2=None):
                    for fn in src_emitters:
                        fn()
                    if extra is not None:
                        extra()
                    b0 = seq.barrier()
                    seq.op("vector", lambda v: v.tensor_reduce(st1[:, 0:1], tmA[:, :], AX.XYZW, ALU.add), wait=b0)
                    seq.op("scalar", lambda s: s.activation(tmB[:, :], tmA[:, :], AF.Square), wait=b0)
                    seq.op("vector", lambda v: v.tensor_reduce(st1[:, 1:2], tmB[:, :], AX.XYZW, ALU.add))
                    seq.op("vector", lambda v: v.tensor_scalar(st1[:, 2:3], st1[:, 0:1], 1.0 / E, None, ALU.mult))
                    seq.op("vector", lambda v: v.tensor_scalar(st1[:, 3:4], st1[:, 1:2], 1.0 / E, None, ALU.mult))
                    seq.op("vector", lambda v: v.tensor_tensor(st1[:, 4:5], st1[:, 2:3], st1[:, 2:3], ALU.mult))
                    seq.op("vector", lambda v: v.tensor_tensor(st1[:, 5:6], st1[:, 3:4], st1[:, 4:5], ALU.subtract))
                    seq.op("scalar", lambda s: s.activation(st1[:, 6:7], st1[:, 5:6], AF.Sqrt, bias=EPS))
                    seq.op("vector", lambda v: v.reciprocal(st1[:, 7:8], st1[:, 6:7]))
                    seq.op("vector", lambda v: v.tensor_scalar(tmA[:, :], tmA[:, :], st1[:, 2:3], st1[:, 7:8], ALU.subtract, ALU.mult))
                    if out_dram is not None:
                        seq.dma("sync", out_dram[tb * 128:(tb + 1) * 128, :], tmA[:, :])
                    if out_dram2 is not None:
                        seq.dma("sync", out_dram2[tb * 128:(tb + 1) * 128, :], tmA[:, :])
                    if transpose_to is not None:
                        for fc in range(8):
                            seq.op("tensor", lambda t, fc=fc: t.transpose(p_tr[:, :], tmA[:, fc * 128:(fc + 1) * 128], sident[:, :]))
                            seq.op("vector", lambda v, fc=fc, tb=tb: v.tensor_copy(transpose_to[fc][:, tb * 128:(tb + 1) * 128], p_tr[:, :]))
                            if do_gating:
                                seq.op("vector", lambda v, fc=fc: v.tensor_copy(xtr[:, :], p_tr[:, :]))
                                seq.op("tensor", lambda t, fc=fc: t.matmul(p_g[:, 0:8], xtr[:, :], gatf[:, fc * 8:(fc + 1) * 8], start=(fc == 0), stop=(fc == 7)))

                def gating_post(tb):
                    seq.op("vector", lambda v: v.tensor_copy(st1[:, 0:8], p_g[:, 0:8]))
                    seq.op("vector", lambda v: v.reduce_max(st1[:, 8:9], st1[:, 0:8], AX.XYZW, negate=True))
                    seq.op("scalar", lambda s: s.activation(st1[:, 0:8], st1[:, 0:8], AF.Exp, bias=st1[:, 8:9], accum_out=st1[:, 9:10]))
                    seq.op("vector", lambda v: v.reciprocal(st1[:, 10:11], st1[:, 9:10]))
                    seq.op("vector", lambda v: v.tensor_scalar(st1[:, 0:8], st1[:, 0:8], st1[:, 10:11], None, ALU.mult))
                    seq.op("vector", lambda v: v.reduce_max(st1[:, 11:12], st1[:, 0:8], AX.XYZW))
                    seq.op("vector", lambda v: v.tensor_scalar(tmB[:, 0:8], st1[:, 0:8], st1[:, 11:12], None, ALU.is_ge))
                    seq.op("vector", lambda v: v.scalar_tensor_tensor(tmB[:, 8:16], tmB[:, 0:8], -2.0, st1[:, 0:8], ALU.mult, ALU.add))
                    seq.op("vector", lambda v: v.reduce_max(st1[:, 13:14], tmB[:, 8:16], AX.XYZW))
                    seq.op("vector", lambda v: v.tensor_scalar(tmB[:, 0:8], st1[:, 0:8], st1[:, 13:14], None, ALU.is_ge))
                    seq.op("vector", lambda v: v.tensor_tensor(st1[:, 14:15], st1[:, 11:12], st1[:, 13:14], ALU.add))
                    seq.op("vector", lambda v: v.reciprocal(st1[:, 15:16], st1[:, 14:15]))
                    seq.op("vector", lambda v: v.tensor_tensor(tmB[:, 0:8], tmB[:, 0:8], st1[:, 0:8], ALU.mult))
                    seq.op("vector", lambda v, tb=tb: v.tensor_scalar(wte[:, tb * 8:(tb + 1) * 8], tmB[:, 0:8], st1[:, 15:16], None, ALU.mult))

                for tb in range(NTB):
                    ln_tb(tb,
                          [lambda tb=tb: seq.dma("sync", tmA[:, :], x0[tb * 128:tb * 128 + 128, :]),
                           lambda tb=tb: seq.dma("sync", tmB[:, :], attnD[tb * 128:tb * 128 + 128, :])],
                          x1d, transpose_to=xbf, do_gating=True,
                          extra=lambda: seq.op("vector", lambda v: v.tensor_tensor(tmA[:, :], tmA[:, :], tmB[:, :], ALU.add)))
                    gating_post(tb)

                for e in range(NEXP):
                    for ic in range(8):
                        seq.dma("gpsimd", wA[ic], w1T[e, ic * 128:(ic + 1) * 128, :])
                    for hc in range(NF0):
                        for ts in range(NTS):
                            for ic in range(8):
                                seq.op("tensor", lambda t, ic=ic, hc=hc, ts=ts: t.matmul(p_a[:, 0:TSZ], wA[ic][:, hc * 128:(hc + 1) * 128], xbf[ic][:, ts * TSZ:(ts + 1) * TSZ], start=(ic == 0), stop=(ic == 7)))
                            seq.op("scalar", lambda s, hc=hc, ts=ts: s.activation(hA[hc][:, ts * TSZ:(ts + 1) * TSZ], p_a[:, 0:TSZ], AF.Silu))
                    for ic in range(8):
                        seq.dma("gpsimd", wA[ic], w3T[e, ic * 128:(ic + 1) * 128, :])
                    for hc in range(NF0):
                        for ts in range(NTS):
                            for ic in range(8):
                                seq.op("tensor", lambda t, ic=ic, hc=hc, ts=ts: t.matmul(p_a[:, 0:TSZ], wA[ic][:, hc * 128:(hc + 1) * 128], xbf[ic][:, ts * TSZ:(ts + 1) * TSZ], start=(ic == 0), stop=(ic == 7)))
                            seq.op("vector", lambda v, hc=hc, ts=ts: v.tensor_tensor(hA[hc][:, ts * TSZ:(ts + 1) * TSZ], hA[hc][:, ts * TSZ:(ts + 1) * TSZ], p_a[:, 0:TSZ], ALU.mult))
                    for hc in range(NF0):
                        seq.dma("gpsimd", w2s[hc], w2T[e, hc * 128:(hc + 1) * 128, :])
                    for tb in range(NTB):
                        for oh in range(2):
                            for hc in range(NF0):
                                seq.op("tensor", lambda t, hc=hc, tb=tb, oh=oh: t.matmul(p_b[:, 0:512], hA[hc][:, tb * 128:(tb + 1) * 128], w2s[hc][:, oh * 512:(oh + 1) * 512], start=(hc == 0), stop=(hc == NF0 - 1)))
                            seq.op("vector", lambda v, tb=tb, e=e: v.tensor_scalar(yev[:, 0:512], p_b[:, 0:512], wte[:, tb * 8 + e:tb * 8 + e + 1], None, ALU.mult))
                            if e == 0:
                                seq.dma("gpsimd", ydA[tb * 128:(tb + 1) * 128, oh * 512:(oh + 1) * 512], yev[:, 0:512])
                            else:
                                seq.dma("gpsimd", ydA[tb * 128:(tb + 1) * 128, oh * 512:(oh + 1) * 512], yev[:, 0:512], accum_op=ALU.add)

                for tb in range(NTB):
                    ln_tb(tb,
                          [lambda tb=tb: seq.dma("sync", tmA[:, :], x1d[tb * 128:tb * 128 + 128, :]),
                           lambda tb=tb: seq.dma("sync", tmB[:, :], ydA[tb * 128:tb * 128 + 128, :])],
                          x2d, transpose_to=None,
                          extra=lambda: seq.op("vector", lambda v: v.tensor_tensor(tmA[:, :], tmA[:, :], tmB[:, :], ALU.add)))

        if 2 in phases: _phase2()
        def _phase3():
            with ExitStack() as es:
                sb = lambda n, s, d=F32: es.enter_context(nc.sbuf_tensor(n, s, d)).ap()
                xbf = [sb(f"c_xbf{i}", [128, TC], BF16) for i in range(8)]
                tmA = sb("c_tmA", [128, E])
                tmB = sb("c_tmB", [128, E])
                grep = sb("c_grep", [128, E])
                brep = sb("c_brep", [128, E])
                growS = sb("c_growS", [1, E])
                browS = sb("c_browS", [1, E])
                sones = sb("c_sones", [1, 128])
                sident = sb("c_sident", [128, 128])
                wupS = [sb(f"c_wupS{i}", [128, 2 * DI], BF16) for i in range(8)]
                bupS = sb("c_bupS", [128, 8])
                st1 = sb("c_st1", [128, 16])
                cols = sb("c_cols", [128, 1])
                oev = sb("c_oev", [128, 512])
                p_tr = es.enter_context(nc.psum_tensor("c_ptr", [128, 512], F32)).ap()[:, 0:128]
                p_a = es.enter_context(nc.psum_tensor("c_pa", [128, 512], F32)).ap()
                p_r = es.enter_context(nc.psum_tensor("c_pr", [128, 512], F32)).ap()

                seq.dma("sync", sident[:, :], identf)
                seq.dma("sync", growS[:, :], grow)
                seq.dma("sync", browS[:, :], brow)
                seq.dma("sync", bupS[:, :], bupc)
                for ic in range(8):
                    seq.dma("gpsimd", wupS[ic][:, :], wup[ic * 128:(ic + 1) * 128, :])
                seq.op("vector", lambda v: v.memset(cols[:, 0:1], 1e-5))
                seq.op("vector", lambda v: v.memset(sones[:, :], 1.0))
                for half in range(2):
                    seq.op("tensor", lambda t, half=half: t.matmul(p_r[:, 0:512], sones[0:1, :], growS[0:1, half * 512:(half + 1) * 512], start=True, stop=True))
                    seq.op("vector", lambda v, half=half: v.tensor_copy(grep[:, half * 512:(half + 1) * 512], p_r[:, 0:512]))
                    seq.op("tensor", lambda t, half=half: t.matmul(p_r[:, 0:512], sones[0:1, :], browS[0:1, half * 512:(half + 1) * 512], start=True, stop=True))
                    seq.op("vector", lambda v, half=half: v.tensor_copy(brep[:, half * 512:(half + 1) * 512], p_r[:, 0:512]))
                for tb in range(NTB):
                    seq.dma("sync", tmA[:, :], x2d[tb * 128:tb * 128 + 128, :])
                    seq.op("vector", lambda v: v.tensor_reduce(st1[:, 0:1], tmA[:, :], AX.XYZW, ALU.add))
                    seq.op("scalar", lambda s: s.activation(tmB[:, :], tmA[:, :], AF.Square))
                    seq.op("vector", lambda v: v.tensor_reduce(st1[:, 1:2], tmB[:, :], AX.XYZW, ALU.add))
                    seq.op("vector", lambda v: v.tensor_scalar(st1[:, 2:3], st1[:, 0:1], 1.0 / E, None, ALU.mult))
                    seq.op("vector", lambda v: v.tensor_scalar(st1[:, 3:4], st1[:, 1:2], 1.0 / E, None, ALU.mult))
                    seq.op("vector", lambda v: v.tensor_tensor(st1[:, 4:5], st1[:, 2:3], st1[:, 2:3], ALU.mult))
                    seq.op("vector", lambda v: v.tensor_tensor(st1[:, 5:6], st1[:, 3:4], st1[:, 4:5], ALU.subtract))
                    seq.op("scalar", lambda s: s.activation(st1[:, 6:7], st1[:, 5:6], AF.Sqrt, bias=cols[:, 0:1]))
                    seq.op("vector", lambda v: v.reciprocal(st1[:, 7:8], st1[:, 6:7]))
                    seq.op("vector", lambda v: v.tensor_scalar(tmA[:, :], tmA[:, :], st1[:, 2:3], st1[:, 7:8], ALU.subtract, ALU.mult))
                    seq.op("vector", lambda v: v.tensor_tensor(tmA[:, :], tmA[:, :], grep[:, :], ALU.mult))
                    seq.op("vector", lambda v: v.tensor_tensor(tmA[:, :], tmA[:, :], brep[:, :], ALU.add))
                    for fc in range(8):
                        seq.op("tensor", lambda t, fc=fc: t.transpose(p_tr[:, :], tmA[:, fc * 128:(fc + 1) * 128], sident[:, :]))
                        seq.op("vector", lambda v, fc=fc, tb=tb: v.tensor_copy(xbf[fc][:, tb * 128:(tb + 1) * 128], p_tr[:, :]))
                for of in range(8):
                    for ts in range(NTS):
                        for ic in range(8):
                            seq.op("tensor", lambda t, ic=ic, of=of, ts=ts: t.matmul(p_a[:, 0:TSZ], wupS[ic][:, of * 128:(of + 1) * 128], xbf[ic][:, ts * TSZ:(ts + 1) * TSZ], start=(ic == 0), stop=(ic == 7)))
                        seq.op("scalar", lambda s, of=of: s.activation(oev[:, 0:TSZ], p_a[:, 0:TSZ], AF.Identity, bias=bupS[:, of:of + 1]))
                        dst = leftIn.ap() if of < 4 else rightD
                        ofr = (of % 4) * 128
                        seq.dma("sync", dst[ofr:ofr + 128, ts * TSZ:(ts + 1) * TSZ], oev[:, 0:TSZ])

        if 3 in phases: _phase3()
        if with_cc: seq.cc(lambda g: g.collective_compute(
            "AllGather", ALU.bypass, replica_groups=GROUPS,
            ins=[leftIn.ap().opt()], outs=[leftAG.ap().opt()]))

        def _phase4():
            with ExitStack() as es:
                sb = lambda n, s, d=F32: es.enter_context(nc.sbuf_tensor(n, s, d)).ap()
                lt = [sb(f"d_lt{i}", [128, TCH + 2]) for i in range(4)]
                uf = [sb(f"d_uf{i}", [128, TCH]) for i in range(4)]
                ub = [sb(f"d_ub{i}", [128, TCH], BF16) for i in range(4)]
                uown = sb("d_uown", [128, TCH])
                wdS = [sb(f"d_wdS{i}", [128, 128], BF16) for i in range(4)]
                wbS = [sb(f"d_wbS{i}", [128, NS], BF16) for i in range(4)]
                wcS = [sb(f"d_wcS{i}", [128, NS], BF16) for i in range(4)]
                selS = sb("d_selS", [128, 4 * 128])
                acol = sb("d_acol", [128, NS])
                dcolS = sb("d_dcolS", [128, 1])
                bdcS = sb("d_bdcS", [128, 1])
                cwS = sb("d_cwS", [128, 12])
                cbS = sb("d_cbS", [128, 4])
                patDS = sb("d_patDS", [128, 4 * 128])
                patBS = sb("d_patBS", [128, 32 * 128], BF16)
                patYS = sb("d_patYS", [128, 4 * 128], BF16)
                deltaS = sb("d_deltaS", [128, TCH])
                wSt = sb("d_wSt", [128, TCH])
                BTb = [sb(f"d_BTb{i}", [128, TCH], BF16) for i in range(2)]
                CTb = [sb(f"d_CTb{i}", [128, TCH], BF16) for i in range(2)]
                dRep = [sb(f"d_dRep{i}", [128, TCH]) for i in range(4)]
                wRep = [sb(f"d_wRep{i}", [128, TCH]) for i in range(4)]
                dAS = sb("d_dAS", [128, TCH])
                bS = sb("d_bS", [128, TCH])
                hS = sb("d_hS", [128, TCH])
                ymulb = sb("d_ymulb", [128, TCH], BF16)
                hstate = sb("d_hstate", [128, NS])
                ysS = sb("d_ysS", [128, TCH])
                sc1 = sb("d_sc1", [128, TCH])
                sc2 = sb("d_sc2", [128, TCH])
                consts = sb("d_consts", [128, 4])
                p_d = es.enter_context(nc.psum_tensor("d_pd", [128, 512], F32)).ap()
                p_B = es.enter_context(nc.psum_tensor("d_pB", [128, TCH], F32)).ap()
                p_C = es.enter_context(nc.psum_tensor("d_pC", [128, TCH], F32)).ap()
                p_Y = es.enter_context(nc.psum_tensor("d_pY", [128, TCH], F32)).ap()

                seq.dma("sync", cwS.rearrange("p (f k) -> p f k", f=4), cwc)
                seq.dma("sync", cbS[:, :], cbc)
                seq.dma("sync", bdcS[:, :], bdc)
                seq.dma("sync", dcolS[:, :], dcol)
                seq.dma("sync", patDS[:, :], patD)
                seq.dma("sync", selS[:, :], selpat)
                seq.dma("gpsimd", patBS[:, :], patB)
                seq.dma("gpsimd", patYS[:, :], patY)
                for ic in range(4):
                    seq.dma("gpsimd", wdS[ic][:, :], wdel[ic * 128:(ic + 1) * 128, :])
                    seq.dma("gpsimd", wbS[ic][:, :], wB[ic * 128:(ic + 1) * 128, :])
                    seq.dma("gpsimd", wcS[ic][:, :], wC[ic * 128:(ic + 1) * 128, :])
                seq.op("vector", lambda v: v.memset(consts[:, 0:1], 1.0))
                seq.op("vector", lambda v: v.memset(consts[:, 1:2], -1.0))
                seq.op("vector", lambda v: v.memset(hstate[:, :], 0.0))
                C_ONE, C_M1 = consts[:, 0:1], consts[:, 1:2]
                seq.dma("sync", acol[:, :], aarr)
                seq.op("scalar", lambda s: s.activation(acol[:, :], acol[:, :], AF.Exp))
                seq.op("vector", lambda v: v.tensor_scalar(acol[:, :], acol[:, :], -1.0, None, ALU.mult))

                lag = leftAG.ap()
                for k in range(NCH):
                    ks = k * TCH
                    for fc in range(4):
                        if k == 0:
                            seq.op("vector", lambda v, fc=fc: v.memset(lt[fc][:, 0:2], 0.0))
                        else:
                            seq.op("vector", lambda v, fc=fc: v.tensor_copy(lt[fc][:, 0:2], lt[fc][:, TCH:TCH + 2]))
                        for r_ in range(4):
                            src = bass.AP(leftAG, (r_ * 512 + fc * 128) * TC + (k * (TCH // 512)) * 128,
                                          [[TC, 128], [128, TCH // 512], [1, 128]])
                            dst = lt[fc][:, 2:TCH + 2].rearrange("p (a rr q) -> p a rr q", a=TCH // 512, rr=4)[:, :, r_:r_ + 1, :].squeeze(2)
                            seq.dma("sync", dst, src)
                        seq.op("vector", lambda v, fc=fc: v.tensor_scalar(sc1[:, :], lt[fc][:, 2:TCH + 2], cwS[:, fc * 3 + 2:fc * 3 + 3], cbS[:, fc:fc + 1], ALU.mult, ALU.add))
                        seq.op("vector", lambda v, fc=fc: v.scalar_tensor_tensor(sc1[:, :], lt[fc][:, 1:TCH + 1], cwS[:, fc * 3 + 1:fc * 3 + 2], sc1[:, :], ALU.mult, ALU.add))
                        seq.op("vector", lambda v, fc=fc: v.scalar_tensor_tensor(sc1[:, :], lt[fc][:, 0:TCH], cwS[:, fc * 3 + 0:fc * 3 + 1], sc1[:, :], ALU.mult, ALU.add))
                        seq.op("scalar", lambda s, fc=fc: s.activation(uf[fc][:, :], sc1[:, :], AF.Silu))
                        seq.op("vector", lambda v, fc=fc: v.tensor_copy(ub[fc][:, :], uf[fc][:, :]))
                    for sub in range(NS2):
                        for ic in range(4):
                            seq.op("tensor", lambda t, ic=ic, sub=sub: t.matmul(p_d[:, 0:512], selS[:, ic * 128:(ic + 1) * 128], uf[ic][:, sub * 512:(sub + 1) * 512], start=(ic == 0), stop=(ic == 3)))
                        seq.op("vector", lambda v, sub=sub: v.tensor_copy(uown[:, sub * 512:(sub + 1) * 512], p_d[:, 0:512]))
                    for sub in range(NS2):
                        for ic in range(4):
                            seq.op("tensor", lambda t, ic=ic, sub=sub: t.matmul(p_d[:, 0:512], wdS[ic][:, :], ub[ic][:, sub * 512:(sub + 1) * 512], start=(ic == 0), stop=(ic == 3)))
                        seq.op("scalar", lambda s, sub=sub: s.activation(sc1[:, 0:512], p_d[:, 0:512], AF.Identity, bias=bdcS[:, 0:1]))
                        seq.op("scalar", lambda s: s.activation(sc2[:, 0:512], sc1[:, 0:512], AF.Abs))
                        seq.op("scalar", lambda s: s.activation(sc2[:, 0:512], sc2[:, 0:512], AF.Exp, scale=C_M1))
                        seq.op("scalar", lambda s: s.activation(sc2[:, 0:512], sc2[:, 0:512], AF.Ln, bias=C_ONE))
                        seq.op("vector", lambda v: v.tensor_scalar(sc1[:, 0:512], sc1[:, 0:512], 0.0, None, ALU.max))
                        seq.op("vector", lambda v, sub=sub: v.tensor_tensor(deltaS[:, sub * 512:(sub + 1) * 512], sc1[:, 0:512], sc2[:, 0:512], ALU.add))
                    seq.op("vector", lambda v: v.tensor_tensor(wSt[:, :], deltaS[:, :], uown[:, :], ALU.mult))
                    for nh in range(2):
                        for sub in range(NS2):
                            for ic in range(4):
                                seq.op("tensor", lambda t, ic=ic, nh=nh, sub=sub: t.matmul(p_d[:, 0:512], wbS[ic][:, nh * 128:(nh + 1) * 128], ub[ic][:, sub * 512:(sub + 1) * 512], start=(ic == 0), stop=(ic == 3)))
                            seq.op("vector", lambda v, nh=nh, sub=sub: v.tensor_copy(BTb[nh][:, sub * 512:(sub + 1) * 512], p_d[:, 0:512]))
                            for ic in range(4):
                                seq.op("tensor", lambda t, ic=ic, nh=nh, sub=sub: t.matmul(p_d[:, 0:512], wcS[ic][:, nh * 128:(nh + 1) * 128], ub[ic][:, sub * 512:(sub + 1) * 512], start=(ic == 0), stop=(ic == 3)))
                            seq.op("vector", lambda v, nh=nh, sub=sub: v.tensor_copy(CTb[nh][:, sub * 512:(sub + 1) * 512], p_d[:, 0:512]))
                    for dg in range(4):
                        for sub in range(NS2):
                            seq.op("tensor", lambda t, dg=dg, sub=sub: t.matmul(p_d[:, 0:512], patDS[:, dg * 128:(dg + 1) * 128], deltaS[:, sub * 512:(sub + 1) * 512], start=True, stop=True))
                            seq.op("vector", lambda v, dg=dg, sub=sub: v.tensor_copy(dRep[dg][:, sub * 512:(sub + 1) * 512], p_d[:, 0:512]))
                            seq.op("tensor", lambda t, dg=dg, sub=sub: t.matmul(p_d[:, 0:512], patDS[:, dg * 128:(dg + 1) * 128], wSt[:, sub * 512:(sub + 1) * 512], start=True, stop=True))
                            seq.op("vector", lambda v, dg=dg, sub=sub: v.tensor_copy(wRep[dg][:, sub * 512:(sub + 1) * 512], p_d[:, 0:512]))
                    for ng in range(64):
                        j, nh = ng % 32, ng // 32
                        for sub in range(NS2):
                            seq.op("tensor", lambda t, j=j, nh=nh, sub=sub: t.matmul(p_B[:, sub * 512:(sub + 1) * 512], patBS[:, j * 128:(j + 1) * 128], BTb[nh][:, sub * 512:(sub + 1) * 512], start=True, stop=True))
                            seq.op("tensor", lambda t, j=j, nh=nh, sub=sub: t.matmul(p_C[:, sub * 512:(sub + 1) * 512], patBS[:, j * 128:(j + 1) * 128], CTb[nh][:, sub * 512:(sub + 1) * 512], start=True, stop=True))
                        for dg in range(4):
                            g = ng * 4 + dg
                            seq.op("scalar", lambda s, dg=dg, ng=ng: s.activation(dAS[:, :], dRep[dg][:, :], AF.Exp, scale=acol[:, ng * 4 + dg:ng * 4 + dg + 1]))
                            seq.op("vector", lambda v, dg=dg: v.tensor_tensor(bS[:, :], wRep[dg][:, :], p_B[:, :], ALU.mult))
                            seq.op("vector", lambda v, g=g: v.tensor_tensor_scan(hS[:, :], dAS[:, :], bS[:, :], hstate[:, g:g + 1], ALU.mult, ALU.add))
                            seq.op("vector", lambda v, g=g: v.tensor_copy(hstate[:, g:g + 1], hS[:, TCH - 1:TCH]))
                            seq.op("vector", lambda v: v.tensor_tensor(ymulb[:, :], hS[:, :], p_C[:, :], ALU.mult))
                            for sub in range(NS2):
                                seq.op("tensor", lambda t, dg=dg, ng=ng, sub=sub: t.matmul(p_Y[:, sub * 512:(sub + 1) * 512], patYS[:, dg * 128:(dg + 1) * 128], ymulb[:, sub * 512:(sub + 1) * 512], start=(ng == 0 and dg == 0), stop=(ng == 63 and dg == 3)))
                    for sub in range(NS2):
                        seq.op("vector", lambda v, sub=sub: v.scalar_tensor_tensor(ysS[:, sub * 512:(sub + 1) * 512], uown[:, sub * 512:(sub + 1) * 512], dcolS[:, 0:1], p_Y[:, sub * 512:(sub + 1) * 512], ALU.mult, ALU.add))
                    seq.dma("sync", ysIn.ap()[:, ks:ks + TCH], ysS[:, :])

        if 4 in phases: _phase4()
        if with_cc: seq.cc(lambda g: g.collective_compute(
            "AllGather", ALU.bypass, replica_groups=GROUPS,
            ins=[ysIn.ap().opt()], outs=[ysAG.ap().opt()]))

        def _phase5():
            with ExitStack() as es:
                sb = lambda n, s, d=F32: es.enter_context(nc.sbuf_tensor(n, s, d)).ap()
                arena1 = sb("e_arena1", [128, NHC * 1024], BF16)
                arena2 = sb("e_arena2", [128, 8 * FF1], BF16)
                x3bf = [sb(f"e_x3bf{i}", [128, TC], BF16) for i in range(8)]
                tmA = sb("e_tmA", [128, E])
                tmB = sb("e_tmB", [128, E])
                tmC = sb("e_tmC", [128, E])
                xtr = sb("e_xtr", [128, 128])
                wredf = sb("e_wredf", [128, 8 * GATE])
                wgS = sb("e_wgS", [8, GATE])
                wgT = sb("e_wgT", [16, 8])
                redT = sb("e_redT", [16, 128])
                bdr = sb("e_bdr", [1, E], BF16)
                b2S = sb("e_b2S", [1, E], BF16)
                b1S = sb("e_b1S", [128, NHC])
                onesb = sb("e_onesb", [1, 128], BF16)
                sident = sb("e_sident", [128, 128])
                st1 = sb("e_st1", [128, 16])
                sel8 = sb("e_sel8", [128, 8])
                wsel = sb("e_wsel", [128, NTB * 8])
                cols = sb("e_cols", [128, 4])
                yev = sb("e_yev", [128, 512])
                p_tr = es.enter_context(nc.psum_tensor("e_ptr", [128, 512], F32)).ap()[:, 0:128]
                p_a = es.enter_context(nc.psum_tensor("e_pa", [128, 512], F32)).ap()
                p_b = es.enter_context(nc.psum_tensor("e_pb", [128, 512], F32)).ap()
                p_g = es.enter_context(nc.psum_tensor("e_pg", [128, 512], F32)).ap()[:, 0:16]
                p_r = es.enter_context(nc.psum_tensor("e_pr", [128, 512], F32)).ap()[:, 0:16]

                a1f = arena1.bitcast(F32)
                ysS = [a1f[:, i * TC:(i + 1) * TC] for i in range(4)]
                srS = [a1f[:, (4 + i) * TC:(5 + i) * TC] for i in range(4)]
                hG = [arena1[:, i * 1024:(i + 1) * 1024] for i in range(NHC)]
                gdS = [arena2[:, i * TC:(i + 1) * TC] for i in range(4)]
                wdnS = [arena2[:, 4 * TC + i * E:4 * TC + (i + 1) * E] for i in range(4)]
                w1S = [arena2[:, i * FF1:(i + 1) * FF1] for i in range(8)]
                w2S = [arena2[:, i * 1024:(i + 1) * 1024] for i in range(NHC)]

                seq.dma("sync", sident[:, :], identf)
                seq.dma("sync", wredf.rearrange("p (i g) -> p i g", i=8), wred.rearrange("(i p) g -> p i g", p=128))
                seq.dma("sync", wgS[:, :], wg)
                seq.dma("gpsimd", bdr[:, :], bdrow)
                seq.op("vector", lambda v: v.memset(cols[:, 0:1], 1e-5))
                seq.op("vector", lambda v: v.memset(cols[:, 1:2], 0.0))
                seq.op("vector", lambda v: v.memset(onesb[:, :], 1.0))
                EPS, ZERO = cols[:, 0:1], cols[:, 1:2]
                seq.op("scalar", lambda s: s.activation(st1[0:8, 0:GATE], wgS[:, :], AF.Square))
                seq.op("vector", lambda v: v.tensor_reduce(sel8[0:8, 0:1], st1[0:8, 0:GATE], AX.XYZW, ALU.add))
                seq.op("scalar", lambda s: s.activation(sel8[0:8, 1:2], sel8[0:8, 0:1], AF.Sqrt, bias=ZERO[0:8, :]))
                seq.op("vector", lambda v: v.tensor_scalar(sel8[0:8, 1:2], sel8[0:8, 1:2], 1e-4, None, ALU.max))
                seq.op("vector", lambda v: v.reciprocal(sel8[0:8, 2:3], sel8[0:8, 1:2]))
                seq.op("vector", lambda v: v.tensor_scalar(wgS[:, :], wgS[:, :], sel8[0:8, 2:3], None, ALU.mult))
                seq.op("tensor", lambda t: t.transpose(p_g[0:16, 0:8], wgS[:, :], sident[0:8, 0:8]))
                seq.op("vector", lambda v: v.tensor_copy(wgT[:, :], p_g[0:16, 0:8]))
                for i in range(4):
                    def _ys(g, i=i):
                        g.reg_alu(rtmp, rj, i * 128 * T, ALU.add)
                        src = bass.AP(ysAG, rtmp, [[T, 128], [512, NSL], [1, 128]])
                        return g.dma_start(out=ysS[i].rearrange("p (a q) -> p a q", a=NSL), in_=src)
                    seq.op("gpsimd", _ys, inc=16)
                    seq.dma("sync", srS[i], rightD[i * 128:(i + 1) * 128, :])
                    seq.op("scalar", lambda s, i=i: s.activation(srS[i], srS[i], AF.Silu))
                    seq.op("vector", lambda v, i=i: v.tensor_tensor(gdS[i], ysS[i], srS[i], ALU.mult))
                    seq.dma("gpsimd", wdnS[i], wdown[i * 128:(i + 1) * 128, :])
                for tb in range(NTB):
                    seq.dma("sync", tmA[:, :], x2d[tb * 128:tb * 128 + 128, :])
                    for oh in range(2):
                        for ic in range(4):
                            seq.op("tensor", lambda t, ic=ic, oh=oh, tb=tb: t.matmul(p_a[:, 0:512], gdS[ic][:, tb * 128:(tb + 1) * 128], wdnS[ic][:, oh * 512:(oh + 1) * 512], start=(ic == 0), stop=False))
                        seq.op("tensor", lambda t, oh=oh: t.matmul(p_a[:, 0:512], onesb[0:1, :], bdr[0:1, oh * 512:(oh + 1) * 512], start=False, stop=True))
                        seq.op("vector", lambda v, oh=oh: v.scalar_tensor_tensor(tmB[:, oh * 512:(oh + 1) * 512], tmA[:, oh * 512:(oh + 1) * 512], 2.0, p_a[:, 0:512], ALU.mult, ALU.add))
                    seq.op("vector", lambda v: v.tensor_reduce(st1[:, 0:1], tmB[:, :], AX.XYZW, ALU.add))
                    seq.op("scalar", lambda s: s.activation(tmC[:, :], tmB[:, :], AF.Square))
                    seq.op("vector", lambda v: v.tensor_reduce(st1[:, 1:2], tmC[:, :], AX.XYZW, ALU.add))
                    seq.op("vector", lambda v: v.tensor_scalar(st1[:, 2:3], st1[:, 0:1], 1.0 / E, None, ALU.mult))
                    seq.op("vector", lambda v: v.tensor_scalar(st1[:, 3:4], st1[:, 1:2], 1.0 / E, None, ALU.mult))
                    seq.op("vector", lambda v: v.tensor_tensor(st1[:, 4:5], st1[:, 2:3], st1[:, 2:3], ALU.mult))
                    seq.op("vector", lambda v: v.tensor_tensor(st1[:, 5:6], st1[:, 3:4], st1[:, 4:5], ALU.subtract))
                    seq.op("scalar", lambda s: s.activation(st1[:, 6:7], st1[:, 5:6], AF.Sqrt, bias=EPS))
                    seq.op("vector", lambda v: v.reciprocal(st1[:, 7:8], st1[:, 6:7]))
                    seq.op("vector", lambda v: v.tensor_scalar(tmB[:, :], tmB[:, :], st1[:, 2:3], st1[:, 7:8], ALU.subtract, ALU.mult))
                    seq.dma("sync", x3d[tb * 128:(tb + 1) * 128, :], tmB[:, :])
                    for fc in range(8):
                        seq.op("tensor", lambda t, fc=fc: t.transpose(p_tr[:, :], tmB[:, fc * 128:(fc + 1) * 128], sident[:, :]))
                        seq.op("vector", lambda v, fc=fc, tb=tb: v.tensor_copy(x3bf[fc][:, tb * 128:(tb + 1) * 128], p_tr[:, :]))
                        seq.op("vector", lambda v: v.tensor_copy(xtr[:, :], p_tr[:, :]))
                        seq.op("tensor", lambda t, fc=fc: t.matmul(p_r[:, 0:GATE], xtr[:, :], wredf[:, fc * GATE:(fc + 1) * GATE], start=(fc == 0), stop=(fc == 7)))
                    seq.op("vector", lambda v: v.tensor_copy(tmC[:, 0:GATE], p_r[:, 0:GATE]))
                    seq.op("tensor", lambda t: t.transpose(p_tr[0:16, 0:128], tmC[:, 0:GATE], sident[:, :]))
                    seq.op("vector", lambda v: v.tensor_copy(redT[:, :], p_tr[0:16, 0:128]))
                    seq.op("tensor", lambda t: t.matmul(p_g[:, 0:8], redT[:, :], wgT[:, :], start=True, stop=True))
                    seq.op("vector", lambda v: v.tensor_copy(st1[:, 0:8], p_g[:, 0:8]))
                    seq.op("vector", lambda v: v.reduce_max(st1[:, 8:9], st1[:, 0:8], AX.XYZW, negate=True))
                    seq.op("scalar", lambda s: s.activation(tmC[:, 0:8], st1[:, 0:8], AF.Exp, bias=st1[:, 8:9], accum_out=st1[:, 9:10]))
                    seq.op("vector", lambda v: v.reciprocal(st1[:, 10:11], st1[:, 9:10]))
                    seq.op("vector", lambda v: v.tensor_scalar(st1[:, 11:12], st1[:, 8:9], -1.0, None, ALU.mult))
                    seq.op("vector", lambda v: v.tensor_scalar(sel8[:, 0:8], st1[:, 0:8], st1[:, 11:12], None, ALU.is_ge))
                    seq.op("vector", lambda v: v.memset(st1[:, 12:13], 1.0))
                    for e0 in range(8):
                        seq.op("vector", lambda v, e0=e0: v.tensor_tensor(sel8[:, e0:e0 + 1], sel8[:, e0:e0 + 1], st1[:, 12:13], ALU.mult))
                        if e0 < 7:
                            seq.op("vector", lambda v, e0=e0: v.tensor_scalar(st1[:, 13:14], sel8[:, e0:e0 + 1], -1.0, 1.0, ALU.mult, ALU.add))
                            seq.op("vector", lambda v: v.tensor_tensor(st1[:, 12:13], st1[:, 12:13], st1[:, 13:14], ALU.mult))
                    seq.op("vector", lambda v, tb=tb: v.tensor_scalar(wsel[:, tb * 8:(tb + 1) * 8], sel8[:, 0:8], st1[:, 10:11], None, ALU.mult))
                for e in range(NEXP):
                    seq.dma("sync", b1S[:, :], b1c[e])
                    seq.dma("gpsimd", b2S[:, :], b2r[e])
                    for tcg in range(NTCG):
                        t0 = tcg * TSZ2
                        for ic in range(8):
                            seq.dma("gpsimd", w1S[ic], w1f[e, ic * 128:(ic + 1) * 128, :])
                        for hc in range(NHC):
                            for s5 in range(NS5):
                                for ic in range(8):
                                    seq.op("tensor", lambda t, ic=ic, hc=hc, s5=s5, t0=t0: t.matmul(p_a[:, 0:S5], w1S[ic][:, hc * 128:(hc + 1) * 128], x3bf[ic][:, t0 + s5 * 512:t0 + s5 * 512 + S5], start=(ic == 0), stop=(ic == 7)))
                                seq.op("scalar", lambda s, hc=hc, s5=s5: s.activation(hG[hc][:, s5 * 512:s5 * 512 + S5], p_a[:, 0:S5], AF.Gelu, bias=b1S[:, hc:hc + 1]))
                        for hc in range(NHC):
                            seq.dma("gpsimd", w2S[hc], w2f[e, hc * 128:(hc + 1) * 128, :])
                        for tb0 in range(TSZ2 // 128):
                            tb = tcg * (TSZ2 // 128) + tb0
                            for oh in range(2):
                                for hc in range(NHC):
                                    seq.op("tensor", lambda t, hc=hc, tb0=tb0, oh=oh: t.matmul(p_b[:, 0:512], hG[hc][:, tb0 * 128:(tb0 + 1) * 128], w2S[hc][:, oh * 512:(oh + 1) * 512], start=(hc == 0), stop=False))
                                seq.op("tensor", lambda t, oh=oh: t.matmul(p_b[:, 0:512], onesb[0:1, :], b2S[0:1, oh * 512:(oh + 1) * 512], start=False, stop=True))
                                seq.op("vector", lambda v, tb=tb, e=e: v.tensor_scalar(yev[:, 0:512], p_b[:, 0:512], wsel[:, tb * 8 + e:tb * 8 + e + 1], None, ALU.mult))
                                if e == 0:
                                    seq.dma("gpsimd", ydB[tb * 128:(tb + 1) * 128, oh * 512:(oh + 1) * 512], yev[:, 0:512])
                                else:
                                    seq.dma("gpsimd", ydB[tb * 128:(tb + 1) * 128, oh * 512:(oh + 1) * 512], yev[:, 0:512], accum_op=ALU.add)
                for tb in range(NTB):
                    seq.dma("sync", tmA[:, :], x3d[tb * 128:tb * 128 + 128, :])
                    seq.dma("sync", tmB[:, :], ydB[tb * 128:tb * 128 + 128, :])
                    seq.op("vector", lambda v: v.tensor_tensor(tmA[:, :], tmA[:, :], tmB[:, :], ALU.add))
                    seq.op("vector", lambda v: v.tensor_reduce(st1[:, 0:1], tmA[:, :], AX.XYZW, ALU.add))
                    seq.op("scalar", lambda s: s.activation(tmC[:, :], tmA[:, :], AF.Square))
                    seq.op("vector", lambda v: v.tensor_reduce(st1[:, 1:2], tmC[:, :], AX.XYZW, ALU.add))
                    seq.op("vector", lambda v: v.tensor_scalar(st1[:, 2:3], st1[:, 0:1], 1.0 / E, None, ALU.mult))
                    seq.op("vector", lambda v: v.tensor_scalar(st1[:, 3:4], st1[:, 1:2], 1.0 / E, None, ALU.mult))
                    seq.op("vector", lambda v: v.tensor_tensor(st1[:, 4:5], st1[:, 2:3], st1[:, 2:3], ALU.mult))
                    seq.op("vector", lambda v: v.tensor_tensor(st1[:, 5:6], st1[:, 3:4], st1[:, 4:5], ALU.subtract))
                    seq.op("scalar", lambda s: s.activation(st1[:, 6:7], st1[:, 5:6], AF.Sqrt, bias=EPS))
                    seq.op("vector", lambda v: v.reciprocal(st1[:, 7:8], st1[:, 6:7]))
                    seq.op("vector", lambda v: v.tensor_scalar(tmA[:, :], tmA[:, :], st1[:, 2:3], st1[:, 7:8], ALU.subtract, ALU.mult))
                    seq.dma("sync", outT[tb * 128:(tb + 1) * 128, :], tmA[:, :])
        if 5 in phases: _phase5()
        if debug:
            seq.dma("sync", dbgL.ap()[:, :], leftIn.ap()[:, :])
            seq.dma("sync", dbgLA.ap()[:, :], leftAG.ap()[:, :])
            seq.dma("sync", dbgY.ap()[:, :], ysIn.ap()[:, :])
            seq.dma("sync", dbgYA.ap()[:, :], ysAG.ap()[:, :])
        seq.build(block)
    _es_sems.close()
    return nc, seq


def host_fused_inputs(T, inp, gumbel):
    m1 = host_l1_inputs(T, inp, gumbel)
    m2 = host_l2_inputs(T, inp, np.zeros((2, T, 1), np.float32))
    m3 = host_l3_inputs(T, inp, inp["L0_attn_hidden_states"])
    maps = []
    cw = inp["L1_attn_conv_w"][:, 0, :]
    cb = inp["L1_attn_conv_b"]
    Wd = inp["L1_attn_W_delta"]; bd = inp["L1_attn_b_delta"]
    WB_ = inp["L1_attn_W_B"]; WC_ = inp["L1_attn_W_C"]
    Alog = inp["L1_attn_A_log"]; Dv = inp["L1_attn_D"]
    patDh = np.zeros((128, 4, 128), np.float32)
    for dg in range(4):
        for p in range(128):
            patDh[dg * 32 + p // 4, dg, p] = 1.0
    patBh = np.zeros((128, 32, 128), np.float32)
    for j in range(32):
        for p in range(128):
            patBh[j * 4 + p % 4, j, p] = 1.0
    patYh = np.zeros((128, 4, 128), np.float32)
    for dg in range(4):
        for p in range(128):
            patYh[p, dg, dg * 32 + p // 4] = 1.0
    b1cols = np.ascontiguousarray(inp["L1_ffn_b1"].reshape(NEXP, FF1 // 128, 128).transpose(0, 2, 1))
    for c in range(8):
        b, ds = c // 4, c % 4
        dlo = ds * 128
        p = np.arange(128)
        i_, nsub = p // 4, p % 4
        aarr = np.zeros((128, 256), np.float32)
        for dg in range(4):
            for ng in range(64):
                aarr[:, ng * 4 + dg] = Alog[dlo + dg * 32 + i_, ng * 4 + nsub]
        selp = np.zeros((128, 4, 128), np.float32)
        selp[:, ds] = np.eye(128, dtype=np.float32)
        mp = dict(m1[c])
        mp.update(
            x0=np.ascontiguousarray(host_core_token_slice(inp["L0_attn_hidden_states"], T, c)),
            gateT=m2[c]["gateT"], w1T=m2[c]["w1T"], w3T=m2[c]["w3T"], w2T=m2[c]["w2T"],
            grow=m3[c]["grow"], brow=m3[c]["brow"], wup=m3[c]["wup"], bupc=m3[c]["bupc"],
            cwc=np.ascontiguousarray(cw.reshape(4, 128, 3).transpose(1, 0, 2)),
            cbc=np.ascontiguousarray(cb.reshape(4, 128).T),
            wdel=np.ascontiguousarray(Wd[:, dlo:dlo + 128]),
            bdc=np.ascontiguousarray(bd[dlo:dlo + 128, None]),
            wB=np.ascontiguousarray(WB_), wC=np.ascontiguousarray(WC_),
            aarr=aarr, dcol=np.ascontiguousarray(Dv[dlo:dlo + 128, None]),
            patD=np.ascontiguousarray(patDh.reshape(128, 512)),
            patB=np.ascontiguousarray(patBh.reshape(128, 32 * 128)),
            patY=np.ascontiguousarray(patYh.reshape(128, 512)),
            selpat=np.ascontiguousarray(selp.reshape(128, 512)),
            wdown=np.ascontiguousarray(inp["L1_attn_W_down"]),
            bdrow=np.ascontiguousarray(inp["L1_attn_b_down"][None, :]),
            wred=np.ascontiguousarray(inp["L1_ffn_w_red"]),
            wg=np.ascontiguousarray(inp["L1_ffn_wg"]),
            w1f=np.ascontiguousarray(inp["L1_ffn_w1"]), b1c=b1cols,
            w2f=np.ascontiguousarray(inp["L1_ffn_w2"]),
            b2r=np.ascontiguousarray(inp["L1_ffn_b2"][:, None, :]),
            jval=np.array([[128 * (c % 4)]], np.int32))
        maps.append(mp)
    return maps


_BUILT = {}

def _get_sched(T):
    key = ("fused", T)
    if key not in _BUILT:
        _p, sq = build_fused(T, seq_mode="record")
        waits = sq.compute_waits()
        nc, _ = build_fused(T, seq_mode="sched", seq_waits=waits)
        _BUILT[key] = nc
    return _BUILT[key]

def kernel(**inputs):
    from concourse.bass_utils import run_bass_kernel_spmd
    T = inputs["L0_attn_hidden_states"].shape[1]
    inp = {k: np.asarray(v) for k, v in inputs.items()}
    import jax
    cpu = jax.devices("cpu")[0]
    with jax.default_device(cpu):
        import jax.numpy as jnp
        u = jax.random.uniform(jax.random.key(42, impl="threefry2x32"), (2, T, T), jnp.float32, 1e-06, 1.0 - 1e-06)
        gumbel = np.asarray(-jnp.log(-jnp.log(u)), np.float32)
    nc = _get_sched(T)
    maps = host_fused_inputs(T, inp, gumbel)
    del gumbel
    res = run_bass_kernel_spmd(nc, maps, core_ids=list(range(8)))
    out = host_assemble_full([r.get("out") for r in res.results], T)
    return out.astype(np.float32)
